# version 01
# baseline (speedup 1.0000x reference)
import numpy as np
from contextlib import ExitStack
import ml_dtypes
import concourse.bass as bass
import concourse.mybir as mybir
from concourse.bass_utils import run_bass_kernel_spmd
from concourse.bass_core_rust import IndirectOffsetOnAxis

F32 = mybir.dt.float32
BF16 = mybir.dt.bfloat16
U32 = mybir.dt.uint32
AF = mybir.ActivationFunctionType
ALU = mybir.AluOpType
AX = mybir.AxisListType

NTOK = 8192
NTILE = 16
D = 1024
E = 8
FTRUE = 2730
FP = 2816
NFC = FP // 128
CAP = 1280
NSLOT = CAP + 128
YROWS = NTOK + 128
SCL = 0.08838834764831845
SCG = 1.0 / 16.0
EPS = 1e-5
NEG = -30000.0
BIG = float(1 << 20)
BISECT_ITERS = 30


class EM:
    ENGINES = ("sync", "gpsimd", "vector", "scalar", "tensor")

    def __init__(self):
        self.prog = {e: [] for e in self.ENGINES}
        self.sem = {}

    def add_sem(self, name, handle):
        self.sem[name] = [handle, 0]

    def v(self, name):
        return self.sem[name][1]

    def emit(self, eng, fn, waits=(), inc=None, k=1):
        ws = [(self.sem[n][0], val) for (n, val) in waits if val > 0]
        post = None
        if inc is not None:
            self.sem[inc][1] += k
            post = (self.sem[inc][0], k)

        def run(engobj, fn=fn, ws=ws, post=post, eng=eng):
            for h, val in ws:
                engobj.wait_ge(h, val)
            inst = fn(engobj)
            if eng == "vector":
                inst = engobj.drain()
            if post is not None:
                inst.then_inc(post[0], post[1])
        self.prog[eng].append(run)

    def run_engine(self, name, engobj):
        for fn in self.prog[name]:
            fn(engobj)


def _consts_a():
    ident_bf = np.eye(128, dtype=ml_dtypes.bfloat16)
    slide = np.zeros((128, 4, 192), dtype=ml_dtypes.bfloat16)
    for c in range(4):
        for p in range(128):
            slide[p, c, 64 + 2 * c + (1 if p >= 64 else 0)] = 1.0 / 64.0
    mask = np.full((128, 128), NEG, dtype=np.float32)
    mask[0:64, 0:64] = 0.0
    mask[64:128, 64:128] = 0.0
    return ident_bf, slide.reshape(128, 4 * 192), mask



def build_launch_a(dbg_stage=None):
    nc = bass.Bass("TRN2", target_bir_lowering=False, detect_race_conditions=False)

    x_dr = nc.dram_tensor("x", [NTOK, D], F32, kind="ExternalInput")
    wInT_l_dr = nc.dram_tensor("wInT_l", [D, 3 * D], F32, kind="ExternalInput")
    wOutT_l_dr = nc.dram_tensor("wOutT_l", [D, D], F32, kind="ExternalInput")
    wInT_g_dr = nc.dram_tensor("wInT_g", [D, 3 * D], F32, kind="ExternalInput")
    wOutT_g_dr = nc.dram_tensor("wOutT_g", [D, D], F32, kind="ExternalInput")
    bInL_dr = nc.dram_tensor("bInL_c", [128, 24], F32, kind="ExternalInput")
    bInG_dr = nc.dram_tensor("bInG_c", [128, 24], F32, kind="ExternalInput")
    bOutL_dr = nc.dram_tensor("bOutL_c", [128, 8], F32, kind="ExternalInput")
    bOutG_dr = nc.dram_tensor("bOutG_c", [128, 8], F32, kind="ExternalInput")
    bInL_row_dr = nc.dram_tensor("bInL_r", [1, D], F32, kind="ExternalInput")
    bInG_row_dr = nc.dram_tensor("bInG_r", [1, D], F32, kind="ExternalInput")
    wr_dr = nc.dram_tensor("wr_l", [128, 8, E], F32, kind="ExternalInput")
    br_dr = nc.dram_tensor("br_c", [E, 1], F32, kind="ExternalInput")

    x1_dr = nc.dram_tensor("x1", [NTOK, D], F32, kind="ExternalOutput")
    xln2_dr = nc.dram_tensor("xln2", [NTOK, D], BF16, kind="ExternalOutput")
    sm_dr = nc.dram_tensor("sm", [E, NTOK], F32, kind="ExternalOutput")
    dbg_dr = nc.dram_tensor("dbg", [128, 8192], F32, kind="ExternalOutput") if dbg_stage else None

    identc, slidec, maskc = _consts_a()
    ident_dr = nc.inline_tensor(identc, "ident_bf")
    identf_dr = nc.inline_tensor(np.eye(128, dtype=np.float32), "ident_f32")
    slide_dr = nc.inline_tensor(slidec, "slide")
    mask_dr = nc.inline_tensor(maskc, "segmask")
    onesb_dr = nc.inline_tensor(np.ones((128, 128), dtype=ml_dtypes.bfloat16), "onesb")

    em = EM()

    with ExitStack() as ctx:
        identb = ctx.enter_context(nc.sbuf_tensor("identb", [128, 128], BF16))
        ident_f32_s = ctx.enter_context(nc.sbuf_tensor("ident_f32_s", [128, 128], F32))
        slide_s = ctx.enter_context(nc.sbuf_tensor("slide_s", [128, 4, 192], BF16))
        mask_s = ctx.enter_context(nc.sbuf_tensor("mask_s", [128, 128], F32))
        onesb_s = ctx.enter_context(nc.sbuf_tensor("onesb_s", [128, 128], BF16))
        epsc = ctx.enter_context(nc.sbuf_tensor("epsc", [128, 1], F32))
        bInL_c = ctx.enter_context(nc.sbuf_tensor("bInLc_s", [128, 24], F32))
        bInG_c = ctx.enter_context(nc.sbuf_tensor("bInGc_s", [128, 24], F32))
        bOutSum = ctx.enter_context(nc.sbuf_tensor("bOutSum", [128, 8], F32))
        bOutG_c = ctx.enter_context(nc.sbuf_tensor("bOutGc_s", [128, 8], F32))
        bOutL_c = ctx.enter_context(nc.sbuf_tensor("bOutLc_s", [128, 8], F32))
        bInL_r = ctx.enter_context(nc.sbuf_tensor("bInLr_s", [1, D], BF16))
        bInG_r = ctx.enter_context(nc.sbuf_tensor("bInGr_s", [1, D], BF16))
        bvl_b = ctx.enter_context(nc.sbuf_tensor("bvl_b", [128, 1024], F32))
        bvg_b = ctx.enter_context(nc.sbuf_tensor("bvg_b", [128, 1024], F32))
        wr_s = ctx.enter_context(nc.sbuf_tensor("wr_s", [128, 8, E], BF16))
        br_c = ctx.enter_context(nc.sbuf_tensor("brc_s", [E, 1], F32))
        means0 = ctx.enter_context(nc.sbuf_tensor("means0", [64, 1024], BF16))
        means1 = ctx.enter_context(nc.sbuf_tensor("means1", [64, 1024], BF16))
        means_fm = ctx.enter_context(nc.sbuf_tensor("means_fm", [128, 8, 128], BF16))
        globB = ctx.enter_context(nc.sbuf_tensor("globB", [128, 8, 128], F32))
        sc_s = ctx.enter_context(nc.sbuf_tensor("sc_s", [128, 4], F32))
        sc_q = ctx.enter_context(nc.sbuf_tensor("sc_q", [128, 4], F32))
        sc_m = ctx.enter_context(nc.sbuf_tensor("sc_m", [128, 4], F32))
        sc_v = ctx.enter_context(nc.sbuf_tensor("sc_v", [128, 4], F32))
        sc_r = ctx.enter_context(nc.sbuf_tensor("sc_r", [128, 4], F32))
        sc_nmr = ctx.enter_context(nc.sbuf_tensor("sc_nmr", [128, 4], F32))
        smx8 = ctx.enter_context(nc.sbuf_tensor("smx8", [128, 8], F32))
        ssum8 = ctx.enter_context(nc.sbuf_tensor("ssum8", [128, 8], F32))
        x_s = ctx.enter_context(nc.sbuf_tensor("x_s", [128, 2, 4, 1024], F32))
        xln_s = ctx.enter_context(nc.sbuf_tensor("xln_s", [128, 4, 1024], BF16))
        xln_fm = ctx.enter_context(nc.sbuf_tensor("xln_fm", [128, 8, 512], BF16))
        qk_sb = ctx.enter_context(nc.sbuf_tensor("qk_sb", [128, 16, 512], BF16))
        v_tm = ctx.enter_context(nc.sbuf_tensor("v_tm", [128, 4, 1024], BF16))
        scm_s = ctx.enter_context(nc.sbuf_tensor("scm_s", [128, 8, 128], F32))
        prob_f = ctx.enter_context(nc.sbuf_tensor("prob_f", [128, 8, 128], F32))
        prob_b = ctx.enter_context(nc.sbuf_tensor("prob_b", [128, 8, 128], BF16))
        probT = ctx.enter_context(nc.sbuf_tensor("probT", [128, 8, 128], BF16))
        trash = prob_b[:].rearrange("p a k -> p (a k)")
        attn_fm = ctx.enter_context(nc.sbuf_tensor("attn_fm", [128, 8, 512], BF16))
        xln2_s = ctx.enter_context(nc.sbuf_tensor("xln2_s", [128, 4, 1024], BF16))
        sum_fm = xln_s[:].rearrange("p c (u k) -> p (c u) k", u=2)
        xln2_fm = ctx.enter_context(nc.sbuf_tensor("xln2_fm", [128, 8, 512], BF16))
        logit_em = ctx.enter_context(nc.sbuf_tensor("logit_em", [8, 512], F32))
        sm_tm = ctx.enter_context(nc.sbuf_tensor("sm_tm", [128, 4, 8], F32))
        sm_mx = ctx.enter_context(nc.sbuf_tensor("sm_mx", [128, 4], F32))
        sm_sum = ctx.enter_context(nc.sbuf_tensor("sm_sum", [128, 4], F32))
        sm_em = ctx.enter_context(nc.sbuf_tensor("sm_em", [8, 512], F32))
        qkg = qk_sb[:, 0:4, :].rearrange("p a (b k) -> p (a b) k", k=128)
        vg_tm = qk_sb[:, 4:6, :].rearrange("p a k -> p (a k)")
        vg_pair = qk_sb[0:64, 6:10, :].rearrange("p a k -> p (a k)").rearrange("p (b f) -> p b f", b=2)
        sg_f = scm_s[:, 0:2, :].rearrange("p a k -> p (a k)").rearrange("p (a k) -> p a k", a=4)
        pg_f = scm_s[:, 2:4, :].rearrange("p a k -> p (a k)").rearrange("p (a k) -> p a k", a=4)
        pg_b = prob_b[:, 0:2, :].rearrange("p a k -> p (a k)").rearrange("p (a k) -> p a k", a=4)
        pgT = probT[0:64, 0:4, :].rearrange("p a k -> p (a k)").rearrange("p (a k) -> p a k", a=4)
        ag_fm = attn_fm[:, :, 0:128]
        gmx = ctx.enter_context(nc.sbuf_tensor("gmx", [128, 4], F32))
        gsum = ctx.enter_context(nc.sbuf_tensor("gsum", [128, 4], F32))
        wInTl = ctx.enter_context(nc.sbuf_tensor("wInTl", [128, 8, 3 * D], BF16))
        wOutTl = ctx.enter_context(nc.sbuf_tensor("wOutTl", [128, 8, D], BF16))
        wInTg = wInTl
        wOutTg = wOutTl
        pm = ctx.enter_context(nc.psum_tensor("pm", [128, 1024], F32))
        ps_a = ctx.enter_context(nc.psum_tensor("ps_a", [128, 512], F32))
        ps_b = ctx.enter_context(nc.psum_tensor("ps_b", [128, 512], F32))
        ps_c = ctx.enter_context(nc.psum_tensor("ps_c", [128, 512], F32))
        ps_d = ctx.enter_context(nc.psum_tensor("ps_d", [128, 512], F32))
        ps_t = ctx.enter_context(nc.psum_tensor("ps_t", [128, 512], BF16))
        ps_t2 = ctx.enter_context(nc.psum_tensor("ps_t2", [128, 512], BF16))
        h_ld = ctx.enter_context(nc.semaphore("s_ld"))
        h_x0 = ctx.enter_context(nc.semaphore("s_x0"))
        h_x1 = ctx.enter_context(nc.semaphore("s_x1"))
        h_st = ctx.enter_context(nc.semaphore("s_st"))
        h_gw = ctx.enter_context(nc.semaphore("s_gw"))
        h_sm = ctx.enter_context(nc.semaphore("s_sm"))
        h_ve = ctx.enter_context(nc.semaphore("s_ve"))
        h_ac = ctx.enter_context(nc.semaphore("s_ac"))
        h_pe = ctx.enter_context(nc.semaphore("s_pe"))
        block = ctx.enter_context(nc.Block())
        for nm, hh in (("ld", h_ld), ("x0", h_x0), ("x1", h_x1), ("st", h_st),
                       ("gw", h_gw), ("sm", h_sm), ("ve", h_ve), ("ac", h_ac), ("pe", h_pe)):
            em.add_sem(nm, hh)

        for dst, src in (
            (identb[:], ident_dr.ap()),
            (ident_f32_s[:], identf_dr.ap()),
            (slide_s[:].rearrange("p c k -> p (c k)"), slide_dr.ap()),
            (mask_s[:], mask_dr.ap()),
            (onesb_s[:], onesb_dr.ap()),
            (bInL_c[:], bInL_dr.ap()),
            (bInG_c[:], bInG_dr.ap()),
            (bOutG_c[:], bOutG_dr.ap()),
            (bOutL_c[:], bOutL_dr.ap()),
            (br_c[:], br_dr.ap()),
        ):
            em.emit("sync", lambda s, d=dst, a=src: s.dma_start(d, a), inc="ld", k=16)
        v_consts = em.v("ld")

        for dst, src in (
            (bInL_r[:], bInL_row_dr.ap()),
            (bInG_r[:], bInG_row_dr.ap()),
            (wr_s[:], wr_dr.ap()),
            (wInTg[:], wInT_g_dr.ap().rearrange("(c p) f -> p c f", p=128)),
            (wOutTg[:], wOutT_g_dr.ap().rearrange("(c p) f -> p c f", p=128)),
        ):
            em.emit("gpsimd", lambda g, d=dst, a=src: g.dma_start(d, a), inc="gw", k=16)
        v_weights = em.v("gw")

        em.emit("vector", lambda v: v.memset(epsc[:], EPS), inc="ve")
        em.emit("vector",
                lambda v: v.tensor_tensor(bOutSum[:], bOutG_c[:], bOutL_c[:], ALU.add),
                waits=[("ld", v_consts)], inc="ve")
        v_bsum = em.v("ve")

        for dst, row in ((bvg_b, bInG_r), (bvl_b, bInL_r)):
            for n in range(2):
                em.emit("tensor",
                        lambda t, n=n, row=row: t.matmul(
                            ps_a[:, 0:512], onesb_s[0:1, :],
                            row[0:1, 512 * n:512 * (n + 1)],
                            start=True, stop=True),
                        waits=[("gw", v_weights), ("ld", v_consts), ("ve", em.v("ve"))],
                        inc="pe")
                em.emit("vector",
                        lambda v, n=n, dst=dst: v.tensor_copy(
                            dst[:, 512 * n:512 * (n + 1)], ps_a[:, 0:512]),
                        waits=[("pe", em.v("pe"))], inc="ve")
        v_vbias = em.v("ve")

        def cap(stage, ap, extra_waits=()):
            if dbg_stage != stage:
                return
            def _cap_dma(g, ap=ap):
                dst = dbg_dr.ap()[0:ap.shape[0], 0:int(np.prod(ap.shape[1:]))]
                if len(ap.shape) == 3:
                    dst = dst.rearrange("p (a k) -> p a k", a=ap.shape[1])
                return g.dma_start(dst, ap)
            em.emit("gpsimd", _cap_dma,
                    waits=[("ve", em.v("ve")), ("ac", em.v("ac")), ("pe", em.v("pe"))]
                          + list(extra_waits),
                    inc="sm", k=16)

        cap("wg", wInTg[:, 0:2, :].rearrange("p c f -> p (c f)"), [("gw", v_weights)])

        def emit_ln(src4, dst4, first_waits):
            for c in range(4):
                em.emit("scalar",
                        lambda s, c=c: s.activation(trash[:], src4[:, c, :], AF.Copy,
                                                    bias=0.0, scale=1.0,
                                                    accum_out=sc_s[:, c:c + 1]),
                        waits=(first_waits + [("ve", em.v("ve"))]) if c == 0 else [])
            for c in range(4):
                em.emit("scalar",
                        lambda s, c=c: s.activation(trash[:], src4[:, c, :], AF.Square,
                                                    bias=0.0, scale=1.0,
                                                    accum_out=sc_q[:, c:c + 1]),
                        inc=("ac" if c == 3 else None))
            v_stats = em.v("ac")
            em.emit("vector", lambda v: v.tensor_scalar(sc_m[:], sc_s[:], 1.0 / D, None, ALU.mult),
                    waits=[("ac", v_stats)])
            em.emit("vector", lambda v: v.tensor_scalar(sc_v[:], sc_q[:], 1.0 / D, None, ALU.mult))
            em.emit("vector", lambda v: v.tensor_tensor(sc_r[:], sc_m[:], sc_m[:], ALU.mult))
            em.emit("vector", lambda v: v.tensor_tensor(sc_v[:], sc_v[:], sc_r[:], ALU.subtract),
                    inc="ve")
            v_var = em.v("ve")
            em.emit("scalar", lambda s: s.activation(sc_r[:], sc_v[:], AF.Sqrt,
                                                     bias=epsc[:, 0:1], scale=1.0),
                    waits=[("ve", v_var)], inc="ac")
            v_sd = em.v("ac")
            em.emit("vector", lambda v: v.reciprocal(sc_r[:], sc_r[:]), waits=[("ac", v_sd)])
            em.emit("vector", lambda v: v.tensor_tensor(sc_nmr[:], sc_m[:], sc_r[:], ALU.mult))
            em.emit("vector", lambda v: v.tensor_scalar(sc_nmr[:], sc_nmr[:], -1.0, None, ALU.mult))
            for c in range(4):
                em.emit("vector",
                        lambda v, c=c: v.tensor_scalar(dst4[:, c, :], src4[:, c, :],
                                                       sc_r[:, c:c + 1], sc_nmr[:, c:c + 1],
                                                       ALU.mult, ALU.add),
                        inc=("ve" if c == 3 else None))
            return em.v("ve")

        x_load_done = [0] * (2 * NTILE)
        x_cons_done = [None] * (2 * NTILE)

        def emit_x_load(r):
            t = r % NTILE
            w = list(x_cons_done[r - 2]) if (r >= 2 and x_cons_done[r - 2]) else []
            sem = "x0" if r % 2 == 0 else "x1"
            em.emit("sync",
                    lambda s, t=t, r=r: s.dma_start(
                        x_s[:, r % 2], x_dr.ap()[t * 512:(t + 1) * 512, :].rearrange(
                            "(c p) f -> p c f", p=128)),
                    waits=w, inc=sem, k=16)
            x_load_done[r] = (sem, em.v(sem))

        emit_x_load(0)
        emit_x_load(1)
        cap("x0", x_s[:, 0].rearrange("p c f -> p (c f)"), [x_load_done[0]])

        cap("slide", slide_s[:].rearrange("p c k -> p (c k)"), [("ld", v_consts)])

        for t in range(NTILE):
            r = t
            xb = x_s[:, r % 2]
            v_apply = emit_ln(xb, xln_s, [x_load_done[r], ("pe", em.v("pe"))])
            if t == 0:
                cap("xlnp1", xln_s[:].rearrange("p c f -> p (c f)"))
                if dbg_stage == "stats":
                    for j, buf in enumerate((sc_s, sc_q, sc_m, sc_v, sc_r, sc_nmr)):
                        em.emit("vector",
                                lambda v, j=j, buf=buf: v.tensor_copy(
                                    sm_tm[:].rearrange("p a e -> p (a e)")[:, 4 * j:4 * j + 4],
                                    buf[:]),
                                waits=([("ve", v_apply)] if j == 0 else []),
                                inc=("ve" if j == 5 else None))
                    cap("stats", sm_tm[:].rearrange("p a e -> p (a e)"))
            ti = t % 8
            for c in range(4):
                for u in range(2):
                    em.emit("tensor",
                            lambda tn, c=c, u=u, ti=ti: tn.matmul(
                                pm[:, 512 * u:512 * (u + 1)],
                                slide_s[:, c, 64 - 8 * ti:192 - 8 * ti],
                                xln_s[:, c, 512 * u:512 * (u + 1)],
                                start=(ti == 0 and c == 0), stop=(ti == 7 and c == 3)),
                            waits=([("ve", v_apply)] if (c == 0 and u == 0) else []),
                            inc=("pe" if (c == 3 and u == 1) else None))
            x_cons_done[r] = [("pe", em.v("pe")), ("ve", em.v("ve")), ("ac", em.v("ac"))]
            if ti == 7:
                dstm = means0 if t < 8 else means1
                for u in range(2):
                    em.emit("vector",
                            lambda v, u=u, dstm=dstm: v.tensor_copy(
                                dstm[:, 512 * u:512 * (u + 1)], pm[0:64, 512 * u:512 * (u + 1)]),
                            waits=([("pe", em.v("pe"))] if u == 0 else []),
                            inc=("ve" if u == 1 else None))
                if t == 7:
                    cap("means0", means0[:])
            if r + 2 < 2 * NTILE:
                emit_x_load(r + 2)

        for b in range(2):
            srcm = means0 if b == 0 else means1
            for fc in range(8):
                bank = ps_t if fc % 2 == 0 else ps_t2
                em.emit("tensor",
                        lambda tn, srcm=srcm, fc=fc, bank=bank: tn.transpose(
                            bank[:, 0:64], srcm[:, 128 * fc:128 * (fc + 1)],
                            identb[0:64, 0:64]),
                        waits=[("ve", em.v("ve"))], inc="pe")
                em.emit("vector",
                        lambda v, b=b, fc=fc, bank=bank: v.tensor_copy(
                            means_fm[:, fc, 64 * b:64 * (b + 1)], bank[:, 0:64]),
                        waits=[("pe", em.v("pe"))], inc="ve")
        v_mfm = em.v("ve")
        cap("mfm", means_fm[:].rearrange("p a k -> p (a k)"))

        for m in range(16):
            bank = ps_a if m % 2 == 0 else ps_b
            for k in range(8):
                em.emit("tensor",
                        lambda tn, m=m, k=k, bank=bank: tn.matmul(
                            bank[:, 0:128], wInTg[:, k, 128 * m:128 * (m + 1)],
                            means_fm[:, k, :], start=(k == 0), stop=(k == 7)),
                        waits=(([("ve", v_mfm)] if m == 0 else [("ac", max(0, em.v("ac") - 1))])
                               if k == 0 else []),
                        inc=("pe" if k == 7 else None))
            em.emit("scalar",
                    lambda s, m=m, bank=bank: s.activation(
                        qkg[:, m, :], bank[:, 0:128], AF.Identity,
                        bias=bInG_c[:, m:m + 1], scale=1.0),
                    waits=[("pe", em.v("pe"))], inc="ac")
        v_qkg = em.v("ac")
        cap("qkg", qkg[:].rearrange("p a k -> p (a k)"))

        for n in range(2):
            for k in range(8):
                em.emit("tensor",
                        lambda tn, n=n, k=k: tn.matmul(
                            ps_c[:, 0:512], means_fm[:, k, :],
                            wInTg[:, k, 2048 + 512 * n:2048 + 512 * (n + 1)],
                            start=(k == 0), stop=(k == 7)),
                        waits=([("ve", em.v("ve"))] if k == 0 else []),
                        inc=("pe" if k == 7 else None))
            em.emit("vector",
                    lambda v, n=n: v.tensor_tensor(
                        vg_tm[:, 512 * n:512 * (n + 1)], ps_c[:, 0:512],
                        bvg_b[:, 512 * n:512 * (n + 1)], ALU.add),
                    waits=[("pe", em.v("pe")), ("ve", v_vbias)], inc="ve")
        v_vg = em.v("ve")
        cap("vg", vg_tm[:])

        em.emit("vector", lambda v: v.tensor_copy(vg_pair[:, 0, :], vg_tm[0:64, :]),
                waits=[("ve", v_vg)], inc="ve")
        em.emit("gpsimd", lambda g: g.dma_start(vg_pair[:, 1, :], vg_tm[64:128, :]),
                waits=[("ve", em.v("ve"))], inc="gw", k=16)
        v_vgp_gp = em.v("gw")

        for h in range(4):
            for b in range(2):
                for kc in range(2):
                    em.emit("tensor",
                            lambda tn, h=h, b=b, kc=kc: tn.matmul(
                                ps_a[64 * b:64 * (b + 1), 64 * h:64 * (h + 1)],
                                qkg[:, 2 * h + kc, 64 * b:64 * (b + 1)],
                                qkg[:, 8 + 2 * h + kc, 64 * b:64 * (b + 1)],
                                start=(kc == 0), stop=(kc == 1),
                                tile_position=(0, 64 * b)),
                            waits=([("ac", v_qkg)] if (h == 0 and b == 0 and kc == 0) else []),
                            inc=("pe" if (h == 3 and b == 1 and kc == 1) else None))
        v_sgp = em.v("pe")

        em.emit("vector", lambda v: v.tensor_copy(
            sg_f[:].rearrange("p a k -> p (a k)"), ps_a[:, 0:256]),
            waits=[("pe", v_sgp)], inc="ve")
        cap("sg", sg_f[:].rearrange("p a k -> p (a k)"))
        em.emit("vector", lambda v: v.reduce_max(gmx[:], sg_f[:], axis=AX.X))
        em.emit("vector", lambda v: v.tensor_tensor(
            pg_f[:], sg_f[:],
            gmx[:].rearrange("p (g o) -> p g o", o=1).broadcast_to([128, 4, 64]),
            ALU.subtract), inc="ve")
        em.emit("scalar", lambda s: s.activation(
            pg_f[:].rearrange("p a k -> p (a k)"), pg_f[:].rearrange("p a k -> p (a k)"),
            AF.Exp, bias=0.0, scale=SCG), waits=[("ve", em.v("ve"))], inc="ac")
        em.emit("vector", lambda v: v.reduce_sum(gsum[:], pg_f[:], axis=AX.X),
                waits=[("ac", em.v("ac"))])
        em.emit("vector", lambda v: v.reciprocal(gsum[:], gsum[:]))
        em.emit("vector", lambda v: v.tensor_tensor(
            pg_b[:], pg_f[:],
            gsum[:].rearrange("p (g o) -> p g o", o=1).broadcast_to([128, 4, 64]),
            ALU.mult), inc="ve")
        v_pgb = em.v("ve")
        cap("pgb", pg_b[:].rearrange("p a k -> p (a k)"))

        for h in range(4):
            em.emit("tensor",
                    lambda tn, h=h: tn.transpose(ps_t[0:64, 128 * h:128 * (h + 1)],
                                                 pg_b[:, h, :], identb[:]),
                    waits=([("ve", v_pgb)] if h == 0 else []),
                    inc=("pe" if h == 3 else None))
        em.emit("vector", lambda v: v.tensor_copy(
            pgT[:].rearrange("p a k -> p (a k)"), ps_t[0:64, 0:512]),
            waits=[("pe", em.v("pe"))], inc="ve")
        v_pgT = em.v("ve")

        for b in range(2):
            bank = ps_b if b == 0 else ps_c
            for h in range(4):
                for j in range(2):
                    em.emit("tensor",
                            lambda tn, b=b, h=h, j=j, bank=bank: tn.matmul(
                                bank[:, 64 * (2 * h + j):64 * (2 * h + j) + 64],
                                vg_pair[:, b, 128 * (2 * h + j):128 * (2 * h + j) + 128],
                                pgT[:, h, 64 * b:64 * (b + 1)],
                                start=True, stop=True),
                            waits=([("ve", v_pgT), ("gw", v_vgp_gp)]
                                   if (b == 0 and h == 0 and j == 0) else []),
                            inc=("pe" if (h == 3 and j == 1) else None))
            em.emit("vector",
                    lambda v, b=b, bank=bank: v.tensor_copy(
                        ag_fm[:, :, 64 * b:64 * (b + 1)],
                        bank[:, 0:512].rearrange("p (a k) -> p a k", a=8)),
                    waits=[("pe", em.v("pe"))], inc="ve")
        v_ag = em.v("ve")
        cap("ag", ag_fm[:])

        for m in range(8):
            bank = ps_a if m % 2 == 0 else ps_b
            for k in range(8):
                em.emit("tensor",
                        lambda tn, m=m, k=k, bank=bank: tn.matmul(
                            bank[:, 0:128], wOutTg[:, k, 128 * m:128 * (m + 1)],
                            ag_fm[:, k, :], start=(k == 0), stop=(k == 7)),
                        waits=(([("ve", v_ag)] if m == 0 else [("ac", max(0, em.v("ac") - 1))])
                               if k == 0 else []),
                        inc=("pe" if k == 7 else None))
            em.emit("scalar",
                    lambda s, m=m, bank=bank: s.activation(
                        globB[:, m, :], bank[:, 0:128], AF.Identity,
                        bias=bOutSum[:, m:m + 1], scale=1.0),
                    waits=[("pe", em.v("pe")), ("ve", v_bsum)], inc="ac")
        v_glob = em.v("ac")
        cap("glob", globB[:].rearrange("p a k -> p (a k)"))

        em.emit("gpsimd",
                lambda g: g.dma_start(
                    wInTl[:], wInT_l_dr.ap().rearrange("(c p) f -> p c f", p=128)),
                waits=[("pe", em.v("pe")), ("ac", v_glob)], inc="gw", k=16)
        em.emit("gpsimd",
                lambda g: g.dma_start(
                    wOutTl[:], wOutT_l_dr.ap().rearrange("(c p) f -> p c f", p=128)),
                inc="gw", k=16)
        v_wl = em.v("gw")

        store_x1_prev = 0
        store_xln2_prev = 0
        store_sm_prev = 0
        for t in range(NTILE):
            r = NTILE + t
            xb = x_s[:, r % 2]
            b = t // 8
            ti = t % 8
            v_apply = emit_ln(xb, xln_s, [x_load_done[r], ("pe", em.v("pe"))])
            if t == 0:
                cap("xln1", xln_s[:].rearrange("p c f -> p (c f)"))

            ev_par = {0: 0, 1: 0}
            for g in range(8):
                c, half = divmod(g, 2)
                bank = ps_t if g % 2 == 0 else ps_t2
                for q in range(4):
                    fc = 4 * half + q
                    em.emit("tensor",
                            lambda tn, c=c, fc=fc, q=q, bank=bank: tn.transpose(
                                bank[:, 128 * q:128 * (q + 1)],
                                xln_s[:, c, 128 * fc:128 * (fc + 1)], identb[:]),
                            waits=(([("ve", v_apply)] if g == 0 else [("ve", ev_par[g % 2])])
                                   if q == 0 else []),
                            inc=("pe" if q == 3 else None))
                em.emit("vector",
                        lambda v, c=c, half=half, bank=bank: v.tensor_copy(
                            xln_fm[:, 4 * half:4 * half + 4, 128 * c:128 * (c + 1)],
                            bank[:, 0:512].rearrange("p (a k) -> p a k", a=4)),
                        waits=[("pe", em.v("pe"))], inc="ve")
                ev_par[g % 2] = em.v("ve")
            v_xfm = em.v("ve")

            ev_par = {0: 0, 1: 0}
            for m in range(16):
                bank = ps_a if m % 2 == 0 else ps_b
                for k in range(8):
                    em.emit("tensor",
                            lambda tn, m=m, k=k, bank=bank: tn.matmul(
                                bank[:, 0:512], wInTl[:, k, 128 * m:128 * (m + 1)],
                                xln_fm[:, k, :], start=(k == 0), stop=(k == 7)),
                            waits=(([("ve", v_xfm), ("gw", v_wl)] if m == 0 else
                                    ([("ac", ev_par[m % 2])] if m >= 2 else []))
                                   if k == 0 else []),
                            inc=("pe" if k == 7 else None))
                em.emit("scalar",
                        lambda s, m=m, bank=bank: s.activation(
                            qk_sb[:, m, :], bank[:, 0:512], AF.Identity,
                            bias=bInL_c[:, m:m + 1], scale=1.0),
                        waits=[("pe", em.v("pe"))] + ([("ac", v_glob)] if (t == 0 and m == 0) else []),
                        inc="ac")
                ev_par[m % 2] = em.v("ac")
            v_qk = em.v("ac")
            if t == 0:
                cap("qk0", qk_sb[:, 0:4, :].rearrange("p a k -> p (a k)"))

            ev_par = {0: 0, 1: 0}
            for c in range(4):
                for n in range(2):
                    gidx = 2 * c + n
                    bank = ps_c if gidx % 2 == 0 else ps_d
                    for k in range(8):
                        em.emit("tensor",
                                lambda tn, c=c, n=n, k=k, bank=bank: tn.matmul(
                                    bank[:, 0:512], xln_fm[:, k, 128 * c:128 * (c + 1)],
                                    wInTl[:, k, 2048 + 512 * n:2048 + 512 * (n + 1)],
                                    start=(k == 0), stop=(k == 7)),
                                waits=([("ve", ev_par[gidx % 2])] if (k == 0 and gidx >= 2) else []),
                                inc=("pe" if k == 7 else None))
                    em.emit("vector",
                            lambda v, c=c, n=n, bank=bank: v.tensor_tensor(
                                v_tm[:, c, 512 * n:512 * (n + 1)], bank[:, 0:512],
                                bvl_b[:, 512 * n:512 * (n + 1)], ALU.add),
                            waits=[("pe", em.v("pe"))], inc="ve")
                    ev_par[gidx % 2] = em.v("ve")
            v_v = em.v("ve")

            for c in range(4):
                for h in range(8):
                    bank = ps_a if h < 4 else ps_b
                    em.emit("tensor",
                            lambda tn, c=c, h=h, bank=bank: tn.matmul(
                                bank[:, 128 * (h % 4):128 * (h % 4) + 128],
                                qk_sb[:, h, 128 * c:128 * (c + 1)],
                                qk_sb[:, 8 + h, 128 * c:128 * (c + 1)],
                                start=True, stop=True),
                            waits=([("ac", v_qk), ("ve", em.v("ve"))] if h == 0 else []),
                            inc=("pe" if h == 7 else None))
                v_scp = em.v("pe")
                for h in range(8):
                    bank = ps_a if h < 4 else ps_b
                    em.emit("vector",
                            lambda v, h=h, bank=bank: v.tensor_tensor(
                                scm_s[:, h, :], bank[:, 128 * (h % 4):128 * (h % 4) + 128],
                                mask_s[:], ALU.add),
                            waits=([("pe", v_scp)] if h == 0 else []),
                            inc=("ve" if h == 7 else None))
                em.emit("vector", lambda v: v.reduce_max(smx8[:], scm_s[:], axis=AX.X))
                em.emit("vector", lambda v: v.tensor_tensor(
                    prob_f[:], scm_s[:],
                    smx8[:].rearrange("p (g o) -> p g o", o=1).broadcast_to([128, 8, 128]),
                    ALU.subtract), inc="ve")
                em.emit("scalar", lambda s: s.activation(
                    scm_s[:].rearrange("p a k -> p (a k)"),
                    prob_f[:].rearrange("p a k -> p (a k)"),
                    AF.Exp, bias=0.0, scale=SCL),
                    waits=[("ve", em.v("ve"))], inc="ac")
                em.emit("vector", lambda v: v.reduce_sum(ssum8[:], scm_s[:], axis=AX.X),
                        waits=[("ac", em.v("ac"))])
                em.emit("vector", lambda v: v.reciprocal(ssum8[:], ssum8[:]))
                em.emit("vector", lambda v: v.tensor_tensor(
                    prob_b[:], scm_s[:],
                    ssum8[:].rearrange("p (g o) -> p g o", o=1).broadcast_to([128, 8, 128]),
                    ALU.mult), inc="ve")
                v_pb = em.v("ve")

                for h in range(8):
                    bank = ps_t if h < 4 else ps_t2
                    em.emit("tensor",
                            lambda tn, h=h, bank=bank: tn.transpose(
                                bank[:, 128 * (h % 4):128 * (h % 4) + 128],
                                prob_b[:, h, :], identb[:]),
                            waits=([("ve", v_pb)] if h == 0 else []),
                            inc=("pe" if h % 4 == 3 else None))
                v_ptp = em.v("pe")
                em.emit("vector", lambda v: v.tensor_copy(
                    probT[:, 0:4, :].rearrange("p a k -> p (a k)"), ps_t[:, 0:512]),
                    waits=[("pe", v_ptp - 1)])
                em.emit("vector", lambda v: v.tensor_copy(
                    probT[:, 4:8, :].rearrange("p a k -> p (a k)"), ps_t2[:, 0:512]),
                    waits=[("pe", v_ptp)], inc="ve")
                v_pT = em.v("ve")

                for h in range(8):
                    bank = ps_a if h < 4 else ps_b
                    em.emit("tensor",
                            lambda tn, c=c, h=h, bank=bank: tn.matmul(
                                bank[:, 128 * (h % 4):128 * (h % 4) + 128],
                                v_tm[:, c, 128 * h:128 * (h + 1)],
                                probT[:, h, :], start=True, stop=True),
                            waits=([("ve", v_pT)] if h == 0 else []),
                            inc=("pe" if h == 7 else None))
                v_avp = em.v("pe")
                em.emit("vector",
                        lambda v, c=c: v.tensor_copy(
                            attn_fm[:, 0:4, 128 * c:128 * (c + 1)],
                            ps_a[:, 0:512].rearrange("p (a k) -> p a k", a=4)),
                        waits=[("pe", v_avp)])
                em.emit("vector",
                        lambda v, c=c: v.tensor_copy(
                            attn_fm[:, 4:8, 128 * c:128 * (c + 1)],
                            ps_b[:, 0:512].rearrange("p (a k) -> p a k", a=4)),
                        inc="ve")
            v_attn = em.v("ve")

            seg0 = 8 * ti
            ev_par = {0: 0, 1: 0}
            for m in range(8):
                bank = ps_c if m % 2 == 0 else ps_d
                for k in range(8):
                    em.emit("tensor",
                            lambda tn, m=m, k=k, bank=bank: tn.matmul(
                                bank[:, 0:512], wOutTl[:, k, 128 * m:128 * (m + 1)],
                                attn_fm[:, k, :], start=(k == 0), stop=(k == 7)),
                            waits=(([("ve", v_attn)] if m == 0 else
                                    ([("ve", ev_par[m % 2])] if m >= 2 else []))
                                   if k == 0 else []),
                            inc=("pe" if k == 7 else None))
                em.emit("vector",
                        lambda v, m=m, b=b, seg0=seg0, bank=bank: v.tensor_tensor(
                            sum_fm[:, m, :].rearrange("p (s q) -> p s q", s=8),
                            bank[:, 0:512].rearrange("p (s q) -> p s q", s=8),
                            globB[:, m, 64 * b + seg0:64 * b + seg0 + 8]
                            .rearrange("p (s o) -> p s o", o=1).broadcast_to([128, 8, 64]),
                            ALU.add),
                        waits=[("pe", em.v("pe"))] + ([("ac", v_glob)] if (t == 0 and m == 0) else []),
                        inc="ve")
                ev_par[m % 2] = em.v("ve")
            v_sum = em.v("ve")

            ev_par = {0: 0, 1: 0}
            for g in range(8):
                c, half = divmod(g, 2)
                bank = ps_t if g % 2 == 0 else ps_t2
                for q in range(4):
                    fc = 4 * half + q
                    em.emit("tensor",
                            lambda tn, c=c, fc=fc, q=q, bank=bank: tn.transpose(
                                bank[:, 128 * q:128 * (q + 1)],
                                sum_fm[:, fc, 128 * c:128 * (c + 1)], identb[:]),
                            waits=(([("ve", v_sum)] if g == 0 else [("ve", ev_par[g % 2])])
                                   if q == 0 else []),
                            inc=("pe" if q == 3 else None))
                em.emit("vector",
                        lambda v, c=c, half=half, bank=bank, xb=xb: v.tensor_tensor(
                            xb[:, c, 512 * half:512 * (half + 1)],
                            xb[:, c, 512 * half:512 * (half + 1)],
                            bank[:, 0:512], ALU.add),
                        waits=[("pe", em.v("pe"))],
                        inc="ve")
                ev_par[g % 2] = em.v("ve")

            if t == 0:
                cap("x1t0", xb[:].rearrange("p c f -> p (c f)"))

            v_ln2 = emit_ln(xb, xln2_s,
                            [("ve", em.v("ve")), ("st", store_xln2_prev)])

            em.emit("sync",
                    lambda s, t=t, xb=xb: s.dma_start(
                        x1_dr.ap()[t * 512:(t + 1) * 512, :].rearrange("(c p) f -> p c f", p=128),
                        xb[:]),
                    waits=[("ve", em.v("ve"))], inc="st", k=16)
            store_x1_prev = em.v("st")
            x_cons_done[r] = [("ve", em.v("ve")), ("ac", em.v("ac")), ("st", store_x1_prev)]
            if r + 2 < 2 * NTILE:
                emit_x_load(r + 2)
            em.emit("sync",
                    lambda s, t=t: s.dma_start(
                        xln2_dr.ap()[t * 512:(t + 1) * 512, :].rearrange("(c p) f -> p c f", p=128),
                        xln2_s[:]),
                    waits=[("ve", v_ln2)], inc="st", k=16)
            store_xln2_prev = em.v("st")

            ev_par = {0: 0, 1: 0}
            for g in range(8):
                c, half = divmod(g, 2)
                bank = ps_t if g % 2 == 0 else ps_t2
                for q in range(4):
                    fc = 4 * half + q
                    em.emit("tensor",
                            lambda tn, c=c, fc=fc, q=q, bank=bank: tn.transpose(
                                bank[:, 128 * q:128 * (q + 1)],
                                xln2_s[:, c, 128 * fc:128 * (fc + 1)], identb[:]),
                            waits=(([("ve", v_ln2)] if g == 0 else [("ve", ev_par[g % 2])])
                                   if q == 0 else []),
                            inc=("pe" if q == 3 else None))
                em.emit("vector",
                        lambda v, c=c, half=half, bank=bank: v.tensor_copy(
                            xln2_fm[:, 4 * half:4 * half + 4, 128 * c:128 * (c + 1)],
                            bank[:, 0:512].rearrange("p (a k) -> p a k", a=4)),
                        waits=[("pe", em.v("pe"))], inc="ve")
                ev_par[g % 2] = em.v("ve")
            v_x2fm = em.v("ve")

            for k in range(8):
                em.emit("tensor",
                        lambda tn, k=k: tn.matmul(
                            ps_c[0:8, 0:512], wr_s[:, k, :], xln2_fm[:, k, :],
                            start=(k == 0), stop=(k == 7)),
                        waits=([("ve", v_x2fm)] if k == 0 else []),
                        inc=("pe" if k == 7 else None))
            em.emit("scalar",
                    lambda s: s.activation(logit_em[:], ps_c[0:8, 0:512], AF.Identity,
                                           bias=br_c[:, 0:1], scale=1.0),
                    waits=[("pe", em.v("pe"))], inc="ac")
            v_lg = em.v("ac")
            for c in range(4):
                em.emit("tensor",
                        lambda tn, c=c: tn.transpose(
                            ps_d[:, 8 * c:8 * (c + 1)],
                            logit_em[:, 128 * c:128 * (c + 1)], ident_f32_s[0:8, 0:8]),
                        waits=([("ac", v_lg)] if c == 0 else []),
                        inc=("pe" if c == 3 else None))
            v_lgt = em.v("pe")
            em.emit("vector", lambda v: v.tensor_copy(
                sm_tm[:].rearrange("p a e -> p (a e)"), ps_d[:, 0:32]),
                waits=[("pe", v_lgt)])
            em.emit("vector", lambda v: v.reduce_max(sm_mx[:], sm_tm[:], axis=AX.X))
            em.emit("vector", lambda v: v.tensor_tensor(
                sm_tm[:], sm_tm[:],
                sm_mx[:].rearrange("p (g o) -> p g o", o=1).broadcast_to([128, 4, 8]),
                ALU.subtract), inc="ve")
            em.emit("scalar", lambda s: s.activation(
                sm_tm[:].rearrange("p a e -> p (a e)"), sm_tm[:].rearrange("p a e -> p (a e)"),
                AF.Exp, bias=0.0, scale=1.0), waits=[("ve", em.v("ve"))], inc="ac")
            em.emit("vector", lambda v: v.reduce_sum(sm_sum[:], sm_tm[:], axis=AX.X),
                    waits=[("ac", em.v("ac"))])
            em.emit("vector", lambda v: v.reciprocal(sm_sum[:], sm_sum[:]))
            em.emit("vector", lambda v: v.tensor_tensor(
                sm_tm[:], sm_tm[:],
                sm_sum[:].rearrange("p (g o) -> p g o", o=1).broadcast_to([128, 4, 8]),
                ALU.mult), inc="ve")
            v_smtm = em.v("ve")
            for c in range(4):
                em.emit("tensor",
                        lambda tn, c=c: tn.transpose(
                            ps_c[0:8, 128 * c:128 * (c + 1)], sm_tm[:, c, :],
                            ident_f32_s[:]),
                        waits=([("ve", v_smtm)] if c == 0 else []),
                        inc=("pe" if c == 3 else None))
            em.emit("vector", lambda v: v.tensor_copy(sm_em[:], ps_c[0:8, 0:512]),
                    waits=[("pe", em.v("pe")), ("sm", store_sm_prev)], inc="ve")
            em.emit("gpsimd",
                    lambda g, t=t: g.dma_start(
                        sm_dr.ap()[:, t * 512:(t + 1) * 512], sm_em[:]),
                    waits=[("ve", em.v("ve"))], inc="sm", k=16)
            store_sm_prev = em.v("sm")

        v_all_st = em.v("st")
        v_all_sm = em.v("sm")

        @block.sync
        def _(sync):
            em.run_engine("sync", sync)

        @block.gpsimd
        def _(gpsimd):
            em.run_engine("gpsimd", gpsimd)
            gpsimd.wait_ge(h_st, v_all_st)
            gpsimd.wait_ge(h_sm, v_all_sm)

        @block.vector
        def _(vector):
            em.run_engine("vector", vector)

        @block.scalar
        def _(scalar):
            em.run_engine("scalar", scalar)

        @block.tensor
        def _(tensor):
            em.run_engine("tensor", tensor)

    return nc



def _consts_b():
    ident_bf = np.eye(128, dtype=ml_dtypes.bfloat16)
    tril = np.zeros((128, 128), dtype=ml_dtypes.bfloat16)
    for p in range(128):
        tril[p, p + 1:] = 1.0
    diagbig = (np.eye(128) * BIG).astype(ml_dtypes.bfloat16)
    onesb = np.ones((128, 128), dtype=ml_dtypes.bfloat16)
    onesf = np.ones((128, 128), dtype=np.float32)
    t = np.arange(NTOK)
    ids = np.stack([t >> 6, t & 63], axis=1).astype(ml_dtypes.bfloat16)
    ids = ids.reshape(64, 128, 2).transpose(1, 0, 2).copy()
    s = np.arange(CAP)
    im = np.zeros((CAP, 8), dtype=ml_dtypes.bfloat16)
    im[:, 0] = ((NTOK + (s % 128)) >> 6).astype(ml_dtypes.bfloat16)
    im[:, 1] = ((NTOK + (s % 128)) & 63).astype(ml_dtypes.bfloat16)
    return ident_bf, tril, diagbig, onesb, onesf, ids, im


def build_launch_b(dbg_stage=None):
    nc = bass.Bass("TRN2", target_bir_lowering=False, detect_race_conditions=False)

    x1_dr = nc.dram_tensor("x1", [NTOK, D], F32, kind="ExternalInput")
    xln2_dr = nc.dram_tensor("xln2", [NTOK, D], BF16, kind="ExternalInput")
    smg_dr = nc.dram_tensor("smg", [E, 65536], F32, kind="ExternalInput")
    smtm_dr = nc.dram_tensor("smtm", [NTOK, E], F32, kind="ExternalInput")
    wu_dr = nc.dram_tensor("wu", [4, D, FP], F32, kind="ExternalInput")
    buc_dr = nc.dram_tensor("bu_cols", [128, 4 * NFC], F32, kind="ExternalInput")
    wd_dr = nc.dram_tensor("wd", [E, FP, D], F32, kind="ExternalInput")
    bd_dr = nc.dram_tensor("bd", [E, D], F32, kind="ExternalInput")

    y_dr = nc.dram_tensor("y", [YROWS, D], F32, kind="ExternalOutput")
    dbg_dr = nc.dram_tensor("dbg", [128, 8192], F32, kind="ExternalOutput") if dbg_stage else None

    meta_dr = nc.dram_tensor("meta_scr", [E, CAP, 8], BF16, kind="Internal")

    identc, trilc, diagc, onesbc, onesfc, idsc, imc = _consts_b()
    ident_dr = nc.inline_tensor(identc, "ident_bf")
    tril_dr = nc.inline_tensor(trilc, "trilx")
    diag_dr = nc.inline_tensor(diagc, "diagbig")
    onesb_dr = nc.inline_tensor(onesbc, "onesb")
    onesf_dr = nc.inline_tensor(onesfc, "onesf")
    ids_dr = nc.inline_tensor(idsc.reshape(128, 128), "ids")
    im_dr = nc.inline_tensor(imc, "initmeta")

    em = EM()

    with ExitStack() as ctx:
        identb = ctx.enter_context(nc.sbuf_tensor("identb", [128, 128], BF16))
        tril_s = ctx.enter_context(nc.sbuf_tensor("tril_s", [128, 128], BF16))
        diag_s = ctx.enter_context(nc.sbuf_tensor("diag_s", [128, 128], BF16))
        onesb_s = ctx.enter_context(nc.sbuf_tensor("onesb_s", [128, 128], BF16))
        onesf_s = ctx.enter_context(nc.sbuf_tensor("onesf_s", [128, 128], F32))
        ids_s = ctx.enter_context(nc.sbuf_tensor("ids_s", [128, 64, 2], BF16))
        im_s = ctx.enter_context(nc.sbuf_tensor("im_s", [128, 10, 8], BF16))
        smtm_s = ctx.enter_context(nc.sbuf_tensor("smtm_s", [128, 64, 8], F32))
        lo_s = ctx.enter_context(nc.sbuf_tensor("lo_s", [128, 8], F32))
        hi_s = ctx.enter_context(nc.sbuf_tensor("hi_s", [128, 8], F32))
        mid_s = ctx.enter_context(nc.sbuf_tensor("mid_s", [128, 8], F32))
        cnt_s = ctx.enter_context(nc.sbuf_tensor("cnt_s", [128, 8], F32))
        flag_s = ctx.enter_context(nc.sbuf_tensor("flag_s", [128, 8], F32))
        tmp1_s = ctx.enter_context(nc.sbuf_tensor("tmp1_s", [128, 8], F32))
        tmp2_s = ctx.enter_context(nc.sbuf_tensor("tmp2_s", [128, 8], F32))
        trash_b = ctx.enter_context(nc.sbuf_tensor("trash_b", [128, 512], BF16))
        m_all = ctx.enter_context(nc.sbuf_tensor("m_all", [128, 64, 8], BF16))
        minv_all = ctx.enter_context(nc.sbuf_tensor("minv_all", [128, 64, 8], BF16))
        pos_all = ctx.enter_context(nc.sbuf_tensor("pos_all", [128, 64, 8], U32))
        meta_all = ctx.enter_context(nc.sbuf_tensor("meta_all", [128, 64, 8, 8], BF16))
        trow = ctx.enter_context(nc.sbuf_tensor("trow", [1, 512], F32))
        trow2 = ctx.enter_context(nc.sbuf_tensor("trow2", [1, 512], F32))
        offs = ctx.enter_context(nc.sbuf_tensor("offs", [1, 512], F32))
        meta_ld = ctx.enter_context(nc.sbuf_tensor("meta_ld", [128, 10, 8], BF16))
        idx_f = ctx.enter_context(nc.sbuf_tensor("idx_f", [128, 10], F32))
        idx_u = ctx.enter_context(nc.sbuf_tensor("idx_u", [128, 10], U32))
        gates = ctx.enter_context(nc.sbuf_tensor("gates", [128, 10], F32))
        bu_c = ctx.enter_context(nc.sbuf_tensor("bu_c", [128, 4 * NFC], F32))
        bd_r = ctx.enter_context(nc.sbuf_tensor("bd_r", [1, 1024], BF16))
        bdb = ctx.enter_context(nc.sbuf_tensor("bdb", [128, 1024], F32))
        wu_s = ctx.enter_context(nc.sbuf_tensor("wu_s", [128, 8, FP], BF16))
        wd_s = ctx.enter_context(nc.sbuf_tensor("wd_s", [128, NFC, 1024], BF16))
        xg_tm = ctx.enter_context(nc.sbuf_tensor("xg_tm", [128, 2, 1024], BF16))
        xg_fm = ctx.enter_context(nc.sbuf_tensor("xg_fm", [128, 8, CAP], BF16))
        mid_arena = ctx.enter_context(nc.sbuf_tensor("mid_arena", [128, NFC, CAP], BF16))
        contrib = ctx.enter_context(nc.sbuf_tensor("contrib", [128, 2, 1024], F32))
        ps_a = ctx.enter_context(nc.psum_tensor("ps_a", [128, 512], F32))
        ps_b = ctx.enter_context(nc.psum_tensor("ps_b", [128, 512], F32))
        ps_c = ctx.enter_context(nc.psum_tensor("ps_c", [128, 512], F32))
        ps_d = ctx.enter_context(nc.psum_tensor("ps_d", [128, 512], F32))
        ps_t = ctx.enter_context(nc.psum_tensor("ps_t", [128, 512], BF16))
        ps_t2 = ctx.enter_context(nc.psum_tensor("ps_t2", [128, 512], BF16))
        ps_x = ctx.enter_context(nc.psum_tensor("ps_x", [128, 8], F32))
        h_ld = ctx.enter_context(nc.semaphore("s_ld"))
        h_mi = ctx.enter_context(nc.semaphore("s_mi"))
        h_ml = ctx.enter_context(nc.semaphore("s_ml"))
        h_w = ctx.enter_context(nc.semaphore("s_w"))
        h_gx0 = ctx.enter_context(nc.semaphore("s_gx0"))
        h_gx1 = ctx.enter_context(nc.semaphore("s_gx1"))
        h_ys0 = ctx.enter_context(nc.semaphore("s_ys0"))
        h_ys1 = ctx.enter_context(nc.semaphore("s_ys1"))
        h_sc = [ctx.enter_context(nc.semaphore(f"s_sc{i}")) for i in range(E)]
        h_ve = ctx.enter_context(nc.semaphore("s_ve"))
        h_ac = ctx.enter_context(nc.semaphore("s_ac"))
        h_pe = ctx.enter_context(nc.semaphore("s_pe"))
        block = ctx.enter_context(nc.Block())
        for nm, hh in (("ld", h_ld), ("mi", h_mi), ("ml", h_ml), ("w", h_w),
                       ("gx0", h_gx0), ("gx1", h_gx1), ("ys0", h_ys0), ("ys1", h_ys1),
                       ("ve", h_ve), ("ac", h_ac), ("pe", h_pe)):
            em.add_sem(nm, hh)
        for i in range(E):
            em.add_sem(f"sc{i}", h_sc[i])

        smg_v = mid_arena[:, 0:8, 0:1024].bitcast(F32)

        regs = {}

        def _mkregs(g):
            regs["cap"] = g.to_reg(CAP - 1)
            regs["ntok"] = g.to_reg(NTOK - 1)

        em.emit("gpsimd", _mkregs)

        def gp_dma(fn, sem, waits=()):
            em.emit("gpsimd", fn, waits=list(waits), inc=sem, k=16)
            return em.v(sem)

        def cap(stage, ap, extra_waits=()):
            if dbg_stage != stage:
                return
            def _cap_dma(g, ap=ap):
                dst = dbg_dr.ap()[0:ap.shape[0], 0:int(np.prod(ap.shape[1:]))]
                if len(ap.shape) == 3:
                    dst = dst.rearrange("p (a k) -> p a k", a=ap.shape[1])
                return g.dma_start(dst, ap)
            em.emit("gpsimd", _cap_dma,
                    waits=[("ve", em.v("ve")), ("ac", em.v("ac")), ("pe", em.v("pe"))]
                          + list(extra_waits),
                    inc="mi", k=16)

        for dst, src in (
            (identb[:], ident_dr.ap()),
            (tril_s[:], tril_dr.ap()),
            (diag_s[:], diag_dr.ap()),
            (onesb_s[:], onesb_dr.ap()),
            (onesf_s[:], onesf_dr.ap()),
            (ids_s[:].rearrange("p a b -> p (a b)"), ids_dr.ap()),
            (bu_c[:], buc_dr.ap()),
            (smtm_s[:], smtm_dr.ap().rearrange("(j p) e -> p j e", p=128)),
        ):
            em.emit("sync", lambda s, d=dst, a=src: s.dma_start(d, a), inc="ld", k=16)
        for e in range(E):
            em.emit("sync",
                    lambda s, e=e: s.dma_start(
                        smg_v[:, e, :], smg_dr.ap()[e, :].rearrange("(p f) -> p f", p=128)),
                    inc="ld", k=16)
        v_smg = em.v("ld")

        for q in range(4):
            em.emit("sync",
                    lambda s, q=q: s.dma_start(
                        y_dr.ap()[q * 2048:(q + 1) * 2048, :],
                        x1_dr.ap()[q * 2048:(q + 1) * 2048, :]),
                    inc="ld", k=16)
        v_yinit = em.v("ld")

        v_ims = gp_dma(
            lambda g: g.dma_start(im_s[:], im_dr.ap().rearrange("(m p) c -> p m c", p=128)),
            "mi")
        v_meta_init = 0
        for e in range(E):
            v_meta_init = gp_dma(
                lambda g, e=e: g.dma_start(
                    meta_dr.ap()[e].rearrange("(m p) c -> p m c", p=128), im_s[:]),
                "mi", waits=[("mi", v_ims)])

        v_wu = [0] * 4
        v_wd = [0] * E
        v_wu[0] = gp_dma(lambda g: g.dma_start(
            wu_s[:], wu_dr.ap()[0].rearrange("(c p) f -> p c f", p=128)), "w")
        v_wd[0] = gp_dma(lambda g: g.dma_start(
            wd_s[:], wd_dr.ap()[0].rearrange("(c p) f -> p c f", p=128)), "w")

        em.emit("vector", lambda v: v.memset(xg_tm[:].rearrange("p a b -> p (a b)"), 0.0))
        em.emit("vector", lambda v: v.memset(xg_fm[:].rearrange("p a b -> p (a b)"), 0.0))
        em.emit("vector", lambda v: v.memset(lo_s[:], 0.0))
        em.emit("vector", lambda v: v.memset(hi_s[:], 1.0), inc="ve")
        v_binit = em.v("ve")

        for it in range(BISECT_ITERS):
            em.emit("vector", lambda v: v.tensor_tensor(mid_s[:], lo_s[:], hi_s[:], ALU.add),
                    waits=([("ld", v_smg), ("ve", v_binit)] if it == 0 else []))
            em.emit("vector", lambda v: v.tensor_scalar(mid_s[:], mid_s[:], 0.5, None, ALU.mult),
                    inc="ve")
            v_mid = em.v("ve")
            for e in range(E):
                em.emit("vector",
                        lambda v, e=e: v.tensor_scalar(
                            trash_b[:], smg_v[:, e, :], mid_s[:, e:e + 1], 0.0,
                            ALU.is_ge, ALU.add, accum_out=cnt_s[:, e:e + 1]),
                        inc=("ve" if e == E - 1 else None))
            v_cnt = em.v("ve")
            em.emit("tensor",
                    lambda tn: tn.matmul(ps_x[:], onesf_s[:], cnt_s[:], start=True, stop=True),
                    waits=[("ve", v_cnt)], inc="pe")
            v_cntmm = em.v("pe")
            em.emit("vector",
                    lambda v: v.tensor_scalar(flag_s[:], ps_x[:], float(65536 // E), 0.0,
                                              ALU.is_ge, ALU.add),
                    waits=[("pe", v_cntmm)])
            em.emit("vector", lambda v: v.tensor_tensor(tmp1_s[:], mid_s[:], lo_s[:], ALU.subtract))
            em.emit("vector", lambda v: v.tensor_tensor(tmp1_s[:], tmp1_s[:], flag_s[:], ALU.mult))
            em.emit("vector", lambda v: v.tensor_tensor(lo_s[:], lo_s[:], tmp1_s[:], ALU.add))
            em.emit("vector", lambda v: v.tensor_tensor(tmp2_s[:], hi_s[:], mid_s[:], ALU.subtract))
            em.emit("vector", lambda v: v.tensor_tensor(tmp2_s[:], tmp2_s[:], flag_s[:], ALU.mult))
            em.emit("vector", lambda v: v.tensor_tensor(hi_s[:], mid_s[:], tmp2_s[:], ALU.add),
                    inc="ve")

        v_theta = em.v("ve")
        cap("theta", lo_s[:])
        cap("smgchk", smg_v[:, 0:8, :], [("ld", v_smg)])

        for j in range(64):
            em.emit("vector",
                    lambda v, j=j: v.tensor_tensor(m_all[:, j, :], smtm_s[:, j, :], lo_s[:],
                                                   ALU.is_ge),
                    waits=([("ve", v_theta)] if j == 0 else []))
            em.emit("vector",
                    lambda v, j=j: v.tensor_scalar(minv_all[:, j, :], m_all[:, j, :],
                                                   -1.0, 1.0, ALU.mult, ALU.add),
                    inc="ve")
            em.emit("tensor",
                    lambda tn, j=j: tn.matmul(ps_a[0:64, 8 * j:8 * (j + 1)],
                                              onesb_s[:, 0:64], m_all[:, j, :],
                                              start=True, stop=True),
                    waits=[("ve", em.v("ve"))], inc=("pe" if j == 63 else None))
        v_tot = em.v("pe")
        if dbg_stage == "tots":
            em.emit("vector", lambda v: v.tensor_copy(trash_b[0:64, :], ps_a[0:64, 0:512]),
                    waits=[("pe", v_tot)], inc="ve")
            cap("tots", trash_b[0:64, :])
        em.emit("vector", lambda v: v.tensor_copy(trow[:], ps_a[0:1, 0:512]),
                waits=[("pe", v_tot)], inc="ve")
        cap("trow", trow[:])

        cur, nxt = trow, trow2
        for st in (8, 16, 32, 64, 128, 256):
            em.emit("vector", lambda v, cur=cur, nxt=nxt, st=st: v.tensor_copy(
                nxt[:, 0:st], cur[:, 0:st]))
            em.emit("vector", lambda v, cur=cur, nxt=nxt, st=st: v.tensor_tensor(
                nxt[:, st:512], cur[:, st:512], cur[:, 0:512 - st], ALU.add), inc="ve")
            cur, nxt = nxt, cur
        em.emit("vector", lambda v: v.memset(offs[:, 0:8], 0.0))
        em.emit("vector", lambda v, cur=cur: v.tensor_copy(offs[:, 8:512], cur[:, 0:504]),
                inc="ve")
        v_offs = em.v("ve")
        cap("m0", m_all[:])
        cap("offs", offs[:])

        for j in range(64):
            em.emit("tensor",
                    lambda tn, j=j: tn.matmul(ps_x[:], tril_s[:], m_all[:, j, :],
                                              start=True, stop=False),
                    waits=([("ve", v_offs)] if j == 0 else [("ve", em.v("ve"))]))
            em.emit("tensor",
                    lambda tn, j=j: tn.matmul(ps_x[:], onesf_s[0:1, :],
                                              offs[:, 8 * j:8 * (j + 1)],
                                              start=False, stop=False))
            em.emit("tensor",
                    lambda tn, j=j: tn.matmul(ps_x[:], diag_s[:], minv_all[:, j, :],
                                              start=False, stop=True), inc="pe")
            v_pj = em.v("pe")
            em.emit("vector",
                    lambda v, j=j: v.tensor_copy(pos_all[:, j, :], ps_x[:]),
                    waits=[("pe", v_pj)])
            em.emit("vector",
                    lambda v, j=j: v.tensor_copy(
                        meta_all[:, j, :, 0], ids_s[:, j, 0:1].broadcast_to([128, 8])))
            em.emit("vector",
                    lambda v, j=j: v.tensor_copy(
                        meta_all[:, j, :, 1], ids_s[:, j, 1:2].broadcast_to([128, 8])))
            em.emit("vector",
                    lambda v, j=j: v.tensor_copy(meta_all[:, j, :, 2], smtm_s[:, j, :]))
            em.emit("vector",
                    lambda v, j=j: v.tensor_tensor(meta_all[:, j, :, 3], smtm_s[:, j, :],
                                                   meta_all[:, j, :, 2], ALU.subtract),
                    inc="ve")
        v_sel = em.v("ve")
        em.emit("vector", lambda v: v.memset(meta_all[:, :, :, 4:8], 0.0), inc="ve")
        v_meta = em.v("ve")
        cap("pos", pos_all[:].bitcast(F32))
        cap("meta8", meta_all[:, :, :, 0:8].rearrange("p j e c -> p (j e c)"))

        v_scat = [0] * E
        for e in range(E):
            for j in range(64):
                v_scat[e] = gp_dma(
                    lambda g, e=e, j=j: g.indirect_dma_start(
                        meta_dr.ap().rearrange("e m c -> (e m) c"),
                        IndirectOffsetOnAxis(pos_all[:, j, e:e + 1], 0),
                        meta_all[:, j, e, :], None,
                        element_offset=e * CAP * 8,
                        bounds_check=regs["cap"], oob_is_err=False),
                    f"sc{e}",
                    waits=([("ve", v_meta), ("mi", v_meta_init)] if (e == 0 and j == 0) else []))

        cap("meta0", meta_dr.ap()[0].rearrange("(m p) c -> p m c", p=128),
            [(f"sc{e}", v_scat[e]) for e in range(E)] + [("mi", v_meta_init)])

        v_meta_ld = [0] * E
        v_y_final = 0
        v_prev_dec = 0
        v_prev_scat_end = []
        for e in range(E):
            g_grp = e // 2
            em.emit("sync",
                    lambda s, e=e: s.dma_start(
                        meta_ld[:], meta_dr.ap()[e].rearrange("(m p) c -> p m c", p=128)),
                    waits=[(f"sc{e}", v_scat[e]), ("ve", v_prev_dec)], inc="ml", k=16)
            v_meta_ld[e] = em.v("ml")
            em.emit("vector",
                    lambda v: v.tensor_scalar(idx_f[:], meta_ld[:, :, 0], 64.0, None, ALU.mult),
                    waits=[("ml", v_meta_ld[e])] + list(v_prev_scat_end))
            em.emit("vector",
                    lambda v: v.tensor_tensor(idx_f[:], idx_f[:], meta_ld[:, :, 1], ALU.add))
            em.emit("vector", lambda v: v.tensor_copy(idx_u[:], idx_f[:]))
            em.emit("vector",
                    lambda v: v.tensor_tensor(gates[:], meta_ld[:, :, 2], meta_ld[:, :, 3],
                                              ALU.add), inc="ve")
            v_dec = em.v("ve")
            v_prev_dec = v_dec

            v_bdr = gp_dma(lambda g, e=e: g.dma_start(
                bd_r[:], bd_dr.ap()[e, :].rearrange("(o f) -> o f", o=1)), "w")
            for hhalf in range(2):
                em.emit("tensor",
                        lambda tn, hhalf=hhalf: tn.matmul(
                            ps_c[:, 0:512],
                            onesb_s[0:1, :], bd_r[0:1, 512 * hhalf:512 * (hhalf + 1)],
                            start=True, stop=True),
                        waits=[("w", v_bdr), ("ve", em.v("ve"))], inc="pe")
                em.emit("vector",
                        lambda v, hhalf=hhalf: v.tensor_copy(
                            bdb[:, 512 * hhalf:512 * (hhalf + 1)], ps_c[:, 0:512]),
                        waits=[("pe", em.v("pe"))], inc="ve")
            v_bdb = em.v("ve")

            ev_par = {0: 0, 1: 0}
            xg_done = [0, 0]
            for m in range(10):
                buf = m % 2
                v_g = gp_dma(
                    lambda g, m=m, buf=buf: g.indirect_dma_start(
                        xg_tm[:, buf, :], None, xln2_dr.ap(),
                        IndirectOffsetOnAxis(idx_u[:, m:m + 1], 0),
                        bounds_check=regs["ntok"], oob_is_err=False),
                    f"gx{buf}",
                    waits=[("ve", v_dec), ("pe", xg_done[buf])])
                for hb in range(2):
                    bank = ps_t if hb == 0 else ps_t2
                    for q in range(4):
                        k = 4 * hb + q
                        em.emit("tensor",
                                lambda tn, m=m, buf=buf, k=k, q=q, bank=bank: tn.transpose(
                                    bank[:, 128 * q:128 * (q + 1)],
                                    xg_tm[:, buf, 128 * k:128 * (k + 1)], identb[:]),
                                waits=(([(f"gx{buf}", v_g), ("ve", ev_par[hb])] if q == 0 else [])),
                                inc=("pe" if q == 3 else None))
                    em.emit("vector",
                            lambda v, m=m, hb=hb, bank=bank: v.tensor_copy(
                                xg_fm[:, 4 * hb:4 * hb + 4, 128 * m:128 * (m + 1)],
                                bank[:, 0:512].rearrange("p (a k) -> p a k", a=4)),
                            waits=[("pe", em.v("pe"))], inc="ve")
                    ev_par[hb] = em.v("ve")
                xg_done[buf] = em.v("pe")
            v_xg = em.v("ve")

            ev_up = {0: 0, 1: 0}
            nn_off = (0, 512, 1024)
            nn_sz = (512, 512, 256)
            for fc in range(NFC):
                for nn in range(3):
                    gi = fc * 3 + nn
                    bank = ps_a if gi % 2 == 0 else ps_b
                    for k in range(8):
                        em.emit("tensor",
                                lambda tn, fc=fc, nn=nn, k=k, bank=bank: tn.matmul(
                                    bank[:, 0:nn_sz[nn]],
                                    wu_s[:, k, 128 * fc:128 * (fc + 1)],
                                    xg_fm[:, k, nn_off[nn]:nn_off[nn] + nn_sz[nn]],
                                    start=(k == 0), stop=(k == 7)),
                                waits=(([("ve", v_xg), ("w", v_wu[g_grp]),
                                         ("ac", ev_up[gi % 2])] if k == 0 else [])),
                                inc=("pe" if k == 7 else None))
                    em.emit("scalar",
                            lambda s, fc=fc, nn=nn, g_grp=g_grp, bank=bank: s.activation(
                                mid_arena[:, fc, nn_off[nn]:nn_off[nn] + nn_sz[nn]],
                                bank[:, 0:nn_sz[nn]], AF.Silu,
                                bias=bu_c[:, NFC * g_grp + fc:NFC * g_grp + fc + 1],
                                scale=1.0),
                            waits=[("pe", em.v("pe")), ("ve", v_theta), ("ld", v_smg)],
                            inc="ac")
                    ev_up[gi % 2] = em.v("ac")
            v_up = em.v("ac")
            v_up_pe = em.v("pe")

            if e + 1 < E and (e + 1) % 2 == 0:
                v_wu[g_grp + 1] = gp_dma(
                    lambda g, gg=g_grp + 1: g.dma_start(
                        wu_s[:], wu_dr.ap()[gg].rearrange("(c p) f -> p c f", p=128)),
                    "w", waits=[("pe", v_up_pe)])

            ev_dn = {0: 0, 1: 0}
            scat_done = [0, 0]
            for m in range(10):
                buf = m % 2
                for hhalf in range(2):
                    bank = ps_c if hhalf == 0 else ps_d
                    for k in range(NFC):
                        em.emit("tensor",
                                lambda tn, m=m, hhalf=hhalf, k=k, bank=bank: tn.matmul(
                                    bank[:, 0:512],
                                    mid_arena[:, k, 128 * m:128 * (m + 1)],
                                    wd_s[:, k, 512 * hhalf:512 * (hhalf + 1)],
                                    start=(k == 0), stop=(k == NFC - 1)),
                                waits=(([("ac", v_up), ("w", v_wd[e]),
                                         ("ve", ev_dn[hhalf])] if k == 0 else [])),
                                inc=("pe" if k == NFC - 1 else None))
                    em.emit("vector",
                            lambda v, m=m, buf=buf, hhalf=hhalf, bank=bank: v.tensor_tensor(
                                contrib[:, buf, 512 * hhalf:512 * (hhalf + 1)],
                                bank[:, 0:512], bdb[:, 512 * hhalf:512 * (hhalf + 1)],
                                ALU.add),
                            waits=[("pe", em.v("pe")), ("ve", v_bdb),
                                   (f"ys{buf}", scat_done[buf])])
                    em.emit("vector",
                            lambda v, m=m, buf=buf, hhalf=hhalf: v.tensor_scalar(
                                contrib[:, buf, 512 * hhalf:512 * (hhalf + 1)],
                                contrib[:, buf, 512 * hhalf:512 * (hhalf + 1)],
                                gates[:, m:m + 1], None, ALU.mult),
                            inc="ve")
                    ev_dn[hhalf] = em.v("ve")
                v_ct = em.v("ve")
                v_y_final = gp_dma(
                    lambda g, m=m, buf=buf: g.indirect_dma_start(
                        y_dr.ap(), IndirectOffsetOnAxis(idx_u[:, m:m + 1], 0),
                        contrib[:, buf, :], None,
                        compute_op=ALU.add, oob_is_err=True),
                    f"ys{buf}",
                    waits=[("ve", v_ct)] + ([("ld", v_yinit)] if (e == 0 and m == 0) else []))
                scat_done[buf] = v_y_final
            v_prev_scat_end = [("ys0", em.v("ys0")), ("ys1", em.v("ys1"))]

            if e + 1 < E:
                v_wd[e + 1] = gp_dma(
                    lambda g, ee=e + 1: g.dma_start(
                        wd_s[:], wd_dr.ap()[ee].rearrange("(c p) f -> p c f", p=128)),
                    "w", waits=[("pe", em.v("pe"))])

        v_fin_ys0 = em.v("ys0")
        v_fin_ys1 = em.v("ys1")
        v_fin_mi = em.v("mi")

        @block.sync
        def _(sync):
            em.run_engine("sync", sync)

        @block.gpsimd
        def _(gpsimd):
            em.run_engine("gpsimd", gpsimd)
            gpsimd.wait_ge(h_ys0, v_fin_ys0)
            gpsimd.wait_ge(h_ys1, v_fin_ys1)
            gpsimd.wait_ge(h_mi, v_fin_mi)

        @block.vector
        def _(vector):
            em.run_engine("vector", vector)

        @block.scalar
        def _(scalar):
            em.run_engine("scalar", scalar)

        @block.tensor
        def _(tensor):
            em.run_engine("tensor", tensor)

    return nc



_CACHE = {}


def _get_programs():
    if "a" not in _CACHE:
        _CACHE["a"] = build_launch_a()
    if "b" not in _CACHE:
        _CACHE["b"] = build_launch_b()
    return _CACHE["a"], _CACHE["b"]


def kernel(**inputs):
    x = np.ascontiguousarray(np.asarray(inputs["L0_attn_x"], dtype=np.float32))
    w_in_l = np.asarray(inputs["L0_attn_w_in_local"], dtype=np.float32)
    b_in_l = np.asarray(inputs["L0_attn_b_in_local"], dtype=np.float32)
    w_out_l = np.asarray(inputs["L0_attn_w_out_local"], dtype=np.float32)
    b_out_l = np.asarray(inputs["L0_attn_b_out_local"], dtype=np.float32)
    w_in_g = np.asarray(inputs["L0_attn_w_in_global"], dtype=np.float32)
    b_in_g = np.asarray(inputs["L0_attn_b_in_global"], dtype=np.float32)
    w_out_g = np.asarray(inputs["L0_attn_w_out_global"], dtype=np.float32)
    b_out_g = np.asarray(inputs["L0_attn_b_out_global"], dtype=np.float32)
    wr = np.asarray(inputs["L0_ffn_Wr"], dtype=np.float32)
    br = np.asarray(inputs["L0_ffn_br"], dtype=np.float32)
    wu = np.asarray(inputs["L0_ffn_W_up"], dtype=np.float32)
    bu = np.asarray(inputs["L0_ffn_b_up"], dtype=np.float32)
    wd = np.asarray(inputs["L0_ffn_W_down"], dtype=np.float32)
    bd = np.asarray(inputs["L0_ffn_b_down"], dtype=np.float32)

    wInT_l = np.ascontiguousarray(w_in_l.T)
    wOutT_l = np.ascontiguousarray(w_out_l.T)
    wInT_g = np.ascontiguousarray(w_in_g.T)
    wOutT_g = np.ascontiguousarray(w_out_g.T)
    bInL_c = np.ascontiguousarray(b_in_l.reshape(24, 128).T)
    bInG_c = np.ascontiguousarray(b_in_g.reshape(24, 128).T)
    bOutL_c = np.ascontiguousarray(b_out_l.reshape(8, 128).T)
    bOutG_c = np.ascontiguousarray(b_out_g.reshape(8, 128).T)
    bInL_r = np.ascontiguousarray(b_in_l[2 * D:].reshape(1, D))
    bInG_r = np.ascontiguousarray(b_in_g[2 * D:].reshape(1, D))
    wr_l = np.ascontiguousarray(wr.reshape(8, 128, E).transpose(1, 0, 2))
    br_c = br.reshape(E, 1)

    wu_p = np.zeros((4, D, FP), dtype=np.float32)
    wu_p[:, :, :FTRUE] = wu
    bu_p = np.zeros((4, FP), dtype=np.float32)
    bu_p[:, :FTRUE] = bu
    bu_cols = np.ascontiguousarray(
        bu_p.reshape(4, NFC, 128).transpose(2, 0, 1).reshape(128, 4 * NFC))
    wd_p = np.zeros((E, FP, D), dtype=np.float32)
    wd_p[:, :FTRUE, :] = wd

    nca, ncb = _get_programs()

    in_maps_a = []
    for c in range(8):
        in_maps_a.append({
            "x": np.ascontiguousarray(x[2 * c:2 * c + 2].reshape(NTOK, D)),
            "wInT_l": wInT_l, "wOutT_l": wOutT_l,
            "wInT_g": wInT_g, "wOutT_g": wOutT_g,
            "bInL_c": bInL_c, "bInG_c": bInG_c,
            "bOutL_c": bOutL_c, "bOutG_c": bOutG_c,
            "bInL_r": bInL_r, "bInG_r": bInG_r,
            "wr_l": wr_l, "br_c": br_c,
        })
    res_a = run_bass_kernel_spmd(nca, in_maps_a, core_ids=list(range(8)))

    sm_all = np.ascontiguousarray(
        np.concatenate([res_a.results[c]["sm"] for c in range(8)], axis=1))

    in_maps_b = []
    for c in range(8):
        in_maps_b.append({
            "x1": res_a.results[c]["x1"],
            "xln2": res_a.results[c]["xln2"],
            "smg": sm_all,
            "smtm": np.ascontiguousarray(res_a.results[c]["sm"].T),
            "wu": wu_p, "bu_cols": bu_cols, "wd": wd_p, "bd": bd,
        })
    res_b = run_bass_kernel_spmd(ncb, in_maps_b, core_ids=list(range(8)))

    y = np.concatenate(
        [res_b.results[c]["y"][0:NTOK].reshape(2, 4096, D) for c in range(8)], axis=0)
    return y


# version 02
# speedup vs baseline: 1.1173x; 1.1173x over previous
import numpy as np
from contextlib import ExitStack
import ml_dtypes
import concourse.bass as bass
import concourse.mybir as mybir
from concourse.bass_utils import run_bass_kernel_spmd
from concourse.bass_core_rust import IndirectOffsetOnAxis

F32 = mybir.dt.float32
BF16 = mybir.dt.bfloat16
U32 = mybir.dt.uint32
AF = mybir.ActivationFunctionType
ALU = mybir.AluOpType
AX = mybir.AxisListType

NTOK = 8192
NTILE = 16
D = 1024
E = 8
FTRUE = 2730
FP = 2816
NFC = FP // 128
CAP = 1280
NSLOT = CAP + 128
YROWS = NTOK + 128
SCL = 0.08838834764831845
SCG = 1.0 / 16.0
EPS = 1e-5
NEG = -30000.0
BIG = float(1 << 20)
BISECT_ITERS = 30


class EM:
    ENGINES = ("sync", "gpsimd", "vector", "scalar", "tensor")

    def __init__(self):
        self.prog = {e: [] for e in self.ENGINES}
        self.sem = {}

    def add_sem(self, name, handle):
        self.sem[name] = [handle, 0]

    def v(self, name):
        return self.sem[name][1]

    def emit(self, eng, fn, waits=(), inc=None, k=1):
        ws = [(self.sem[n][0], val) for (n, val) in waits if val > 0]
        post = None
        if inc is not None:
            self.sem[inc][1] += k
            post = (self.sem[inc][0], k)

        def run(engobj, fn=fn, ws=ws, post=post, eng=eng):
            for h, val in ws:
                engobj.wait_ge(h, val)
            inst = fn(engobj)
            if eng == "vector":
                inst = engobj.drain()
            if post is not None:
                inst.then_inc(post[0], post[1])
        self.prog[eng].append(run)

    def run_engine(self, name, engobj):
        for fn in self.prog[name]:
            fn(engobj)


def _consts_a():
    ident_bf = np.eye(128, dtype=ml_dtypes.bfloat16)
    slide = np.zeros((128, 4, 192), dtype=ml_dtypes.bfloat16)
    for c in range(4):
        for p in range(128):
            slide[p, c, 64 + 2 * c + (1 if p >= 64 else 0)] = 1.0 / 64.0
    mask = np.full((128, 128), NEG, dtype=np.float32)
    mask[0:64, 0:64] = 0.0
    mask[64:128, 64:128] = 0.0
    return ident_bf, slide.reshape(128, 4 * 192), mask



def build_launch_a(dbg_stage=None):
    nc = bass.Bass("TRN2", target_bir_lowering=False, detect_race_conditions=False)

    x_dr = nc.dram_tensor("x", [NTOK, D], F32, kind="ExternalInput")
    wInT_l_dr = nc.dram_tensor("wInT_l", [D, 3 * D], F32, kind="ExternalInput")
    wOutT_l_dr = nc.dram_tensor("wOutT_l", [D, D], F32, kind="ExternalInput")
    wInT_g_dr = nc.dram_tensor("wInT_g", [D, 3 * D], F32, kind="ExternalInput")
    wOutT_g_dr = nc.dram_tensor("wOutT_g", [D, D], F32, kind="ExternalInput")
    bInL_dr = nc.dram_tensor("bInL_c", [128, 24], F32, kind="ExternalInput")
    bInG_dr = nc.dram_tensor("bInG_c", [128, 24], F32, kind="ExternalInput")
    bOutL_dr = nc.dram_tensor("bOutL_c", [128, 8], F32, kind="ExternalInput")
    bOutG_dr = nc.dram_tensor("bOutG_c", [128, 8], F32, kind="ExternalInput")
    bInL_row_dr = nc.dram_tensor("bInL_r", [1, D], F32, kind="ExternalInput")
    bInG_row_dr = nc.dram_tensor("bInG_r", [1, D], F32, kind="ExternalInput")
    wr_dr = nc.dram_tensor("wr_l", [128, 8, E], F32, kind="ExternalInput")
    br_dr = nc.dram_tensor("br_c", [E, 1], F32, kind="ExternalInput")

    x1_dr = nc.dram_tensor("x1", [NTOK, D], F32, kind="ExternalOutput")
    xln2_dr = nc.dram_tensor("xln2", [NTOK, D], BF16, kind="ExternalOutput")
    sm_dr = nc.dram_tensor("sm", [E, NTOK], F32, kind="ExternalOutput")
    dbg_dr = nc.dram_tensor("dbg", [128, 8192], F32, kind="ExternalOutput") if dbg_stage else None

    identc, slidec, maskc = _consts_a()
    ident_dr = nc.inline_tensor(identc, "ident_bf")
    identf_dr = nc.inline_tensor(np.eye(128, dtype=np.float32), "ident_f32")
    slide_dr = nc.inline_tensor(slidec, "slide")
    mask_dr = nc.inline_tensor(maskc, "segmask")
    onesb_dr = nc.inline_tensor(np.ones((128, 128), dtype=ml_dtypes.bfloat16), "onesb")

    em = EM()

    with ExitStack() as ctx:
        identb = ctx.enter_context(nc.sbuf_tensor("identb", [128, 128], BF16))
        ident_f32_s = ctx.enter_context(nc.sbuf_tensor("ident_f32_s", [128, 128], F32))
        slide_s = ctx.enter_context(nc.sbuf_tensor("slide_s", [128, 4, 192], BF16))
        mask_s = ctx.enter_context(nc.sbuf_tensor("mask_s", [128, 128], F32))
        onesb_s = ctx.enter_context(nc.sbuf_tensor("onesb_s", [128, 128], BF16))
        epsc = ctx.enter_context(nc.sbuf_tensor("epsc", [128, 1], F32))
        bInL_c = ctx.enter_context(nc.sbuf_tensor("bInLc_s", [128, 24], F32))
        bInG_c = ctx.enter_context(nc.sbuf_tensor("bInGc_s", [128, 24], F32))
        bOutSum = ctx.enter_context(nc.sbuf_tensor("bOutSum", [128, 8], F32))
        bOutG_c = ctx.enter_context(nc.sbuf_tensor("bOutGc_s", [128, 8], F32))
        bOutL_c = ctx.enter_context(nc.sbuf_tensor("bOutLc_s", [128, 8], F32))
        bInL_r = ctx.enter_context(nc.sbuf_tensor("bInLr_s", [1, D], BF16))
        bInG_r = ctx.enter_context(nc.sbuf_tensor("bInGr_s", [1, D], BF16))
        bvl_b = ctx.enter_context(nc.sbuf_tensor("bvl_b", [128, 1024], F32))
        bvg_b = ctx.enter_context(nc.sbuf_tensor("bvg_b", [128, 1024], F32))
        wr_s = ctx.enter_context(nc.sbuf_tensor("wr_s", [128, 8, E], BF16))
        br_c = ctx.enter_context(nc.sbuf_tensor("brc_s", [E, 1], F32))
        means0 = ctx.enter_context(nc.sbuf_tensor("means0", [64, 1024], BF16))
        means1 = ctx.enter_context(nc.sbuf_tensor("means1", [64, 1024], BF16))
        means_fm = ctx.enter_context(nc.sbuf_tensor("means_fm", [128, 8, 128], BF16))
        globB = ctx.enter_context(nc.sbuf_tensor("globB", [128, 8, 128], F32))
        sc_s = ctx.enter_context(nc.sbuf_tensor("sc_s", [128, 4], F32))
        sc_q = ctx.enter_context(nc.sbuf_tensor("sc_q", [128, 4], F32))
        sc_m = ctx.enter_context(nc.sbuf_tensor("sc_m", [128, 4], F32))
        sc_v = ctx.enter_context(nc.sbuf_tensor("sc_v", [128, 4], F32))
        sc_r = ctx.enter_context(nc.sbuf_tensor("sc_r", [128, 4], F32))
        sc_nmr = ctx.enter_context(nc.sbuf_tensor("sc_nmr", [128, 4], F32))
        smx8 = ctx.enter_context(nc.sbuf_tensor("smx8", [128, 8], F32))
        ssum8 = ctx.enter_context(nc.sbuf_tensor("ssum8", [128, 8], F32))
        x_s = ctx.enter_context(nc.sbuf_tensor("x_s", [128, 2, 4, 1024], F32))
        xln_s = ctx.enter_context(nc.sbuf_tensor("xln_s", [128, 4, 1024], BF16))
        xln_fm = ctx.enter_context(nc.sbuf_tensor("xln_fm", [128, 8, 512], BF16))
        qk_sb = ctx.enter_context(nc.sbuf_tensor("qk_sb", [128, 16, 512], BF16))
        v_tm = ctx.enter_context(nc.sbuf_tensor("v_tm", [128, 4, 1024], BF16))
        scm_s = ctx.enter_context(nc.sbuf_tensor("scm_s", [128, 8, 128], F32))
        prob_f = ctx.enter_context(nc.sbuf_tensor("prob_f", [128, 8, 128], F32))
        prob_b = ctx.enter_context(nc.sbuf_tensor("prob_b", [128, 8, 128], BF16))
        probT = ctx.enter_context(nc.sbuf_tensor("probT", [128, 8, 128], BF16))
        trash = prob_b[:].rearrange("p a k -> p (a k)")
        attn_fm = ctx.enter_context(nc.sbuf_tensor("attn_fm", [128, 8, 512], BF16))
        xln2_s = ctx.enter_context(nc.sbuf_tensor("xln2_s", [128, 4, 1024], BF16))
        sum_fm = xln_s[:].rearrange("p c (u k) -> p (c u) k", u=2)
        xln2_fm = ctx.enter_context(nc.sbuf_tensor("xln2_fm", [128, 8, 512], BF16))
        logit_em = ctx.enter_context(nc.sbuf_tensor("logit_em", [8, 512], F32))
        sm_tm = ctx.enter_context(nc.sbuf_tensor("sm_tm", [128, 4, 8], F32))
        sm_mx = ctx.enter_context(nc.sbuf_tensor("sm_mx", [128, 4], F32))
        sm_sum = ctx.enter_context(nc.sbuf_tensor("sm_sum", [128, 4], F32))
        sm_em = ctx.enter_context(nc.sbuf_tensor("sm_em", [8, 512], F32))
        qkg = qk_sb[:, 0:4, :].rearrange("p a (b k) -> p (a b) k", k=128)
        vg_tm = qk_sb[:, 4:6, :].rearrange("p a k -> p (a k)")
        vg_pair = qk_sb[0:64, 6:10, :].rearrange("p a k -> p (a k)").rearrange("p (b f) -> p b f", b=2)
        sg_f = scm_s[:, 0:2, :].rearrange("p a k -> p (a k)").rearrange("p (a k) -> p a k", a=4)
        pg_f = scm_s[:, 2:4, :].rearrange("p a k -> p (a k)").rearrange("p (a k) -> p a k", a=4)
        pg_b = prob_b[:, 0:2, :].rearrange("p a k -> p (a k)").rearrange("p (a k) -> p a k", a=4)
        pgT = probT[0:64, 0:4, :].rearrange("p a k -> p (a k)").rearrange("p (a k) -> p a k", a=4)
        ag_fm = attn_fm[:, :, 0:128]
        gmx = ctx.enter_context(nc.sbuf_tensor("gmx", [128, 4], F32))
        gsum = ctx.enter_context(nc.sbuf_tensor("gsum", [128, 4], F32))
        wInTl = ctx.enter_context(nc.sbuf_tensor("wInTl", [128, 8, 3 * D], BF16))
        wOutTl = ctx.enter_context(nc.sbuf_tensor("wOutTl", [128, 8, D], BF16))
        wInTg = wInTl
        wOutTg = wOutTl
        pm = ctx.enter_context(nc.psum_tensor("pm", [128, 1024], F32))
        ps_a = ctx.enter_context(nc.psum_tensor("ps_a", [128, 512], F32))
        ps_b = ctx.enter_context(nc.psum_tensor("ps_b", [128, 512], F32))
        ps_c = ctx.enter_context(nc.psum_tensor("ps_c", [128, 512], F32))
        ps_d = ctx.enter_context(nc.psum_tensor("ps_d", [128, 512], F32))
        ps_t = ctx.enter_context(nc.psum_tensor("ps_t", [128, 512], BF16))
        ps_t2 = ctx.enter_context(nc.psum_tensor("ps_t2", [128, 512], BF16))
        h_ld = ctx.enter_context(nc.semaphore("s_ld"))
        h_x0 = ctx.enter_context(nc.semaphore("s_x0"))
        h_x1 = ctx.enter_context(nc.semaphore("s_x1"))
        h_st = ctx.enter_context(nc.semaphore("s_st"))
        h_gw = ctx.enter_context(nc.semaphore("s_gw"))
        h_sm = ctx.enter_context(nc.semaphore("s_sm"))
        h_ve = ctx.enter_context(nc.semaphore("s_ve"))
        h_ac = ctx.enter_context(nc.semaphore("s_ac"))
        h_pe = ctx.enter_context(nc.semaphore("s_pe"))
        block = ctx.enter_context(nc.Block())
        for nm, hh in (("ld", h_ld), ("x0", h_x0), ("x1", h_x1), ("st", h_st),
                       ("gw", h_gw), ("sm", h_sm), ("ve", h_ve), ("ac", h_ac), ("pe", h_pe)):
            em.add_sem(nm, hh)

        for dst, src in (
            (identb[:], ident_dr.ap()),
            (ident_f32_s[:], identf_dr.ap()),
            (slide_s[:].rearrange("p c k -> p (c k)"), slide_dr.ap()),
            (mask_s[:], mask_dr.ap()),
            (onesb_s[:], onesb_dr.ap()),
            (bInL_c[:], bInL_dr.ap()),
            (bInG_c[:], bInG_dr.ap()),
            (bOutG_c[:], bOutG_dr.ap()),
            (bOutL_c[:], bOutL_dr.ap()),
            (br_c[:], br_dr.ap()),
        ):
            em.emit("sync", lambda s, d=dst, a=src: s.dma_start(d, a), inc="ld", k=16)
        v_consts = em.v("ld")

        for dst, src in (
            (bInL_r[:], bInL_row_dr.ap()),
            (bInG_r[:], bInG_row_dr.ap()),
            (wr_s[:], wr_dr.ap()),
            (wInTg[:], wInT_g_dr.ap().rearrange("(c p) f -> p c f", p=128)),
            (wOutTg[:], wOutT_g_dr.ap().rearrange("(c p) f -> p c f", p=128)),
        ):
            em.emit("gpsimd", lambda g, d=dst, a=src: g.dma_start(d, a), inc="gw", k=16)
        v_weights = em.v("gw")

        em.emit("vector", lambda v: v.memset(epsc[:], EPS), inc="ve")
        em.emit("vector",
                lambda v: v.tensor_tensor(bOutSum[:], bOutG_c[:], bOutL_c[:], ALU.add),
                waits=[("ld", v_consts)], inc="ve")
        v_bsum = em.v("ve")

        for dst, row in ((bvg_b, bInG_r), (bvl_b, bInL_r)):
            for n in range(2):
                em.emit("tensor",
                        lambda t, n=n, row=row: t.matmul(
                            ps_a[:, 0:512], onesb_s[0:1, :],
                            row[0:1, 512 * n:512 * (n + 1)],
                            start=True, stop=True),
                        waits=[("gw", v_weights), ("ld", v_consts), ("ve", em.v("ve"))],
                        inc="pe")
                em.emit("vector",
                        lambda v, n=n, dst=dst: v.tensor_copy(
                            dst[:, 512 * n:512 * (n + 1)], ps_a[:, 0:512]),
                        waits=[("pe", em.v("pe"))], inc="ve")
        v_vbias = em.v("ve")

        def cap(stage, ap, extra_waits=()):
            if dbg_stage != stage:
                return
            def _cap_dma(g, ap=ap):
                dst = dbg_dr.ap()[0:ap.shape[0], 0:int(np.prod(ap.shape[1:]))]
                if len(ap.shape) == 3:
                    dst = dst.rearrange("p (a k) -> p a k", a=ap.shape[1])
                return g.dma_start(dst, ap)
            em.emit("gpsimd", _cap_dma,
                    waits=[("ve", em.v("ve")), ("ac", em.v("ac")), ("pe", em.v("pe"))]
                          + list(extra_waits),
                    inc="sm", k=16)

        cap("wg", wInTg[:, 0:2, :].rearrange("p c f -> p (c f)"), [("gw", v_weights)])

        def emit_ln(src4, dst4, first_waits):
            for c in range(4):
                em.emit("scalar",
                        lambda s, c=c: s.activation(trash[:], src4[:, c, :], AF.Copy,
                                                    bias=0.0, scale=1.0,
                                                    accum_out=sc_s[:, c:c + 1]),
                        waits=(first_waits + [("ve", em.v("ve"))]) if c == 0 else [])
            for c in range(4):
                em.emit("scalar",
                        lambda s, c=c: s.activation(trash[:], src4[:, c, :], AF.Square,
                                                    bias=0.0, scale=1.0,
                                                    accum_out=sc_q[:, c:c + 1]),
                        inc=("ac" if c == 3 else None))
            v_stats = em.v("ac")
            em.emit("vector", lambda v: v.tensor_scalar(sc_m[:], sc_s[:], 1.0 / D, None, ALU.mult),
                    waits=[("ac", v_stats)])
            em.emit("vector", lambda v: v.tensor_scalar(sc_v[:], sc_q[:], 1.0 / D, None, ALU.mult))
            em.emit("vector", lambda v: v.tensor_tensor(sc_r[:], sc_m[:], sc_m[:], ALU.mult))
            em.emit("vector", lambda v: v.tensor_tensor(sc_v[:], sc_v[:], sc_r[:], ALU.subtract),
                    inc="ve")
            v_var = em.v("ve")
            em.emit("scalar", lambda s: s.activation(sc_r[:], sc_v[:], AF.Sqrt,
                                                     bias=epsc[:, 0:1], scale=1.0),
                    waits=[("ve", v_var)], inc="ac")
            v_sd = em.v("ac")
            em.emit("vector", lambda v: v.reciprocal(sc_r[:], sc_r[:]), waits=[("ac", v_sd)])
            em.emit("vector", lambda v: v.tensor_tensor(sc_nmr[:], sc_m[:], sc_r[:], ALU.mult))
            em.emit("vector", lambda v: v.tensor_scalar(sc_nmr[:], sc_nmr[:], -1.0, None, ALU.mult))
            for c in range(4):
                em.emit("vector",
                        lambda v, c=c: v.tensor_scalar(dst4[:, c, :], src4[:, c, :],
                                                       sc_r[:, c:c + 1], sc_nmr[:, c:c + 1],
                                                       ALU.mult, ALU.add),
                        inc=("ve" if c == 3 else None))
            return em.v("ve")

        x_load_done = [0] * (2 * NTILE)
        x_cons_done = [None] * (2 * NTILE)

        def emit_x_load(r):
            t = r % NTILE
            w = list(x_cons_done[r - 2]) if (r >= 2 and x_cons_done[r - 2]) else []
            sem = "x0" if r % 2 == 0 else "x1"
            em.emit("sync",
                    lambda s, t=t, r=r: s.dma_start(
                        x_s[:, r % 2], x_dr.ap()[t * 512:(t + 1) * 512, :].rearrange(
                            "(c p) f -> p c f", p=128)),
                    waits=w, inc=sem, k=16)
            x_load_done[r] = (sem, em.v(sem))

        emit_x_load(0)
        emit_x_load(1)
        cap("x0", x_s[:, 0].rearrange("p c f -> p (c f)"), [x_load_done[0]])

        cap("slide", slide_s[:].rearrange("p c k -> p (c k)"), [("ld", v_consts)])

        for t in range(NTILE):
            r = t
            xb = x_s[:, r % 2]
            v_apply = emit_ln(xb, xln_s, [x_load_done[r], ("pe", em.v("pe"))])
            if t == 0:
                cap("xlnp1", xln_s[:].rearrange("p c f -> p (c f)"))
                if dbg_stage == "stats":
                    for j, buf in enumerate((sc_s, sc_q, sc_m, sc_v, sc_r, sc_nmr)):
                        em.emit("vector",
                                lambda v, j=j, buf=buf: v.tensor_copy(
                                    sm_tm[:].rearrange("p a e -> p (a e)")[:, 4 * j:4 * j + 4],
                                    buf[:]),
                                waits=([("ve", v_apply)] if j == 0 else []),
                                inc=("ve" if j == 5 else None))
                    cap("stats", sm_tm[:].rearrange("p a e -> p (a e)"))
            ti = t % 8
            for c in range(4):
                for u in range(2):
                    em.emit("tensor",
                            lambda tn, c=c, u=u, ti=ti: tn.matmul(
                                pm[:, 512 * u:512 * (u + 1)],
                                slide_s[:, c, 64 - 8 * ti:192 - 8 * ti],
                                xln_s[:, c, 512 * u:512 * (u + 1)],
                                start=(ti == 0 and c == 0), stop=(ti == 7 and c == 3)),
                            waits=([("ve", v_apply)] if (c == 0 and u == 0) else []),
                            inc=("pe" if (c == 3 and u == 1) else None))
            x_cons_done[r] = [("pe", em.v("pe")), ("ve", em.v("ve")), ("ac", em.v("ac"))]
            if ti == 7:
                dstm = means0 if t < 8 else means1
                for u in range(2):
                    em.emit("vector",
                            lambda v, u=u, dstm=dstm: v.tensor_copy(
                                dstm[:, 512 * u:512 * (u + 1)], pm[0:64, 512 * u:512 * (u + 1)]),
                            waits=([("pe", em.v("pe"))] if u == 0 else []),
                            inc=("ve" if u == 1 else None))
                if t == 7:
                    cap("means0", means0[:])
            if r + 2 < 2 * NTILE:
                emit_x_load(r + 2)

        for b in range(2):
            srcm = means0 if b == 0 else means1
            for fc in range(8):
                bank = ps_t if fc % 2 == 0 else ps_t2
                em.emit("tensor",
                        lambda tn, srcm=srcm, fc=fc, bank=bank: tn.transpose(
                            bank[:, 0:64], srcm[:, 128 * fc:128 * (fc + 1)],
                            identb[0:64, 0:64]),
                        waits=[("ve", em.v("ve"))], inc="pe")
                em.emit("vector",
                        lambda v, b=b, fc=fc, bank=bank: v.tensor_copy(
                            means_fm[:, fc, 64 * b:64 * (b + 1)], bank[:, 0:64]),
                        waits=[("pe", em.v("pe"))], inc="ve")
        v_mfm = em.v("ve")
        cap("mfm", means_fm[:].rearrange("p a k -> p (a k)"))

        for m in range(16):
            bank = ps_a if m % 2 == 0 else ps_b
            for k in range(8):
                em.emit("tensor",
                        lambda tn, m=m, k=k, bank=bank: tn.matmul(
                            bank[:, 0:128], wInTg[:, k, 128 * m:128 * (m + 1)],
                            means_fm[:, k, :], start=(k == 0), stop=(k == 7)),
                        waits=(([("ve", v_mfm)] if m == 0 else [("ac", max(0, em.v("ac") - 1))])
                               if k == 0 else []),
                        inc=("pe" if k == 7 else None))
            em.emit("scalar",
                    lambda s, m=m, bank=bank: s.activation(
                        qkg[:, m, :], bank[:, 0:128], AF.Identity,
                        bias=bInG_c[:, m:m + 1], scale=1.0),
                    waits=[("pe", em.v("pe"))], inc="ac")
        v_qkg = em.v("ac")
        cap("qkg", qkg[:].rearrange("p a k -> p (a k)"))

        for n in range(2):
            for k in range(8):
                em.emit("tensor",
                        lambda tn, n=n, k=k: tn.matmul(
                            ps_c[:, 0:512], means_fm[:, k, :],
                            wInTg[:, k, 2048 + 512 * n:2048 + 512 * (n + 1)],
                            start=(k == 0), stop=(k == 7)),
                        waits=([("ve", em.v("ve"))] if k == 0 else []),
                        inc=("pe" if k == 7 else None))
            em.emit("vector",
                    lambda v, n=n: v.tensor_tensor(
                        vg_tm[:, 512 * n:512 * (n + 1)], ps_c[:, 0:512],
                        bvg_b[:, 512 * n:512 * (n + 1)], ALU.add),
                    waits=[("pe", em.v("pe")), ("ve", v_vbias)], inc="ve")
        v_vg = em.v("ve")
        cap("vg", vg_tm[:])

        em.emit("vector", lambda v: v.tensor_copy(vg_pair[:, 0, :], vg_tm[0:64, :]),
                waits=[("ve", v_vg)], inc="ve")
        em.emit("gpsimd", lambda g: g.dma_start(vg_pair[:, 1, :], vg_tm[64:128, :]),
                waits=[("ve", em.v("ve"))], inc="gw", k=16)
        v_vgp_gp = em.v("gw")

        for h in range(4):
            for b in range(2):
                for kc in range(2):
                    em.emit("tensor",
                            lambda tn, h=h, b=b, kc=kc: tn.matmul(
                                ps_a[64 * b:64 * (b + 1), 64 * h:64 * (h + 1)],
                                qkg[:, 2 * h + kc, 64 * b:64 * (b + 1)],
                                qkg[:, 8 + 2 * h + kc, 64 * b:64 * (b + 1)],
                                start=(kc == 0), stop=(kc == 1),
                                tile_position=(0, 64 * b)),
                            waits=([("ac", v_qkg)] if (h == 0 and b == 0 and kc == 0) else []),
                            inc=("pe" if (h == 3 and b == 1 and kc == 1) else None))
        v_sgp = em.v("pe")

        em.emit("vector", lambda v: v.tensor_copy(
            sg_f[:].rearrange("p a k -> p (a k)"), ps_a[:, 0:256]),
            waits=[("pe", v_sgp)], inc="ve")
        cap("sg", sg_f[:].rearrange("p a k -> p (a k)"))
        em.emit("vector", lambda v: v.reduce_max(gmx[:], sg_f[:], axis=AX.X))
        em.emit("vector", lambda v: v.tensor_tensor(
            pg_f[:], sg_f[:],
            gmx[:].rearrange("p (g o) -> p g o", o=1).broadcast_to([128, 4, 64]),
            ALU.subtract), inc="ve")
        em.emit("scalar", lambda s: s.activation(
            pg_f[:].rearrange("p a k -> p (a k)"), pg_f[:].rearrange("p a k -> p (a k)"),
            AF.Exp, bias=0.0, scale=SCG), waits=[("ve", em.v("ve"))], inc="ac")
        em.emit("vector", lambda v: v.reduce_sum(gsum[:], pg_f[:], axis=AX.X),
                waits=[("ac", em.v("ac"))])
        em.emit("vector", lambda v: v.reciprocal(gsum[:], gsum[:]))
        em.emit("vector", lambda v: v.tensor_tensor(
            pg_b[:], pg_f[:],
            gsum[:].rearrange("p (g o) -> p g o", o=1).broadcast_to([128, 4, 64]),
            ALU.mult), inc="ve")
        v_pgb = em.v("ve")
        cap("pgb", pg_b[:].rearrange("p a k -> p (a k)"))

        for h in range(4):
            em.emit("tensor",
                    lambda tn, h=h: tn.transpose(ps_t[0:64, 128 * h:128 * (h + 1)],
                                                 pg_b[:, h, :], identb[:]),
                    waits=([("ve", v_pgb)] if h == 0 else []),
                    inc=("pe" if h == 3 else None))
        em.emit("vector", lambda v: v.tensor_copy(
            pgT[:].rearrange("p a k -> p (a k)"), ps_t[0:64, 0:512]),
            waits=[("pe", em.v("pe"))], inc="ve")
        v_pgT = em.v("ve")

        for b in range(2):
            bank = ps_b if b == 0 else ps_c
            for h in range(4):
                for j in range(2):
                    em.emit("tensor",
                            lambda tn, b=b, h=h, j=j, bank=bank: tn.matmul(
                                bank[:, 64 * (2 * h + j):64 * (2 * h + j) + 64],
                                vg_pair[:, b, 128 * (2 * h + j):128 * (2 * h + j) + 128],
                                pgT[:, h, 64 * b:64 * (b + 1)],
                                start=True, stop=True),
                            waits=([("ve", v_pgT), ("gw", v_vgp_gp)]
                                   if (b == 0 and h == 0 and j == 0) else []),
                            inc=("pe" if (h == 3 and j == 1) else None))
            em.emit("vector",
                    lambda v, b=b, bank=bank: v.tensor_copy(
                        ag_fm[:, :, 64 * b:64 * (b + 1)],
                        bank[:, 0:512].rearrange("p (a k) -> p a k", a=8)),
                    waits=[("pe", em.v("pe"))], inc="ve")
        v_ag = em.v("ve")
        cap("ag", ag_fm[:])

        for m in range(8):
            bank = ps_a if m % 2 == 0 else ps_b
            for k in range(8):
                em.emit("tensor",
                        lambda tn, m=m, k=k, bank=bank: tn.matmul(
                            bank[:, 0:128], wOutTg[:, k, 128 * m:128 * (m + 1)],
                            ag_fm[:, k, :], start=(k == 0), stop=(k == 7)),
                        waits=(([("ve", v_ag)] if m == 0 else [("ac", max(0, em.v("ac") - 1))])
                               if k == 0 else []),
                        inc=("pe" if k == 7 else None))
            em.emit("scalar",
                    lambda s, m=m, bank=bank: s.activation(
                        globB[:, m, :], bank[:, 0:128], AF.Identity,
                        bias=bOutSum[:, m:m + 1], scale=1.0),
                    waits=[("pe", em.v("pe")), ("ve", v_bsum)], inc="ac")
        v_glob = em.v("ac")
        cap("glob", globB[:].rearrange("p a k -> p (a k)"))

        em.emit("gpsimd",
                lambda g: g.dma_start(
                    wInTl[:], wInT_l_dr.ap().rearrange("(c p) f -> p c f", p=128)),
                waits=[("pe", em.v("pe")), ("ac", v_glob)], inc="gw", k=16)
        em.emit("gpsimd",
                lambda g: g.dma_start(
                    wOutTl[:], wOutT_l_dr.ap().rearrange("(c p) f -> p c f", p=128)),
                inc="gw", k=16)
        v_wl = em.v("gw")

        store_x1_prev = 0
        store_xln2_prev = 0
        store_sm_prev = 0
        for t in range(NTILE):
            r = NTILE + t
            xb = x_s[:, r % 2]
            b = t // 8
            ti = t % 8
            v_apply = emit_ln(xb, xln_s, [x_load_done[r], ("pe", em.v("pe"))])
            if t == 0:
                cap("xln1", xln_s[:].rearrange("p c f -> p (c f)"))

            ev_par = {0: 0, 1: 0}
            for g in range(8):
                c, half = divmod(g, 2)
                bank = ps_t if g % 2 == 0 else ps_t2
                for q in range(4):
                    fc = 4 * half + q
                    em.emit("tensor",
                            lambda tn, c=c, fc=fc, q=q, bank=bank: tn.transpose(
                                bank[:, 128 * q:128 * (q + 1)],
                                xln_s[:, c, 128 * fc:128 * (fc + 1)], identb[:]),
                            waits=(([("ve", v_apply)] if g == 0 else [("ve", ev_par[g % 2])])
                                   if q == 0 else []),
                            inc=("pe" if q == 3 else None))
                em.emit("vector",
                        lambda v, c=c, half=half, bank=bank: v.tensor_copy(
                            xln_fm[:, 4 * half:4 * half + 4, 128 * c:128 * (c + 1)],
                            bank[:, 0:512].rearrange("p (a k) -> p a k", a=4)),
                        waits=[("pe", em.v("pe"))], inc="ve")
                ev_par[g % 2] = em.v("ve")
            v_xfm = em.v("ve")

            ev_par = {0: 0, 1: 0}
            for m in range(16):
                bank = ps_a if m % 2 == 0 else ps_b
                for k in range(8):
                    em.emit("tensor",
                            lambda tn, m=m, k=k, bank=bank: tn.matmul(
                                bank[:, 0:512], wInTl[:, k, 128 * m:128 * (m + 1)],
                                xln_fm[:, k, :], start=(k == 0), stop=(k == 7)),
                            waits=(([("ve", v_xfm), ("gw", v_wl)] if m == 0 else
                                    ([("ac", ev_par[m % 2])] if m >= 2 else []))
                                   if k == 0 else []),
                            inc=("pe" if k == 7 else None))
                em.emit("scalar",
                        lambda s, m=m, bank=bank: s.activation(
                            qk_sb[:, m, :], bank[:, 0:512], AF.Identity,
                            bias=bInL_c[:, m:m + 1], scale=1.0),
                        waits=[("pe", em.v("pe"))] + ([("ac", v_glob)] if (t == 0 and m == 0) else []),
                        inc="ac")
                ev_par[m % 2] = em.v("ac")
            v_qk = em.v("ac")
            if t == 0:
                cap("qk0", qk_sb[:, 0:4, :].rearrange("p a k -> p (a k)"))

            ev_par = {0: 0, 1: 0}
            for c in range(4):
                for n in range(2):
                    gidx = 2 * c + n
                    bank = ps_c if gidx % 2 == 0 else ps_d
                    for k in range(8):
                        em.emit("tensor",
                                lambda tn, c=c, n=n, k=k, bank=bank: tn.matmul(
                                    bank[:, 0:512], xln_fm[:, k, 128 * c:128 * (c + 1)],
                                    wInTl[:, k, 2048 + 512 * n:2048 + 512 * (n + 1)],
                                    start=(k == 0), stop=(k == 7)),
                                waits=([("ve", ev_par[gidx % 2])] if (k == 0 and gidx >= 2) else []),
                                inc=("pe" if k == 7 else None))
                    em.emit("vector",
                            lambda v, c=c, n=n, bank=bank: v.tensor_tensor(
                                v_tm[:, c, 512 * n:512 * (n + 1)], bank[:, 0:512],
                                bvl_b[:, 512 * n:512 * (n + 1)], ALU.add),
                            waits=[("pe", em.v("pe"))], inc="ve")
                    ev_par[gidx % 2] = em.v("ve")
            v_v = em.v("ve")

            for c in range(4):
                for h in range(8):
                    bank = ps_a if h < 4 else ps_b
                    em.emit("tensor",
                            lambda tn, c=c, h=h, bank=bank: tn.matmul(
                                bank[:, 128 * (h % 4):128 * (h % 4) + 128],
                                qk_sb[:, h, 128 * c:128 * (c + 1)],
                                qk_sb[:, 8 + h, 128 * c:128 * (c + 1)],
                                start=True, stop=True),
                            waits=([("ac", v_qk), ("ve", em.v("ve"))] if h == 0 else []),
                            inc=("pe" if h == 7 else None))
                v_scp = em.v("pe")
                for h in range(8):
                    bank = ps_a if h < 4 else ps_b
                    em.emit("vector",
                            lambda v, h=h, bank=bank: v.tensor_tensor(
                                scm_s[:, h, :], bank[:, 128 * (h % 4):128 * (h % 4) + 128],
                                mask_s[:], ALU.add),
                            waits=([("pe", v_scp)] if h == 0 else []),
                            inc=("ve" if h == 7 else None))
                em.emit("vector", lambda v: v.reduce_max(smx8[:], scm_s[:], axis=AX.X))
                em.emit("vector", lambda v: v.tensor_tensor(
                    prob_f[:], scm_s[:],
                    smx8[:].rearrange("p (g o) -> p g o", o=1).broadcast_to([128, 8, 128]),
                    ALU.subtract), inc="ve")
                em.emit("scalar", lambda s: s.activation(
                    scm_s[:].rearrange("p a k -> p (a k)"),
                    prob_f[:].rearrange("p a k -> p (a k)"),
                    AF.Exp, bias=0.0, scale=SCL),
                    waits=[("ve", em.v("ve"))], inc="ac")
                em.emit("vector", lambda v: v.reduce_sum(ssum8[:], scm_s[:], axis=AX.X),
                        waits=[("ac", em.v("ac"))])
                em.emit("vector", lambda v: v.reciprocal(ssum8[:], ssum8[:]))
                em.emit("vector", lambda v: v.tensor_tensor(
                    prob_b[:], scm_s[:],
                    ssum8[:].rearrange("p (g o) -> p g o", o=1).broadcast_to([128, 8, 128]),
                    ALU.mult), inc="ve")
                v_pb = em.v("ve")

                for h in range(8):
                    bank = ps_t if h < 4 else ps_t2
                    em.emit("tensor",
                            lambda tn, h=h, bank=bank: tn.transpose(
                                bank[:, 128 * (h % 4):128 * (h % 4) + 128],
                                prob_b[:, h, :], identb[:]),
                            waits=([("ve", v_pb)] if h == 0 else []),
                            inc=("pe" if h % 4 == 3 else None))
                v_ptp = em.v("pe")
                em.emit("vector", lambda v: v.tensor_copy(
                    probT[:, 0:4, :].rearrange("p a k -> p (a k)"), ps_t[:, 0:512]),
                    waits=[("pe", v_ptp - 1)])
                em.emit("vector", lambda v: v.tensor_copy(
                    probT[:, 4:8, :].rearrange("p a k -> p (a k)"), ps_t2[:, 0:512]),
                    waits=[("pe", v_ptp)], inc="ve")
                v_pT = em.v("ve")

                for h in range(8):
                    bank = ps_a if h < 4 else ps_b
                    em.emit("tensor",
                            lambda tn, c=c, h=h, bank=bank: tn.matmul(
                                bank[:, 128 * (h % 4):128 * (h % 4) + 128],
                                v_tm[:, c, 128 * h:128 * (h + 1)],
                                probT[:, h, :], start=True, stop=True),
                            waits=([("ve", v_pT)] if h == 0 else []),
                            inc=("pe" if h == 7 else None))
                v_avp = em.v("pe")
                em.emit("vector",
                        lambda v, c=c: v.tensor_copy(
                            attn_fm[:, 0:4, 128 * c:128 * (c + 1)],
                            ps_a[:, 0:512].rearrange("p (a k) -> p a k", a=4)),
                        waits=[("pe", v_avp)])
                em.emit("vector",
                        lambda v, c=c: v.tensor_copy(
                            attn_fm[:, 4:8, 128 * c:128 * (c + 1)],
                            ps_b[:, 0:512].rearrange("p (a k) -> p a k", a=4)),
                        inc="ve")
            v_attn = em.v("ve")

            seg0 = 8 * ti
            ev_par = {0: 0, 1: 0}
            for m in range(8):
                bank = ps_c if m % 2 == 0 else ps_d
                for k in range(8):
                    em.emit("tensor",
                            lambda tn, m=m, k=k, bank=bank: tn.matmul(
                                bank[:, 0:512], wOutTl[:, k, 128 * m:128 * (m + 1)],
                                attn_fm[:, k, :], start=(k == 0), stop=(k == 7)),
                            waits=(([("ve", v_attn)] if m == 0 else
                                    ([("ve", ev_par[m % 2])] if m >= 2 else []))
                                   if k == 0 else []),
                            inc=("pe" if k == 7 else None))
                em.emit("vector",
                        lambda v, m=m, b=b, seg0=seg0, bank=bank: v.tensor_tensor(
                            sum_fm[:, m, :].rearrange("p (s q) -> p s q", s=8),
                            bank[:, 0:512].rearrange("p (s q) -> p s q", s=8),
                            globB[:, m, 64 * b + seg0:64 * b + seg0 + 8]
                            .rearrange("p (s o) -> p s o", o=1).broadcast_to([128, 8, 64]),
                            ALU.add),
                        waits=[("pe", em.v("pe"))] + ([("ac", v_glob)] if (t == 0 and m == 0) else []),
                        inc="ve")
                ev_par[m % 2] = em.v("ve")
            v_sum = em.v("ve")

            ev_par = {0: 0, 1: 0}
            for g in range(8):
                c, half = divmod(g, 2)
                bank = ps_t if g % 2 == 0 else ps_t2
                for q in range(4):
                    fc = 4 * half + q
                    em.emit("tensor",
                            lambda tn, c=c, fc=fc, q=q, bank=bank: tn.transpose(
                                bank[:, 128 * q:128 * (q + 1)],
                                sum_fm[:, fc, 128 * c:128 * (c + 1)], identb[:]),
                            waits=(([("ve", v_sum)] if g == 0 else [("ve", ev_par[g % 2])])
                                   if q == 0 else []),
                            inc=("pe" if q == 3 else None))
                em.emit("vector",
                        lambda v, c=c, half=half, bank=bank, xb=xb: v.tensor_tensor(
                            xb[:, c, 512 * half:512 * (half + 1)],
                            xb[:, c, 512 * half:512 * (half + 1)],
                            bank[:, 0:512], ALU.add),
                        waits=[("pe", em.v("pe"))],
                        inc="ve")
                ev_par[g % 2] = em.v("ve")

            if t == 0:
                cap("x1t0", xb[:].rearrange("p c f -> p (c f)"))

            v_ln2 = emit_ln(xb, xln2_s,
                            [("ve", em.v("ve")), ("st", store_xln2_prev)])

            em.emit("sync",
                    lambda s, t=t, xb=xb: s.dma_start(
                        x1_dr.ap()[t * 512:(t + 1) * 512, :].rearrange("(c p) f -> p c f", p=128),
                        xb[:]),
                    waits=[("ve", em.v("ve"))], inc="st", k=16)
            store_x1_prev = em.v("st")
            x_cons_done[r] = [("ve", em.v("ve")), ("ac", em.v("ac")), ("st", store_x1_prev)]
            if r + 2 < 2 * NTILE:
                emit_x_load(r + 2)
            em.emit("sync",
                    lambda s, t=t: s.dma_start(
                        xln2_dr.ap()[t * 512:(t + 1) * 512, :].rearrange("(c p) f -> p c f", p=128),
                        xln2_s[:]),
                    waits=[("ve", v_ln2)], inc="st", k=16)
            store_xln2_prev = em.v("st")

            ev_par = {0: 0, 1: 0}
            for g in range(8):
                c, half = divmod(g, 2)
                bank = ps_t if g % 2 == 0 else ps_t2
                for q in range(4):
                    fc = 4 * half + q
                    em.emit("tensor",
                            lambda tn, c=c, fc=fc, q=q, bank=bank: tn.transpose(
                                bank[:, 128 * q:128 * (q + 1)],
                                xln2_s[:, c, 128 * fc:128 * (fc + 1)], identb[:]),
                            waits=(([("ve", v_ln2)] if g == 0 else [("ve", ev_par[g % 2])])
                                   if q == 0 else []),
                            inc=("pe" if q == 3 else None))
                em.emit("vector",
                        lambda v, c=c, half=half, bank=bank: v.tensor_copy(
                            xln2_fm[:, 4 * half:4 * half + 4, 128 * c:128 * (c + 1)],
                            bank[:, 0:512].rearrange("p (a k) -> p a k", a=4)),
                        waits=[("pe", em.v("pe"))], inc="ve")
                ev_par[g % 2] = em.v("ve")
            v_x2fm = em.v("ve")

            for k in range(8):
                em.emit("tensor",
                        lambda tn, k=k: tn.matmul(
                            ps_c[0:8, 0:512], wr_s[:, k, :], xln2_fm[:, k, :],
                            start=(k == 0), stop=(k == 7)),
                        waits=([("ve", v_x2fm)] if k == 0 else []),
                        inc=("pe" if k == 7 else None))
            em.emit("scalar",
                    lambda s: s.activation(logit_em[:], ps_c[0:8, 0:512], AF.Identity,
                                           bias=br_c[:, 0:1], scale=1.0),
                    waits=[("pe", em.v("pe"))], inc="ac")
            v_lg = em.v("ac")
            for c in range(4):
                em.emit("tensor",
                        lambda tn, c=c: tn.transpose(
                            ps_d[:, 8 * c:8 * (c + 1)],
                            logit_em[:, 128 * c:128 * (c + 1)], ident_f32_s[0:8, 0:8]),
                        waits=([("ac", v_lg)] if c == 0 else []),
                        inc=("pe" if c == 3 else None))
            v_lgt = em.v("pe")
            em.emit("vector", lambda v: v.tensor_copy(
                sm_tm[:].rearrange("p a e -> p (a e)"), ps_d[:, 0:32]),
                waits=[("pe", v_lgt)])
            em.emit("vector", lambda v: v.reduce_max(sm_mx[:], sm_tm[:], axis=AX.X))
            em.emit("vector", lambda v: v.tensor_tensor(
                sm_tm[:], sm_tm[:],
                sm_mx[:].rearrange("p (g o) -> p g o", o=1).broadcast_to([128, 4, 8]),
                ALU.subtract), inc="ve")
            em.emit("scalar", lambda s: s.activation(
                sm_tm[:].rearrange("p a e -> p (a e)"), sm_tm[:].rearrange("p a e -> p (a e)"),
                AF.Exp, bias=0.0, scale=1.0), waits=[("ve", em.v("ve"))], inc="ac")
            em.emit("vector", lambda v: v.reduce_sum(sm_sum[:], sm_tm[:], axis=AX.X),
                    waits=[("ac", em.v("ac"))])
            em.emit("vector", lambda v: v.reciprocal(sm_sum[:], sm_sum[:]))
            em.emit("vector", lambda v: v.tensor_tensor(
                sm_tm[:], sm_tm[:],
                sm_sum[:].rearrange("p (g o) -> p g o", o=1).broadcast_to([128, 4, 8]),
                ALU.mult), inc="ve")
            v_smtm = em.v("ve")
            for c in range(4):
                em.emit("tensor",
                        lambda tn, c=c: tn.transpose(
                            ps_c[0:8, 128 * c:128 * (c + 1)], sm_tm[:, c, :],
                            ident_f32_s[:]),
                        waits=([("ve", v_smtm)] if c == 0 else []),
                        inc=("pe" if c == 3 else None))
            em.emit("vector", lambda v: v.tensor_copy(sm_em[:], ps_c[0:8, 0:512]),
                    waits=[("pe", em.v("pe")), ("sm", store_sm_prev)], inc="ve")
            em.emit("gpsimd",
                    lambda g, t=t: g.dma_start(
                        sm_dr.ap()[:, t * 512:(t + 1) * 512], sm_em[:]),
                    waits=[("ve", em.v("ve"))], inc="sm", k=16)
            store_sm_prev = em.v("sm")

        v_all_st = em.v("st")
        v_all_sm = em.v("sm")

        @block.sync
        def _(sync):
            em.run_engine("sync", sync)

        @block.gpsimd
        def _(gpsimd):
            em.run_engine("gpsimd", gpsimd)
            gpsimd.wait_ge(h_st, v_all_st)
            gpsimd.wait_ge(h_sm, v_all_sm)

        @block.vector
        def _(vector):
            em.run_engine("vector", vector)

        @block.scalar
        def _(scalar):
            em.run_engine("scalar", scalar)

        @block.tensor
        def _(tensor):
            em.run_engine("tensor", tensor)

    return nc



def _consts_b():
    ident_bf = np.eye(128, dtype=ml_dtypes.bfloat16)
    tril = np.zeros((128, 128), dtype=ml_dtypes.bfloat16)
    for p in range(128):
        tril[p, p + 1:] = 1.0
    diagbig = (np.eye(128) * BIG).astype(ml_dtypes.bfloat16)
    onesb = np.ones((128, 128), dtype=ml_dtypes.bfloat16)
    onesf = np.ones((128, 128), dtype=np.float32)
    t = np.arange(NTOK)
    ids = np.stack([t >> 6, t & 63], axis=1).astype(ml_dtypes.bfloat16)
    ids = ids.reshape(64, 128, 2).transpose(1, 0, 2).copy()
    s = np.arange(CAP)
    im = np.zeros((CAP, 8), dtype=ml_dtypes.bfloat16)
    im[:, 0] = ((NTOK + (s % 128)) >> 6).astype(ml_dtypes.bfloat16)
    im[:, 1] = ((NTOK + (s % 128)) & 63).astype(ml_dtypes.bfloat16)
    return ident_bf, tril, diagbig, onesb, onesf, ids, im


def build_launch_b(dbg_stage=None):
    nc = bass.Bass("TRN2", target_bir_lowering=False, detect_race_conditions=False)

    x1_dr = nc.dram_tensor("x1", [NTOK, D], F32, kind="ExternalInput")
    xln2_dr = nc.dram_tensor("xln2", [NTOK, D], BF16, kind="ExternalInput")
    smg_dr = nc.dram_tensor("smg", [E, 65536], F32, kind="ExternalInput")
    smtm_dr = nc.dram_tensor("smtm", [NTOK, E], F32, kind="ExternalInput")
    wu_dr = nc.dram_tensor("wu", [4, D, FP], F32, kind="ExternalInput")
    buc_dr = nc.dram_tensor("bu_cols", [128, 4 * NFC], F32, kind="ExternalInput")
    wd_dr = nc.dram_tensor("wd", [E, FP, D], F32, kind="ExternalInput")
    bd_dr = nc.dram_tensor("bd", [E, D], F32, kind="ExternalInput")

    y_dr = nc.dram_tensor("y", [YROWS, D], F32, kind="ExternalOutput")
    dbg_dr = nc.dram_tensor("dbg", [128, 8192], F32, kind="ExternalOutput") if dbg_stage else None

    meta_dr = nc.dram_tensor("meta_scr", [E, CAP, 8], BF16, kind="Internal")

    identc, trilc, diagc, onesbc, onesfc, idsc, imc = _consts_b()
    ident_dr = nc.inline_tensor(identc, "ident_bf")
    tril_dr = nc.inline_tensor(trilc, "trilx")
    diag_dr = nc.inline_tensor(diagc, "diagbig")
    onesb_dr = nc.inline_tensor(onesbc, "onesb")
    onesf_dr = nc.inline_tensor(onesfc, "onesf")
    ids_dr = nc.inline_tensor(idsc.reshape(128, 128), "ids")
    im_dr = nc.inline_tensor(imc, "initmeta")

    em = EM()

    with ExitStack() as ctx:
        identb = ctx.enter_context(nc.sbuf_tensor("identb", [128, 128], BF16))
        tril_s = ctx.enter_context(nc.sbuf_tensor("tril_s", [128, 128], BF16))
        diag_s = ctx.enter_context(nc.sbuf_tensor("diag_s", [128, 128], BF16))
        onesb_s = ctx.enter_context(nc.sbuf_tensor("onesb_s", [128, 128], BF16))
        onesf_s = ctx.enter_context(nc.sbuf_tensor("onesf_s", [128, 128], F32))
        ids_s = ctx.enter_context(nc.sbuf_tensor("ids_s", [128, 64, 2], BF16))
        im_s = ctx.enter_context(nc.sbuf_tensor("im_s", [128, 10, 8], BF16))
        smtm_s = ctx.enter_context(nc.sbuf_tensor("smtm_s", [128, 64, 8], F32))
        lo_s = ctx.enter_context(nc.sbuf_tensor("lo_s", [128, 8], F32))
        hi_s = ctx.enter_context(nc.sbuf_tensor("hi_s", [128, 8], F32))
        mid_s = ctx.enter_context(nc.sbuf_tensor("mid_s", [128, 8], F32))
        cnt_s = ctx.enter_context(nc.sbuf_tensor("cnt_s", [128, 8], F32))
        flag_s = ctx.enter_context(nc.sbuf_tensor("flag_s", [128, 8], F32))
        tmp1_s = ctx.enter_context(nc.sbuf_tensor("tmp1_s", [128, 8], F32))
        tmp2_s = ctx.enter_context(nc.sbuf_tensor("tmp2_s", [128, 8], F32))
        trash_b = ctx.enter_context(nc.sbuf_tensor("trash_b", [128, 512], BF16))
        m_all = ctx.enter_context(nc.sbuf_tensor("m_all", [128, 64, 8], BF16))
        minv_all = ctx.enter_context(nc.sbuf_tensor("minv_all", [128, 64, 8], BF16))
        pos_all = ctx.enter_context(nc.sbuf_tensor("pos_all", [128, 64, 8], U32))
        meta_all = ctx.enter_context(nc.sbuf_tensor("meta_all", [128, 64, 8, 8], BF16))
        trow = ctx.enter_context(nc.sbuf_tensor("trow", [1, 512], F32))
        trow2 = ctx.enter_context(nc.sbuf_tensor("trow2", [1, 512], F32))
        offs = ctx.enter_context(nc.sbuf_tensor("offs", [1, 512], F32))
        meta_ld = ctx.enter_context(nc.sbuf_tensor("meta_ld", [128, 10, 8], BF16))
        idx_f = ctx.enter_context(nc.sbuf_tensor("idx_f", [128, 10], F32))
        idx_u = ctx.enter_context(nc.sbuf_tensor("idx_u", [128, 10], U32))
        gates = ctx.enter_context(nc.sbuf_tensor("gates", [128, 10], F32))
        bu_c = ctx.enter_context(nc.sbuf_tensor("bu_c", [128, 4 * NFC], F32))
        bd_r = ctx.enter_context(nc.sbuf_tensor("bd_r", [1, 1024], BF16))
        bdb = ctx.enter_context(nc.sbuf_tensor("bdb", [128, 1024], F32))
        wu_s = ctx.enter_context(nc.sbuf_tensor("wu_s", [128, 8, FP], BF16))
        wd_s = ctx.enter_context(nc.sbuf_tensor("wd_s", [128, NFC, 1024], BF16))
        xg_tm = ctx.enter_context(nc.sbuf_tensor("xg_tm", [128, 2, 1024], BF16))
        xg_fm = ctx.enter_context(nc.sbuf_tensor("xg_fm", [128, 8, CAP], BF16))
        mid_arena = ctx.enter_context(nc.sbuf_tensor("mid_arena", [128, NFC, CAP], BF16))
        contrib = ctx.enter_context(nc.sbuf_tensor("contrib", [128, 2, 1024], F32))
        ps_a = ctx.enter_context(nc.psum_tensor("ps_a", [128, 512], F32))
        ps_b = ctx.enter_context(nc.psum_tensor("ps_b", [128, 512], F32))
        ps_c = ctx.enter_context(nc.psum_tensor("ps_c", [128, 512], F32))
        ps_d = ctx.enter_context(nc.psum_tensor("ps_d", [128, 512], F32))
        ps_t = ctx.enter_context(nc.psum_tensor("ps_t", [128, 512], BF16))
        ps_t2 = ctx.enter_context(nc.psum_tensor("ps_t2", [128, 512], BF16))
        ps_x = ctx.enter_context(nc.psum_tensor("ps_x", [128, 8], F32))
        h_ld = ctx.enter_context(nc.semaphore("s_ld"))
        h_mi = ctx.enter_context(nc.semaphore("s_mi"))
        h_ml = ctx.enter_context(nc.semaphore("s_ml"))
        h_w = ctx.enter_context(nc.semaphore("s_w"))
        h_gx0 = ctx.enter_context(nc.semaphore("s_gx0"))
        h_gx1 = ctx.enter_context(nc.semaphore("s_gx1"))
        h_ys0 = ctx.enter_context(nc.semaphore("s_ys0"))
        h_ys1 = ctx.enter_context(nc.semaphore("s_ys1"))
        h_sc = [ctx.enter_context(nc.semaphore(f"s_sc{i}")) for i in range(E)]
        h_ve = ctx.enter_context(nc.semaphore("s_ve"))
        h_ac = ctx.enter_context(nc.semaphore("s_ac"))
        h_pe = ctx.enter_context(nc.semaphore("s_pe"))
        block = ctx.enter_context(nc.Block())
        for nm, hh in (("ld", h_ld), ("mi", h_mi), ("ml", h_ml), ("w", h_w),
                       ("gx0", h_gx0), ("gx1", h_gx1), ("ys0", h_ys0), ("ys1", h_ys1),
                       ("ve", h_ve), ("ac", h_ac), ("pe", h_pe)):
            em.add_sem(nm, hh)
        for i in range(E):
            em.add_sem(f"sc{i}", h_sc[i])

        smg_v = mid_arena[:, 0:8, 0:1024].bitcast(F32)

        regs = {}

        def _mkregs(g):
            regs["cap"] = g.to_reg(CAP - 1)
            regs["ntok"] = g.to_reg(NTOK - 1)

        em.emit("gpsimd", _mkregs)

        def gp_dma(fn, sem, waits=()):
            em.emit("gpsimd", fn, waits=list(waits), inc=sem, k=16)
            return em.v(sem)

        def cap(stage, ap, extra_waits=()):
            if dbg_stage != stage:
                return
            def _cap_dma(g, ap=ap):
                dst = dbg_dr.ap()[0:ap.shape[0], 0:int(np.prod(ap.shape[1:]))]
                if len(ap.shape) == 3:
                    dst = dst.rearrange("p (a k) -> p a k", a=ap.shape[1])
                return g.dma_start(dst, ap)
            em.emit("gpsimd", _cap_dma,
                    waits=[("ve", em.v("ve")), ("ac", em.v("ac")), ("pe", em.v("pe"))]
                          + list(extra_waits),
                    inc="mi", k=16)

        for dst, src in (
            (identb[:], ident_dr.ap()),
            (tril_s[:], tril_dr.ap()),
            (diag_s[:], diag_dr.ap()),
            (onesb_s[:], onesb_dr.ap()),
            (onesf_s[:], onesf_dr.ap()),
            (ids_s[:].rearrange("p a b -> p (a b)"), ids_dr.ap()),
            (bu_c[:], buc_dr.ap()),
            (smtm_s[:], smtm_dr.ap().rearrange("(j p) e -> p j e", p=128)),
        ):
            em.emit("sync", lambda s, d=dst, a=src: s.dma_start(d, a), inc="ld", k=16)
        for e in range(E):
            em.emit("sync",
                    lambda s, e=e: s.dma_start(
                        smg_v[:, e, :], smg_dr.ap()[e, :].rearrange("(p f) -> p f", p=128)),
                    inc="ld", k=16)
        v_smg = em.v("ld")

        for q in range(4):
            em.emit("sync",
                    lambda s, q=q: s.dma_start(
                        y_dr.ap()[q * 2048:(q + 1) * 2048, :],
                        x1_dr.ap()[q * 2048:(q + 1) * 2048, :]),
                    inc="ld", k=16)
        v_yinit = em.v("ld")

        v_ims = gp_dma(
            lambda g: g.dma_start(im_s[:], im_dr.ap().rearrange("(m p) c -> p m c", p=128)),
            "mi")
        v_meta_init = 0
        for e in range(E):
            v_meta_init = gp_dma(
                lambda g, e=e: g.dma_start(
                    meta_dr.ap()[e].rearrange("(m p) c -> p m c", p=128), im_s[:]),
                "mi", waits=[("mi", v_ims)])

        v_wu = [0] * 4
        v_wd = [0] * E
        v_wu[0] = gp_dma(lambda g: g.dma_start(
            wu_s[:], wu_dr.ap()[0].rearrange("(c p) f -> p c f", p=128)), "w")
        v_wd[0] = gp_dma(lambda g: g.dma_start(
            wd_s[:], wd_dr.ap()[0].rearrange("(c p) f -> p c f", p=128)), "w")

        em.emit("vector", lambda v: v.memset(xg_tm[:].rearrange("p a b -> p (a b)"), 0.0))
        em.emit("vector", lambda v: v.memset(xg_fm[:].rearrange("p a b -> p (a b)"), 0.0))
        em.emit("vector", lambda v: v.memset(lo_s[:], 0.0))
        em.emit("vector", lambda v: v.memset(hi_s[:], 1.0), inc="ve")
        v_binit = em.v("ve")

        for it in range(BISECT_ITERS):
            em.emit("vector", lambda v: v.tensor_tensor(mid_s[:], lo_s[:], hi_s[:], ALU.add),
                    waits=([("ld", v_smg), ("ve", v_binit)] if it == 0 else []))
            em.emit("vector", lambda v: v.tensor_scalar(mid_s[:], mid_s[:], 0.5, None, ALU.mult),
                    inc="ve")
            v_mid = em.v("ve")
            cnt3 = meta_all[:].rearrange("p j e c -> p (j e c)").rearrange(
                "p (e k) -> p e k", e=E)
            em.emit("vector",
                    lambda v: v.tensor_tensor(
                        cnt3, smg_v[:],
                        mid_s[:].rearrange("p (e o) -> p e o", o=1).broadcast_to([128, E, 512]),
                        ALU.is_ge))
            em.emit("vector",
                    lambda v: v.reduce_sum(cnt_s[:], cnt3, axis=AX.X), inc="ve")
            v_cnt = em.v("ve")
            em.emit("tensor",
                    lambda tn: tn.matmul(ps_x[:], onesf_s[:], cnt_s[:], start=True, stop=True),
                    waits=[("ve", v_cnt)], inc="pe")
            v_cntmm = em.v("pe")
            em.emit("vector",
                    lambda v: v.tensor_scalar(flag_s[:], ps_x[:], float(65536 // E), 0.0,
                                              ALU.is_ge, ALU.add),
                    waits=[("pe", v_cntmm)])
            em.emit("vector", lambda v: v.tensor_tensor(tmp1_s[:], mid_s[:], lo_s[:], ALU.subtract))
            em.emit("vector", lambda v: v.tensor_tensor(tmp1_s[:], tmp1_s[:], flag_s[:], ALU.mult))
            em.emit("vector", lambda v: v.tensor_tensor(lo_s[:], lo_s[:], tmp1_s[:], ALU.add))
            em.emit("vector", lambda v: v.tensor_tensor(tmp2_s[:], hi_s[:], mid_s[:], ALU.subtract))
            em.emit("vector", lambda v: v.tensor_tensor(tmp2_s[:], tmp2_s[:], flag_s[:], ALU.mult))
            em.emit("vector", lambda v: v.tensor_tensor(hi_s[:], mid_s[:], tmp2_s[:], ALU.add),
                    inc="ve")

        v_theta = em.v("ve")
        cap("theta", lo_s[:])
        cap("smgchk", smg_v[:, 0:8, :], [("ld", v_smg)])

        em.emit("vector",
                lambda v: v.tensor_tensor(
                    m_all[:], smtm_s[:],
                    lo_s[:].rearrange("p (o e) -> p o e", o=1).broadcast_to([128, 64, E]),
                    ALU.is_ge),
                waits=[("ve", v_theta)])
        em.emit("vector",
                lambda v: v.tensor_scalar(minv_all[:], m_all[:], -1.0, 1.0,
                                          ALU.mult, ALU.add),
                inc="ve")
        em.emit("tensor",
                lambda tn: tn.matmul(ps_a[0:64, 0:512], onesb_s[:, 0:64],
                                     m_all[:].rearrange("p j e -> p (j e)"),
                                     start=True, stop=True),
                waits=[("ve", em.v("ve"))], inc="pe")
        v_tot = em.v("pe")
        if dbg_stage == "tots":
            em.emit("vector", lambda v: v.tensor_copy(trash_b[0:64, :], ps_a[0:64, 0:512]),
                    waits=[("pe", v_tot)], inc="ve")
            cap("tots", trash_b[0:64, :])
        em.emit("vector", lambda v: v.tensor_copy(trow[:], ps_a[0:1, 0:512]),
                waits=[("pe", v_tot)], inc="ve")
        cap("trow", trow[:])

        cur, nxt = trow, trow2
        for st in (8, 16, 32, 64, 128, 256):
            em.emit("vector", lambda v, cur=cur, nxt=nxt, st=st: v.tensor_copy(
                nxt[:, 0:st], cur[:, 0:st]))
            em.emit("vector", lambda v, cur=cur, nxt=nxt, st=st: v.tensor_tensor(
                nxt[:, st:512], cur[:, st:512], cur[:, 0:512 - st], ALU.add), inc="ve")
            cur, nxt = nxt, cur
        em.emit("vector", lambda v: v.memset(offs[:, 0:8], 0.0))
        em.emit("vector", lambda v, cur=cur: v.tensor_copy(offs[:, 8:512], cur[:, 0:504]),
                inc="ve")
        v_offs = em.v("ve")
        cap("m0", m_all[:])
        cap("offs", offs[:])

        em.emit("tensor",
                lambda tn: tn.matmul(ps_b[:, 0:512], tril_s[:],
                                     m_all[:].rearrange("p j e -> p (j e)"),
                                     start=True, stop=False),
                waits=[("ve", v_offs)])
        em.emit("tensor",
                lambda tn: tn.matmul(ps_b[:, 0:512], onesf_s[0:1, :], offs[:],
                                     start=False, stop=False))
        em.emit("tensor",
                lambda tn: tn.matmul(ps_b[:, 0:512], diag_s[:],
                                     minv_all[:].rearrange("p j e -> p (j e)"),
                                     start=False, stop=True), inc="pe")
        v_pos = em.v("pe")
        em.emit("vector",
                lambda v: v.tensor_copy(pos_all[:].rearrange("p j e -> p (j e)"),
                                        ps_b[:, 0:512]),
                waits=[("pe", v_pos)])
        em.emit("vector",
                lambda v: v.tensor_copy(
                    meta_all[:, :, :, 0],
                    ids_s[:, :, 0:1].broadcast_to([128, 64, 8])))
        em.emit("vector",
                lambda v: v.tensor_copy(
                    meta_all[:, :, :, 1],
                    ids_s[:, :, 1:2].broadcast_to([128, 64, 8])))
        em.emit("vector",
                lambda v: v.tensor_copy(meta_all[:, :, :, 2], smtm_s[:]))
        em.emit("vector",
                lambda v: v.tensor_tensor(meta_all[:, :, :, 3], smtm_s[:],
                                          meta_all[:, :, :, 2], ALU.subtract),
                inc="ve")
        v_sel = em.v("ve")
        em.emit("vector", lambda v: v.memset(meta_all[:, :, :, 4:8], 0.0), inc="ve")
        v_meta = em.v("ve")
        cap("pos", pos_all[:].bitcast(F32))
        cap("meta8", meta_all[:, :, :, 0:8].rearrange("p j e c -> p (j e c)"))

        v_scat = [0] * E

        def emit_meta_scatters(e):
            for j in range(64):
                v_scat[e] = gp_dma(
                    lambda g, e=e, j=j: g.indirect_dma_start(
                        meta_dr.ap().rearrange("e m c -> (e m) c"),
                        IndirectOffsetOnAxis(pos_all[:, j, e:e + 1], 0),
                        meta_all[:, j, e, :], None,
                        element_offset=e * CAP * 8,
                        bounds_check=regs["cap"], oob_is_err=False),
                    f"sc{e}",
                    waits=([("ve", v_meta), ("mi", v_meta_init)] if j == 0 else []))

        emit_meta_scatters(0)

        cap("meta0", meta_dr.ap()[0].rearrange("(m p) c -> p m c", p=128),
            [(f"sc{e}", v_scat[e]) for e in range(E)] + [("mi", v_meta_init)])

        v_meta_ld = [0] * E
        v_y_final = 0
        v_prev_dec = 0
        v_prev_scat_end = []
        for e in range(E):
            g_grp = e // 2
            em.emit("sync",
                    lambda s, e=e: s.dma_start(
                        meta_ld[:], meta_dr.ap()[e].rearrange("(m p) c -> p m c", p=128)),
                    waits=[(f"sc{e}", v_scat[e]), ("ve", v_prev_dec)], inc="ml", k=16)
            v_meta_ld[e] = em.v("ml")
            em.emit("vector",
                    lambda v: v.tensor_scalar(idx_f[:], meta_ld[:, :, 0], 64.0, None, ALU.mult),
                    waits=[("ml", v_meta_ld[e])] + list(v_prev_scat_end))
            em.emit("vector",
                    lambda v: v.tensor_tensor(idx_f[:], idx_f[:], meta_ld[:, :, 1], ALU.add))
            em.emit("vector", lambda v: v.tensor_copy(idx_u[:], idx_f[:]))
            em.emit("vector",
                    lambda v: v.tensor_tensor(gates[:], meta_ld[:, :, 2], meta_ld[:, :, 3],
                                              ALU.add), inc="ve")
            v_dec = em.v("ve")
            v_prev_dec = v_dec

            v_bdr = gp_dma(lambda g, e=e: g.dma_start(
                bd_r[:], bd_dr.ap()[e, :].rearrange("(o f) -> o f", o=1)), "w")
            for hhalf in range(2):
                em.emit("tensor",
                        lambda tn, hhalf=hhalf: tn.matmul(
                            ps_c[:, 0:512],
                            onesb_s[0:1, :], bd_r[0:1, 512 * hhalf:512 * (hhalf + 1)],
                            start=True, stop=True),
                        waits=[("w", v_bdr), ("ve", em.v("ve"))], inc="pe")
                em.emit("vector",
                        lambda v, hhalf=hhalf: v.tensor_copy(
                            bdb[:, 512 * hhalf:512 * (hhalf + 1)], ps_c[:, 0:512]),
                        waits=[("pe", em.v("pe"))], inc="ve")
            v_bdb = em.v("ve")

            ev_par = {0: 0, 1: 0}
            xg_done = [0, 0]
            for m in range(10):
                buf = m % 2
                v_g = gp_dma(
                    lambda g, m=m, buf=buf: g.indirect_dma_start(
                        xg_tm[:, buf, :], None, xln2_dr.ap(),
                        IndirectOffsetOnAxis(idx_u[:, m:m + 1], 0),
                        bounds_check=regs["ntok"], oob_is_err=False),
                    f"gx{buf}",
                    waits=[("ve", v_dec), ("pe", xg_done[buf])])
                for hb in range(2):
                    bank = ps_t if hb == 0 else ps_t2
                    for q in range(4):
                        k = 4 * hb + q
                        em.emit("tensor",
                                lambda tn, m=m, buf=buf, k=k, q=q, bank=bank: tn.transpose(
                                    bank[:, 128 * q:128 * (q + 1)],
                                    xg_tm[:, buf, 128 * k:128 * (k + 1)], identb[:]),
                                waits=(([(f"gx{buf}", v_g), ("ve", ev_par[hb])] if q == 0 else [])),
                                inc=("pe" if q == 3 else None))
                    em.emit("vector",
                            lambda v, m=m, hb=hb, bank=bank: v.tensor_copy(
                                xg_fm[:, 4 * hb:4 * hb + 4, 128 * m:128 * (m + 1)],
                                bank[:, 0:512].rearrange("p (a k) -> p a k", a=4)),
                            waits=[("pe", em.v("pe"))], inc="ve")
                    ev_par[hb] = em.v("ve")
                xg_done[buf] = em.v("pe")
            v_xg = em.v("ve")

            if e + 1 < E:
                emit_meta_scatters(e + 1)

            ev_up = {0: 0, 1: 0}
            nn_off = (0, 512, 1024)
            nn_sz = (512, 512, 256)
            for fc in range(NFC):
                for nn in range(3):
                    gi = fc * 3 + nn
                    bank = ps_a if gi % 2 == 0 else ps_b
                    for k in range(8):
                        em.emit("tensor",
                                lambda tn, fc=fc, nn=nn, k=k, bank=bank: tn.matmul(
                                    bank[:, 0:nn_sz[nn]],
                                    wu_s[:, k, 128 * fc:128 * (fc + 1)],
                                    xg_fm[:, k, nn_off[nn]:nn_off[nn] + nn_sz[nn]],
                                    start=(k == 0), stop=(k == 7)),
                                waits=(([("ve", v_xg), ("w", v_wu[g_grp]),
                                         ("ac", ev_up[gi % 2])] if k == 0 else [])),
                                inc=("pe" if k == 7 else None))
                    em.emit("scalar",
                            lambda s, fc=fc, nn=nn, g_grp=g_grp, bank=bank: s.activation(
                                mid_arena[:, fc, nn_off[nn]:nn_off[nn] + nn_sz[nn]],
                                bank[:, 0:nn_sz[nn]], AF.Silu,
                                bias=bu_c[:, NFC * g_grp + fc:NFC * g_grp + fc + 1],
                                scale=1.0),
                            waits=[("pe", em.v("pe")), ("ve", v_theta), ("ld", v_smg)],
                            inc="ac")
                    ev_up[gi % 2] = em.v("ac")
            v_up = em.v("ac")
            v_up_pe = em.v("pe")

            if e + 1 < E and (e + 1) % 2 == 0:
                v_wu[g_grp + 1] = gp_dma(
                    lambda g, gg=g_grp + 1: g.dma_start(
                        wu_s[:], wu_dr.ap()[gg].rearrange("(c p) f -> p c f", p=128)),
                    "w", waits=[("pe", v_up_pe)])

            ev_dn = {0: 0, 1: 0}
            scat_done = [0, 0]
            for m in range(10):
                buf = m % 2
                for hhalf in range(2):
                    bank = ps_c if hhalf == 0 else ps_d
                    for k in range(NFC):
                        em.emit("tensor",
                                lambda tn, m=m, hhalf=hhalf, k=k, bank=bank: tn.matmul(
                                    bank[:, 0:512],
                                    mid_arena[:, k, 128 * m:128 * (m + 1)],
                                    wd_s[:, k, 512 * hhalf:512 * (hhalf + 1)],
                                    start=(k == 0), stop=(k == NFC - 1)),
                                waits=(([("ac", v_up), ("w", v_wd[e]),
                                         ("ve", ev_dn[hhalf])] if k == 0 else [])),
                                inc=("pe" if k == NFC - 1 else None))
                    em.emit("vector",
                            lambda v, m=m, buf=buf, hhalf=hhalf, bank=bank: v.tensor_tensor(
                                contrib[:, buf, 512 * hhalf:512 * (hhalf + 1)],
                                bank[:, 0:512], bdb[:, 512 * hhalf:512 * (hhalf + 1)],
                                ALU.add),
                            waits=[("pe", em.v("pe")), ("ve", v_bdb),
                                   (f"ys{buf}", scat_done[buf])])
                    em.emit("vector",
                            lambda v, m=m, buf=buf, hhalf=hhalf: v.tensor_scalar(
                                contrib[:, buf, 512 * hhalf:512 * (hhalf + 1)],
                                contrib[:, buf, 512 * hhalf:512 * (hhalf + 1)],
                                gates[:, m:m + 1], None, ALU.mult),
                            inc="ve")
                    ev_dn[hhalf] = em.v("ve")
                v_ct = em.v("ve")
                v_y_final = gp_dma(
                    lambda g, m=m, buf=buf: g.indirect_dma_start(
                        y_dr.ap(), IndirectOffsetOnAxis(idx_u[:, m:m + 1], 0),
                        contrib[:, buf, :], None,
                        compute_op=ALU.add, oob_is_err=True),
                    f"ys{buf}",
                    waits=[("ve", v_ct)] + ([("ld", v_yinit)] if (e == 0 and m == 0) else []))
                scat_done[buf] = v_y_final
            v_prev_scat_end = [("ys0", em.v("ys0")), ("ys1", em.v("ys1"))]

            if e + 1 < E:
                v_wd[e + 1] = gp_dma(
                    lambda g, ee=e + 1: g.dma_start(
                        wd_s[:], wd_dr.ap()[ee].rearrange("(c p) f -> p c f", p=128)),
                    "w", waits=[("pe", em.v("pe"))])

        v_fin_ys0 = em.v("ys0")
        v_fin_ys1 = em.v("ys1")
        v_fin_mi = em.v("mi")

        @block.sync
        def _(sync):
            em.run_engine("sync", sync)

        @block.gpsimd
        def _(gpsimd):
            em.run_engine("gpsimd", gpsimd)
            gpsimd.wait_ge(h_ys0, v_fin_ys0)
            gpsimd.wait_ge(h_ys1, v_fin_ys1)
            gpsimd.wait_ge(h_mi, v_fin_mi)

        @block.vector
        def _(vector):
            em.run_engine("vector", vector)

        @block.scalar
        def _(scalar):
            em.run_engine("scalar", scalar)

        @block.tensor
        def _(tensor):
            em.run_engine("tensor", tensor)

    return nc



_CACHE = {}


def _get_programs():
    if "a" not in _CACHE:
        _CACHE["a"] = build_launch_a()
    if "b" not in _CACHE:
        _CACHE["b"] = build_launch_b()
    return _CACHE["a"], _CACHE["b"]


def kernel(**inputs):
    x = np.ascontiguousarray(np.asarray(inputs["L0_attn_x"], dtype=np.float32))
    w_in_l = np.asarray(inputs["L0_attn_w_in_local"], dtype=np.float32)
    b_in_l = np.asarray(inputs["L0_attn_b_in_local"], dtype=np.float32)
    w_out_l = np.asarray(inputs["L0_attn_w_out_local"], dtype=np.float32)
    b_out_l = np.asarray(inputs["L0_attn_b_out_local"], dtype=np.float32)
    w_in_g = np.asarray(inputs["L0_attn_w_in_global"], dtype=np.float32)
    b_in_g = np.asarray(inputs["L0_attn_b_in_global"], dtype=np.float32)
    w_out_g = np.asarray(inputs["L0_attn_w_out_global"], dtype=np.float32)
    b_out_g = np.asarray(inputs["L0_attn_b_out_global"], dtype=np.float32)
    wr = np.asarray(inputs["L0_ffn_Wr"], dtype=np.float32)
    br = np.asarray(inputs["L0_ffn_br"], dtype=np.float32)
    wu = np.asarray(inputs["L0_ffn_W_up"], dtype=np.float32)
    bu = np.asarray(inputs["L0_ffn_b_up"], dtype=np.float32)
    wd = np.asarray(inputs["L0_ffn_W_down"], dtype=np.float32)
    bd = np.asarray(inputs["L0_ffn_b_down"], dtype=np.float32)

    wInT_l = np.ascontiguousarray(w_in_l.T)
    wOutT_l = np.ascontiguousarray(w_out_l.T)
    wInT_g = np.ascontiguousarray(w_in_g.T)
    wOutT_g = np.ascontiguousarray(w_out_g.T)
    bInL_c = np.ascontiguousarray(b_in_l.reshape(24, 128).T)
    bInG_c = np.ascontiguousarray(b_in_g.reshape(24, 128).T)
    bOutL_c = np.ascontiguousarray(b_out_l.reshape(8, 128).T)
    bOutG_c = np.ascontiguousarray(b_out_g.reshape(8, 128).T)
    bInL_r = np.ascontiguousarray(b_in_l[2 * D:].reshape(1, D))
    bInG_r = np.ascontiguousarray(b_in_g[2 * D:].reshape(1, D))
    wr_l = np.ascontiguousarray(wr.reshape(8, 128, E).transpose(1, 0, 2))
    br_c = br.reshape(E, 1)

    wu_p = np.zeros((4, D, FP), dtype=np.float32)
    wu_p[:, :, :FTRUE] = wu
    bu_p = np.zeros((4, FP), dtype=np.float32)
    bu_p[:, :FTRUE] = bu
    bu_cols = np.ascontiguousarray(
        bu_p.reshape(4, NFC, 128).transpose(2, 0, 1).reshape(128, 4 * NFC))
    wd_p = np.zeros((E, FP, D), dtype=np.float32)
    wd_p[:, :FTRUE, :] = wd

    nca, ncb = _get_programs()

    in_maps_a = []
    for c in range(8):
        in_maps_a.append({
            "x": np.ascontiguousarray(x[2 * c:2 * c + 2].reshape(NTOK, D)),
            "wInT_l": wInT_l, "wOutT_l": wOutT_l,
            "wInT_g": wInT_g, "wOutT_g": wOutT_g,
            "bInL_c": bInL_c, "bInG_c": bInG_c,
            "bOutL_c": bOutL_c, "bOutG_c": bOutG_c,
            "bInL_r": bInL_r, "bInG_r": bInG_r,
            "wr_l": wr_l, "br_c": br_c,
        })
    res_a = run_bass_kernel_spmd(nca, in_maps_a, core_ids=list(range(8)))

    sm_all = np.ascontiguousarray(
        np.concatenate([res_a.results[c]["sm"] for c in range(8)], axis=1))

    in_maps_b = []
    for c in range(8):
        in_maps_b.append({
            "x1": res_a.results[c]["x1"],
            "xln2": res_a.results[c]["xln2"],
            "smg": sm_all,
            "smtm": np.ascontiguousarray(res_a.results[c]["sm"].T),
            "wu": wu_p, "bu_cols": bu_cols, "wd": wd_p, "bd": bd,
        })
    res_b = run_bass_kernel_spmd(ncb, in_maps_b, core_ids=list(range(8)))

    y = np.concatenate(
        [res_b.results[c]["y"][0:NTOK].reshape(2, 4096, D) for c in range(8)], axis=0)
    return y


# version 03
# speedup vs baseline: 1.1796x; 1.0557x over previous
import numpy as np
from contextlib import ExitStack
import ml_dtypes
import concourse.bass as bass
import concourse.mybir as mybir
from concourse.bass_utils import run_bass_kernel_spmd
from concourse.bass_core_rust import IndirectOffsetOnAxis

F32 = mybir.dt.float32
BF16 = mybir.dt.bfloat16
U32 = mybir.dt.uint32
AF = mybir.ActivationFunctionType
ALU = mybir.AluOpType
AX = mybir.AxisListType

NTOK = 8192
NTILE = 16
D = 1024
E = 8
FTRUE = 2730
FP = 2816
NFC = FP // 128
CAP = 1280
NSLOT = CAP + 128
YROWS = NTOK + 128
SCL = 0.08838834764831845
SCG = 1.0 / 16.0
EPS = 1e-5
NEG = -30000.0
BIG = float(1 << 20)
BISECT_ITERS = 30


class EM:
    ENGINES = ("sync", "gpsimd", "vector", "scalar", "tensor")

    def __init__(self):
        self.prog = {e: [] for e in self.ENGINES}
        self.sem = {}

    def add_sem(self, name, handle):
        self.sem[name] = [handle, 0]

    def v(self, name):
        return self.sem[name][1]

    def emit(self, eng, fn, waits=(), inc=None, k=1, nodrain=False):
        ws = [(self.sem[n][0], val) for (n, val) in waits if val > 0]
        post = None
        if inc is not None:
            self.sem[inc][1] += k
            post = (self.sem[inc][0], k)

        def run(engobj, fn=fn, ws=ws, post=post, eng=eng, nodrain=nodrain):
            for h, val in ws:
                engobj.wait_ge(h, val)
            inst = fn(engobj)
            if eng == "vector" and not (nodrain and post is None):
                inst = engobj.drain()
            if post is not None:
                inst.then_inc(post[0], post[1])
        self.prog[eng].append(run)

    def run_engine(self, name, engobj):
        for fn in self.prog[name]:
            fn(engobj)


def _consts_a():
    ident_bf = np.eye(128, dtype=ml_dtypes.bfloat16)
    slide = np.zeros((128, 4, 192), dtype=ml_dtypes.bfloat16)
    for c in range(4):
        for p in range(128):
            slide[p, c, 64 + 2 * c + (1 if p >= 64 else 0)] = 1.0 / 64.0
    mask01 = np.zeros((128, 128), dtype=ml_dtypes.bfloat16)
    mask01[0:64, 0:64] = 1.0
    mask01[64:128, 64:128] = 1.0
    return ident_bf, slide.reshape(128, 4 * 192), mask01



def build_launch_a(dbg_stage=None):
    nc = bass.Bass("TRN2", target_bir_lowering=False, detect_race_conditions=False)

    x_dr = nc.dram_tensor("x", [NTOK, D], F32, kind="ExternalInput")
    wInT_l_dr = nc.dram_tensor("wInT_l", [D, 3 * D], F32, kind="ExternalInput")
    wOutT_l_dr = nc.dram_tensor("wOutT_l", [D, D], F32, kind="ExternalInput")
    wInT_g_dr = nc.dram_tensor("wInT_g", [D, 3 * D], F32, kind="ExternalInput")
    wOutT_g_dr = nc.dram_tensor("wOutT_g", [D, D], F32, kind="ExternalInput")
    bInL_dr = nc.dram_tensor("bInL_c", [128, 24], F32, kind="ExternalInput")
    bInG_dr = nc.dram_tensor("bInG_c", [128, 24], F32, kind="ExternalInput")
    bOutL_dr = nc.dram_tensor("bOutL_c", [128, 8], F32, kind="ExternalInput")
    bOutG_dr = nc.dram_tensor("bOutG_c", [128, 8], F32, kind="ExternalInput")
    bInL_row_dr = nc.dram_tensor("bInL_r", [1, D], F32, kind="ExternalInput")
    bInG_row_dr = nc.dram_tensor("bInG_r", [1, D], F32, kind="ExternalInput")
    wr_dr = nc.dram_tensor("wr_l", [128, 8, E], F32, kind="ExternalInput")
    br_dr = nc.dram_tensor("br_c", [E, 1], F32, kind="ExternalInput")

    x1_dr = nc.dram_tensor("x1", [NTOK, D], F32, kind="ExternalOutput")
    xln2_dr = nc.dram_tensor("xln2", [NTOK, D], BF16, kind="ExternalOutput")
    sm_dr = nc.dram_tensor("sm", [E, NTOK], F32, kind="ExternalOutput")
    dbg_dr = nc.dram_tensor("dbg", [128, 8192], F32, kind="ExternalOutput") if dbg_stage else None

    identc, slidec, maskc = _consts_a()
    ident_dr = nc.inline_tensor(identc, "ident_bf")
    identf_dr = nc.inline_tensor(np.eye(128, dtype=np.float32), "ident_f32")
    slide_dr = nc.inline_tensor(slidec, "slide")
    mask_dr = nc.inline_tensor(maskc, "segmask")
    onesb_dr = nc.inline_tensor(np.ones((128, 128), dtype=ml_dtypes.bfloat16), "onesb")

    em = EM()

    with ExitStack() as ctx:
        identb = ctx.enter_context(nc.sbuf_tensor("identb", [128, 128], BF16))
        ident_f32_s = ctx.enter_context(nc.sbuf_tensor("ident_f32_s", [128, 128], F32))
        slide_s = ctx.enter_context(nc.sbuf_tensor("slide_s", [128, 4, 192], BF16))
        mask_s = ctx.enter_context(nc.sbuf_tensor("mask_s", [128, 128], BF16))
        onesb_s = ctx.enter_context(nc.sbuf_tensor("onesb_s", [128, 128], BF16))
        epsc = ctx.enter_context(nc.sbuf_tensor("epsc", [128, 1], F32))
        bInL_c = ctx.enter_context(nc.sbuf_tensor("bInLc_s", [128, 24], F32))
        bInG_c = ctx.enter_context(nc.sbuf_tensor("bInGc_s", [128, 24], F32))
        bOutSum = ctx.enter_context(nc.sbuf_tensor("bOutSum", [128, 8], F32))
        bOutG_c = ctx.enter_context(nc.sbuf_tensor("bOutGc_s", [128, 8], F32))
        bOutL_c = ctx.enter_context(nc.sbuf_tensor("bOutLc_s", [128, 8], F32))
        bInL_r = ctx.enter_context(nc.sbuf_tensor("bInLr_s", [1, D], BF16))
        bInG_r = ctx.enter_context(nc.sbuf_tensor("bInGr_s", [1, D], BF16))
        bvl_b = ctx.enter_context(nc.sbuf_tensor("bvl_b", [128, 1024], F32))
        bvg_b = ctx.enter_context(nc.sbuf_tensor("bvg_b", [128, 1024], F32))
        wr_s = ctx.enter_context(nc.sbuf_tensor("wr_s", [128, 8, E], BF16))
        br_c = ctx.enter_context(nc.sbuf_tensor("brc_s", [E, 1], F32))
        means0 = ctx.enter_context(nc.sbuf_tensor("means0", [64, 1024], BF16))
        means1 = ctx.enter_context(nc.sbuf_tensor("means1", [64, 1024], BF16))
        means_fm = ctx.enter_context(nc.sbuf_tensor("means_fm", [128, 8, 128], BF16))
        globB = ctx.enter_context(nc.sbuf_tensor("globB", [128, 8, 128], F32))
        sc_s = ctx.enter_context(nc.sbuf_tensor("sc_s", [128, 4], F32))
        sc_q = ctx.enter_context(nc.sbuf_tensor("sc_q", [128, 4], F32))
        sc_m = ctx.enter_context(nc.sbuf_tensor("sc_m", [128, 4], F32))
        sc_v = ctx.enter_context(nc.sbuf_tensor("sc_v", [128, 4], F32))
        sc_r = ctx.enter_context(nc.sbuf_tensor("sc_r", [128, 4], F32))
        sc_nmr = ctx.enter_context(nc.sbuf_tensor("sc_nmr", [128, 4], F32))
        smx8 = ctx.enter_context(nc.sbuf_tensor("smx8", [128, 8], F32))
        ssum8 = ctx.enter_context(nc.sbuf_tensor("ssum8", [128, 8], F32))
        x_s = ctx.enter_context(nc.sbuf_tensor("x_s", [128, 2, 4, 1024], F32))
        xln_s = ctx.enter_context(nc.sbuf_tensor("xln_s", [128, 4, 1024], BF16))
        xln_fm = ctx.enter_context(nc.sbuf_tensor("xln_fm", [128, 8, 512], BF16))
        qk_sb = ctx.enter_context(nc.sbuf_tensor("qk_sb", [128, 16, 512], BF16))
        v_tm = ctx.enter_context(nc.sbuf_tensor("v_tm", [128, 4, 1024], BF16))
        prob2 = ctx.enter_context(nc.sbuf_tensor("prob2", [128, 2, 8, 128], BF16))
        probT2 = ctx.enter_context(nc.sbuf_tensor("probT2", [128, 2, 8, 128], BF16))
        trash_t = ctx.enter_context(nc.sbuf_tensor("trash_t", [128, 1024], BF16))
        trash = trash_t[:]
        attn_fm = ctx.enter_context(nc.sbuf_tensor("attn_fm", [128, 8, 512], BF16))
        xln2_s = ctx.enter_context(nc.sbuf_tensor("xln2_s", [128, 4, 1024], BF16))
        xln2_fm = ctx.enter_context(nc.sbuf_tensor("xln2_fm", [128, 8, 512], BF16))
        sum_fm = xln2_fm
        logit_em = ctx.enter_context(nc.sbuf_tensor("logit_em", [8, 512], F32))
        sm_tm = ctx.enter_context(nc.sbuf_tensor("sm_tm", [128, 4, 8], F32))
        sm_mx = ctx.enter_context(nc.sbuf_tensor("sm_mx", [128, 4], F32))
        sm_sum = ctx.enter_context(nc.sbuf_tensor("sm_sum", [128, 4], F32))
        sm_em = ctx.enter_context(nc.sbuf_tensor("sm_em", [8, 512], F32))
        qkg = qk_sb[:, 0:4, :].rearrange("p a (b k) -> p (a b) k", k=128)
        vg_tm = qk_sb[:, 4:6, :].rearrange("p a k -> p (a k)")
        vg_pair = qk_sb[0:64, 6:10, :].rearrange("p a k -> p (a k)").rearrange("p (b f) -> p b f", b=2)
        afm_f32 = attn_fm[:].rearrange("p a k -> p (a k)").bitcast(F32)
        sg_f = afm_f32[:, 256:512].rearrange("p (a k) -> p a k", a=4)
        pg_f = afm_f32[:, 512:768].rearrange("p (a k) -> p a k", a=4)
        pg_b = prob2[:, 0, 0:2, :].rearrange("p a k -> p (a k)").rearrange("p (a k) -> p a k", a=4)
        pgT = probT2[0:64, 0, 0:4, :].rearrange("p a k -> p (a k)").rearrange("p (a k) -> p a k", a=4)
        ag_fm = attn_fm[:, :, 0:128]
        gmx = ctx.enter_context(nc.sbuf_tensor("gmx", [128, 4], F32))
        gsum = ctx.enter_context(nc.sbuf_tensor("gsum", [128, 4], F32))
        wInTl = ctx.enter_context(nc.sbuf_tensor("wInTl", [128, 8, 3 * D], BF16))
        wOutTl = ctx.enter_context(nc.sbuf_tensor("wOutTl", [128, 8, D], BF16))
        wInTg = wInTl
        wOutTg = wOutTl
        pm = ctx.enter_context(nc.psum_tensor("pm", [128, 1024], F32))
        ps_a = ctx.enter_context(nc.psum_tensor("ps_a", [128, 512], F32))
        ps_b = ctx.enter_context(nc.psum_tensor("ps_b", [128, 512], F32))
        ps_c = ctx.enter_context(nc.psum_tensor("ps_c", [128, 512], F32))
        ps_d = ctx.enter_context(nc.psum_tensor("ps_d", [128, 512], F32))
        ps_t = ctx.enter_context(nc.psum_tensor("ps_t", [128, 512], BF16))
        ps_t2 = ctx.enter_context(nc.psum_tensor("ps_t2", [128, 512], BF16))
        h_ld = ctx.enter_context(nc.semaphore("s_ld"))
        h_x0 = ctx.enter_context(nc.semaphore("s_x0"))
        h_x1 = ctx.enter_context(nc.semaphore("s_x1"))
        h_st = ctx.enter_context(nc.semaphore("s_st"))
        h_gw = ctx.enter_context(nc.semaphore("s_gw"))
        h_sm = ctx.enter_context(nc.semaphore("s_sm"))
        h_ve = ctx.enter_context(nc.semaphore("s_ve"))
        h_ac = ctx.enter_context(nc.semaphore("s_ac"))
        h_pe = ctx.enter_context(nc.semaphore("s_pe"))
        block = ctx.enter_context(nc.Block())
        for nm, hh in (("ld", h_ld), ("x0", h_x0), ("x1", h_x1), ("st", h_st),
                       ("gw", h_gw), ("sm", h_sm), ("ve", h_ve), ("ac", h_ac), ("pe", h_pe)):
            em.add_sem(nm, hh)

        for dst, src in (
            (identb[:], ident_dr.ap()),
            (ident_f32_s[:], identf_dr.ap()),
            (slide_s[:].rearrange("p c k -> p (c k)"), slide_dr.ap()),
            (mask_s[:], mask_dr.ap()),
            (onesb_s[:], onesb_dr.ap()),
            (bInL_c[:], bInL_dr.ap()),
            (bInG_c[:], bInG_dr.ap()),
            (bOutG_c[:], bOutG_dr.ap()),
            (bOutL_c[:], bOutL_dr.ap()),
            (br_c[:], br_dr.ap()),
        ):
            em.emit("sync", lambda s, d=dst, a=src: s.dma_start(d, a), inc="ld", k=16)
        v_consts = em.v("ld")

        for dst, src in (
            (bInL_r[:], bInL_row_dr.ap()),
            (bInG_r[:], bInG_row_dr.ap()),
            (wr_s[:], wr_dr.ap()),
            (wInTg[:], wInT_g_dr.ap().rearrange("(c p) f -> p c f", p=128)),
            (wOutTg[:], wOutT_g_dr.ap().rearrange("(c p) f -> p c f", p=128)),
        ):
            em.emit("gpsimd", lambda g, d=dst, a=src: g.dma_start(d, a), inc="gw", k=16)
        v_weights = em.v("gw")

        em.emit("vector", lambda v: v.memset(epsc[:], EPS), inc="ve")
        em.emit("vector",
                lambda v: v.tensor_tensor(bOutSum[:], bOutG_c[:], bOutL_c[:], ALU.add),
                waits=[("ld", v_consts)], inc="ve")
        v_bsum = em.v("ve")

        for dst, row in ((bvg_b, bInG_r), (bvl_b, bInL_r)):
            for n in range(2):
                em.emit("tensor",
                        lambda t, n=n, row=row: t.matmul(
                            ps_a[:, 0:512], onesb_s[0:1, :],
                            row[0:1, 512 * n:512 * (n + 1)],
                            start=True, stop=True),
                        waits=[("gw", v_weights), ("ld", v_consts), ("ve", em.v("ve"))],
                        inc="pe")
                em.emit("vector",
                        lambda v, n=n, dst=dst: v.tensor_copy(
                            dst[:, 512 * n:512 * (n + 1)], ps_a[:, 0:512]),
                        waits=[("pe", em.v("pe"))], inc="ve")
        v_vbias = em.v("ve")

        def cap(stage, ap, extra_waits=()):
            if dbg_stage != stage:
                return
            def _cap_dma(g, ap=ap):
                dst = dbg_dr.ap()[0:ap.shape[0], 0:int(np.prod(ap.shape[1:]))]
                if len(ap.shape) == 3:
                    dst = dst.rearrange("p (a k) -> p a k", a=ap.shape[1])
                return g.dma_start(dst, ap)
            em.emit("gpsimd", _cap_dma,
                    waits=[("ve", em.v("ve")), ("ac", em.v("ac")), ("pe", em.v("pe"))]
                          + list(extra_waits),
                    inc="sm", k=16)

        cap("wg", wInTg[:, 0:2, :].rearrange("p c f -> p (c f)"), [("gw", v_weights)])

        def emit_ln(src4, dst4, first_waits):
            for c in range(4):
                em.emit("scalar",
                        lambda s, c=c: s.activation(trash[:], src4[:, c, :], AF.Copy,
                                                    bias=0.0, scale=1.0,
                                                    accum_out=sc_s[:, c:c + 1]),
                        waits=(first_waits + [("ve", em.v("ve"))]) if c == 0 else [])
            for c in range(4):
                em.emit("scalar",
                        lambda s, c=c: s.activation(trash[:], src4[:, c, :], AF.Square,
                                                    bias=0.0, scale=1.0,
                                                    accum_out=sc_q[:, c:c + 1]),
                        inc=("ac" if c == 3 else None))
            v_stats = em.v("ac")
            em.emit("vector", lambda v: v.tensor_scalar(sc_m[:], sc_s[:], 1.0 / D, None, ALU.mult),
                    waits=[("ac", v_stats)])
            em.emit("vector", lambda v: v.tensor_scalar(sc_v[:], sc_q[:], 1.0 / D, None, ALU.mult))
            em.emit("vector", lambda v: v.tensor_tensor(sc_r[:], sc_m[:], sc_m[:], ALU.mult))
            em.emit("vector", lambda v: v.tensor_tensor(sc_v[:], sc_v[:], sc_r[:], ALU.subtract),
                    inc="ve")
            v_var = em.v("ve")
            em.emit("scalar", lambda s: s.activation(sc_r[:], sc_v[:], AF.Sqrt,
                                                     bias=epsc[:, 0:1], scale=1.0),
                    waits=[("ve", v_var)], inc="ac")
            v_sd = em.v("ac")
            em.emit("vector", lambda v: v.reciprocal(sc_r[:], sc_r[:]), waits=[("ac", v_sd)])
            em.emit("vector", lambda v: v.tensor_tensor(sc_nmr[:], sc_m[:], sc_r[:], ALU.mult))
            em.emit("vector", lambda v: v.tensor_scalar(sc_nmr[:], sc_nmr[:], -1.0, None, ALU.mult))
            for c in range(4):
                em.emit("vector",
                        lambda v, c=c: v.tensor_scalar(dst4[:, c, :], src4[:, c, :],
                                                       sc_r[:, c:c + 1], sc_nmr[:, c:c + 1],
                                                       ALU.mult, ALU.add),
                        inc=("ve" if c == 3 else None))
            return em.v("ve")

        x_load_done = [0] * (2 * NTILE)
        x_cons_done = [None] * (2 * NTILE)

        def emit_x_load(r):
            t = r % NTILE
            w = list(x_cons_done[r - 2]) if (r >= 2 and x_cons_done[r - 2]) else []
            sem = "x0" if r % 2 == 0 else "x1"
            em.emit("sync",
                    lambda s, t=t, r=r: s.dma_start(
                        x_s[:, r % 2], x_dr.ap()[t * 512:(t + 1) * 512, :].rearrange(
                            "(c p) f -> p c f", p=128)),
                    waits=w, inc=sem, k=16)
            x_load_done[r] = (sem, em.v(sem))

        emit_x_load(0)
        emit_x_load(1)
        cap("x0", x_s[:, 0].rearrange("p c f -> p (c f)"), [x_load_done[0]])

        cap("slide", slide_s[:].rearrange("p c k -> p (c k)"), [("ld", v_consts)])

        for t in range(NTILE):
            r = t
            xb = x_s[:, r % 2]
            v_apply = emit_ln(xb, xln_s, [x_load_done[r], ("pe", em.v("pe"))])
            if t == 0:
                cap("xlnp1", xln_s[:].rearrange("p c f -> p (c f)"))
                if dbg_stage == "stats":
                    for j, buf in enumerate((sc_s, sc_q, sc_m, sc_v, sc_r, sc_nmr)):
                        em.emit("vector",
                                lambda v, j=j, buf=buf: v.tensor_copy(
                                    sm_tm[:].rearrange("p a e -> p (a e)")[:, 4 * j:4 * j + 4],
                                    buf[:]),
                                waits=([("ve", v_apply)] if j == 0 else []),
                                inc=("ve" if j == 5 else None))
                    cap("stats", sm_tm[:].rearrange("p a e -> p (a e)"))
            ti = t % 8
            for c in range(4):
                for u in range(2):
                    em.emit("tensor",
                            lambda tn, c=c, u=u, ti=ti: tn.matmul(
                                pm[:, 512 * u:512 * (u + 1)],
                                slide_s[:, c, 64 - 8 * ti:192 - 8 * ti],
                                xln_s[:, c, 512 * u:512 * (u + 1)],
                                start=(ti == 0 and c == 0), stop=(ti == 7 and c == 3)),
                            waits=([("ve", v_apply)] if (c == 0 and u == 0) else []),
                            inc=("pe" if (c == 3 and u == 1) else None))
            x_cons_done[r] = [("pe", em.v("pe")), ("ve", em.v("ve")), ("ac", em.v("ac"))]
            if ti == 7:
                dstm = means0 if t < 8 else means1
                for u in range(2):
                    em.emit("vector",
                            lambda v, u=u, dstm=dstm: v.tensor_copy(
                                dstm[:, 512 * u:512 * (u + 1)], pm[0:64, 512 * u:512 * (u + 1)]),
                            waits=([("pe", em.v("pe"))] if u == 0 else []),
                            inc=("ve" if u == 1 else None))
                if t == 7:
                    cap("means0", means0[:])
            if r + 2 < 2 * NTILE:
                emit_x_load(r + 2)

        for b in range(2):
            srcm = means0 if b == 0 else means1
            for fc in range(8):
                bank = ps_t if fc % 2 == 0 else ps_t2
                em.emit("tensor",
                        lambda tn, srcm=srcm, fc=fc, bank=bank: tn.transpose(
                            bank[:, 0:64], srcm[:, 128 * fc:128 * (fc + 1)],
                            identb[0:64, 0:64]),
                        waits=[("ve", em.v("ve"))], inc="pe")
                em.emit("vector",
                        lambda v, b=b, fc=fc, bank=bank: v.tensor_copy(
                            means_fm[:, fc, 64 * b:64 * (b + 1)], bank[:, 0:64]),
                        waits=[("pe", em.v("pe"))], inc="ve")
        v_mfm = em.v("ve")
        cap("mfm", means_fm[:].rearrange("p a k -> p (a k)"))

        for m in range(16):
            bank = ps_a if m % 2 == 0 else ps_b
            for k in range(8):
                em.emit("tensor",
                        lambda tn, m=m, k=k, bank=bank: tn.matmul(
                            bank[:, 0:128], wInTg[:, k, 128 * m:128 * (m + 1)],
                            means_fm[:, k, :], start=(k == 0), stop=(k == 7)),
                        waits=(([("ve", v_mfm)] if m == 0 else [("ac", max(0, em.v("ac") - 1))])
                               if k == 0 else []),
                        inc=("pe" if k == 7 else None))
            em.emit("scalar",
                    lambda s, m=m, bank=bank: s.activation(
                        qkg[:, m, :], bank[:, 0:128], AF.Identity,
                        bias=bInG_c[:, m:m + 1], scale=1.0),
                    waits=[("pe", em.v("pe"))], inc="ac")
        v_qkg = em.v("ac")
        cap("qkg", qkg[:].rearrange("p a k -> p (a k)"))

        for n in range(2):
            for k in range(8):
                em.emit("tensor",
                        lambda tn, n=n, k=k: tn.matmul(
                            ps_c[:, 0:512], means_fm[:, k, :],
                            wInTg[:, k, 2048 + 512 * n:2048 + 512 * (n + 1)],
                            start=(k == 0), stop=(k == 7)),
                        waits=([("ve", em.v("ve"))] if k == 0 else []),
                        inc=("pe" if k == 7 else None))
            em.emit("vector",
                    lambda v, n=n: v.tensor_tensor(
                        vg_tm[:, 512 * n:512 * (n + 1)], ps_c[:, 0:512],
                        bvg_b[:, 512 * n:512 * (n + 1)], ALU.add),
                    waits=[("pe", em.v("pe")), ("ve", v_vbias)], inc="ve")
        v_vg = em.v("ve")
        cap("vg", vg_tm[:])

        em.emit("vector", lambda v: v.tensor_copy(vg_pair[:, 0, :], vg_tm[0:64, :]),
                waits=[("ve", v_vg)], inc="ve")
        em.emit("gpsimd", lambda g: g.dma_start(vg_pair[:, 1, :], vg_tm[64:128, :]),
                waits=[("ve", em.v("ve"))], inc="gw", k=16)
        v_vgp_gp = em.v("gw")

        for h in range(4):
            for b in range(2):
                for kc in range(2):
                    em.emit("tensor",
                            lambda tn, h=h, b=b, kc=kc: tn.matmul(
                                ps_a[64 * b:64 * (b + 1), 64 * h:64 * (h + 1)],
                                qkg[:, 2 * h + kc, 64 * b:64 * (b + 1)],
                                qkg[:, 8 + 2 * h + kc, 64 * b:64 * (b + 1)],
                                start=(kc == 0), stop=(kc == 1),
                                tile_position=(0, 64 * b)),
                            waits=([("ac", v_qkg)] if (h == 0 and b == 0 and kc == 0) else []),
                            inc=("pe" if (h == 3 and b == 1 and kc == 1) else None))
        v_sgp = em.v("pe")

        em.emit("vector", lambda v: v.tensor_copy(
            sg_f[:].rearrange("p a k -> p (a k)"), ps_a[:, 0:256]),
            waits=[("pe", v_sgp)], inc="ve")
        cap("sg", sg_f[:].rearrange("p a k -> p (a k)"))
        em.emit("vector", lambda v: v.reduce_max(gmx[:], sg_f[:], axis=AX.X))
        em.emit("vector", lambda v: v.tensor_tensor(
            pg_f[:], sg_f[:],
            gmx[:].rearrange("p (g o) -> p g o", o=1).broadcast_to([128, 4, 64]),
            ALU.subtract), inc="ve")
        em.emit("scalar", lambda s: s.activation(
            pg_f[:].rearrange("p a k -> p (a k)"), pg_f[:].rearrange("p a k -> p (a k)"),
            AF.Exp, bias=0.0, scale=SCG), waits=[("ve", em.v("ve"))], inc="ac")
        em.emit("vector", lambda v: v.reduce_sum(gsum[:], pg_f[:], axis=AX.X),
                waits=[("ac", em.v("ac"))])
        em.emit("vector", lambda v: v.reciprocal(gsum[:], gsum[:]))
        em.emit("vector", lambda v: v.tensor_tensor(
            pg_b[:], pg_f[:],
            gsum[:].rearrange("p (g o) -> p g o", o=1).broadcast_to([128, 4, 64]),
            ALU.mult), inc="ve")
        v_pgb = em.v("ve")
        cap("pgb", pg_b[:].rearrange("p a k -> p (a k)"))

        for h in range(4):
            em.emit("tensor",
                    lambda tn, h=h: tn.transpose(ps_t[0:64, 128 * h:128 * (h + 1)],
                                                 pg_b[:, h, :], identb[:]),
                    waits=([("ve", v_pgb)] if h == 0 else []),
                    inc=("pe" if h == 3 else None))
        em.emit("vector", lambda v: v.tensor_copy(
            pgT[:].rearrange("p a k -> p (a k)"), ps_t[0:64, 0:512]),
            waits=[("pe", em.v("pe"))], inc="ve")
        v_pgT = em.v("ve")

        for b in range(2):
            bank = ps_b if b == 0 else ps_c
            for h in range(4):
                for j in range(2):
                    em.emit("tensor",
                            lambda tn, b=b, h=h, j=j, bank=bank: tn.matmul(
                                bank[:, 64 * (2 * h + j):64 * (2 * h + j) + 64],
                                vg_pair[:, b, 128 * (2 * h + j):128 * (2 * h + j) + 128],
                                pgT[:, h, 64 * b:64 * (b + 1)],
                                start=True, stop=True),
                            waits=([("ve", v_pgT), ("gw", v_vgp_gp)]
                                   if (b == 0 and h == 0 and j == 0) else []),
                            inc=("pe" if (h == 3 and j == 1) else None))
            em.emit("vector",
                    lambda v, b=b, bank=bank: v.tensor_copy(
                        ag_fm[:, :, 64 * b:64 * (b + 1)],
                        bank[:, 0:512].rearrange("p (a k) -> p a k", a=8)),
                    waits=[("pe", em.v("pe"))], inc="ve")
        v_ag = em.v("ve")
        cap("ag", ag_fm[:])

        for m in range(8):
            bank = ps_a if m % 2 == 0 else ps_b
            for k in range(8):
                em.emit("tensor",
                        lambda tn, m=m, k=k, bank=bank: tn.matmul(
                            bank[:, 0:128], wOutTg[:, k, 128 * m:128 * (m + 1)],
                            ag_fm[:, k, :], start=(k == 0), stop=(k == 7)),
                        waits=(([("ve", v_ag)] if m == 0 else [("ac", max(0, em.v("ac") - 1))])
                               if k == 0 else []),
                        inc=("pe" if k == 7 else None))
            em.emit("scalar",
                    lambda s, m=m, bank=bank: s.activation(
                        globB[:, m, :], bank[:, 0:128], AF.Identity,
                        bias=bOutSum[:, m:m + 1], scale=1.0),
                    waits=[("pe", em.v("pe")), ("ve", v_bsum)], inc="ac")
        v_glob = em.v("ac")
        cap("glob", globB[:].rearrange("p a k -> p (a k)"))

        em.emit("gpsimd",
                lambda g: g.dma_start(
                    wInTl[:], wInT_l_dr.ap().rearrange("(c p) f -> p c f", p=128)),
                waits=[("pe", em.v("pe")), ("ac", v_glob)], inc="gw", k=16)
        em.emit("gpsimd",
                lambda g: g.dma_start(
                    wOutTl[:], wOutT_l_dr.ap().rearrange("(c p) f -> p c f", p=128)),
                inc="gw", k=16)
        v_wl = em.v("gw")

        store_x1_prev = 0
        store_xln2_prev = 0
        store_sm_prev = 0
        v_xcons_prev = em.v("pe")
        v_attnev_prev = 0
        v_smdone_prev = 0
        v_t3ev_prev = 0
        for t in range(NTILE):
            r = NTILE + t
            xb = x_s[:, r % 2]
            b = t // 8
            ti = t % 8
            v_apply = emit_ln(xb, xln_s, [x_load_done[r], ("pe", v_xcons_prev)])
            if t == 0:
                cap("xln1", xln_s[:].rearrange("p c f -> p (c f)"))

            ev_par = {0: 0, 1: 0}
            for g in range(8):
                c, half = divmod(g, 2)
                bank = ps_t if g % 2 == 0 else ps_t2
                for q in range(4):
                    fc = 4 * half + q
                    w0 = []
                    if q == 0:
                        if g == 0:
                            w0 = [("ve", v_apply), ("ac", v_t3ev_prev)]
                        else:
                            w0 = [("ve", ev_par[g % 2])]
                    em.emit("tensor",
                            lambda tn, c=c, fc=fc, q=q, bank=bank: tn.transpose(
                                bank[:, 128 * q:128 * (q + 1)],
                                xln_s[:, c, 128 * fc:128 * (fc + 1)], identb[:]),
                            waits=w0, inc=("pe" if q == 3 else None))
                em.emit("vector",
                        lambda v, c=c, half=half, bank=bank: v.tensor_copy(
                            xln_fm[:, 4 * half:4 * half + 4, 128 * c:128 * (c + 1)],
                            bank[:, 0:512].rearrange("p (a k) -> p a k", a=4)),
                        waits=[("pe", em.v("pe"))], inc="ve")
                ev_par[g % 2] = em.v("ve")
            v_xfm = em.v("ve")

            evq_par = {0: None, 1: None}
            for m in range(16):
                bank = ps_a if m % 2 == 0 else ps_b
                for k in range(8):
                    w0 = []
                    if k == 0:
                        if m == 0:
                            w0 = [("ve", v_xfm), ("gw", v_wl)]
                        elif m >= 2:
                            w0 = [evq_par[m % 2]]
                    em.emit("tensor",
                            lambda tn, m=m, k=k, bank=bank: tn.matmul(
                                bank[:, 0:512], wInTl[:, k, 128 * m:128 * (m + 1)],
                                xln_fm[:, k, :], start=(k == 0), stop=(k == 7)),
                            waits=w0, inc=("pe" if k == 7 else None))
                if m % 2 == 0:
                    em.emit("vector",
                            lambda v, m=m, bank=bank: v.tensor_scalar(
                                qk_sb[:, m, :], bank[:, 0:512],
                                bInL_c[:, m:m + 1], None, ALU.add),
                            waits=[("pe", em.v("pe"))], inc="ve")
                    evq_par[0] = ("ve", em.v("ve"))
                else:
                    em.emit("scalar",
                            lambda s, m=m, bank=bank: s.activation(
                                qk_sb[:, m, :], bank[:, 0:512], AF.Identity,
                                bias=bInL_c[:, m:m + 1], scale=1.0),
                            waits=[("pe", em.v("pe"))], inc="ac")
                    evq_par[1] = ("ac", em.v("ac"))
            v_qkev_ve = em.v("ve")
            v_qkev_ac = em.v("ac")

            evv_par = {0: None, 1: None}
            for c4 in range(4):
                for n in range(2):
                    gidx = 2 * c4 + n
                    bank = ps_c if gidx % 2 == 0 else ps_d
                    for k in range(8):
                        w0 = []
                        if k == 0:
                            if gidx == 0:
                                w0 = ([("ve", v_smdone_prev)] if t > 0
                                      else [("ve", v_ag)])
                            elif gidx >= 2:
                                w0 = [evv_par[gidx % 2]]
                        em.emit("tensor",
                                lambda tn, c4=c4, n=n, k=k, bank=bank: tn.matmul(
                                    bank[:, 0:512], xln_fm[:, k, 128 * c4:128 * (c4 + 1)],
                                    wInTl[:, k, 2048 + 512 * n:2048 + 512 * (n + 1)],
                                    start=(k == 0), stop=False),
                                waits=w0)
                    em.emit("tensor",
                            lambda tn, n=n, bank=bank: tn.matmul(
                                bank[:, 0:512], onesb_s[0:1, 0:128],
                                bInL_r[0:1, 512 * n:512 * (n + 1)],
                                start=False, stop=True),
                            inc="pe")
                    em.emit("scalar",
                            lambda s, c4=c4, n=n, bank=bank: s.activation(
                                v_tm[:, c4, 512 * n:512 * (n + 1)], bank[:, 0:512],
                                AF.Copy, bias=0.0, scale=1.0),
                            waits=[("pe", em.v("pe"))], inc="ac")
                    evv_par[gidx % 2] = ("ac", em.v("ac"))
            v_vv = em.v("ac")
            v_xcons_prev = em.v("pe")

            v_exp = [0] * 4
            v_pT = [0] * 4
            v_pTev = [0] * 4
            v_av = [0] * 4
            v_attnev = [0] * 4
            v_pb = [0] * 4
            for c in range(4):
                par = c % 2
                lo_bank = ps_a if par == 0 else ps_c
                hi_bank = ps_b if par == 0 else ps_d
                for h in range(8):
                    bank = lo_bank if h < 4 else hi_bank
                    w0 = []
                    if h == 0:
                        if c == 0:
                            w0 = [("ve", v_qkev_ve), ("ac", v_qkev_ac)]
                        elif c == 1:
                            w0 = [("ac", v_vv)]
                        else:
                            w0 = [("ac", v_exp[c - 2])]
                    em.emit("tensor",
                            lambda tn, c=c, h=h, bank=bank: tn.matmul(
                                bank[:, 128 * (h % 4):128 * (h % 4) + 128],
                                qk_sb[:, h, 128 * c:128 * (c + 1)],
                                qk_sb[:, 8 + h, 128 * c:128 * (c + 1)],
                                start=True, stop=True),
                            waits=w0, inc=("pe" if h == 7 else None))
                v_sc = em.v("pe")
                em.emit("scalar",
                        lambda s, par=par, lo_bank=lo_bank: s.activation(
                            prob2[:, par, 0:4, :].rearrange("p a k -> p (a k)"),
                            lo_bank[:, 0:512], AF.Exp, bias=0.0, scale=SCL),
                        waits=[("pe", v_sc)] + ([("pe", v_pT[c - 2])] if c >= 2 else []))
                em.emit("scalar",
                        lambda s, par=par, hi_bank=hi_bank: s.activation(
                            prob2[:, par, 4:8, :].rearrange("p a k -> p (a k)"),
                            hi_bank[:, 0:512], AF.Exp, bias=0.0, scale=SCL),
                        inc="ac")
                v_exp[c] = em.v("ac")
                em.emit("vector",
                        lambda v, par=par: v.tensor_tensor(
                            prob2[:, par], prob2[:, par],
                            mask_s[:].rearrange("p (o k) -> p o k", o=1)
                            .broadcast_to([128, 8, 128]),
                            ALU.mult),
                        waits=[("ac", v_exp[c])])
                em.emit("vector",
                        lambda v, par=par: v.reduce_sum(ssum8[:], prob2[:, par], axis=AX.X))
                em.emit("vector", lambda v: v.reciprocal(ssum8[:], ssum8[:]))
                em.emit("vector",
                        lambda v, par=par: v.tensor_tensor(
                            prob2[:, par], prob2[:, par],
                            ssum8[:].rearrange("p (g o) -> p g o", o=1)
                            .broadcast_to([128, 8, 128]),
                            ALU.mult),
                        inc="ve")
                v_pb[c] = em.v("ve")
                for h in range(8):
                    bank = ps_t if h < 4 else ps_t2
                    w0 = []
                    if h == 0:
                        w0 = [("ve", v_pb[c])] + ([("ac", v_pTev[c - 1])] if c >= 1 else [])
                    em.emit("tensor",
                            lambda tn, par=par, h=h, bank=bank: tn.transpose(
                                bank[:, 128 * (h % 4):128 * (h % 4) + 128],
                                prob2[:, par, h, :], identb[:]),
                            waits=w0, inc=("pe" if h % 4 == 3 else None))
                v_pT[c] = em.v("pe")
                em.emit("scalar",
                        lambda s, par=par: s.activation(
                            probT2[:, par, 0:4, :].rearrange("p a k -> p (a k)"),
                            ps_t[:, 0:512], AF.Copy, bias=0.0, scale=1.0),
                        waits=[("pe", v_pT[c] - 1)])
                em.emit("scalar",
                        lambda s, par=par: s.activation(
                            probT2[:, par, 4:8, :].rearrange("p a k -> p (a k)"),
                            ps_t2[:, 0:512], AF.Copy, bias=0.0, scale=1.0),
                        waits=[("pe", v_pT[c])], inc="ac")
                v_pTev[c] = em.v("ac")
                for h in range(8):
                    w0 = []
                    if h == 0:
                        w0 = [("ac", v_pTev[c])]
                        if c >= 1:
                            w0.append(("ac", v_attnev[c - 1]))
                        elif t > 0:
                            w0.append(("ac", v_attnev_prev))
                        else:
                            w0.append(("ve", v_ag))
                    em.emit("tensor",
                            lambda tn, c=c, par=par, h=h: tn.matmul(
                                pm[:, 128 * h:128 * (h + 1)],
                                v_tm[:, c, 128 * h:128 * (h + 1)],
                                probT2[:, par, h, :], start=True, stop=True),
                            waits=w0, inc=("pe" if h == 7 else None))
                v_av[c] = em.v("pe")
                em.emit("scalar",
                        lambda s, c=c: s.activation(
                            attn_fm[:, 0:4, 128 * c:128 * (c + 1)],
                            pm[:, 0:512].rearrange("p (a k) -> p a k", a=4),
                            AF.Copy, bias=0.0, scale=1.0),
                        waits=[("pe", v_av[c])])
                em.emit("scalar",
                        lambda s, c=c: s.activation(
                            attn_fm[:, 4:8, 128 * c:128 * (c + 1)],
                            pm[:, 512:1024].rearrange("p (a k) -> p a k", a=4),
                            AF.Copy, bias=0.0, scale=1.0),
                        inc="ac")
                v_attnev[c] = em.v("ac")
            v_attnev_prev = v_attnev[3]

            seg0 = 8 * ti
            evo_par = {0: 0, 1: 0}
            for m in range(8):
                bank = ps_a if m % 2 == 0 else ps_b
                for k in range(8):
                    w0 = []
                    if k == 0:
                        if m == 0:
                            w0 = [("ac", v_attnev[3])]
                        elif m >= 2:
                            w0 = [("ve", evo_par[m % 2])]
                    em.emit("tensor",
                            lambda tn, m=m, k=k, bank=bank: tn.matmul(
                                bank[:, 0:512], wOutTl[:, k, 128 * m:128 * (m + 1)],
                                attn_fm[:, k, :], start=(k == 0), stop=(k == 7)),
                            waits=w0, inc=("pe" if k == 7 else None))
                em.emit("vector",
                        lambda v, m=m, b=b, seg0=seg0, bank=bank: v.tensor_tensor(
                            sum_fm[:, m, :].rearrange("p (s q) -> p s q", s=8),
                            bank[:, 0:512].rearrange("p (s q) -> p s q", s=8),
                            globB[:, m, 64 * b + seg0:64 * b + seg0 + 8]
                            .rearrange("p (s o) -> p s o", o=1).broadcast_to([128, 8, 64]),
                            ALU.add),
                        waits=[("pe", em.v("pe"))] + ([("ac", v_glob)] if (t == 0 and m == 0) else []),
                        inc="ve")
                evo_par[m % 2] = em.v("ve")
            v_sum = em.v("ve")

            ev_par = {0: 0, 1: 0}
            for g in range(8):
                c, half = divmod(g, 2)
                bank = ps_t if g % 2 == 0 else ps_t2
                for q in range(4):
                    fc = 4 * half + q
                    w0 = []
                    if q == 0:
                        if g == 0:
                            w0 = [("ve", v_sum), ("ac", v_pTev[3])]
                        else:
                            w0 = [("ve", ev_par[g % 2])]
                    em.emit("tensor",
                            lambda tn, c=c, fc=fc, q=q, bank=bank: tn.transpose(
                                bank[:, 128 * q:128 * (q + 1)],
                                sum_fm[:, fc, 128 * c:128 * (c + 1)], identb[:]),
                            waits=w0, inc=("pe" if q == 3 else None))
                em.emit("vector",
                        lambda v, c=c, half=half, bank=bank, xb=xb: v.tensor_tensor(
                            xb[:, c, 512 * half:512 * (half + 1)],
                            xb[:, c, 512 * half:512 * (half + 1)],
                            bank[:, 0:512], ALU.add),
                        waits=[("pe", em.v("pe"))], inc="ve")
                ev_par[g % 2] = em.v("ve")
            v_resid = em.v("ve")

            v_ln2 = emit_ln(xb, xln2_s, [("ve", v_resid), ("st", store_xln2_prev)])
            if t == 0:
                cap("x1t0", xb[:].rearrange("p c f -> p (c f)"))

            em.emit("sync",
                    lambda s, t=t, xb=xb: s.dma_start(
                        x1_dr.ap()[t * 512:(t + 1) * 512, :].rearrange("(c p) f -> p c f", p=128),
                        xb[:]),
                    waits=[("ve", v_resid)], inc="st", k=16)
            store_x1_prev = em.v("st")
            x_cons_done[r] = [("ve", em.v("ve")), ("ac", em.v("ac")), ("st", store_x1_prev)]
            if r + 2 < 2 * NTILE:
                emit_x_load(r + 2)
            em.emit("sync",
                    lambda s, t=t: s.dma_start(
                        xln2_dr.ap()[t * 512:(t + 1) * 512, :].rearrange("(c p) f -> p c f", p=128),
                        xln2_s[:]),
                    waits=[("ve", v_ln2)], inc="st", k=16)
            store_xln2_prev = em.v("st")

            ev_par = {0: 0, 1: 0}
            for g in range(8):
                c, half = divmod(g, 2)
                bank = ps_t if g % 2 == 0 else ps_t2
                for q in range(4):
                    fc = 4 * half + q
                    w0 = []
                    if q == 0:
                        w0 = ([("ve", v_ln2)] if g == 0 else [("ac", ev_par[g % 2])])
                    em.emit("tensor",
                            lambda tn, c=c, fc=fc, q=q, bank=bank: tn.transpose(
                                bank[:, 128 * q:128 * (q + 1)],
                                xln2_s[:, c, 128 * fc:128 * (fc + 1)], identb[:]),
                            waits=w0, inc=("pe" if q == 3 else None))
                em.emit("scalar",
                        lambda s, c=c, half=half, bank=bank: s.activation(
                            xln2_fm[:, 4 * half:4 * half + 4, 128 * c:128 * (c + 1)],
                            bank[:, 0:512].rearrange("p (a k) -> p a k", a=4),
                            AF.Copy, bias=0.0, scale=1.0),
                        waits=[("pe", em.v("pe"))], inc="ac")
                ev_par[g % 2] = em.v("ac")
            v_x2fm = em.v("ac")
            v_t3ev_prev = v_x2fm

            for k in range(8):
                em.emit("tensor",
                        lambda tn, k=k: tn.matmul(
                            ps_c[0:8, 0:512], wr_s[:, k, :], xln2_fm[:, k, :],
                            start=(k == 0), stop=(k == 7)),
                        waits=([("ac", v_x2fm)] if k == 0 else []),
                        inc=("pe" if k == 7 else None))
            em.emit("scalar",
                    lambda s: s.activation(logit_em[:], ps_c[0:8, 0:512], AF.Identity,
                                           bias=br_c[:, 0:1], scale=1.0),
                    waits=[("pe", em.v("pe"))], inc="ac")
            v_lg = em.v("ac")
            for c in range(4):
                em.emit("tensor",
                        lambda tn, c=c: tn.transpose(
                            ps_d[:, 8 * c:8 * (c + 1)],
                            logit_em[:, 128 * c:128 * (c + 1)], ident_f32_s[0:8, 0:8]),
                        waits=([("ac", v_lg)] if c == 0 else []),
                        inc=("pe" if c == 3 else None))
            v_lgt = em.v("pe")
            em.emit("vector", lambda v: v.tensor_copy(
                sm_tm[:].rearrange("p a e -> p (a e)"), ps_d[:, 0:32]),
                waits=[("pe", v_lgt)])
            em.emit("vector", lambda v: v.reduce_max(sm_mx[:], sm_tm[:], axis=AX.X))
            em.emit("vector", lambda v: v.tensor_tensor(
                sm_tm[:], sm_tm[:],
                sm_mx[:].rearrange("p (g o) -> p g o", o=1).broadcast_to([128, 4, 8]),
                ALU.subtract), inc="ve")
            em.emit("scalar", lambda s: s.activation(
                sm_tm[:].rearrange("p a e -> p (a e)"), sm_tm[:].rearrange("p a e -> p (a e)"),
                AF.Exp, bias=0.0, scale=1.0), waits=[("ve", em.v("ve"))], inc="ac")
            em.emit("vector", lambda v: v.reduce_sum(sm_sum[:], sm_tm[:], axis=AX.X),
                    waits=[("ac", em.v("ac"))])
            em.emit("vector", lambda v: v.reciprocal(sm_sum[:], sm_sum[:]))
            em.emit("vector", lambda v: v.tensor_tensor(
                sm_tm[:], sm_tm[:],
                sm_sum[:].rearrange("p (g o) -> p g o", o=1).broadcast_to([128, 4, 8]),
                ALU.mult), inc="ve")
            v_smtm = em.v("ve")
            for c in range(4):
                em.emit("tensor",
                        lambda tn, c=c: tn.transpose(
                            ps_c[0:8, 128 * c:128 * (c + 1)], sm_tm[:, c, :],
                            ident_f32_s[:]),
                        waits=([("ve", v_smtm)] if c == 0 else []),
                        inc=("pe" if c == 3 else None))
            em.emit("vector", lambda v: v.tensor_copy(sm_em[:], ps_c[0:8, 0:512]),
                    waits=[("pe", em.v("pe")), ("sm", store_sm_prev)], inc="ve")
            v_smdone_prev = em.v("ve")
            em.emit("gpsimd",
                    lambda g, t=t: g.dma_start(
                        sm_dr.ap()[:, t * 512:(t + 1) * 512], sm_em[:]),
                    waits=[("ve", v_smdone_prev)], inc="sm", k=16)
            store_sm_prev = em.v("sm")


        v_all_st = em.v("st")
        v_all_sm = em.v("sm")

        @block.sync
        def _(sync):
            em.run_engine("sync", sync)

        @block.gpsimd
        def _(gpsimd):
            em.run_engine("gpsimd", gpsimd)
            gpsimd.wait_ge(h_st, v_all_st)
            gpsimd.wait_ge(h_sm, v_all_sm)

        @block.vector
        def _(vector):
            em.run_engine("vector", vector)

        @block.scalar
        def _(scalar):
            em.run_engine("scalar", scalar)

        @block.tensor
        def _(tensor):
            em.run_engine("tensor", tensor)

    return nc



def _consts_b():
    ident_bf = np.eye(128, dtype=ml_dtypes.bfloat16)
    tril = np.zeros((128, 128), dtype=ml_dtypes.bfloat16)
    for p in range(128):
        tril[p, p + 1:] = 1.0
    diagbig = (np.eye(128) * BIG).astype(ml_dtypes.bfloat16)
    onesb = np.ones((128, 128), dtype=ml_dtypes.bfloat16)
    onesf = np.ones((128, 128), dtype=np.float32)
    t = np.arange(NTOK)
    ids = np.stack([t >> 6, t & 63], axis=1).astype(ml_dtypes.bfloat16)
    ids = ids.reshape(64, 128, 2).transpose(1, 0, 2).copy()
    s = np.arange(CAP)
    im = np.zeros((CAP, 8), dtype=ml_dtypes.bfloat16)
    im[:, 0] = ((NTOK + (s % 128)) >> 6).astype(ml_dtypes.bfloat16)
    im[:, 1] = ((NTOK + (s % 128)) & 63).astype(ml_dtypes.bfloat16)
    return ident_bf, tril, diagbig, onesb, onesf, ids, im


def build_launch_b(dbg_stage=None):
    nc = bass.Bass("TRN2", target_bir_lowering=False, detect_race_conditions=False)

    x1_dr = nc.dram_tensor("x1", [NTOK, D], F32, kind="ExternalInput")
    xln2_dr = nc.dram_tensor("xln2", [NTOK, D], BF16, kind="ExternalInput")
    smg_dr = nc.dram_tensor("smg", [E, 65536], F32, kind="ExternalInput")
    smtm_dr = nc.dram_tensor("smtm", [NTOK, E], F32, kind="ExternalInput")
    wu_dr = nc.dram_tensor("wu", [4, D, FP], F32, kind="ExternalInput")
    buc_dr = nc.dram_tensor("bu_cols", [128, 4 * NFC], F32, kind="ExternalInput")
    wd_dr = nc.dram_tensor("wd", [E, FP, D], F32, kind="ExternalInput")
    bd_dr = nc.dram_tensor("bd", [E, D], F32, kind="ExternalInput")

    y_dr = nc.dram_tensor("y", [YROWS, D], F32, kind="ExternalOutput")
    dbg_dr = nc.dram_tensor("dbg", [128, 8192], F32, kind="ExternalOutput") if dbg_stage else None

    meta_dr = nc.dram_tensor("meta_scr", [E, CAP, 8], BF16, kind="Internal")

    identc, trilc, diagc, onesbc, onesfc, idsc, imc = _consts_b()
    ident_dr = nc.inline_tensor(identc, "ident_bf")
    tril_dr = nc.inline_tensor(trilc, "trilx")
    diag_dr = nc.inline_tensor(diagc, "diagbig")
    onesb_dr = nc.inline_tensor(onesbc, "onesb")
    onesf_dr = nc.inline_tensor(onesfc, "onesf")
    ids_dr = nc.inline_tensor(idsc.reshape(128, 128), "ids")
    im_dr = nc.inline_tensor(imc, "initmeta")

    em = EM()

    with ExitStack() as ctx:
        identb = ctx.enter_context(nc.sbuf_tensor("identb", [128, 128], BF16))
        tril_s = ctx.enter_context(nc.sbuf_tensor("tril_s", [128, 128], BF16))
        diag_s = ctx.enter_context(nc.sbuf_tensor("diag_s", [128, 128], BF16))
        onesb_s = ctx.enter_context(nc.sbuf_tensor("onesb_s", [128, 128], BF16))
        onesf_s = ctx.enter_context(nc.sbuf_tensor("onesf_s", [128, 128], F32))
        ids_s = ctx.enter_context(nc.sbuf_tensor("ids_s", [128, 64, 2], BF16))
        im_s = ctx.enter_context(nc.sbuf_tensor("im_s", [128, 10, 8], BF16))
        smtm_s = ctx.enter_context(nc.sbuf_tensor("smtm_s", [128, 64, 8], F32))
        lo_s = ctx.enter_context(nc.sbuf_tensor("lo_s", [128, 8], F32))
        hi_s = ctx.enter_context(nc.sbuf_tensor("hi_s", [128, 8], F32))
        mid_s = ctx.enter_context(nc.sbuf_tensor("mid_s", [128, 8], F32))
        cnt_s = ctx.enter_context(nc.sbuf_tensor("cnt_s", [128, 8], F32))
        flag_s = ctx.enter_context(nc.sbuf_tensor("flag_s", [128, 8], F32))
        tmp1_s = ctx.enter_context(nc.sbuf_tensor("tmp1_s", [128, 8], F32))
        tmp2_s = ctx.enter_context(nc.sbuf_tensor("tmp2_s", [128, 8], F32))
        trash_b = ctx.enter_context(nc.sbuf_tensor("trash_b", [128, 512], BF16))
        m_all = ctx.enter_context(nc.sbuf_tensor("m_all", [128, 64, 8], BF16))
        minv_all = ctx.enter_context(nc.sbuf_tensor("minv_all", [128, 64, 8], BF16))
        pos_all = ctx.enter_context(nc.sbuf_tensor("pos_all", [128, 64, 8], U32))
        meta_all = ctx.enter_context(nc.sbuf_tensor("meta_all", [128, 64, 8, 8], BF16))
        trow = ctx.enter_context(nc.sbuf_tensor("trow", [1, 512], F32))
        trow2 = ctx.enter_context(nc.sbuf_tensor("trow2", [1, 512], F32))
        offs = ctx.enter_context(nc.sbuf_tensor("offs", [1, 512], F32))
        meta_ld = ctx.enter_context(nc.sbuf_tensor("meta_ld", [128, 10, 8], BF16))
        idx_f = ctx.enter_context(nc.sbuf_tensor("idx_f", [128, 10], F32))
        idx_u = ctx.enter_context(nc.sbuf_tensor("idx_u", [128, 10], U32))
        gates = ctx.enter_context(nc.sbuf_tensor("gates", [128, 10], F32))
        bu_c = ctx.enter_context(nc.sbuf_tensor("bu_c", [128, 4 * NFC], F32))
        bd_r = ctx.enter_context(nc.sbuf_tensor("bd_r", [1, 1024], BF16))
        bdb = ctx.enter_context(nc.sbuf_tensor("bdb", [128, 1024], F32))
        wu_s = ctx.enter_context(nc.sbuf_tensor("wu_s", [128, 8, FP], BF16))
        wd_s = ctx.enter_context(nc.sbuf_tensor("wd_s", [128, NFC, 1024], BF16))
        xg_tm = ctx.enter_context(nc.sbuf_tensor("xg_tm", [128, 2, 1024], BF16))
        xg_fm = ctx.enter_context(nc.sbuf_tensor("xg_fm", [128, 8, CAP], BF16))
        mid_arena = ctx.enter_context(nc.sbuf_tensor("mid_arena", [128, NFC, CAP], BF16))
        contrib = ctx.enter_context(nc.sbuf_tensor("contrib", [128, 2, 1024], F32))
        ps_a = ctx.enter_context(nc.psum_tensor("ps_a", [128, 512], F32))
        ps_b = ctx.enter_context(nc.psum_tensor("ps_b", [128, 512], F32))
        ps_c = ctx.enter_context(nc.psum_tensor("ps_c", [128, 512], F32))
        ps_d = ctx.enter_context(nc.psum_tensor("ps_d", [128, 512], F32))
        ps_t = ctx.enter_context(nc.psum_tensor("ps_t", [128, 512], BF16))
        ps_t2 = ctx.enter_context(nc.psum_tensor("ps_t2", [128, 512], BF16))
        ps_x = ctx.enter_context(nc.psum_tensor("ps_x", [128, 8], F32))
        h_ld = ctx.enter_context(nc.semaphore("s_ld"))
        h_mi = ctx.enter_context(nc.semaphore("s_mi"))
        h_ml = ctx.enter_context(nc.semaphore("s_ml"))
        h_w = ctx.enter_context(nc.semaphore("s_w"))
        h_gx0 = ctx.enter_context(nc.semaphore("s_gx0"))
        h_gx1 = ctx.enter_context(nc.semaphore("s_gx1"))
        h_ys0 = ctx.enter_context(nc.semaphore("s_ys0"))
        h_ys1 = ctx.enter_context(nc.semaphore("s_ys1"))
        h_sc = [ctx.enter_context(nc.semaphore(f"s_sc{i}")) for i in range(E)]
        h_ve = ctx.enter_context(nc.semaphore("s_ve"))
        h_ac = ctx.enter_context(nc.semaphore("s_ac"))
        h_pe = ctx.enter_context(nc.semaphore("s_pe"))
        block = ctx.enter_context(nc.Block())
        for nm, hh in (("ld", h_ld), ("mi", h_mi), ("ml", h_ml), ("w", h_w),
                       ("gx0", h_gx0), ("gx1", h_gx1), ("ys0", h_ys0), ("ys1", h_ys1),
                       ("ve", h_ve), ("ac", h_ac), ("pe", h_pe)):
            em.add_sem(nm, hh)
        for i in range(E):
            em.add_sem(f"sc{i}", h_sc[i])

        smg_v = mid_arena[:, 0:8, 0:1024].bitcast(F32)

        regs = {}

        def _mkregs(g):
            regs["cap"] = g.to_reg(CAP - 1)
            regs["ntok"] = g.to_reg(NTOK - 1)

        em.emit("gpsimd", _mkregs)

        def gp_dma(fn, sem, waits=()):
            em.emit("gpsimd", fn, waits=list(waits), inc=sem, k=16)
            return em.v(sem)

        def cap(stage, ap, extra_waits=()):
            if dbg_stage != stage:
                return
            def _cap_dma(g, ap=ap):
                dst = dbg_dr.ap()[0:ap.shape[0], 0:int(np.prod(ap.shape[1:]))]
                if len(ap.shape) == 3:
                    dst = dst.rearrange("p (a k) -> p a k", a=ap.shape[1])
                return g.dma_start(dst, ap)
            em.emit("gpsimd", _cap_dma,
                    waits=[("ve", em.v("ve")), ("ac", em.v("ac")), ("pe", em.v("pe"))]
                          + list(extra_waits),
                    inc="mi", k=16)

        for dst, src in (
            (identb[:], ident_dr.ap()),
            (tril_s[:], tril_dr.ap()),
            (diag_s[:], diag_dr.ap()),
            (onesb_s[:], onesb_dr.ap()),
            (onesf_s[:], onesf_dr.ap()),
            (ids_s[:].rearrange("p a b -> p (a b)"), ids_dr.ap()),
            (bu_c[:], buc_dr.ap()),
            (smtm_s[:], smtm_dr.ap().rearrange("(j p) e -> p j e", p=128)),
        ):
            em.emit("sync", lambda s, d=dst, a=src: s.dma_start(d, a), inc="ld", k=16)
        for e in range(E):
            em.emit("sync",
                    lambda s, e=e: s.dma_start(
                        smg_v[:, e, :], smg_dr.ap()[e, :].rearrange("(p f) -> p f", p=128)),
                    inc="ld", k=16)
        v_smg = em.v("ld")

        for q in range(4):
            em.emit("sync",
                    lambda s, q=q: s.dma_start(
                        y_dr.ap()[q * 2048:(q + 1) * 2048, :],
                        x1_dr.ap()[q * 2048:(q + 1) * 2048, :]),
                    inc="ld", k=16)
        v_yinit = em.v("ld")

        v_ims = gp_dma(
            lambda g: g.dma_start(im_s[:], im_dr.ap().rearrange("(m p) c -> p m c", p=128)),
            "mi")
        v_meta_init = 0
        for e in range(E):
            v_meta_init = gp_dma(
                lambda g, e=e: g.dma_start(
                    meta_dr.ap()[e].rearrange("(m p) c -> p m c", p=128), im_s[:]),
                "mi", waits=[("mi", v_ims)])

        v_wu = [0] * 4
        v_wd = [0] * E
        v_wu[0] = gp_dma(lambda g: g.dma_start(
            wu_s[:], wu_dr.ap()[0].rearrange("(c p) f -> p c f", p=128)), "w")
        v_wd[0] = gp_dma(lambda g: g.dma_start(
            wd_s[:], wd_dr.ap()[0].rearrange("(c p) f -> p c f", p=128)), "w")

        em.emit("vector", lambda v: v.memset(xg_tm[:].rearrange("p a b -> p (a b)"), 0.0))
        em.emit("vector", lambda v: v.memset(xg_fm[:].rearrange("p a b -> p (a b)"), 0.0))
        em.emit("vector", lambda v: v.memset(lo_s[:], 0.0))
        em.emit("vector", lambda v: v.memset(hi_s[:], 1.0), inc="ve")
        v_binit = em.v("ve")

        for it in range(BISECT_ITERS):
            em.emit("vector", lambda v: v.tensor_tensor(mid_s[:], lo_s[:], hi_s[:], ALU.add),
                    waits=([("ld", v_smg), ("ve", v_binit)] if it == 0 else []))
            em.emit("vector", lambda v: v.tensor_scalar(mid_s[:], mid_s[:], 0.5, None, ALU.mult),
                    inc="ve")
            v_mid = em.v("ve")
            cnt3 = meta_all[:].rearrange("p j e c -> p (j e c)").rearrange(
                "p (e k) -> p e k", e=E)
            em.emit("vector",
                    lambda v: v.tensor_tensor(
                        cnt3, smg_v[:],
                        mid_s[:].rearrange("p (e o) -> p e o", o=1).broadcast_to([128, E, 512]),
                        ALU.is_ge))
            em.emit("vector",
                    lambda v: v.reduce_sum(cnt_s[:], cnt3, axis=AX.X), inc="ve")
            v_cnt = em.v("ve")
            em.emit("tensor",
                    lambda tn: tn.matmul(ps_x[:], onesf_s[:], cnt_s[:], start=True, stop=True),
                    waits=[("ve", v_cnt)], inc="pe")
            v_cntmm = em.v("pe")
            em.emit("vector",
                    lambda v: v.tensor_scalar(flag_s[:], ps_x[:], float(65536 // E), 0.0,
                                              ALU.is_ge, ALU.add),
                    waits=[("pe", v_cntmm)])
            em.emit("vector", lambda v: v.tensor_tensor(tmp1_s[:], mid_s[:], lo_s[:], ALU.subtract))
            em.emit("vector", lambda v: v.tensor_tensor(tmp1_s[:], tmp1_s[:], flag_s[:], ALU.mult))
            em.emit("vector", lambda v: v.tensor_tensor(lo_s[:], lo_s[:], tmp1_s[:], ALU.add))
            em.emit("vector", lambda v: v.tensor_tensor(tmp2_s[:], hi_s[:], mid_s[:], ALU.subtract))
            em.emit("vector", lambda v: v.tensor_tensor(tmp2_s[:], tmp2_s[:], flag_s[:], ALU.mult))
            em.emit("vector", lambda v: v.tensor_tensor(hi_s[:], mid_s[:], tmp2_s[:], ALU.add),
                    inc="ve")

        v_theta = em.v("ve")
        cap("theta", lo_s[:])
        cap("smgchk", smg_v[:, 0:8, :], [("ld", v_smg)])

        em.emit("vector",
                lambda v: v.tensor_tensor(
                    m_all[:], smtm_s[:],
                    lo_s[:].rearrange("p (o e) -> p o e", o=1).broadcast_to([128, 64, E]),
                    ALU.is_ge),
                waits=[("ve", v_theta)])
        em.emit("vector",
                lambda v: v.tensor_scalar(minv_all[:], m_all[:], -1.0, 1.0,
                                          ALU.mult, ALU.add),
                inc="ve")
        em.emit("tensor",
                lambda tn: tn.matmul(ps_a[0:64, 0:512], onesb_s[:, 0:64],
                                     m_all[:].rearrange("p j e -> p (j e)"),
                                     start=True, stop=True),
                waits=[("ve", em.v("ve"))], inc="pe")
        v_tot = em.v("pe")
        if dbg_stage == "tots":
            em.emit("vector", lambda v: v.tensor_copy(trash_b[0:64, :], ps_a[0:64, 0:512]),
                    waits=[("pe", v_tot)], inc="ve")
            cap("tots", trash_b[0:64, :])
        em.emit("vector", lambda v: v.tensor_copy(trow[:], ps_a[0:1, 0:512]),
                waits=[("pe", v_tot)], inc="ve")
        cap("trow", trow[:])

        cur, nxt = trow, trow2
        for st in (8, 16, 32, 64, 128, 256):
            em.emit("vector", lambda v, cur=cur, nxt=nxt, st=st: v.tensor_copy(
                nxt[:, 0:st], cur[:, 0:st]))
            em.emit("vector", lambda v, cur=cur, nxt=nxt, st=st: v.tensor_tensor(
                nxt[:, st:512], cur[:, st:512], cur[:, 0:512 - st], ALU.add), inc="ve")
            cur, nxt = nxt, cur
        em.emit("vector", lambda v: v.memset(offs[:, 0:8], 0.0))
        em.emit("vector", lambda v, cur=cur: v.tensor_copy(offs[:, 8:512], cur[:, 0:504]),
                inc="ve")
        v_offs = em.v("ve")
        cap("m0", m_all[:])
        cap("offs", offs[:])

        em.emit("tensor",
                lambda tn: tn.matmul(ps_b[:, 0:512], tril_s[:],
                                     m_all[:].rearrange("p j e -> p (j e)"),
                                     start=True, stop=False),
                waits=[("ve", v_offs)])
        em.emit("tensor",
                lambda tn: tn.matmul(ps_b[:, 0:512], onesf_s[0:1, :], offs[:],
                                     start=False, stop=False))
        em.emit("tensor",
                lambda tn: tn.matmul(ps_b[:, 0:512], diag_s[:],
                                     minv_all[:].rearrange("p j e -> p (j e)"),
                                     start=False, stop=True), inc="pe")
        v_pos = em.v("pe")
        em.emit("vector",
                lambda v: v.tensor_copy(pos_all[:].rearrange("p j e -> p (j e)"),
                                        ps_b[:, 0:512]),
                waits=[("pe", v_pos)])
        em.emit("vector",
                lambda v: v.tensor_copy(
                    meta_all[:, :, :, 0],
                    ids_s[:, :, 0:1].broadcast_to([128, 64, 8])))
        em.emit("vector",
                lambda v: v.tensor_copy(
                    meta_all[:, :, :, 1],
                    ids_s[:, :, 1:2].broadcast_to([128, 64, 8])))
        em.emit("vector",
                lambda v: v.tensor_copy(meta_all[:, :, :, 2], smtm_s[:]))
        em.emit("vector",
                lambda v: v.tensor_tensor(meta_all[:, :, :, 3], smtm_s[:],
                                          meta_all[:, :, :, 2], ALU.subtract),
                inc="ve")
        v_sel = em.v("ve")
        em.emit("vector", lambda v: v.memset(meta_all[:, :, :, 4:8], 0.0), inc="ve")
        v_meta = em.v("ve")
        cap("pos", pos_all[:].bitcast(F32))
        cap("meta8", meta_all[:, :, :, 0:8].rearrange("p j e c -> p (j e c)"))

        v_scat = [0] * E

        def emit_meta_scatters(e):
            for j in range(64):
                v_scat[e] = gp_dma(
                    lambda g, e=e, j=j: g.indirect_dma_start(
                        meta_dr.ap().rearrange("e m c -> (e m) c"),
                        IndirectOffsetOnAxis(pos_all[:, j, e:e + 1], 0),
                        meta_all[:, j, e, :], None,
                        element_offset=e * CAP * 8,
                        bounds_check=regs["cap"], oob_is_err=False),
                    f"sc{e}",
                    waits=([("ve", v_meta), ("mi", v_meta_init)] if j == 0 else []))

        emit_meta_scatters(0)

        cap("meta0", meta_dr.ap()[0].rearrange("(m p) c -> p m c", p=128),
            [(f"sc{e}", v_scat[e]) for e in range(E)] + [("mi", v_meta_init)])

        v_meta_ld = [0] * E
        v_y_final = 0
        v_prev_dec = 0
        v_prev_scat_end = []
        for e in range(E):
            g_grp = e // 2
            em.emit("sync",
                    lambda s, e=e: s.dma_start(
                        meta_ld[:], meta_dr.ap()[e].rearrange("(m p) c -> p m c", p=128)),
                    waits=[(f"sc{e}", v_scat[e]), ("ve", v_prev_dec)], inc="ml", k=16)
            v_meta_ld[e] = em.v("ml")
            em.emit("vector",
                    lambda v: v.tensor_scalar(idx_f[:], meta_ld[:, :, 0], 64.0, None, ALU.mult),
                    waits=[("ml", v_meta_ld[e])] + list(v_prev_scat_end))
            em.emit("vector",
                    lambda v: v.tensor_tensor(idx_f[:], idx_f[:], meta_ld[:, :, 1], ALU.add))
            em.emit("vector", lambda v: v.tensor_copy(idx_u[:], idx_f[:]))
            em.emit("vector",
                    lambda v: v.tensor_tensor(gates[:], meta_ld[:, :, 2], meta_ld[:, :, 3],
                                              ALU.add), inc="ve")
            v_dec = em.v("ve")
            v_prev_dec = v_dec

            v_bdr = gp_dma(lambda g, e=e: g.dma_start(
                bd_r[:], bd_dr.ap()[e, :].rearrange("(o f) -> o f", o=1)), "w")
            for hhalf in range(2):
                em.emit("tensor",
                        lambda tn, hhalf=hhalf: tn.matmul(
                            ps_c[:, 0:512],
                            onesb_s[0:1, :], bd_r[0:1, 512 * hhalf:512 * (hhalf + 1)],
                            start=True, stop=True),
                        waits=[("w", v_bdr), ("ve", em.v("ve"))], inc="pe")
                em.emit("vector",
                        lambda v, hhalf=hhalf: v.tensor_copy(
                            bdb[:, 512 * hhalf:512 * (hhalf + 1)], ps_c[:, 0:512]),
                        waits=[("pe", em.v("pe"))], inc="ve")
            v_bdb = em.v("ve")

            ev_par = {0: 0, 1: 0}
            xg_done = [0, 0]
            for m in range(10):
                buf = m % 2
                v_g = gp_dma(
                    lambda g, m=m, buf=buf: g.indirect_dma_start(
                        xg_tm[:, buf, :], None, xln2_dr.ap(),
                        IndirectOffsetOnAxis(idx_u[:, m:m + 1], 0),
                        bounds_check=regs["ntok"], oob_is_err=False),
                    f"gx{buf}",
                    waits=[("ve", v_dec), ("pe", xg_done[buf])])
                for hb in range(2):
                    bank = ps_t if hb == 0 else ps_t2
                    for q in range(4):
                        k = 4 * hb + q
                        em.emit("tensor",
                                lambda tn, m=m, buf=buf, k=k, q=q, bank=bank: tn.transpose(
                                    bank[:, 128 * q:128 * (q + 1)],
                                    xg_tm[:, buf, 128 * k:128 * (k + 1)], identb[:]),
                                waits=(([(f"gx{buf}", v_g), ("ve", ev_par[hb])] if q == 0 else [])),
                                inc=("pe" if q == 3 else None))
                    em.emit("vector",
                            lambda v, m=m, hb=hb, bank=bank: v.tensor_copy(
                                xg_fm[:, 4 * hb:4 * hb + 4, 128 * m:128 * (m + 1)],
                                bank[:, 0:512].rearrange("p (a k) -> p a k", a=4)),
                            waits=[("pe", em.v("pe"))], inc="ve")
                    ev_par[hb] = em.v("ve")
                xg_done[buf] = em.v("pe")
            v_xg = em.v("ve")

            if e + 1 < E:
                emit_meta_scatters(e + 1)

            ev_up = {0: 0, 1: 0}
            nn_off = (0, 512, 1024)
            nn_sz = (512, 512, 256)
            for fc in range(NFC):
                for nn in range(3):
                    gi = fc * 3 + nn
                    bank = ps_a if gi % 2 == 0 else ps_b
                    for k in range(8):
                        em.emit("tensor",
                                lambda tn, fc=fc, nn=nn, k=k, bank=bank: tn.matmul(
                                    bank[:, 0:nn_sz[nn]],
                                    wu_s[:, k, 128 * fc:128 * (fc + 1)],
                                    xg_fm[:, k, nn_off[nn]:nn_off[nn] + nn_sz[nn]],
                                    start=(k == 0), stop=(k == 7)),
                                waits=(([("ve", v_xg), ("w", v_wu[g_grp]),
                                         ("ac", ev_up[gi % 2])] if k == 0 else [])),
                                inc=("pe" if k == 7 else None))
                    em.emit("scalar",
                            lambda s, fc=fc, nn=nn, g_grp=g_grp, bank=bank: s.activation(
                                mid_arena[:, fc, nn_off[nn]:nn_off[nn] + nn_sz[nn]],
                                bank[:, 0:nn_sz[nn]], AF.Silu,
                                bias=bu_c[:, NFC * g_grp + fc:NFC * g_grp + fc + 1],
                                scale=1.0),
                            waits=[("pe", em.v("pe")), ("ve", v_theta), ("ld", v_smg)],
                            inc="ac")
                    ev_up[gi % 2] = em.v("ac")
            v_up = em.v("ac")
            v_up_pe = em.v("pe")

            if e + 1 < E and (e + 1) % 2 == 0:
                v_wu[g_grp + 1] = gp_dma(
                    lambda g, gg=g_grp + 1: g.dma_start(
                        wu_s[:], wu_dr.ap()[gg].rearrange("(c p) f -> p c f", p=128)),
                    "w", waits=[("pe", v_up_pe)])

            ev_dn = {0: 0, 1: 0}
            scat_done = [0, 0]
            for m in range(10):
                buf = m % 2
                for hhalf in range(2):
                    bank = ps_c if hhalf == 0 else ps_d
                    for k in range(NFC):
                        em.emit("tensor",
                                lambda tn, m=m, hhalf=hhalf, k=k, bank=bank: tn.matmul(
                                    bank[:, 0:512],
                                    mid_arena[:, k, 128 * m:128 * (m + 1)],
                                    wd_s[:, k, 512 * hhalf:512 * (hhalf + 1)],
                                    start=(k == 0), stop=(k == NFC - 1)),
                                waits=(([("ac", v_up), ("w", v_wd[e]),
                                         ("ve", ev_dn[hhalf])] if k == 0 else [])),
                                inc=("pe" if k == NFC - 1 else None))
                    em.emit("vector",
                            lambda v, m=m, buf=buf, hhalf=hhalf, bank=bank: v.tensor_tensor(
                                contrib[:, buf, 512 * hhalf:512 * (hhalf + 1)],
                                bank[:, 0:512], bdb[:, 512 * hhalf:512 * (hhalf + 1)],
                                ALU.add),
                            waits=[("pe", em.v("pe")), ("ve", v_bdb),
                                   (f"ys{buf}", scat_done[buf])])
                    em.emit("vector",
                            lambda v, m=m, buf=buf, hhalf=hhalf: v.tensor_scalar(
                                contrib[:, buf, 512 * hhalf:512 * (hhalf + 1)],
                                contrib[:, buf, 512 * hhalf:512 * (hhalf + 1)],
                                gates[:, m:m + 1], None, ALU.mult),
                            inc="ve")
                    ev_dn[hhalf] = em.v("ve")
                v_ct = em.v("ve")
                v_y_final = gp_dma(
                    lambda g, m=m, buf=buf: g.indirect_dma_start(
                        y_dr.ap(), IndirectOffsetOnAxis(idx_u[:, m:m + 1], 0),
                        contrib[:, buf, :], None,
                        compute_op=ALU.add, oob_is_err=True),
                    f"ys{buf}",
                    waits=[("ve", v_ct)] + ([("ld", v_yinit)] if (e == 0 and m == 0) else []))
                scat_done[buf] = v_y_final
            v_prev_scat_end = [("ys0", em.v("ys0")), ("ys1", em.v("ys1"))]

            if e + 1 < E:
                v_wd[e + 1] = gp_dma(
                    lambda g, ee=e + 1: g.dma_start(
                        wd_s[:], wd_dr.ap()[ee].rearrange("(c p) f -> p c f", p=128)),
                    "w", waits=[("pe", em.v("pe"))])

        v_fin_ys0 = em.v("ys0")
        v_fin_ys1 = em.v("ys1")
        v_fin_mi = em.v("mi")

        @block.sync
        def _(sync):
            em.run_engine("sync", sync)

        @block.gpsimd
        def _(gpsimd):
            em.run_engine("gpsimd", gpsimd)
            gpsimd.wait_ge(h_ys0, v_fin_ys0)
            gpsimd.wait_ge(h_ys1, v_fin_ys1)
            gpsimd.wait_ge(h_mi, v_fin_mi)

        @block.vector
        def _(vector):
            em.run_engine("vector", vector)

        @block.scalar
        def _(scalar):
            em.run_engine("scalar", scalar)

        @block.tensor
        def _(tensor):
            em.run_engine("tensor", tensor)

    return nc



_CACHE = {}


def _get_programs():
    if "a" not in _CACHE:
        _CACHE["a"] = build_launch_a()
    if "b" not in _CACHE:
        _CACHE["b"] = build_launch_b()
    return _CACHE["a"], _CACHE["b"]


def kernel(**inputs):
    x = np.ascontiguousarray(np.asarray(inputs["L0_attn_x"], dtype=np.float32))
    w_in_l = np.asarray(inputs["L0_attn_w_in_local"], dtype=np.float32)
    b_in_l = np.asarray(inputs["L0_attn_b_in_local"], dtype=np.float32)
    w_out_l = np.asarray(inputs["L0_attn_w_out_local"], dtype=np.float32)
    b_out_l = np.asarray(inputs["L0_attn_b_out_local"], dtype=np.float32)
    w_in_g = np.asarray(inputs["L0_attn_w_in_global"], dtype=np.float32)
    b_in_g = np.asarray(inputs["L0_attn_b_in_global"], dtype=np.float32)
    w_out_g = np.asarray(inputs["L0_attn_w_out_global"], dtype=np.float32)
    b_out_g = np.asarray(inputs["L0_attn_b_out_global"], dtype=np.float32)
    wr = np.asarray(inputs["L0_ffn_Wr"], dtype=np.float32)
    br = np.asarray(inputs["L0_ffn_br"], dtype=np.float32)
    wu = np.asarray(inputs["L0_ffn_W_up"], dtype=np.float32)
    bu = np.asarray(inputs["L0_ffn_b_up"], dtype=np.float32)
    wd = np.asarray(inputs["L0_ffn_W_down"], dtype=np.float32)
    bd = np.asarray(inputs["L0_ffn_b_down"], dtype=np.float32)

    wInT_l = np.ascontiguousarray(w_in_l.T)
    wOutT_l = np.ascontiguousarray(w_out_l.T)
    wInT_g = np.ascontiguousarray(w_in_g.T)
    wOutT_g = np.ascontiguousarray(w_out_g.T)
    bInL_c = np.ascontiguousarray(b_in_l.reshape(24, 128).T)
    bInG_c = np.ascontiguousarray(b_in_g.reshape(24, 128).T)
    bOutL_c = np.ascontiguousarray(b_out_l.reshape(8, 128).T)
    bOutG_c = np.ascontiguousarray(b_out_g.reshape(8, 128).T)
    bInL_r = np.ascontiguousarray(b_in_l[2 * D:].reshape(1, D))
    bInG_r = np.ascontiguousarray(b_in_g[2 * D:].reshape(1, D))
    wr_l = np.ascontiguousarray(wr.reshape(8, 128, E).transpose(1, 0, 2))
    br_c = br.reshape(E, 1)

    wu_p = np.zeros((4, D, FP), dtype=np.float32)
    wu_p[:, :, :FTRUE] = wu
    bu_p = np.zeros((4, FP), dtype=np.float32)
    bu_p[:, :FTRUE] = bu
    bu_cols = np.ascontiguousarray(
        bu_p.reshape(4, NFC, 128).transpose(2, 0, 1).reshape(128, 4 * NFC))
    wd_p = np.zeros((E, FP, D), dtype=np.float32)
    wd_p[:, :FTRUE, :] = wd

    nca, ncb = _get_programs()

    in_maps_a = []
    for c in range(8):
        in_maps_a.append({
            "x": np.ascontiguousarray(x[2 * c:2 * c + 2].reshape(NTOK, D)),
            "wInT_l": wInT_l, "wOutT_l": wOutT_l,
            "wInT_g": wInT_g, "wOutT_g": wOutT_g,
            "bInL_c": bInL_c, "bInG_c": bInG_c,
            "bOutL_c": bOutL_c, "bOutG_c": bOutG_c,
            "bInL_r": bInL_r, "bInG_r": bInG_r,
            "wr_l": wr_l, "br_c": br_c,
        })
    res_a = run_bass_kernel_spmd(nca, in_maps_a, core_ids=list(range(8)))

    sm_all = np.ascontiguousarray(
        np.concatenate([res_a.results[c]["sm"] for c in range(8)], axis=1))

    in_maps_b = []
    for c in range(8):
        in_maps_b.append({
            "x1": res_a.results[c]["x1"],
            "xln2": res_a.results[c]["xln2"],
            "smg": sm_all,
            "smtm": np.ascontiguousarray(res_a.results[c]["sm"].T),
            "wu": wu_p, "bu_cols": bu_cols, "wd": wd_p, "bd": bd,
        })
    res_b = run_bass_kernel_spmd(ncb, in_maps_b, core_ids=list(range(8)))

    y = np.concatenate(
        [res_b.results[c]["y"][0:NTOK].reshape(2, 4096, D) for c in range(8)], axis=0)
    return y


# version 04
# speedup vs baseline: 1.2264x; 1.0396x over previous
import numpy as np
from contextlib import ExitStack
import ml_dtypes
import concourse.bass as bass
import concourse.mybir as mybir
from concourse.bass_utils import run_bass_kernel_spmd
from concourse.bass_core_rust import IndirectOffsetOnAxis

F32 = mybir.dt.float32
BF16 = mybir.dt.bfloat16
U32 = mybir.dt.uint32
AF = mybir.ActivationFunctionType
ALU = mybir.AluOpType
AX = mybir.AxisListType

NTOK = 8192
NTILE = 16
D = 1024
E = 8
FTRUE = 2730
FP = 2816
NFC = FP // 128
CAP = 1280
NSLOT = CAP + 128
YROWS = NTOK + 128
SCL = 0.08838834764831845
SCG = 1.0 / 16.0
EPS = 1e-5
NEG = -30000.0
BIG = float(1 << 20)
BISECT_ITERS = 30


class EM:
    ENGINES = ("sync", "gpsimd", "vector", "scalar", "tensor")

    def __init__(self):
        self.prog = {e: [] for e in self.ENGINES}
        self.sem = {}

    def add_sem(self, name, handle):
        self.sem[name] = [handle, 0]

    def v(self, name):
        return self.sem[name][1]

    def emit(self, eng, fn, waits=(), inc=None, k=1, nodrain=False):
        ws = [(self.sem[n][0], val) for (n, val) in waits if val > 0]
        post = None
        if inc is not None:
            self.sem[inc][1] += k
            post = (self.sem[inc][0], k)

        def run(engobj, fn=fn, ws=ws, post=post, eng=eng, nodrain=nodrain):
            for h, val in ws:
                engobj.wait_ge(h, val)
            inst = fn(engobj)
            if eng == "vector" and not (nodrain and post is None):
                inst = engobj.drain()
            if post is not None:
                inst.then_inc(post[0], post[1])
        self.prog[eng].append(run)

    def run_engine(self, name, engobj):
        for fn in self.prog[name]:
            fn(engobj)


def _consts_a():
    ident_bf = np.eye(128, dtype=ml_dtypes.bfloat16)
    slide = np.zeros((128, 4, 192), dtype=ml_dtypes.bfloat16)
    for c in range(4):
        for p in range(128):
            slide[p, c, 64 + 2 * c + (1 if p >= 64 else 0)] = 1.0 / 64.0
    negb = float(ml_dtypes.bfloat16(NEG))
    seg2 = np.zeros((2, 128), dtype=ml_dtypes.bfloat16)
    seg2[0, 0:64] = 1.0
    seg2[1, 64:128] = 1.0
    segmov = np.zeros((2, 512), dtype=ml_dtypes.bfloat16)
    for col in range(512):
        segmov[(col % 128) // 64, col] = -negb
    negrow = np.full((1, 128), negb, dtype=ml_dtypes.bfloat16)
    return ident_bf, slide.reshape(128, 4 * 192), seg2, segmov, negrow



def build_launch_a(dbg_stage=None):
    nc = bass.Bass("TRN2", target_bir_lowering=False, detect_race_conditions=False)

    x_dr = nc.dram_tensor("x", [NTOK, D], F32, kind="ExternalInput")
    wInT_l_dr = nc.dram_tensor("wInT_l", [D, 3 * D], F32, kind="ExternalInput")
    wOutT_l_dr = nc.dram_tensor("wOutT_l", [D, D], F32, kind="ExternalInput")
    wInT_g_dr = nc.dram_tensor("wInT_g", [D, 3 * D], F32, kind="ExternalInput")
    wOutT_g_dr = nc.dram_tensor("wOutT_g", [D, D], F32, kind="ExternalInput")
    bInG_dr = nc.dram_tensor("bInG_c", [128, 24], F32, kind="ExternalInput")
    bOutL_dr = nc.dram_tensor("bOutL_c", [128, 8], F32, kind="ExternalInput")
    bOutG_dr = nc.dram_tensor("bOutG_c", [128, 8], F32, kind="ExternalInput")
    bInL_row_dr = nc.dram_tensor("bInL_r", [1, 3 * D], F32, kind="ExternalInput")
    bInG_row_dr = nc.dram_tensor("bInG_r", [1, 3 * D], F32, kind="ExternalInput")
    wr_dr = nc.dram_tensor("wr_l", [128, 8, E], F32, kind="ExternalInput")
    br_dr = nc.dram_tensor("br_c", [E, 1], F32, kind="ExternalInput")

    x1_dr = nc.dram_tensor("x1", [NTOK, D], F32, kind="ExternalOutput")
    xln2_dr = nc.dram_tensor("xln2", [NTOK, D], BF16, kind="ExternalOutput")
    sm_dr = nc.dram_tensor("sm", [E, NTOK], F32, kind="ExternalOutput")
    dbg_dr = nc.dram_tensor("dbg", [128, 8192], F32, kind="ExternalOutput") if dbg_stage else None

    identc, slidec, seg2c, segmovc, negrowc = _consts_a()
    ident_dr = nc.inline_tensor(identc, "ident_bf")
    identf_dr = nc.inline_tensor(np.eye(128, dtype=np.float32), "ident_f32")
    slide_dr = nc.inline_tensor(slidec, "slide")
    seg2_dr = nc.inline_tensor(seg2c, "seg2")
    segmov_dr = nc.inline_tensor(segmovc, "segmov")
    negrow_dr = nc.inline_tensor(negrowc, "negrow")
    onesb_dr = nc.inline_tensor(np.ones((128, 128), dtype=ml_dtypes.bfloat16), "onesb")
    ones512_dr = nc.inline_tensor(np.ones((1, 512), dtype=ml_dtypes.bfloat16), "ones512")
    ind8c = np.zeros((8, 512), dtype=ml_dtypes.bfloat16)
    for _s in range(8):
        ind8c[_s, 64 * _s:64 * (_s + 1)] = 1.0
    ind8_dr = nc.inline_tensor(ind8c, "ind8")

    em = EM()

    with ExitStack() as ctx:
        identb = ctx.enter_context(nc.sbuf_tensor("identb", [128, 128], BF16))
        ident_f32_s = ctx.enter_context(nc.sbuf_tensor("ident_f32_s", [128, 128], F32))
        slide_s = ctx.enter_context(nc.sbuf_tensor("slide_s", [128, 4, 192], BF16))
        seg2_s = ctx.enter_context(nc.sbuf_tensor("seg2_s", [2, 128], BF16))
        segmov_s = ctx.enter_context(nc.sbuf_tensor("segmov_s", [2, 512], BF16))
        negrow_s = ctx.enter_context(nc.sbuf_tensor("negrow_s", [1, 128], BF16))
        onesb_s = ctx.enter_context(nc.sbuf_tensor("onesb_s", [128, 128], BF16))
        ones512_s = ctx.enter_context(nc.sbuf_tensor("ones512_s", [1, 512], BF16))
        epsc = ctx.enter_context(nc.sbuf_tensor("epsc", [128, 1], F32))
        bInG_c = ctx.enter_context(nc.sbuf_tensor("bInGc_s", [128, 24], F32))
        bOutSum = ctx.enter_context(nc.sbuf_tensor("bOutSum", [128, 8], F32))
        bOutG_c = ctx.enter_context(nc.sbuf_tensor("bOutGc_s", [128, 8], F32))
        bOutL_c = ctx.enter_context(nc.sbuf_tensor("bOutLc_s", [128, 8], F32))
        bInL_r = ctx.enter_context(nc.sbuf_tensor("bInLr_s", [1, 3 * D], BF16))
        bInG_r = ctx.enter_context(nc.sbuf_tensor("bInGr_s", [1, 3 * D], BF16))
        wr_s = ctx.enter_context(nc.sbuf_tensor("wr_s", [128, 8, E], BF16))
        br_c = ctx.enter_context(nc.sbuf_tensor("brc_s", [E, 1], F32))
        means0 = ctx.enter_context(nc.sbuf_tensor("means0", [64, 1024], BF16))
        means1 = ctx.enter_context(nc.sbuf_tensor("means1", [64, 1024], BF16))
        means_fm = ctx.enter_context(nc.sbuf_tensor("means_fm", [128, 8, 128], BF16))
        globB = ctx.enter_context(nc.sbuf_tensor("globB", [128, 8, 128], F32))
        ln1_bufs = tuple(ctx.enter_context(nc.sbuf_tensor(f"sc1_{i}", [128, 4], F32))
                         for i in range(6))
        ln2_bufs = tuple(ctx.enter_context(nc.sbuf_tensor(f"sc2_{i}", [128, 4], F32))
                         for i in range(6))
        ssum8 = ctx.enter_context(nc.sbuf_tensor("ssum8", [128, 8], F32))
        x_s = ctx.enter_context(nc.sbuf_tensor("x_s", [128, 2, 4, 1024], F32))
        xln_s = ctx.enter_context(nc.sbuf_tensor("xln_s", [128, 4, 1024], BF16))
        xln_fm = ctx.enter_context(nc.sbuf_tensor("xln_fm", [128, 8, 512], BF16))
        qk_sb = ctx.enter_context(nc.sbuf_tensor("qk_sb", [128, 16, 512], BF16))
        v_tm = ctx.enter_context(nc.sbuf_tensor("v_tm", [128, 4, 1024], BF16))
        prob2 = ctx.enter_context(nc.sbuf_tensor("prob2", [128, 2, 8, 128], BF16))
        probT2 = ctx.enter_context(nc.sbuf_tensor("probT2", [128, 2, 8, 128], BF16))
        trash_t = ctx.enter_context(nc.sbuf_tensor("trash_t", [128, 1024], BF16))
        trash = trash_t[:]
        attn_fm = ctx.enter_context(nc.sbuf_tensor("attn_fm", [128, 8, 512], BF16))
        xln2_s = ctx.enter_context(nc.sbuf_tensor("xln2_s", [128, 4, 1024], BF16))
        xln2_fm = ctx.enter_context(nc.sbuf_tensor("xln2_fm", [128, 8, 512], BF16))
        sum_fm = xln2_fm
        logit_em = ctx.enter_context(nc.sbuf_tensor("logit_em", [8, 512], F32))
        sm_tm = ctx.enter_context(nc.sbuf_tensor("sm_tm", [128, 4, 8], F32))
        sm_mx = ctx.enter_context(nc.sbuf_tensor("sm_mx", [128, 4], F32))
        sm_sum = ctx.enter_context(nc.sbuf_tensor("sm_sum", [128, 4], F32))
        sm_em = ctx.enter_context(nc.sbuf_tensor("sm_em", [8, 512], F32))
        qkg = qk_sb[:, 0:4, :].rearrange("p a (b k) -> p (a b) k", k=128)
        vg_tm = qk_sb[:, 4:6, :].rearrange("p a k -> p (a k)")
        vg_pair = qk_sb[0:64, 6:10, :].rearrange("p a k -> p (a k)").rearrange("p (b f) -> p b f", b=2)
        afm_f32 = attn_fm[:].rearrange("p a k -> p (a k)").bitcast(F32)
        sg_f = afm_f32[:, 256:512].rearrange("p (a k) -> p a k", a=4)
        pg_f = afm_f32[:, 512:768].rearrange("p (a k) -> p a k", a=4)
        pg_b = prob2[:, 0, 0:2, :].rearrange("p a k -> p (a k)").rearrange("p (a k) -> p a k", a=4)
        pgT = probT2[0:64, 0, 0:4, :].rearrange("p a k -> p (a k)").rearrange("p (a k) -> p a k", a=4)
        ag_fm = attn_fm[:, :, 0:128]
        globT = ctx.enter_context(nc.sbuf_tensor("globT", [128, 1024], BF16))
        globT8 = ctx.enter_context(nc.sbuf_tensor("globT8", [8, 1024], BF16))
        ind8_s = ctx.enter_context(nc.sbuf_tensor("ind8_s", [8, 512], BF16))
        gmx = ctx.enter_context(nc.sbuf_tensor("gmx", [128, 4], F32))
        gsum = ctx.enter_context(nc.sbuf_tensor("gsum", [128, 4], F32))
        wInTl = ctx.enter_context(nc.sbuf_tensor("wInTl", [128, 8, 3 * D], BF16))
        wOutTl = ctx.enter_context(nc.sbuf_tensor("wOutTl", [128, 8, D], BF16))
        wInTg = wInTl
        wOutTg = wOutTl
        ps_A = ctx.enter_context(nc.psum_tensor("ps_A", [128, 1024], F32))
        ps_B = ctx.enter_context(nc.psum_tensor("ps_B", [128, 1024], F32))
        ps_C = ctx.enter_context(nc.psum_tensor("ps_C", [128, 1024], F32))
        ps_t = ctx.enter_context(nc.psum_tensor("ps_t", [128, 1024], BF16))
        ps_t2 = ctx.enter_context(nc.psum_tensor("ps_t2", [128, 1024], BF16))
        h_ld = ctx.enter_context(nc.semaphore("s_ld"))
        h_x0 = ctx.enter_context(nc.semaphore("s_x0"))
        h_x1 = ctx.enter_context(nc.semaphore("s_x1"))
        h_st = ctx.enter_context(nc.semaphore("s_st"))
        h_gw = ctx.enter_context(nc.semaphore("s_gw"))
        h_sm = ctx.enter_context(nc.semaphore("s_sm"))
        h_ve = ctx.enter_context(nc.semaphore("s_ve"))
        h_ac = ctx.enter_context(nc.semaphore("s_ac"))
        h_pe = ctx.enter_context(nc.semaphore("s_pe"))
        block = ctx.enter_context(nc.Block())
        for nm, hh in (("ld", h_ld), ("x0", h_x0), ("x1", h_x1), ("st", h_st),
                       ("gw", h_gw), ("sm", h_sm), ("ve", h_ve), ("ac", h_ac), ("pe", h_pe)):
            em.add_sem(nm, hh)

        for dst, src in (
            (identb[:], ident_dr.ap()),
            (ident_f32_s[:], identf_dr.ap()),
            (slide_s[:].rearrange("p c k -> p (c k)"), slide_dr.ap()),
            (seg2_s[:], seg2_dr.ap()),
            (segmov_s[:], segmov_dr.ap()),
            (negrow_s[:], negrow_dr.ap()),
            (onesb_s[:], onesb_dr.ap()),
            (ones512_s[:], ones512_dr.ap()),
            (ind8_s[:], ind8_dr.ap()),
            (bInG_c[:], bInG_dr.ap()),
            (bOutG_c[:], bOutG_dr.ap()),
            (bOutL_c[:], bOutL_dr.ap()),
            (br_c[:], br_dr.ap()),
        ):
            em.emit("sync", lambda s, d=dst, a=src: s.dma_start(d, a), inc="ld", k=16)
        v_consts = em.v("ld")

        for dst, src in (
            (bInL_r[:], bInL_row_dr.ap()),
            (bInG_r[:], bInG_row_dr.ap()),
            (wr_s[:], wr_dr.ap()),
            (wInTg[:], wInT_g_dr.ap().rearrange("(c p) f -> p c f", p=128)),
            (wOutTg[:], wOutT_g_dr.ap().rearrange("(c p) f -> p c f", p=128)),
        ):
            em.emit("gpsimd", lambda g, d=dst, a=src: g.dma_start(d, a), inc="gw", k=16)
        v_weights = em.v("gw")

        em.emit("vector", lambda v: v.memset(epsc[:], EPS), inc="ve")
        em.emit("vector",
                lambda v: v.tensor_tensor(bOutSum[:], bOutG_c[:], bOutL_c[:], ALU.add),
                waits=[("ld", v_consts)], inc="ve")
        v_bsum = em.v("ve")

        def cap(stage, ap, extra_waits=()):
            if dbg_stage != stage:
                return
            def _cap_dma(g, ap=ap):
                dst = dbg_dr.ap()[0:ap.shape[0], 0:int(np.prod(ap.shape[1:]))]
                if len(ap.shape) == 3:
                    dst = dst.rearrange("p (a k) -> p a k", a=ap.shape[1])
                return g.dma_start(dst, ap)
            em.emit("gpsimd", _cap_dma,
                    waits=[("ve", em.v("ve")), ("ac", em.v("ac")), ("pe", em.v("pe"))]
                          + list(extra_waits),
                    inc="sm", k=16)

        cap("wg", wInTg[:, 0:2, :].rearrange("p c f -> p (c f)"), [("gw", v_weights)])

        def emit_ln_stats(src4, bufs, first_waits, per_c_waits=None):
            b_s, b_q, b_m, b_v, b_r, b_nmr = bufs
            for c in range(4):
                w0 = list(per_c_waits[c]) if per_c_waits else []
                if c == 0:
                    w0 = list(first_waits) + w0
                em.emit("scalar",
                        lambda s, c=c: s.activation(trash[:], src4[:, c, :], AF.Copy,
                                                    bias=0.0, scale=1.0,
                                                    accum_out=b_s[:, c:c + 1]),
                        waits=w0)
            for c in range(4):
                em.emit("scalar",
                        lambda s, c=c: s.activation(trash[:], src4[:, c, :], AF.Square,
                                                    bias=0.0, scale=1.0,
                                                    accum_out=b_q[:, c:c + 1]),
                        inc=("ac" if c == 3 else None))
            v_stats = em.v("ac")
            em.emit("vector", lambda v: v.tensor_scalar(b_m[:], b_s[:], 1.0 / D, None, ALU.mult),
                    waits=[("ac", v_stats)])
            em.emit("vector", lambda v: v.tensor_scalar(b_v[:], b_q[:], 1.0 / D, None, ALU.mult))
            em.emit("vector", lambda v: v.tensor_tensor(b_r[:], b_m[:], b_m[:], ALU.mult))
            em.emit("vector", lambda v: v.tensor_tensor(b_v[:], b_v[:], b_r[:], ALU.subtract),
                    inc="ve")
            v_var = em.v("ve")
            em.emit("scalar", lambda s: s.activation(b_r[:], b_v[:], AF.Sqrt,
                                                     bias=epsc[:, 0:1], scale=1.0),
                    waits=[("ve", v_var)], inc="ac")
            v_sd = em.v("ac")
            em.emit("vector", lambda v: v.reciprocal(b_r[:], b_r[:]), waits=[("ac", v_sd)])
            em.emit("vector", lambda v: v.tensor_tensor(b_nmr[:], b_m[:], b_r[:], ALU.mult))
            em.emit("vector", lambda v: v.tensor_scalar(b_nmr[:], b_nmr[:], -1.0, None, ALU.mult),
                    inc="ve")
            return em.v("ve")

        def emit_ln_apply(src4, dst4, bufs, v_math, apply_waits=()):
            b_r, b_nmr = bufs[4], bufs[5]
            for c in range(4):
                w0 = []
                if c == 0:
                    w0 = list(apply_waits) + [("ve", v_math)]
                em.emit("vector",
                        lambda v, c=c: v.tensor_scalar(dst4[:, c, :], src4[:, c, :],
                                                       b_r[:, c:c + 1], b_nmr[:, c:c + 1],
                                                       ALU.mult, ALU.add),
                        waits=w0, inc=("ve" if c == 3 else None))
            return em.v("ve")

        def emit_ln(src4, dst4, bufs, first_waits, apply_waits=(), per_c_waits=None):
            v_math = emit_ln_stats(src4, bufs,
                                   list(first_waits) + [("ve", em.v("ve"))], per_c_waits)
            return emit_ln_apply(src4, dst4, bufs, v_math, apply_waits)

        x_load_done = [0] * (2 * NTILE)
        x_cons_done = [None] * (2 * NTILE)

        def emit_x_load(r):
            t = r % NTILE
            w = list(x_cons_done[r - 2]) if (r >= 2 and x_cons_done[r - 2]) else []
            sem = "x0" if r % 2 == 0 else "x1"
            em.emit("sync",
                    lambda s, t=t, r=r: s.dma_start(
                        x_s[:, r % 2], x_dr.ap()[t * 512:(t + 1) * 512, :].rearrange(
                            "(c p) f -> p c f", p=128)),
                    waits=w, inc=sem, k=16)
            x_load_done[r] = (sem, em.v(sem))

        emit_x_load(0)
        emit_x_load(1)
        cap("x0", x_s[:, 0].rearrange("p c f -> p (c f)"), [x_load_done[0]])
        cap("slide", slide_s[:].rearrange("p c k -> p (c k)"), [("ld", v_consts)])

        v_means_pe = 0
        for t in range(NTILE):
            r = t
            xb = x_s[:, r % 2]
            v_apply = emit_ln(xb, xln_s, ln1_bufs, [x_load_done[r]],
                              apply_waits=[("pe", v_means_pe)])
            ti = t % 8
            for c in range(4):
                for u in range(2):
                    em.emit("tensor",
                            lambda tn, c=c, u=u, ti=ti: tn.matmul(
                                ps_C[:, 512 * u:512 * (u + 1)],
                                slide_s[:, c, 64 - 8 * ti:192 - 8 * ti],
                                xln_s[:, c, 512 * u:512 * (u + 1)],
                                start=(ti == 0 and c == 0), stop=(ti == 7 and c == 3)),
                            waits=([("ve", v_apply)] if (c == 0 and u == 0) else []),
                            inc=("pe" if (c == 3 and u == 1) else None))
            v_means_pe = em.v("pe")
            x_cons_done[r] = [("pe", em.v("pe")), ("ve", em.v("ve")), ("ac", em.v("ac"))]
            if ti == 7:
                dstm = means0 if t < 8 else means1
                for u in range(2):
                    em.emit("vector",
                            lambda v, u=u, dstm=dstm: v.tensor_copy(
                                dstm[:, 512 * u:512 * (u + 1)], ps_C[0:64, 512 * u:512 * (u + 1)]),
                            waits=([("pe", em.v("pe"))] if u == 0 else []),
                            inc=("ve" if u == 1 else None))
                if t == 7:
                    cap("means0", means0[:])
            if r + 2 < 2 * NTILE:
                emit_x_load(r + 2)

        for b in range(2):
            srcm = means0 if b == 0 else means1
            for fc in range(8):
                bank = ps_t if fc % 2 == 0 else ps_t2
                em.emit("tensor",
                        lambda tn, srcm=srcm, fc=fc, bank=bank: tn.transpose(
                            bank[:, 0:64], srcm[:, 128 * fc:128 * (fc + 1)],
                            identb[0:64, 0:64]),
                        waits=[("ve", em.v("ve"))], inc="pe")
                em.emit("vector",
                        lambda v, b=b, fc=fc, bank=bank: v.tensor_copy(
                            means_fm[:, fc, 64 * b:64 * (b + 1)], bank[:, 0:64]),
                        waits=[("pe", em.v("pe"))], inc="ve")
        v_mfm = em.v("ve")
        cap("mfm", means_fm[:].rearrange("p a k -> p (a k)"))

        for m in range(16):
            bank = ps_A[:, 0:512] if m % 2 == 0 else ps_B[:, 0:512]
            for k in range(8):
                em.emit("tensor",
                        lambda tn, m=m, k=k, bank=bank: tn.matmul(
                            bank[:, 0:128], wInTg[:, k, 128 * m:128 * (m + 1)],
                            means_fm[:, k, :], start=(k == 0), stop=(k == 7)),
                        waits=(([("ve", v_mfm)] if m == 0 else [("ac", max(0, em.v("ac") - 1))])
                               if k == 0 else []),
                        inc=("pe" if k == 7 else None))
            em.emit("scalar",
                    lambda s, m=m, bank=bank: s.activation(
                        qkg[:, m, :], bank[:, 0:128], AF.Identity,
                        bias=bInG_c[:, m:m + 1], scale=1.0),
                    waits=[("pe", em.v("pe"))], inc="ac")
        v_qkg = em.v("ac")
        cap("qkg", qkg[:].rearrange("p a k -> p (a k)"))

        for n in range(2):
            for k in range(8):
                em.emit("tensor",
                        lambda tn, n=n, k=k: tn.matmul(
                            ps_C[:, 0:512], means_fm[:, k, :],
                            wInTg[:, k, 2048 + 512 * n:2048 + 512 * (n + 1)],
                            start=(k == 0), stop=False),
                        waits=([("ve", em.v("ve")), ("ac", em.v("ac"))] if k == 0 else []))
            em.emit("tensor",
                    lambda tn, n=n: tn.matmul(
                        ps_C[:, 0:512], onesb_s[0:1, 0:128],
                        bInG_r[0:1, 2048 + 512 * n:2048 + 512 * (n + 1)],
                        start=False, stop=True),
                    inc="pe")
            em.emit("vector",
                    lambda v, n=n: v.tensor_copy(
                        vg_tm[:, 512 * n:512 * (n + 1)], ps_C[:, 0:512]),
                    waits=[("pe", em.v("pe"))], inc="ve")
        v_vg = em.v("ve")
        cap("vg", vg_tm[:])

        em.emit("vector", lambda v: v.tensor_copy(vg_pair[:, 0, :], vg_tm[0:64, :]),
                waits=[("ve", v_vg)], inc="ve")
        em.emit("gpsimd", lambda g: g.dma_start(vg_pair[:, 1, :], vg_tm[64:128, :]),
                waits=[("ve", em.v("ve"))], inc="gw", k=16)
        v_vgp_gp = em.v("gw")

        for h in range(4):
            for b in range(2):
                for kc in range(2):
                    em.emit("tensor",
                            lambda tn, h=h, b=b, kc=kc: tn.matmul(
                                ps_A[64 * b:64 * (b + 1), 64 * h:64 * (h + 1)],
                                qkg[:, 2 * h + kc, 64 * b:64 * (b + 1)],
                                qkg[:, 8 + 2 * h + kc, 64 * b:64 * (b + 1)],
                                start=(kc == 0), stop=(kc == 1),
                                tile_position=(0, 64 * b)),
                            waits=([("ac", v_qkg)] if (h == 0 and b == 0 and kc == 0) else []),
                            inc=("pe" if (h == 3 and b == 1 and kc == 1) else None))
        v_sgp = em.v("pe")

        em.emit("vector", lambda v: v.tensor_copy(
            sg_f[:].rearrange("p a k -> p (a k)"), ps_A[:, 0:256]),
            waits=[("pe", v_sgp)], inc="ve")
        em.emit("vector", lambda v: v.reduce_max(gmx[:], sg_f[:], axis=AX.X))
        em.emit("vector", lambda v: v.tensor_tensor(
            pg_f[:], sg_f[:],
            gmx[:].rearrange("p (g o) -> p g o", o=1).broadcast_to([128, 4, 64]),
            ALU.subtract), inc="ve")
        em.emit("scalar", lambda s: s.activation(
            pg_f[:].rearrange("p a k -> p (a k)"), pg_f[:].rearrange("p a k -> p (a k)"),
            AF.Exp, bias=0.0, scale=SCG), waits=[("ve", em.v("ve"))], inc="ac")
        em.emit("vector", lambda v: v.reduce_sum(gsum[:], pg_f[:], axis=AX.X),
                waits=[("ac", em.v("ac"))])
        em.emit("vector", lambda v: v.reciprocal(gsum[:], gsum[:]))
        em.emit("vector", lambda v: v.tensor_tensor(
            pg_b[:], pg_f[:],
            gsum[:].rearrange("p (g o) -> p g o", o=1).broadcast_to([128, 4, 64]),
            ALU.mult), inc="ve")
        v_pgb = em.v("ve")

        for h in range(4):
            em.emit("tensor",
                    lambda tn, h=h: tn.transpose(ps_t[0:64, 128 * h:128 * (h + 1)],
                                                 pg_b[:, h, :], identb[:]),
                    waits=([("ve", v_pgb)] if h == 0 else []),
                    inc=("pe" if h == 3 else None))
        em.emit("vector", lambda v: v.tensor_copy(
            pgT[:].rearrange("p a k -> p (a k)"), ps_t[0:64, 0:512]),
            waits=[("pe", em.v("pe"))], inc="ve")
        v_pgT = em.v("ve")

        for b in range(2):
            bank = ps_B[:, 0:512] if b == 0 else ps_C[:, 0:512]
            for h in range(4):
                for j in range(2):
                    em.emit("tensor",
                            lambda tn, b=b, h=h, j=j, bank=bank: tn.matmul(
                                bank[:, 64 * (2 * h + j):64 * (2 * h + j) + 64],
                                vg_pair[:, b, 128 * (2 * h + j):128 * (2 * h + j) + 128],
                                pgT[:, h, 64 * b:64 * (b + 1)],
                                start=True, stop=True),
                            waits=([("ve", v_pgT), ("gw", v_vgp_gp)]
                                   if (b == 0 and h == 0 and j == 0) else []),
                            inc=("pe" if (h == 3 and j == 1) else None))
            em.emit("vector",
                    lambda v, b=b, bank=bank: v.tensor_copy(
                        ag_fm[:, :, 64 * b:64 * (b + 1)],
                        bank[:, 0:512].rearrange("p (a k) -> p a k", a=8)),
                    waits=[("pe", em.v("pe"))], inc="ve")
        v_ag = em.v("ve")

        for m in range(8):
            bank = ps_A[:, 0:512] if m % 2 == 0 else ps_B[:, 0:512]
            for k in range(8):
                em.emit("tensor",
                        lambda tn, m=m, k=k, bank=bank: tn.matmul(
                            bank[:, 0:128], wOutTg[:, k, 128 * m:128 * (m + 1)],
                            ag_fm[:, k, :], start=(k == 0), stop=(k == 7)),
                        waits=(([("ve", v_ag)] if m == 0 else [("ac", max(0, em.v("ac") - 1))])
                               if k == 0 else []),
                        inc=("pe" if k == 7 else None))
            em.emit("scalar",
                    lambda s, m=m, bank=bank: s.activation(
                        globB[:, m, :], bank[:, 0:128], AF.Identity,
                        bias=bOutSum[:, m:m + 1], scale=1.0),
                    waits=[("pe", em.v("pe")), ("ve", v_bsum)], inc="ac")
        v_glob = em.v("ac")
        cap("glob", globB[:].rearrange("p a k -> p (a k)"))

        for m in range(8):
            em.emit("tensor",
                    lambda tn, m=m: tn.transpose(
                        ps_C[:, 128 * m:128 * (m + 1)], globB[:, m, :], ident_f32_s[:]),
                    waits=([("ac", v_glob), ("ve", v_ag)] if m == 0 else []),
                    inc=("pe" if m == 7 else None))
        em.emit("vector",
                lambda v: v.tensor_copy(globT[:], ps_C[:, 0:1024]),
                waits=[("pe", em.v("pe"))], inc="ve")
        v_globT = em.v("ve")

        em.emit("gpsimd",
                lambda g: g.dma_start(
                    wInTl[:], wInT_l_dr.ap().rearrange("(c p) f -> p c f", p=128)),
                waits=[("pe", em.v("pe")), ("ac", v_glob)], inc="gw", k=16)
        em.emit("gpsimd",
                lambda g: g.dma_start(
                    wOutTl[:], wOutT_l_dr.ap().rearrange("(c p) f -> p c f", p=128)),
                inc="gw", k=16)
        v_wl = em.v("gw")

        store_x1_prev = 0
        store_xln2_prev = 0
        store_sm_prev = 0
        st = {}

        def emit_head(t, v_xcons, v_out_pe, v_out_ac, v_math1):
            r = NTILE + t
            xb = x_s[:, r % 2]
            s = {"xb": xb}
            em.emit("gpsimd",
                    lambda g, t=t: g.dma_start(globT8[:], globT[8 * t:8 * t + 8, :]),
                    waits=[("ve", v_globT)] + ([("pe", v_out_pe)] if v_out_pe else []),
                    inc="gw", k=16)
            s["v_gt8"] = em.v("gw")
            v_apply = emit_ln_apply(xb, xln_s, ln1_bufs, v_math1,
                                    apply_waits=[("pe", v_xcons)])
            if t == 0:
                cap("xln1", xln_s[:].rearrange("p c f -> p (c f)"))
            evT = {0: 0, 1: 0}
            for c in range(4):
                bankT = ps_t if c % 2 == 0 else ps_t2
                for fc in range(8):
                    w0 = []
                    if fc == 0:
                        w0 = ([("ve", v_apply)] if c < 2 else [("ve", evT[c % 2])])
                    em.emit("tensor",
                            lambda tn, c=c, fc=fc, bankT=bankT: tn.transpose(
                                bankT[:, 128 * fc:128 * (fc + 1)],
                                xln_s[:, c, 128 * fc:128 * (fc + 1)], identb[:]),
                            waits=w0, inc=("pe" if fc == 7 else None))
                em.emit("vector",
                        lambda v, c=c, bankT=bankT: v.tensor_copy(
                            xln_fm[:, 0:8, 128 * c:128 * (c + 1)],
                            bankT[:, 0:1024].rearrange("p (a k) -> p a k", a=8)),
                        waits=[("pe", em.v("pe"))], inc="ve")
                evT[c % 2] = em.v("ve")
            v_xfm = em.v("ve")
            evq = {0: None, 1: None}
            for blk in range(8):
                bank = ps_A if blk % 2 == 0 else ps_B
                for mm in range(2):
                    m = 2 * blk + mm
                    half = bank[:, 512 * mm:512 * (mm + 1)]
                    for k in range(8):
                        w0 = []
                        if k == 0 and mm == 0:
                            if blk == 0:
                                w0 = [("ve", v_xfm), ("gw", v_wl)]
                                if v_out_ac:
                                    w0.append(("ac", v_out_ac))
                            elif blk >= 2:
                                w0 = [evq[blk % 2]]
                        em.emit("tensor",
                                lambda tn, m=m, k=k, half=half: tn.matmul(
                                    half, wInTl[:, k, 128 * m:128 * (m + 1)],
                                    xln_fm[:, k, :], start=(k == 0), stop=False),
                                waits=w0)
                    em.emit("tensor",
                            lambda tn, m=m, half=half: tn.matmul(
                                half, bInL_r[0:1, 128 * m:128 * (m + 1)],
                                ones512_s[:], start=False, stop=True),
                            inc=("pe" if mm == 1 else None))
                if blk % 2 == 0:
                    em.emit("scalar",
                            lambda s_, blk=blk, bank=bank: s_.activation(
                                qk_sb[:, 2 * blk:2 * blk + 2, :].rearrange("p a k -> p (a k)"),
                                bank[:, 0:1024], AF.Copy, bias=0.0, scale=1.0),
                            waits=[("pe", em.v("pe"))], inc="ac")
                    evq[0] = ("ac", em.v("ac"))
                else:
                    em.emit("vector",
                            lambda v, blk=blk, bank=bank: v.tensor_copy(
                                qk_sb[:, 2 * blk:2 * blk + 2, :].rearrange("p a k -> p (a k)"),
                                bank[:, 0:1024]),
                            waits=[("pe", em.v("pe"))], inc="ve")
                    evq[1] = ("ve", em.v("ve"))
            s["evq"] = evq
            return s

        v_math1 = emit_ln_stats(x_s[:, 0], ln1_bufs,
                                [x_load_done[NTILE], ("ve", em.v("ve"))])
        st = emit_head(0, em.v("pe"), 0, 0, v_math1)
        for t in range(NTILE):
            r = NTILE + t
            xb = st["xb"]
            b = t // 8
            ti = t % 8
            evq = st["evq"]
            v_gt8 = st["v_gt8"]

            if t + 1 < NTILE:
                v_math1 = emit_ln_stats(x_s[:, (r + 1) % 2], ln1_bufs,
                                        [x_load_done[r + 1], ("ve", v_math1)])

            evv = {0: None, 1: None}
            for c4 in range(4):
                bank = ps_A if c4 % 2 == 0 else ps_B
                for n in range(2):
                    half = bank[:, 512 * n:512 * (n + 1)]
                    for k in range(8):
                        w0 = []
                        if k == 0 and n == 0:
                            if c4 < 2:
                                w0 = [evq[c4 % 2]]
                            else:
                                w0 = [evv[c4 % 2]]
                        em.emit("tensor",
                                lambda tn, c4=c4, n=n, k=k, half=half: tn.matmul(
                                    half, xln_fm[:, k, 128 * c4:128 * (c4 + 1)],
                                    wInTl[:, k, 2048 + 512 * n:2048 + 512 * (n + 1)],
                                    start=(k == 0), stop=False),
                                waits=w0)
                    em.emit("tensor",
                            lambda tn, n=n, half=half: tn.matmul(
                                half, onesb_s[0:1, 0:128],
                                bInL_r[0:1, 2048 + 512 * n:2048 + 512 * (n + 1)],
                                start=False, stop=True),
                            inc=("pe" if n == 1 else None))
                if c4 % 2 == 0:
                    em.emit("scalar",
                            lambda s_, c4=c4, bank=bank: s_.activation(
                                v_tm[:, c4, :], bank[:, 0:1024],
                                AF.Copy, bias=0.0, scale=1.0),
                            waits=[("pe", em.v("pe"))], inc="ac")
                    evv[0] = ("ac", em.v("ac"))
                else:
                    em.emit("vector",
                            lambda v, c4=c4, bank=bank: v.tensor_copy(
                                v_tm[:, c4, :], bank[:, 0:1024]),
                            waits=[("pe", em.v("pe"))], inc="ve")
                    evv[1] = ("ve", em.v("ve"))
            v_xcons = em.v("pe")

            v_exp = [0] * 4
            v_pT = [0] * 4
            v_pTev = [0] * 4
            v_av = [0] * 4
            v_attnev = [0] * 4
            v_pb = [0] * 4
            v_sc = [0] * 4

            def emit_scores(c, evv=evv, v_exp=v_exp, v_sc=v_sc):
                par = c % 2
                bank = ps_A if par == 0 else ps_B
                for h in range(8):
                    w0 = []
                    if h == 0:
                        w0 = [evv[par]] if c < 2 else [("ac", v_exp[c - 2])]
                    em.emit("tensor",
                            lambda tn, c=c, h=h, bank=bank: tn.matmul(
                                bank[:, 128 * h:128 * (h + 1)],
                                qk_sb[:, h, 128 * c:128 * (c + 1)],
                                qk_sb[:, 8 + h, 128 * c:128 * (c + 1)],
                                start=(h % 4 == 0), stop=False),
                            waits=w0)
                for hf in range(2):
                    half = bank[:, 512 * hf:512 * (hf + 1)]
                    em.emit("tensor",
                            lambda tn, half=half: tn.matmul(
                                half, negrow_s[0:1, :], ones512_s[:],
                                start=False, stop=False))
                    em.emit("tensor",
                            lambda tn, half=half: tn.matmul(
                                half, seg2_s[:], segmov_s[:, 0:512],
                                start=False, stop=True),
                            inc=("pe" if hf == 1 else None))
                v_sc[c] = em.v("pe")

            emit_scores(0)
            emit_scores(1)
            for c in range(4):
                par = c % 2
                bank = ps_A if par == 0 else ps_B
                bankT = ps_t if par == 0 else ps_t2
                em.emit("scalar",
                        lambda s_, par=par, bank=bank: s_.activation(
                            prob2[:, par].rearrange("p a k -> p (a k)"),
                            bank[:, 0:1024], AF.Exp, bias=0.0, scale=SCL),
                        waits=[("pe", v_sc[c])], inc="ac")
                v_exp[c] = em.v("ac")
                em.emit("vector",
                        lambda v, par=par: v.reduce_sum(ssum8[:], prob2[:, par], axis=AX.X),
                        waits=[("ac", v_exp[c])])
                em.emit("vector", lambda v: v.reciprocal(ssum8[:], ssum8[:]))
                em.emit("vector",
                        lambda v, par=par: v.tensor_tensor(
                            prob2[:, par], prob2[:, par],
                            ssum8[:].rearrange("p (g o) -> p g o", o=1)
                            .broadcast_to([128, 8, 128]),
                            ALU.mult),
                        inc="ve")
                v_pb[c] = em.v("ve")
                for h in range(8):
                    w0 = [("ve", v_pb[c])] if h == 0 else []
                    em.emit("tensor",
                            lambda tn, par=par, h=h, bankT=bankT: tn.transpose(
                                bankT[:, 128 * h:128 * (h + 1)],
                                prob2[:, par, h, :], identb[:]),
                            waits=w0, inc=("pe" if h == 7 else None))
                v_pT[c] = em.v("pe")
                if c + 2 < 4:
                    emit_scores(c + 2)
                em.emit("vector",
                        lambda v, par=par, bankT=bankT: v.tensor_copy(
                            probT2[:, par].rearrange("p a k -> p (a k)"),
                            bankT[:, 0:1024]),
                        waits=[("pe", v_pT[c])], inc="ve")
                v_pTev[c] = em.v("ve")
                for h in range(8):
                    w0 = []
                    if h == 0:
                        w0 = [("ve", v_pTev[c])]
                        if c >= 1:
                            w0.append(("ve", v_attnev[c - 1]))
                        elif t > 0:
                            w0.append(("ve", st["v_attnev_prev"]))
                        else:
                            w0.append(("ve", v_ag))
                    em.emit("tensor",
                            lambda tn, c=c, par=par, h=h: tn.matmul(
                                ps_C[:, 128 * h:128 * (h + 1)],
                                v_tm[:, c, 128 * h:128 * (h + 1)],
                                probT2[:, par, h, :], start=True, stop=True),
                            waits=w0, inc=("pe" if h == 7 else None))
                v_av[c] = em.v("pe")
                em.emit("vector",
                        lambda v, c=c: v.tensor_copy(
                            attn_fm[:, 0:8, 128 * c:128 * (c + 1)],
                            ps_C[:, 0:1024].rearrange("p (a k) -> p a k", a=8)),
                        waits=[("pe", v_av[c])], inc="ve")
                v_attnev[c] = em.v("ve")
            st["v_attnev_prev"] = v_attnev[3]

            evo = {0: None, 1: None}
            for pr in range(4):
                bank = ps_A if pr % 2 == 0 else ps_B
                for mm in range(2):
                    m = 2 * pr + mm
                    half = bank[:, 512 * mm:512 * (mm + 1)]
                    for k in range(8):
                        w0 = []
                        if k == 0 and mm == 0:
                            if pr == 0:
                                w0 = [("ve", v_attnev[3]), ("ac", v_exp[2]),
                                      ("gw", v_gt8)]
                            elif pr == 1:
                                w0 = [("ac", v_exp[3])]
                            else:
                                w0 = [evo[pr % 2]]
                        em.emit("tensor",
                                lambda tn, m=m, k=k, half=half: tn.matmul(
                                    half, wOutTl[:, k, 128 * m:128 * (m + 1)],
                                    attn_fm[:, k, :], start=(k == 0), stop=False),
                                waits=w0)
                    em.emit("tensor",
                            lambda tn, m=m, half=half: tn.matmul(
                                half, globT8[0:8, 128 * m:128 * (m + 1)],
                                ind8_s[:], start=False, stop=True),
                            inc=("pe" if mm == 1 else None))
                if pr % 2 == 0:
                    em.emit("scalar",
                            lambda s_, pr=pr, bank=bank: s_.activation(
                                sum_fm[:, 2 * pr:2 * pr + 2, :].rearrange("p a k -> p (a k)"),
                                bank[:, 0:1024], AF.Copy, bias=0.0, scale=1.0),
                            waits=[("pe", em.v("pe"))], inc="ac")
                    evo[0] = ("ac", em.v("ac"))
                else:
                    em.emit("vector",
                            lambda v, pr=pr, bank=bank: v.tensor_copy(
                                sum_fm[:, 2 * pr:2 * pr + 2, :].rearrange("p a k -> p (a k)"),
                                bank[:, 0:1024]),
                            waits=[("pe", em.v("pe"))], inc="ve")
                    evo[1] = ("ve", em.v("ve"))
            v_out_pe = em.v("pe")
            v_sum_ve = em.v("ve")
            v_sum_ac = em.v("ac")

            evT2 = {0: 0, 1: 0}
            resid_ve = [0] * 4
            for c in range(4):
                bankT = ps_t if c % 2 == 0 else ps_t2
                for fc in range(8):
                    w0 = []
                    if fc == 0:
                        if c == 0:
                            w0 = [("ve", v_sum_ve), ("ac", v_sum_ac), ("ve", v_pTev[3])]
                        elif c == 1:
                            w0 = [("ve", v_pTev[3])]
                        else:
                            w0 = [("ve", evT2[c % 2])]
                    em.emit("tensor",
                            lambda tn, c=c, fc=fc, bankT=bankT: tn.transpose(
                                bankT[:, 128 * fc:128 * (fc + 1)],
                                sum_fm[:, fc, 128 * c:128 * (c + 1)], identb[:]),
                            waits=w0, inc=("pe" if fc == 7 else None))
                em.emit("vector",
                        lambda v, c=c, bankT=bankT, xb=xb: v.tensor_tensor(
                            xb[:, c, :], xb[:, c, :], bankT[:, 0:1024], ALU.add),
                        waits=[("pe", em.v("pe"))], inc="ve")
                evT2[c % 2] = em.v("ve")
                resid_ve[c] = em.v("ve")
            v_resid = em.v("ve")

            v_math2 = emit_ln_stats(xb, ln2_bufs, [],
                                    per_c_waits=[[("ve", resid_ve[c])] for c in range(4)])

            em.emit("sync",
                    lambda s_, t=t, xb=xb: s_.dma_start(
                        x1_dr.ap()[t * 512:(t + 1) * 512, :].rearrange("(c p) f -> p c f", p=128),
                        xb[:]),
                    waits=[("ve", v_resid)], inc="st", k=16)
            store_x1_prev = em.v("st")

            if t + 1 < NTILE:
                st = emit_head(t + 1, v_xcons, v_out_pe, v_sum_ac, v_math1)
                st["v_attnev_prev"] = v_attnev[3]

            v_ln2 = emit_ln_apply(xb, xln2_s, ln2_bufs, v_math2,
                                  apply_waits=[("st", store_xln2_prev)])
            if t == 0:
                cap("x1t0", xb[:].rearrange("p c f -> p (c f)"))
            x_cons_done[r] = [("ve", em.v("ve")), ("ac", em.v("ac")), ("st", store_x1_prev)]
            if r + 2 < 2 * NTILE:
                emit_x_load(r + 2)
            em.emit("sync",
                    lambda s_, t=t: s_.dma_start(
                        xln2_dr.ap()[t * 512:(t + 1) * 512, :].rearrange("(c p) f -> p c f", p=128),
                        xln2_s[:]),
                    waits=[("ve", v_ln2)], inc="st", k=16)
            store_xln2_prev = em.v("st")

            evT3 = {0: 0, 1: 0}
            for c in range(4):
                bankT = ps_t if c % 2 == 0 else ps_t2
                for fc in range(8):
                    w0 = []
                    if fc == 0:
                        w0 = ([("ve", v_ln2)] if c < 2 else [("ve", evT3[c % 2])])
                    em.emit("tensor",
                            lambda tn, c=c, fc=fc, bankT=bankT: tn.transpose(
                                bankT[:, 128 * fc:128 * (fc + 1)],
                                xln2_s[:, c, 128 * fc:128 * (fc + 1)], identb[:]),
                            waits=w0, inc=("pe" if fc == 7 else None))
                em.emit("vector",
                        lambda v, c=c, bankT=bankT: v.tensor_copy(
                            xln2_fm[:, 0:8, 128 * c:128 * (c + 1)],
                            bankT[:, 0:1024].rearrange("p (a k) -> p a k", a=8)),
                        waits=[("pe", em.v("pe"))], inc="ve")
                evT3[c % 2] = em.v("ve")
            v_x2fm = em.v("ve")

            for k in range(8):
                em.emit("tensor",
                        lambda tn, k=k: tn.matmul(
                            ps_C[0:8, 0:512], wr_s[:, k, :], xln2_fm[:, k, :],
                            start=(k == 0), stop=(k == 7)),
                        waits=([("ve", v_x2fm)] if k == 0 else []),
                        inc=("pe" if k == 7 else None))
            em.emit("vector",
                    lambda v: v.tensor_scalar(logit_em[:], ps_C[0:8, 0:512],
                                              br_c[:, 0:1], None, ALU.add),
                    waits=[("pe", em.v("pe"))], inc="ve")
            v_lg = em.v("ve")
            for c in range(4):
                em.emit("tensor",
                        lambda tn, c=c: tn.transpose(
                            ps_C[:, 512 + 8 * c:512 + 8 * (c + 1)],
                            logit_em[:, 128 * c:128 * (c + 1)], ident_f32_s[0:8, 0:8]),
                        waits=([("ve", v_lg)] if c == 0 else []),
                        inc=("pe" if c == 3 else None))
            v_lgt = em.v("pe")
            em.emit("vector", lambda v: v.tensor_copy(
                sm_tm[:].rearrange("p a e -> p (a e)"), ps_C[:, 512:544]),
                waits=[("pe", v_lgt)])
            em.emit("vector", lambda v: v.reduce_max(sm_mx[:], sm_tm[:], axis=AX.X))
            em.emit("vector", lambda v: v.tensor_tensor(
                sm_tm[:], sm_tm[:],
                sm_mx[:].rearrange("p (g o) -> p g o", o=1).broadcast_to([128, 4, 8]),
                ALU.subtract), inc="ve")
            em.emit("scalar", lambda s_: s_.activation(
                sm_tm[:].rearrange("p a e -> p (a e)"), sm_tm[:].rearrange("p a e -> p (a e)"),
                AF.Exp, bias=0.0, scale=1.0), waits=[("ve", em.v("ve"))], inc="ac")
            em.emit("vector", lambda v: v.reduce_sum(sm_sum[:], sm_tm[:], axis=AX.X),
                    waits=[("ac", em.v("ac"))])
            em.emit("vector", lambda v: v.reciprocal(sm_sum[:], sm_sum[:]))
            em.emit("vector", lambda v: v.tensor_tensor(
                sm_tm[:], sm_tm[:],
                sm_sum[:].rearrange("p (g o) -> p g o", o=1).broadcast_to([128, 4, 8]),
                ALU.mult), inc="ve")
            v_smtm = em.v("ve")
            for c in range(4):
                em.emit("tensor",
                        lambda tn, c=c: tn.transpose(
                            ps_C[0:8, 128 * c:128 * (c + 1)], sm_tm[:, c, :],
                            ident_f32_s[:]),
                        waits=([("ve", v_smtm)] if c == 0 else []),
                        inc=("pe" if c == 3 else None))
            em.emit("vector", lambda v: v.tensor_copy(sm_em[:], ps_C[0:8, 0:512]),
                    waits=[("pe", em.v("pe")), ("sm", store_sm_prev)], inc="ve")
            em.emit("gpsimd",
                    lambda g, t=t: g.dma_start(
                        sm_dr.ap()[:, t * 512:(t + 1) * 512], sm_em[:]),
                    waits=[("ve", em.v("ve"))], inc="sm", k=16)
            store_sm_prev = em.v("sm")


        v_all_st = em.v("st")
        v_all_sm = em.v("sm")

        @block.sync
        def _(sync):
            em.run_engine("sync", sync)

        @block.gpsimd
        def _(gpsimd):
            em.run_engine("gpsimd", gpsimd)
            gpsimd.wait_ge(h_st, v_all_st)
            gpsimd.wait_ge(h_sm, v_all_sm)

        @block.vector
        def _(vector):
            em.run_engine("vector", vector)

        @block.scalar
        def _(scalar):
            em.run_engine("scalar", scalar)

        @block.tensor
        def _(tensor):
            em.run_engine("tensor", tensor)

    return nc



def _consts_b():
    ident_bf = np.eye(128, dtype=ml_dtypes.bfloat16)
    tril = np.zeros((128, 128), dtype=ml_dtypes.bfloat16)
    for p in range(128):
        tril[p, p + 1:] = 1.0
    diagbig = (np.eye(128) * BIG).astype(ml_dtypes.bfloat16)
    onesb = np.ones((128, 128), dtype=ml_dtypes.bfloat16)
    onesf = np.ones((128, 128), dtype=np.float32)
    t = np.arange(NTOK)
    ids = np.stack([t >> 6, t & 63], axis=1).astype(ml_dtypes.bfloat16)
    ids = ids.reshape(64, 128, 2).transpose(1, 0, 2).copy()
    s = np.arange(CAP)
    im = np.zeros((CAP, 8), dtype=ml_dtypes.bfloat16)
    im[:, 0] = ((NTOK + (s % 128)) >> 6).astype(ml_dtypes.bfloat16)
    im[:, 1] = ((NTOK + (s % 128)) & 63).astype(ml_dtypes.bfloat16)
    return ident_bf, tril, diagbig, onesb, onesf, ids, im


def build_launch_b(dbg_stage=None):
    nc = bass.Bass("TRN2", target_bir_lowering=False, detect_race_conditions=False)

    x1_dr = nc.dram_tensor("x1", [NTOK, D], F32, kind="ExternalInput")
    xln2_dr = nc.dram_tensor("xln2", [NTOK, D], BF16, kind="ExternalInput")
    smg_dr = nc.dram_tensor("smg", [E, 65536], F32, kind="ExternalInput")
    smtm_dr = nc.dram_tensor("smtm", [NTOK, E], F32, kind="ExternalInput")
    wu_dr = nc.dram_tensor("wu", [4, D, FP], F32, kind="ExternalInput")
    buc_dr = nc.dram_tensor("bu_cols", [128, 4 * NFC], F32, kind="ExternalInput")
    wd_dr = nc.dram_tensor("wd", [E, FP, D], F32, kind="ExternalInput")
    bd_dr = nc.dram_tensor("bd", [E, D], F32, kind="ExternalInput")

    y_dr = nc.dram_tensor("y", [YROWS, D], F32, kind="ExternalOutput")
    dbg_dr = nc.dram_tensor("dbg", [128, 8192], F32, kind="ExternalOutput") if dbg_stage else None

    meta_dr = nc.dram_tensor("meta_scr", [E, CAP, 8], BF16, kind="Internal")

    identc, trilc, diagc, onesbc, onesfc, idsc, imc = _consts_b()
    ident_dr = nc.inline_tensor(identc, "ident_bf")
    tril_dr = nc.inline_tensor(trilc, "trilx")
    diag_dr = nc.inline_tensor(diagc, "diagbig")
    onesb_dr = nc.inline_tensor(onesbc, "onesb")
    onesf_dr = nc.inline_tensor(onesfc, "onesf")
    ids_dr = nc.inline_tensor(idsc.reshape(128, 128), "ids")
    im_dr = nc.inline_tensor(imc, "initmeta")

    em = EM()

    with ExitStack() as ctx:
        identb = ctx.enter_context(nc.sbuf_tensor("identb", [128, 128], BF16))
        tril_s = ctx.enter_context(nc.sbuf_tensor("tril_s", [128, 128], BF16))
        diag_s = ctx.enter_context(nc.sbuf_tensor("diag_s", [128, 128], BF16))
        onesb_s = ctx.enter_context(nc.sbuf_tensor("onesb_s", [128, 128], BF16))
        onesf_s = ctx.enter_context(nc.sbuf_tensor("onesf_s", [128, 128], F32))
        ids_s = ctx.enter_context(nc.sbuf_tensor("ids_s", [128, 64, 2], BF16))
        im_s = ctx.enter_context(nc.sbuf_tensor("im_s", [128, 10, 8], BF16))
        smtm_s = ctx.enter_context(nc.sbuf_tensor("smtm_s", [128, 64, 8], F32))
        lo_s = ctx.enter_context(nc.sbuf_tensor("lo_s", [128, 8], F32))
        hi_s = ctx.enter_context(nc.sbuf_tensor("hi_s", [128, 8], F32))
        mid_s = ctx.enter_context(nc.sbuf_tensor("mid_s", [128, 8], F32))
        cnt_s = ctx.enter_context(nc.sbuf_tensor("cnt_s", [128, 8], F32))
        flag_s = ctx.enter_context(nc.sbuf_tensor("flag_s", [128, 8], F32))
        tmp1_s = ctx.enter_context(nc.sbuf_tensor("tmp1_s", [128, 8], F32))
        tmp2_s = ctx.enter_context(nc.sbuf_tensor("tmp2_s", [128, 8], F32))
        trash_b = ctx.enter_context(nc.sbuf_tensor("trash_b", [128, 512], BF16))
        m_all = ctx.enter_context(nc.sbuf_tensor("m_all", [128, 64, 8], BF16))
        minv_all = ctx.enter_context(nc.sbuf_tensor("minv_all", [128, 64, 8], BF16))
        pos_all = ctx.enter_context(nc.sbuf_tensor("pos_all", [128, 64, 8], U32))
        meta_all = ctx.enter_context(nc.sbuf_tensor("meta_all", [128, 64, 8, 8], BF16))
        trow = ctx.enter_context(nc.sbuf_tensor("trow", [1, 512], F32))
        trow2 = ctx.enter_context(nc.sbuf_tensor("trow2", [1, 512], F32))
        offs = ctx.enter_context(nc.sbuf_tensor("offs", [1, 512], F32))
        meta_ld = ctx.enter_context(nc.sbuf_tensor("meta_ld", [128, 10, 8], BF16))
        idx_f = ctx.enter_context(nc.sbuf_tensor("idx_f", [128, 10], F32))
        idx_u = ctx.enter_context(nc.sbuf_tensor("idx_u", [128, 10], U32))
        gates = ctx.enter_context(nc.sbuf_tensor("gates", [128, 10], F32))
        bu_c = ctx.enter_context(nc.sbuf_tensor("bu_c", [128, 4 * NFC], F32))
        bd_r = ctx.enter_context(nc.sbuf_tensor("bd_r", [1, 1024], BF16))
        bdb = ctx.enter_context(nc.sbuf_tensor("bdb", [128, 1024], F32))
        wu_s = ctx.enter_context(nc.sbuf_tensor("wu_s", [128, 8, FP], BF16))
        wd_s = ctx.enter_context(nc.sbuf_tensor("wd_s", [128, NFC, 1024], BF16))
        xg_tm = ctx.enter_context(nc.sbuf_tensor("xg_tm", [128, 2, 1024], BF16))
        xg_fm = ctx.enter_context(nc.sbuf_tensor("xg_fm", [128, 8, CAP], BF16))
        mid_arena = ctx.enter_context(nc.sbuf_tensor("mid_arena", [128, NFC, CAP], BF16))
        contrib = ctx.enter_context(nc.sbuf_tensor("contrib", [128, 2, 1024], F32))
        ps_a = ctx.enter_context(nc.psum_tensor("ps_a", [128, 512], F32))
        ps_b = ctx.enter_context(nc.psum_tensor("ps_b", [128, 512], F32))
        ps_c = ctx.enter_context(nc.psum_tensor("ps_c", [128, 512], F32))
        ps_d = ctx.enter_context(nc.psum_tensor("ps_d", [128, 512], F32))
        ps_t = ctx.enter_context(nc.psum_tensor("ps_t", [128, 512], BF16))
        ps_t2 = ctx.enter_context(nc.psum_tensor("ps_t2", [128, 512], BF16))
        ps_x = ctx.enter_context(nc.psum_tensor("ps_x", [128, 8], F32))
        h_ld = ctx.enter_context(nc.semaphore("s_ld"))
        h_mi = ctx.enter_context(nc.semaphore("s_mi"))
        h_ml = ctx.enter_context(nc.semaphore("s_ml"))
        h_w = ctx.enter_context(nc.semaphore("s_w"))
        h_gx0 = ctx.enter_context(nc.semaphore("s_gx0"))
        h_gx1 = ctx.enter_context(nc.semaphore("s_gx1"))
        h_ys0 = ctx.enter_context(nc.semaphore("s_ys0"))
        h_ys1 = ctx.enter_context(nc.semaphore("s_ys1"))
        h_sc = [ctx.enter_context(nc.semaphore(f"s_sc{i}")) for i in range(E)]
        h_ve = ctx.enter_context(nc.semaphore("s_ve"))
        h_ac = ctx.enter_context(nc.semaphore("s_ac"))
        h_pe = ctx.enter_context(nc.semaphore("s_pe"))
        block = ctx.enter_context(nc.Block())
        for nm, hh in (("ld", h_ld), ("mi", h_mi), ("ml", h_ml), ("w", h_w),
                       ("gx0", h_gx0), ("gx1", h_gx1), ("ys0", h_ys0), ("ys1", h_ys1),
                       ("ve", h_ve), ("ac", h_ac), ("pe", h_pe)):
            em.add_sem(nm, hh)
        for i in range(E):
            em.add_sem(f"sc{i}", h_sc[i])

        smg_v = mid_arena[:, 0:8, 0:1024].bitcast(F32)

        regs = {}

        def _mkregs(g):
            regs["cap"] = g.to_reg(CAP - 1)
            regs["ntok"] = g.to_reg(NTOK - 1)

        em.emit("gpsimd", _mkregs)

        def gp_dma(fn, sem, waits=()):
            em.emit("gpsimd", fn, waits=list(waits), inc=sem, k=16)
            return em.v(sem)

        def cap(stage, ap, extra_waits=()):
            if dbg_stage != stage:
                return
            def _cap_dma(g, ap=ap):
                dst = dbg_dr.ap()[0:ap.shape[0], 0:int(np.prod(ap.shape[1:]))]
                if len(ap.shape) == 3:
                    dst = dst.rearrange("p (a k) -> p a k", a=ap.shape[1])
                return g.dma_start(dst, ap)
            em.emit("gpsimd", _cap_dma,
                    waits=[("ve", em.v("ve")), ("ac", em.v("ac")), ("pe", em.v("pe"))]
                          + list(extra_waits),
                    inc="mi", k=16)

        for dst, src in (
            (identb[:], ident_dr.ap()),
            (tril_s[:], tril_dr.ap()),
            (diag_s[:], diag_dr.ap()),
            (onesb_s[:], onesb_dr.ap()),
            (onesf_s[:], onesf_dr.ap()),
            (ids_s[:].rearrange("p a b -> p (a b)"), ids_dr.ap()),
            (bu_c[:], buc_dr.ap()),
            (smtm_s[:], smtm_dr.ap().rearrange("(j p) e -> p j e", p=128)),
        ):
            em.emit("sync", lambda s, d=dst, a=src: s.dma_start(d, a), inc="ld", k=16)
        for e in range(E):
            em.emit("sync",
                    lambda s, e=e: s.dma_start(
                        smg_v[:, e, :], smg_dr.ap()[e, :].rearrange("(p f) -> p f", p=128)),
                    inc="ld", k=16)
        v_smg = em.v("ld")

        for q in range(4):
            em.emit("sync",
                    lambda s, q=q: s.dma_start(
                        y_dr.ap()[q * 2048:(q + 1) * 2048, :],
                        x1_dr.ap()[q * 2048:(q + 1) * 2048, :]),
                    inc="ld", k=16)
        v_yinit = em.v("ld")

        v_ims = gp_dma(
            lambda g: g.dma_start(im_s[:], im_dr.ap().rearrange("(m p) c -> p m c", p=128)),
            "mi")
        v_meta_init = 0
        for e in range(E):
            v_meta_init = gp_dma(
                lambda g, e=e: g.dma_start(
                    meta_dr.ap()[e].rearrange("(m p) c -> p m c", p=128), im_s[:]),
                "mi", waits=[("mi", v_ims)])

        v_wu = [0] * 4
        v_wd = [0] * E
        v_wu[0] = gp_dma(lambda g: g.dma_start(
            wu_s[:], wu_dr.ap()[0].rearrange("(c p) f -> p c f", p=128)), "w")
        v_wd[0] = gp_dma(lambda g: g.dma_start(
            wd_s[:], wd_dr.ap()[0].rearrange("(c p) f -> p c f", p=128)), "w")

        em.emit("vector", lambda v: v.memset(xg_tm[:].rearrange("p a b -> p (a b)"), 0.0))
        em.emit("vector", lambda v: v.memset(xg_fm[:].rearrange("p a b -> p (a b)"), 0.0))
        em.emit("vector", lambda v: v.memset(lo_s[:], 0.0))
        em.emit("vector", lambda v: v.memset(hi_s[:], 1.0), inc="ve")
        v_binit = em.v("ve")

        for it in range(BISECT_ITERS):
            em.emit("vector", lambda v: v.tensor_tensor(mid_s[:], lo_s[:], hi_s[:], ALU.add),
                    waits=([("ld", v_smg), ("ve", v_binit)] if it == 0 else []))
            em.emit("vector", lambda v: v.tensor_scalar(mid_s[:], mid_s[:], 0.5, None, ALU.mult),
                    inc="ve")
            v_mid = em.v("ve")
            cnt3 = meta_all[:].rearrange("p j e c -> p (j e c)").rearrange(
                "p (e k) -> p e k", e=E)
            em.emit("vector",
                    lambda v: v.tensor_tensor(
                        cnt3, smg_v[:],
                        mid_s[:].rearrange("p (e o) -> p e o", o=1).broadcast_to([128, E, 512]),
                        ALU.is_ge))
            em.emit("vector",
                    lambda v: v.reduce_sum(cnt_s[:], cnt3, axis=AX.X), inc="ve")
            v_cnt = em.v("ve")
            em.emit("tensor",
                    lambda tn: tn.matmul(ps_x[:], onesf_s[:], cnt_s[:], start=True, stop=True),
                    waits=[("ve", v_cnt)], inc="pe")
            v_cntmm = em.v("pe")
            em.emit("vector",
                    lambda v: v.tensor_scalar(flag_s[:], ps_x[:], float(65536 // E), 0.0,
                                              ALU.is_ge, ALU.add),
                    waits=[("pe", v_cntmm)])
            em.emit("vector", lambda v: v.tensor_tensor(tmp1_s[:], mid_s[:], lo_s[:], ALU.subtract))
            em.emit("vector", lambda v: v.tensor_tensor(tmp1_s[:], tmp1_s[:], flag_s[:], ALU.mult))
            em.emit("vector", lambda v: v.tensor_tensor(lo_s[:], lo_s[:], tmp1_s[:], ALU.add))
            em.emit("vector", lambda v: v.tensor_tensor(tmp2_s[:], hi_s[:], mid_s[:], ALU.subtract))
            em.emit("vector", lambda v: v.tensor_tensor(tmp2_s[:], tmp2_s[:], flag_s[:], ALU.mult))
            em.emit("vector", lambda v: v.tensor_tensor(hi_s[:], mid_s[:], tmp2_s[:], ALU.add),
                    inc="ve")

        v_theta = em.v("ve")
        cap("theta", lo_s[:])
        cap("smgchk", smg_v[:, 0:8, :], [("ld", v_smg)])

        em.emit("vector",
                lambda v: v.tensor_tensor(
                    m_all[:], smtm_s[:],
                    lo_s[:].rearrange("p (o e) -> p o e", o=1).broadcast_to([128, 64, E]),
                    ALU.is_ge),
                waits=[("ve", v_theta)])
        em.emit("vector",
                lambda v: v.tensor_scalar(minv_all[:], m_all[:], -1.0, 1.0,
                                          ALU.mult, ALU.add),
                inc="ve")
        em.emit("tensor",
                lambda tn: tn.matmul(ps_a[0:64, 0:512], onesb_s[:, 0:64],
                                     m_all[:].rearrange("p j e -> p (j e)"),
                                     start=True, stop=True),
                waits=[("ve", em.v("ve"))], inc="pe")
        v_tot = em.v("pe")
        if dbg_stage == "tots":
            em.emit("vector", lambda v: v.tensor_copy(trash_b[0:64, :], ps_a[0:64, 0:512]),
                    waits=[("pe", v_tot)], inc="ve")
            cap("tots", trash_b[0:64, :])
        em.emit("vector", lambda v: v.tensor_copy(trow[:], ps_a[0:1, 0:512]),
                waits=[("pe", v_tot)], inc="ve")
        cap("trow", trow[:])

        cur, nxt = trow, trow2
        for st in (8, 16, 32, 64, 128, 256):
            em.emit("vector", lambda v, cur=cur, nxt=nxt, st=st: v.tensor_copy(
                nxt[:, 0:st], cur[:, 0:st]))
            em.emit("vector", lambda v, cur=cur, nxt=nxt, st=st: v.tensor_tensor(
                nxt[:, st:512], cur[:, st:512], cur[:, 0:512 - st], ALU.add), inc="ve")
            cur, nxt = nxt, cur
        em.emit("vector", lambda v: v.memset(offs[:, 0:8], 0.0))
        em.emit("vector", lambda v, cur=cur: v.tensor_copy(offs[:, 8:512], cur[:, 0:504]),
                inc="ve")
        v_offs = em.v("ve")
        cap("m0", m_all[:])
        cap("offs", offs[:])

        em.emit("tensor",
                lambda tn: tn.matmul(ps_b[:, 0:512], tril_s[:],
                                     m_all[:].rearrange("p j e -> p (j e)"),
                                     start=True, stop=False),
                waits=[("ve", v_offs)])
        em.emit("tensor",
                lambda tn: tn.matmul(ps_b[:, 0:512], onesf_s[0:1, :], offs[:],
                                     start=False, stop=False))
        em.emit("tensor",
                lambda tn: tn.matmul(ps_b[:, 0:512], diag_s[:],
                                     minv_all[:].rearrange("p j e -> p (j e)"),
                                     start=False, stop=True), inc="pe")
        v_pos = em.v("pe")
        em.emit("vector",
                lambda v: v.tensor_copy(pos_all[:].rearrange("p j e -> p (j e)"),
                                        ps_b[:, 0:512]),
                waits=[("pe", v_pos)])
        em.emit("vector",
                lambda v: v.tensor_copy(
                    meta_all[:, :, :, 0],
                    ids_s[:, :, 0:1].broadcast_to([128, 64, 8])))
        em.emit("vector",
                lambda v: v.tensor_copy(
                    meta_all[:, :, :, 1],
                    ids_s[:, :, 1:2].broadcast_to([128, 64, 8])))
        em.emit("vector",
                lambda v: v.tensor_copy(meta_all[:, :, :, 2], smtm_s[:]))
        em.emit("vector",
                lambda v: v.tensor_tensor(meta_all[:, :, :, 3], smtm_s[:],
                                          meta_all[:, :, :, 2], ALU.subtract),
                inc="ve")
        v_sel = em.v("ve")
        em.emit("vector", lambda v: v.memset(meta_all[:, :, :, 4:8], 0.0), inc="ve")
        v_meta = em.v("ve")
        cap("pos", pos_all[:].bitcast(F32))
        cap("meta8", meta_all[:, :, :, 0:8].rearrange("p j e c -> p (j e c)"))

        v_scat = [0] * E

        def emit_meta_scatters(e):
            for j in range(64):
                v_scat[e] = gp_dma(
                    lambda g, e=e, j=j: g.indirect_dma_start(
                        meta_dr.ap().rearrange("e m c -> (e m) c"),
                        IndirectOffsetOnAxis(pos_all[:, j, e:e + 1], 0),
                        meta_all[:, j, e, :], None,
                        element_offset=e * CAP * 8,
                        bounds_check=regs["cap"], oob_is_err=False),
                    f"sc{e}",
                    waits=([("ve", v_meta), ("mi", v_meta_init)] if j == 0 else []))

        emit_meta_scatters(0)

        cap("meta0", meta_dr.ap()[0].rearrange("(m p) c -> p m c", p=128),
            [(f"sc{e}", v_scat[e]) for e in range(E)] + [("mi", v_meta_init)])

        v_meta_ld = [0] * E
        v_y_final = 0
        v_prev_dec = 0
        v_prev_scat_end = []
        for e in range(E):
            g_grp = e // 2
            em.emit("sync",
                    lambda s, e=e: s.dma_start(
                        meta_ld[:], meta_dr.ap()[e].rearrange("(m p) c -> p m c", p=128)),
                    waits=[(f"sc{e}", v_scat[e]), ("ve", v_prev_dec)], inc="ml", k=16)
            v_meta_ld[e] = em.v("ml")
            em.emit("vector",
                    lambda v: v.tensor_scalar(idx_f[:], meta_ld[:, :, 0], 64.0, None, ALU.mult),
                    waits=[("ml", v_meta_ld[e])] + list(v_prev_scat_end))
            em.emit("vector",
                    lambda v: v.tensor_tensor(idx_f[:], idx_f[:], meta_ld[:, :, 1], ALU.add))
            em.emit("vector", lambda v: v.tensor_copy(idx_u[:], idx_f[:]))
            em.emit("vector",
                    lambda v: v.tensor_tensor(gates[:], meta_ld[:, :, 2], meta_ld[:, :, 3],
                                              ALU.add), inc="ve")
            v_dec = em.v("ve")
            v_prev_dec = v_dec

            v_bdr = gp_dma(lambda g, e=e: g.dma_start(
                bd_r[:], bd_dr.ap()[e, :].rearrange("(o f) -> o f", o=1)), "w")
            for hhalf in range(2):
                em.emit("tensor",
                        lambda tn, hhalf=hhalf: tn.matmul(
                            ps_c[:, 0:512],
                            onesb_s[0:1, :], bd_r[0:1, 512 * hhalf:512 * (hhalf + 1)],
                            start=True, stop=True),
                        waits=[("w", v_bdr), ("ve", em.v("ve"))], inc="pe")
                em.emit("vector",
                        lambda v, hhalf=hhalf: v.tensor_copy(
                            bdb[:, 512 * hhalf:512 * (hhalf + 1)], ps_c[:, 0:512]),
                        waits=[("pe", em.v("pe"))], inc="ve")
            v_bdb = em.v("ve")

            ev_par = {0: 0, 1: 0}
            xg_done = [0, 0]
            for m in range(10):
                buf = m % 2
                v_g = gp_dma(
                    lambda g, m=m, buf=buf: g.indirect_dma_start(
                        xg_tm[:, buf, :], None, xln2_dr.ap(),
                        IndirectOffsetOnAxis(idx_u[:, m:m + 1], 0),
                        bounds_check=regs["ntok"], oob_is_err=False),
                    f"gx{buf}",
                    waits=[("ve", v_dec), ("pe", xg_done[buf])])
                for hb in range(2):
                    bank = ps_t if hb == 0 else ps_t2
                    for q in range(4):
                        k = 4 * hb + q
                        em.emit("tensor",
                                lambda tn, m=m, buf=buf, k=k, q=q, bank=bank: tn.transpose(
                                    bank[:, 128 * q:128 * (q + 1)],
                                    xg_tm[:, buf, 128 * k:128 * (k + 1)], identb[:]),
                                waits=(([(f"gx{buf}", v_g), ("ve", ev_par[hb])] if q == 0 else [])),
                                inc=("pe" if q == 3 else None))
                    em.emit("vector",
                            lambda v, m=m, hb=hb, bank=bank: v.tensor_copy(
                                xg_fm[:, 4 * hb:4 * hb + 4, 128 * m:128 * (m + 1)],
                                bank[:, 0:512].rearrange("p (a k) -> p a k", a=4)),
                            waits=[("pe", em.v("pe"))], inc="ve")
                    ev_par[hb] = em.v("ve")
                xg_done[buf] = em.v("pe")
            v_xg = em.v("ve")

            if e + 1 < E:
                emit_meta_scatters(e + 1)

            ev_up = {0: 0, 1: 0}
            nn_off = (0, 512, 1024)
            nn_sz = (512, 512, 256)
            for fc in range(NFC):
                for nn in range(3):
                    gi = fc * 3 + nn
                    bank = ps_a if gi % 2 == 0 else ps_b
                    for k in range(8):
                        em.emit("tensor",
                                lambda tn, fc=fc, nn=nn, k=k, bank=bank: tn.matmul(
                                    bank[:, 0:nn_sz[nn]],
                                    wu_s[:, k, 128 * fc:128 * (fc + 1)],
                                    xg_fm[:, k, nn_off[nn]:nn_off[nn] + nn_sz[nn]],
                                    start=(k == 0), stop=(k == 7)),
                                waits=(([("ve", v_xg), ("w", v_wu[g_grp]),
                                         ("ac", ev_up[gi % 2])] if k == 0 else [])),
                                inc=("pe" if k == 7 else None))
                    em.emit("scalar",
                            lambda s, fc=fc, nn=nn, g_grp=g_grp, bank=bank: s.activation(
                                mid_arena[:, fc, nn_off[nn]:nn_off[nn] + nn_sz[nn]],
                                bank[:, 0:nn_sz[nn]], AF.Silu,
                                bias=bu_c[:, NFC * g_grp + fc:NFC * g_grp + fc + 1],
                                scale=1.0),
                            waits=[("pe", em.v("pe")), ("ve", v_theta), ("ld", v_smg)],
                            inc="ac")
                    ev_up[gi % 2] = em.v("ac")
            v_up = em.v("ac")
            v_up_pe = em.v("pe")

            if e + 1 < E and (e + 1) % 2 == 0:
                v_wu[g_grp + 1] = gp_dma(
                    lambda g, gg=g_grp + 1: g.dma_start(
                        wu_s[:], wu_dr.ap()[gg].rearrange("(c p) f -> p c f", p=128)),
                    "w", waits=[("pe", v_up_pe)])

            ev_dn = {0: 0, 1: 0}
            scat_done = [0, 0]
            for m in range(10):
                buf = m % 2
                for hhalf in range(2):
                    bank = ps_c if hhalf == 0 else ps_d
                    for k in range(NFC):
                        em.emit("tensor",
                                lambda tn, m=m, hhalf=hhalf, k=k, bank=bank: tn.matmul(
                                    bank[:, 0:512],
                                    mid_arena[:, k, 128 * m:128 * (m + 1)],
                                    wd_s[:, k, 512 * hhalf:512 * (hhalf + 1)],
                                    start=(k == 0), stop=(k == NFC - 1)),
                                waits=(([("ac", v_up), ("w", v_wd[e]),
                                         ("ve", ev_dn[hhalf])] if k == 0 else [])),
                                inc=("pe" if k == NFC - 1 else None))
                    em.emit("vector",
                            lambda v, m=m, buf=buf, hhalf=hhalf, bank=bank: v.tensor_tensor(
                                contrib[:, buf, 512 * hhalf:512 * (hhalf + 1)],
                                bank[:, 0:512], bdb[:, 512 * hhalf:512 * (hhalf + 1)],
                                ALU.add),
                            waits=[("pe", em.v("pe")), ("ve", v_bdb),
                                   (f"ys{buf}", scat_done[buf])])
                    em.emit("vector",
                            lambda v, m=m, buf=buf, hhalf=hhalf: v.tensor_scalar(
                                contrib[:, buf, 512 * hhalf:512 * (hhalf + 1)],
                                contrib[:, buf, 512 * hhalf:512 * (hhalf + 1)],
                                gates[:, m:m + 1], None, ALU.mult),
                            inc="ve")
                    ev_dn[hhalf] = em.v("ve")
                v_ct = em.v("ve")
                v_y_final = gp_dma(
                    lambda g, m=m, buf=buf: g.indirect_dma_start(
                        y_dr.ap(), IndirectOffsetOnAxis(idx_u[:, m:m + 1], 0),
                        contrib[:, buf, :], None,
                        compute_op=ALU.add, oob_is_err=True),
                    f"ys{buf}",
                    waits=[("ve", v_ct)] + ([("ld", v_yinit)] if (e == 0 and m == 0) else []))
                scat_done[buf] = v_y_final
            v_prev_scat_end = [("ys0", em.v("ys0")), ("ys1", em.v("ys1"))]

            if e + 1 < E:
                v_wd[e + 1] = gp_dma(
                    lambda g, ee=e + 1: g.dma_start(
                        wd_s[:], wd_dr.ap()[ee].rearrange("(c p) f -> p c f", p=128)),
                    "w", waits=[("pe", em.v("pe"))])

        v_fin_ys0 = em.v("ys0")
        v_fin_ys1 = em.v("ys1")
        v_fin_mi = em.v("mi")

        @block.sync
        def _(sync):
            em.run_engine("sync", sync)

        @block.gpsimd
        def _(gpsimd):
            em.run_engine("gpsimd", gpsimd)
            gpsimd.wait_ge(h_ys0, v_fin_ys0)
            gpsimd.wait_ge(h_ys1, v_fin_ys1)
            gpsimd.wait_ge(h_mi, v_fin_mi)

        @block.vector
        def _(vector):
            em.run_engine("vector", vector)

        @block.scalar
        def _(scalar):
            em.run_engine("scalar", scalar)

        @block.tensor
        def _(tensor):
            em.run_engine("tensor", tensor)

    return nc



_CACHE = {}


def _get_programs():
    if "a" not in _CACHE:
        _CACHE["a"] = build_launch_a()
    if "b" not in _CACHE:
        _CACHE["b"] = build_launch_b()
    return _CACHE["a"], _CACHE["b"]


def kernel(**inputs):
    x = np.ascontiguousarray(np.asarray(inputs["L0_attn_x"], dtype=np.float32))
    w_in_l = np.asarray(inputs["L0_attn_w_in_local"], dtype=np.float32)
    b_in_l = np.asarray(inputs["L0_attn_b_in_local"], dtype=np.float32)
    w_out_l = np.asarray(inputs["L0_attn_w_out_local"], dtype=np.float32)
    b_out_l = np.asarray(inputs["L0_attn_b_out_local"], dtype=np.float32)
    w_in_g = np.asarray(inputs["L0_attn_w_in_global"], dtype=np.float32)
    b_in_g = np.asarray(inputs["L0_attn_b_in_global"], dtype=np.float32)
    w_out_g = np.asarray(inputs["L0_attn_w_out_global"], dtype=np.float32)
    b_out_g = np.asarray(inputs["L0_attn_b_out_global"], dtype=np.float32)
    wr = np.asarray(inputs["L0_ffn_Wr"], dtype=np.float32)
    br = np.asarray(inputs["L0_ffn_br"], dtype=np.float32)
    wu = np.asarray(inputs["L0_ffn_W_up"], dtype=np.float32)
    bu = np.asarray(inputs["L0_ffn_b_up"], dtype=np.float32)
    wd = np.asarray(inputs["L0_ffn_W_down"], dtype=np.float32)
    bd = np.asarray(inputs["L0_ffn_b_down"], dtype=np.float32)

    wInT_l = np.ascontiguousarray(w_in_l.T)
    wOutT_l = np.ascontiguousarray(w_out_l.T)
    wInT_g = np.ascontiguousarray(w_in_g.T)
    wOutT_g = np.ascontiguousarray(w_out_g.T)
    bInG_c = np.ascontiguousarray(b_in_g.reshape(24, 128).T)
    bOutL_c = np.ascontiguousarray(b_out_l.reshape(8, 128).T)
    bOutG_c = np.ascontiguousarray(b_out_g.reshape(8, 128).T)
    bInL_r = np.ascontiguousarray(b_in_l.reshape(1, 3 * D))
    bInG_r = np.ascontiguousarray(b_in_g.reshape(1, 3 * D))
    wr_l = np.ascontiguousarray(wr.reshape(8, 128, E).transpose(1, 0, 2))
    br_c = br.reshape(E, 1)

    wu_p = np.zeros((4, D, FP), dtype=np.float32)
    wu_p[:, :, :FTRUE] = wu
    bu_p = np.zeros((4, FP), dtype=np.float32)
    bu_p[:, :FTRUE] = bu
    bu_cols = np.ascontiguousarray(
        bu_p.reshape(4, NFC, 128).transpose(2, 0, 1).reshape(128, 4 * NFC))
    wd_p = np.zeros((E, FP, D), dtype=np.float32)
    wd_p[:, :FTRUE, :] = wd

    nca, ncb = _get_programs()

    in_maps_a = []
    for c in range(8):
        in_maps_a.append({
            "x": np.ascontiguousarray(x[2 * c:2 * c + 2].reshape(NTOK, D)),
            "wInT_l": wInT_l, "wOutT_l": wOutT_l,
            "wInT_g": wInT_g, "wOutT_g": wOutT_g,
            "bInG_c": bInG_c,
            "bOutL_c": bOutL_c, "bOutG_c": bOutG_c,
            "bInL_r": bInL_r, "bInG_r": bInG_r,
            "wr_l": wr_l, "br_c": br_c,
        })
    res_a = run_bass_kernel_spmd(nca, in_maps_a, core_ids=list(range(8)))

    sm_all = np.ascontiguousarray(
        np.concatenate([res_a.results[c]["sm"] for c in range(8)], axis=1))

    in_maps_b = []
    for c in range(8):
        in_maps_b.append({
            "x1": res_a.results[c]["x1"],
            "xln2": res_a.results[c]["xln2"],
            "smg": sm_all,
            "smtm": np.ascontiguousarray(res_a.results[c]["sm"].T),
            "wu": wu_p, "bu_cols": bu_cols, "wd": wd_p, "bd": bd,
        })
    res_b = run_bass_kernel_spmd(ncb, in_maps_b, core_ids=list(range(8)))

    y = np.concatenate(
        [res_b.results[c]["y"][0:NTOK].reshape(2, 4096, D) for c in range(8)], axis=0)
    return y


# version 05
# speedup vs baseline: 1.2641x; 1.0308x over previous
import numpy as np
from contextlib import ExitStack
import ml_dtypes
import concourse.bass as bass
import concourse.mybir as mybir
from concourse.bass_utils import run_bass_kernel_spmd
from concourse.bass_core_rust import IndirectOffsetOnAxis

F32 = mybir.dt.float32
BF16 = mybir.dt.bfloat16
U32 = mybir.dt.uint32
AF = mybir.ActivationFunctionType
ALU = mybir.AluOpType
AX = mybir.AxisListType

NTOK = 8192
NTILE = 16
D = 1024
E = 8
FTRUE = 2730
FP = 2816
NFC = FP // 128
CAP = 1280
NSLOT = CAP + 128
YROWS = NTOK + 128
SCL = 0.08838834764831845
SCG = 1.0 / 16.0
EPS = 1e-5
NEG = -30000.0
BIG = float(1 << 20)
BISECT_ITERS = 30


class EM:
    ENGINES = ("sync", "gpsimd", "vector", "scalar", "tensor")

    def __init__(self):
        self.prog = {e: [] for e in self.ENGINES}
        self.sem = {}

    def add_sem(self, name, handle):
        self.sem[name] = [handle, 0]

    def v(self, name):
        return self.sem[name][1]

    def emit(self, eng, fn, waits=(), inc=None, k=1, nodrain=False):
        ws = [(self.sem[n][0], val) for (n, val) in waits if val > 0]
        post = None
        if inc is not None:
            self.sem[inc][1] += k
            post = (self.sem[inc][0], k)

        def run(engobj, fn=fn, ws=ws, post=post, eng=eng, nodrain=nodrain):
            for h, val in ws:
                engobj.wait_ge(h, val)
            inst = fn(engobj)
            if eng == "vector" and not (nodrain and post is None):
                inst = engobj.drain()
            if post is not None:
                inst.then_inc(post[0], post[1])
        self.prog[eng].append(run)

    def run_engine(self, name, engobj):
        for fn in self.prog[name]:
            fn(engobj)


def _consts_a():
    ident_bf = np.eye(128, dtype=ml_dtypes.bfloat16)
    slide = np.zeros((128, 4, 192), dtype=ml_dtypes.bfloat16)
    for c in range(4):
        for p in range(128):
            slide[p, c, 64 + 2 * c + (1 if p >= 64 else 0)] = 1.0 / 64.0
    negb = float(ml_dtypes.bfloat16(NEG))
    seg2 = np.zeros((2, 128), dtype=ml_dtypes.bfloat16)
    seg2[0, 0:64] = 1.0
    seg2[1, 64:128] = 1.0
    segmov = np.zeros((2, 512), dtype=ml_dtypes.bfloat16)
    for col in range(512):
        segmov[(col % 128) // 64, col] = -negb
    negrow = np.full((1, 128), negb, dtype=ml_dtypes.bfloat16)
    return ident_bf, slide.reshape(128, 4 * 192), seg2, segmov, negrow



def build_launch_a(dbg_stage=None):
    nc = bass.Bass("TRN2", target_bir_lowering=False, detect_race_conditions=False)

    x_dr = nc.dram_tensor("x", [NTOK, D], F32, kind="ExternalInput")
    wInT_l_dr = nc.dram_tensor("wInT_l", [D, 3 * D], F32, kind="ExternalInput")
    wOutT_l_dr = nc.dram_tensor("wOutT_l", [D, D], F32, kind="ExternalInput")
    wInT_g_dr = nc.dram_tensor("wInT_g", [D, 3 * D], F32, kind="ExternalInput")
    wOutT_g_dr = nc.dram_tensor("wOutT_g", [D, D], F32, kind="ExternalInput")
    bInG_dr = nc.dram_tensor("bInG_c", [128, 24], F32, kind="ExternalInput")
    bOutL_dr = nc.dram_tensor("bOutL_c", [128, 8], F32, kind="ExternalInput")
    bOutG_dr = nc.dram_tensor("bOutG_c", [128, 8], F32, kind="ExternalInput")
    bInL_row_dr = nc.dram_tensor("bInL_r", [1, 3 * D], F32, kind="ExternalInput")
    bInG_row_dr = nc.dram_tensor("bInG_r", [1, 3 * D], F32, kind="ExternalInput")
    wr_dr = nc.dram_tensor("wr_l", [128, 8, E], F32, kind="ExternalInput")
    br_dr = nc.dram_tensor("br_c", [E, 1], F32, kind="ExternalInput")

    x1_dr = nc.dram_tensor("x1", [NTOK, D], F32, kind="ExternalOutput")
    xln2_dr = nc.dram_tensor("xln2", [NTOK, D], BF16, kind="ExternalOutput")
    sm_dr = nc.dram_tensor("sm", [E, NTOK], F32, kind="ExternalOutput")
    dbg_dr = nc.dram_tensor("dbg", [128, 8192], F32, kind="ExternalOutput") if dbg_stage else None

    identc, slidec, seg2c, segmovc, negrowc = _consts_a()
    ident_dr = nc.inline_tensor(identc, "ident_bf")
    identf_dr = nc.inline_tensor(np.eye(128, dtype=np.float32), "ident_f32")
    slide_dr = nc.inline_tensor(slidec, "slide")
    seg2_dr = nc.inline_tensor(seg2c, "seg2")
    segmov_dr = nc.inline_tensor(segmovc, "segmov")
    negrow_dr = nc.inline_tensor(negrowc, "negrow")
    onesb_dr = nc.inline_tensor(np.ones((128, 128), dtype=ml_dtypes.bfloat16), "onesb")
    ones512_dr = nc.inline_tensor(np.ones((1, 512), dtype=ml_dtypes.bfloat16), "ones512")
    ind8c = np.zeros((8, 512), dtype=ml_dtypes.bfloat16)
    for _s in range(8):
        ind8c[_s, 64 * _s:64 * (_s + 1)] = 1.0
    ind8_dr = nc.inline_tensor(ind8c, "ind8")

    em = EM()

    with ExitStack() as ctx:
        identb = ctx.enter_context(nc.sbuf_tensor("identb", [128, 128], BF16))
        ident_f32_s = ctx.enter_context(nc.sbuf_tensor("ident_f32_s", [128, 128], F32))
        slide_s = ctx.enter_context(nc.sbuf_tensor("slide_s", [128, 4, 192], BF16))
        seg2_s = ctx.enter_context(nc.sbuf_tensor("seg2_s", [2, 128], BF16))
        segmov_s = ctx.enter_context(nc.sbuf_tensor("segmov_s", [2, 512], BF16))
        negrow_s = ctx.enter_context(nc.sbuf_tensor("negrow_s", [1, 128], BF16))
        onesb_s = ctx.enter_context(nc.sbuf_tensor("onesb_s", [128, 128], BF16))
        ones512_s = ctx.enter_context(nc.sbuf_tensor("ones512_s", [1, 512], BF16))
        epsc = ctx.enter_context(nc.sbuf_tensor("epsc", [128, 1], F32))
        bInG_c = ctx.enter_context(nc.sbuf_tensor("bInGc_s", [128, 24], F32))
        bOutSum = ctx.enter_context(nc.sbuf_tensor("bOutSum", [128, 8], F32))
        bOutG_c = ctx.enter_context(nc.sbuf_tensor("bOutGc_s", [128, 8], F32))
        bOutL_c = ctx.enter_context(nc.sbuf_tensor("bOutLc_s", [128, 8], F32))
        bInL_r = ctx.enter_context(nc.sbuf_tensor("bInLr_s", [1, 3 * D], BF16))
        bInG_r = ctx.enter_context(nc.sbuf_tensor("bInGr_s", [1, 3 * D], BF16))
        wr_s = ctx.enter_context(nc.sbuf_tensor("wr_s", [128, 8, E], BF16))
        br_c = ctx.enter_context(nc.sbuf_tensor("brc_s", [E, 1], F32))
        means0 = ctx.enter_context(nc.sbuf_tensor("means0", [64, 1024], BF16))
        means1 = ctx.enter_context(nc.sbuf_tensor("means1", [64, 1024], BF16))
        means_fm = ctx.enter_context(nc.sbuf_tensor("means_fm", [128, 8, 128], BF16))
        globB = ctx.enter_context(nc.sbuf_tensor("globB", [128, 8, 128], F32))
        ln1_bufs = tuple(ctx.enter_context(nc.sbuf_tensor(f"sc1_{i}", [128, 4], F32))
                         for i in range(6))
        ln2_bufs = tuple(ctx.enter_context(nc.sbuf_tensor(f"sc2_{i}", [128, 4], F32))
                         for i in range(6))
        ssum8 = ctx.enter_context(nc.sbuf_tensor("ssum8", [128, 8], F32))
        x_s = ctx.enter_context(nc.sbuf_tensor("x_s", [128, 2, 4, 1024], F32))
        xln_s = ctx.enter_context(nc.sbuf_tensor("xln_s", [128, 4, 1024], BF16))
        xln_fm = ctx.enter_context(nc.sbuf_tensor("xln_fm", [128, 8, 512], BF16))
        qk_sb = ctx.enter_context(nc.sbuf_tensor("qk_sb", [128, 16, 512], BF16))
        v_tm = ctx.enter_context(nc.sbuf_tensor("v_tm", [128, 4, 1024], BF16))
        prob2 = ctx.enter_context(nc.sbuf_tensor("prob2", [128, 2, 8, 128], BF16))
        probT2 = ctx.enter_context(nc.sbuf_tensor("probT2", [128, 2, 8, 128], BF16))
        trash_t = ctx.enter_context(nc.sbuf_tensor("trash_t", [128, 1024], BF16))
        trash = trash_t[:]
        attn_fm = ctx.enter_context(nc.sbuf_tensor("attn_fm", [128, 8, 512], BF16))
        xln2_s = ctx.enter_context(nc.sbuf_tensor("xln2_s", [128, 4, 1024], BF16))
        xln2_fm = ctx.enter_context(nc.sbuf_tensor("xln2_fm", [128, 8, 512], BF16))
        sum_fm = xln2_fm
        logit_em = ctx.enter_context(nc.sbuf_tensor("logit_em", [8, 512], F32))
        sm_tm = ctx.enter_context(nc.sbuf_tensor("sm_tm", [128, 4, 8], F32))
        sm_mx = ctx.enter_context(nc.sbuf_tensor("sm_mx", [128, 4], F32))
        sm_sum = ctx.enter_context(nc.sbuf_tensor("sm_sum", [128, 4], F32))
        sm_em = ctx.enter_context(nc.sbuf_tensor("sm_em", [8, 512], F32))
        qkg = qk_sb[:, 0:4, :].rearrange("p a (b k) -> p (a b) k", k=128)
        vg_tm = qk_sb[:, 4:6, :].rearrange("p a k -> p (a k)")
        vg_pair = qk_sb[0:64, 6:10, :].rearrange("p a k -> p (a k)").rearrange("p (b f) -> p b f", b=2)
        afm_f32 = attn_fm[:].rearrange("p a k -> p (a k)").bitcast(F32)
        sg_f = afm_f32[:, 256:512].rearrange("p (a k) -> p a k", a=4)
        pg_f = afm_f32[:, 512:768].rearrange("p (a k) -> p a k", a=4)
        pg_b = prob2[:, 0, 0:2, :].rearrange("p a k -> p (a k)").rearrange("p (a k) -> p a k", a=4)
        pgT = probT2[0:64, 0, 0:4, :].rearrange("p a k -> p (a k)").rearrange("p (a k) -> p a k", a=4)
        ag_fm = attn_fm[:, :, 0:128]
        globT = ctx.enter_context(nc.sbuf_tensor("globT", [128, 1024], BF16))
        globT8 = ctx.enter_context(nc.sbuf_tensor("globT8", [8, 1024], BF16))
        ind8_s = ctx.enter_context(nc.sbuf_tensor("ind8_s", [8, 512], BF16))
        gmx = ctx.enter_context(nc.sbuf_tensor("gmx", [128, 4], F32))
        gsum = ctx.enter_context(nc.sbuf_tensor("gsum", [128, 4], F32))
        wInTl = ctx.enter_context(nc.sbuf_tensor("wInTl", [128, 8, 3 * D], BF16))
        wOutTl = ctx.enter_context(nc.sbuf_tensor("wOutTl", [128, 8, D], BF16))
        wInTg = wInTl
        wOutTg = wOutTl
        ps_A = ctx.enter_context(nc.psum_tensor("ps_A", [128, 1024], F32))
        ps_B = ctx.enter_context(nc.psum_tensor("ps_B", [128, 1024], F32))
        ps_C = ctx.enter_context(nc.psum_tensor("ps_C", [128, 1024], F32))
        ps_t = ctx.enter_context(nc.psum_tensor("ps_t", [128, 1024], BF16))
        ps_t2 = ctx.enter_context(nc.psum_tensor("ps_t2", [128, 1024], BF16))
        h_ld = ctx.enter_context(nc.semaphore("s_ld"))
        h_x0 = ctx.enter_context(nc.semaphore("s_x0"))
        h_x1 = ctx.enter_context(nc.semaphore("s_x1"))
        h_st = ctx.enter_context(nc.semaphore("s_st"))
        h_gw = ctx.enter_context(nc.semaphore("s_gw"))
        h_sm = ctx.enter_context(nc.semaphore("s_sm"))
        h_ve = ctx.enter_context(nc.semaphore("s_ve"))
        h_ac = ctx.enter_context(nc.semaphore("s_ac"))
        h_pe = ctx.enter_context(nc.semaphore("s_pe"))
        block = ctx.enter_context(nc.Block())
        for nm, hh in (("ld", h_ld), ("x0", h_x0), ("x1", h_x1), ("st", h_st),
                       ("gw", h_gw), ("sm", h_sm), ("ve", h_ve), ("ac", h_ac), ("pe", h_pe)):
            em.add_sem(nm, hh)

        for dst, src in (
            (identb[:], ident_dr.ap()),
            (ident_f32_s[:], identf_dr.ap()),
            (slide_s[:].rearrange("p c k -> p (c k)"), slide_dr.ap()),
            (seg2_s[:], seg2_dr.ap()),
            (segmov_s[:], segmov_dr.ap()),
            (negrow_s[:], negrow_dr.ap()),
            (onesb_s[:], onesb_dr.ap()),
            (ones512_s[:], ones512_dr.ap()),
            (ind8_s[:], ind8_dr.ap()),
            (bInG_c[:], bInG_dr.ap()),
            (bOutG_c[:], bOutG_dr.ap()),
            (bOutL_c[:], bOutL_dr.ap()),
            (br_c[:], br_dr.ap()),
        ):
            em.emit("sync", lambda s, d=dst, a=src: s.dma_start(d, a), inc="ld", k=16)
        v_consts = em.v("ld")

        for dst, src in (
            (bInL_r[:], bInL_row_dr.ap()),
            (bInG_r[:], bInG_row_dr.ap()),
            (wr_s[:], wr_dr.ap()),
            (wInTg[:], wInT_g_dr.ap().rearrange("(c p) f -> p c f", p=128)),
            (wOutTg[:], wOutT_g_dr.ap().rearrange("(c p) f -> p c f", p=128)),
        ):
            em.emit("gpsimd", lambda g, d=dst, a=src: g.dma_start(d, a), inc="gw", k=16)
        v_weights = em.v("gw")

        em.emit("vector", lambda v: v.memset(epsc[:], EPS), inc="ve")
        em.emit("vector",
                lambda v: v.tensor_tensor(bOutSum[:], bOutG_c[:], bOutL_c[:], ALU.add),
                waits=[("ld", v_consts)], inc="ve")
        v_bsum = em.v("ve")

        def cap(stage, ap, extra_waits=()):
            if dbg_stage != stage:
                return
            def _cap_dma(g, ap=ap):
                dst = dbg_dr.ap()[0:ap.shape[0], 0:int(np.prod(ap.shape[1:]))]
                if len(ap.shape) == 3:
                    dst = dst.rearrange("p (a k) -> p a k", a=ap.shape[1])
                return g.dma_start(dst, ap)
            em.emit("gpsimd", _cap_dma,
                    waits=[("ve", em.v("ve")), ("ac", em.v("ac")), ("pe", em.v("pe"))]
                          + list(extra_waits),
                    inc="sm", k=16)

        cap("wg", wInTg[:, 0:2, :].rearrange("p c f -> p (c f)"), [("gw", v_weights)])

        def emit_ln_stats(src4, bufs, first_waits, per_c_waits=None):
            b_s, b_q, b_m, b_v, b_r, b_nmr = bufs
            for c in range(4):
                w0 = list(per_c_waits[c]) if per_c_waits else []
                if c == 0:
                    w0 = list(first_waits) + w0
                em.emit("scalar",
                        lambda s, c=c: s.activation(trash[:], src4[:, c, :], AF.Copy,
                                                    bias=0.0, scale=1.0,
                                                    accum_out=b_s[:, c:c + 1]),
                        waits=w0)
            for c in range(4):
                em.emit("scalar",
                        lambda s, c=c: s.activation(trash[:], src4[:, c, :], AF.Square,
                                                    bias=0.0, scale=1.0,
                                                    accum_out=b_q[:, c:c + 1]),
                        inc=("ac" if c == 3 else None))
            v_stats = em.v("ac")
            em.emit("vector", lambda v: v.tensor_scalar(b_m[:], b_s[:], 1.0 / D, None, ALU.mult),
                    waits=[("ac", v_stats)])
            em.emit("vector", lambda v: v.tensor_scalar(b_v[:], b_q[:], 1.0 / D, None, ALU.mult))
            em.emit("vector", lambda v: v.tensor_tensor(b_r[:], b_m[:], b_m[:], ALU.mult))
            em.emit("vector", lambda v: v.tensor_tensor(b_v[:], b_v[:], b_r[:], ALU.subtract),
                    inc="ve")
            v_var = em.v("ve")
            em.emit("scalar", lambda s: s.activation(b_r[:], b_v[:], AF.Sqrt,
                                                     bias=epsc[:, 0:1], scale=1.0),
                    waits=[("ve", v_var)], inc="ac")
            v_sd = em.v("ac")
            em.emit("vector", lambda v: v.reciprocal(b_r[:], b_r[:]), waits=[("ac", v_sd)])
            em.emit("vector", lambda v: v.tensor_tensor(b_nmr[:], b_m[:], b_r[:], ALU.mult))
            em.emit("vector", lambda v: v.tensor_scalar(b_nmr[:], b_nmr[:], -1.0, None, ALU.mult),
                    inc="ve")
            return em.v("ve")

        def emit_ln_apply(src4, dst4, bufs, v_math, apply_waits=()):
            b_r, b_nmr = bufs[4], bufs[5]
            for c in range(4):
                w0 = []
                if c == 0:
                    w0 = list(apply_waits) + [("ve", v_math)]
                em.emit("vector",
                        lambda v, c=c: v.tensor_scalar(dst4[:, c, :], src4[:, c, :],
                                                       b_r[:, c:c + 1], b_nmr[:, c:c + 1],
                                                       ALU.mult, ALU.add),
                        waits=w0, inc=("ve" if c == 3 else None))
            return em.v("ve")

        def emit_ln(src4, dst4, bufs, first_waits, apply_waits=(), per_c_waits=None):
            v_math = emit_ln_stats(src4, bufs,
                                   list(first_waits) + [("ve", em.v("ve"))], per_c_waits)
            return emit_ln_apply(src4, dst4, bufs, v_math, apply_waits)

        x_load_done = [0] * (2 * NTILE)
        x_cons_done = [None] * (2 * NTILE)

        def emit_x_load(r):
            t = r % NTILE
            w = list(x_cons_done[r - 2]) if (r >= 2 and x_cons_done[r - 2]) else []
            sem = "x0" if r % 2 == 0 else "x1"
            em.emit("sync",
                    lambda s, t=t, r=r: s.dma_start(
                        x_s[:, r % 2], x_dr.ap()[t * 512:(t + 1) * 512, :].rearrange(
                            "(c p) f -> p c f", p=128)),
                    waits=w, inc=sem, k=16)
            x_load_done[r] = (sem, em.v(sem))

        emit_x_load(0)
        emit_x_load(1)
        cap("x0", x_s[:, 0].rearrange("p c f -> p (c f)"), [x_load_done[0]])
        cap("slide", slide_s[:].rearrange("p c k -> p (c k)"), [("ld", v_consts)])

        v_means_pe = 0
        for t in range(NTILE):
            r = t
            xb = x_s[:, r % 2]
            v_apply = emit_ln(xb, xln_s, ln1_bufs, [x_load_done[r]],
                              apply_waits=[("pe", v_means_pe)])
            ti = t % 8
            for c in range(4):
                for u in range(2):
                    em.emit("tensor",
                            lambda tn, c=c, u=u, ti=ti: tn.matmul(
                                ps_C[:, 512 * u:512 * (u + 1)],
                                slide_s[:, c, 64 - 8 * ti:192 - 8 * ti],
                                xln_s[:, c, 512 * u:512 * (u + 1)],
                                start=(ti == 0 and c == 0), stop=(ti == 7 and c == 3)),
                            waits=([("ve", v_apply)] if (c == 0 and u == 0) else []),
                            inc=("pe" if (c == 3 and u == 1) else None))
            v_means_pe = em.v("pe")
            x_cons_done[r] = [("pe", em.v("pe")), ("ve", em.v("ve")), ("ac", em.v("ac"))]
            if ti == 7:
                dstm = means0 if t < 8 else means1
                for u in range(2):
                    em.emit("vector",
                            lambda v, u=u, dstm=dstm: v.tensor_copy(
                                dstm[:, 512 * u:512 * (u + 1)], ps_C[0:64, 512 * u:512 * (u + 1)]),
                            waits=([("pe", em.v("pe"))] if u == 0 else []),
                            inc=("ve" if u == 1 else None))
                if t == 7:
                    cap("means0", means0[:])
            if r + 2 < 2 * NTILE:
                emit_x_load(r + 2)

        for b in range(2):
            srcm = means0 if b == 0 else means1
            for fc in range(8):
                bank = ps_t if fc % 2 == 0 else ps_t2
                em.emit("tensor",
                        lambda tn, srcm=srcm, fc=fc, bank=bank: tn.transpose(
                            bank[:, 0:64], srcm[:, 128 * fc:128 * (fc + 1)],
                            identb[0:64, 0:64]),
                        waits=[("ve", em.v("ve"))], inc="pe")
                em.emit("vector",
                        lambda v, b=b, fc=fc, bank=bank: v.tensor_copy(
                            means_fm[:, fc, 64 * b:64 * (b + 1)], bank[:, 0:64]),
                        waits=[("pe", em.v("pe"))], inc="ve")
        v_mfm = em.v("ve")
        cap("mfm", means_fm[:].rearrange("p a k -> p (a k)"))

        for m in range(16):
            bank = ps_A[:, 0:512] if m % 2 == 0 else ps_B[:, 0:512]
            for k in range(8):
                em.emit("tensor",
                        lambda tn, m=m, k=k, bank=bank: tn.matmul(
                            bank[:, 0:128], wInTg[:, k, 128 * m:128 * (m + 1)],
                            means_fm[:, k, :], start=(k == 0), stop=(k == 7)),
                        waits=(([("ve", v_mfm)] if m == 0 else [("ac", max(0, em.v("ac") - 1))])
                               if k == 0 else []),
                        inc=("pe" if k == 7 else None))
            em.emit("scalar",
                    lambda s, m=m, bank=bank: s.activation(
                        qkg[:, m, :], bank[:, 0:128], AF.Identity,
                        bias=bInG_c[:, m:m + 1], scale=1.0),
                    waits=[("pe", em.v("pe"))], inc="ac")
        v_qkg = em.v("ac")
        cap("qkg", qkg[:].rearrange("p a k -> p (a k)"))

        for n in range(2):
            for k in range(8):
                em.emit("tensor",
                        lambda tn, n=n, k=k: tn.matmul(
                            ps_C[:, 0:512], means_fm[:, k, :],
                            wInTg[:, k, 2048 + 512 * n:2048 + 512 * (n + 1)],
                            start=(k == 0), stop=False),
                        waits=([("ve", em.v("ve")), ("ac", em.v("ac"))] if k == 0 else []))
            em.emit("tensor",
                    lambda tn, n=n: tn.matmul(
                        ps_C[:, 0:512], onesb_s[0:1, 0:128],
                        bInG_r[0:1, 2048 + 512 * n:2048 + 512 * (n + 1)],
                        start=False, stop=True),
                    inc="pe")
            em.emit("vector",
                    lambda v, n=n: v.tensor_copy(
                        vg_tm[:, 512 * n:512 * (n + 1)], ps_C[:, 0:512]),
                    waits=[("pe", em.v("pe"))], inc="ve")
        v_vg = em.v("ve")
        cap("vg", vg_tm[:])

        em.emit("vector", lambda v: v.tensor_copy(vg_pair[:, 0, :], vg_tm[0:64, :]),
                waits=[("ve", v_vg)], inc="ve")
        em.emit("gpsimd", lambda g: g.dma_start(vg_pair[:, 1, :], vg_tm[64:128, :]),
                waits=[("ve", em.v("ve"))], inc="gw", k=16)
        v_vgp_gp = em.v("gw")

        for h in range(4):
            for b in range(2):
                for kc in range(2):
                    em.emit("tensor",
                            lambda tn, h=h, b=b, kc=kc: tn.matmul(
                                ps_A[64 * b:64 * (b + 1), 64 * h:64 * (h + 1)],
                                qkg[:, 2 * h + kc, 64 * b:64 * (b + 1)],
                                qkg[:, 8 + 2 * h + kc, 64 * b:64 * (b + 1)],
                                start=(kc == 0), stop=(kc == 1),
                                tile_position=(0, 64 * b)),
                            waits=([("ac", v_qkg)] if (h == 0 and b == 0 and kc == 0) else []),
                            inc=("pe" if (h == 3 and b == 1 and kc == 1) else None))
        v_sgp = em.v("pe")

        em.emit("vector", lambda v: v.tensor_copy(
            sg_f[:].rearrange("p a k -> p (a k)"), ps_A[:, 0:256]),
            waits=[("pe", v_sgp)], inc="ve")
        em.emit("vector", lambda v: v.reduce_max(gmx[:], sg_f[:], axis=AX.X))
        em.emit("vector", lambda v: v.tensor_tensor(
            pg_f[:], sg_f[:],
            gmx[:].rearrange("p (g o) -> p g o", o=1).broadcast_to([128, 4, 64]),
            ALU.subtract), inc="ve")
        em.emit("scalar", lambda s: s.activation(
            pg_f[:].rearrange("p a k -> p (a k)"), pg_f[:].rearrange("p a k -> p (a k)"),
            AF.Exp, bias=0.0, scale=SCG), waits=[("ve", em.v("ve"))], inc="ac")
        em.emit("vector", lambda v: v.reduce_sum(gsum[:], pg_f[:], axis=AX.X),
                waits=[("ac", em.v("ac"))])
        em.emit("vector", lambda v: v.reciprocal(gsum[:], gsum[:]))
        em.emit("vector", lambda v: v.tensor_tensor(
            pg_b[:], pg_f[:],
            gsum[:].rearrange("p (g o) -> p g o", o=1).broadcast_to([128, 4, 64]),
            ALU.mult), inc="ve")
        v_pgb = em.v("ve")

        for h in range(4):
            em.emit("tensor",
                    lambda tn, h=h: tn.transpose(ps_t[0:64, 128 * h:128 * (h + 1)],
                                                 pg_b[:, h, :], identb[:]),
                    waits=([("ve", v_pgb)] if h == 0 else []),
                    inc=("pe" if h == 3 else None))
        em.emit("vector", lambda v: v.tensor_copy(
            pgT[:].rearrange("p a k -> p (a k)"), ps_t[0:64, 0:512]),
            waits=[("pe", em.v("pe"))], inc="ve")
        v_pgT = em.v("ve")

        for b in range(2):
            bank = ps_B[:, 0:512] if b == 0 else ps_C[:, 0:512]
            for h in range(4):
                for j in range(2):
                    em.emit("tensor",
                            lambda tn, b=b, h=h, j=j, bank=bank: tn.matmul(
                                bank[:, 64 * (2 * h + j):64 * (2 * h + j) + 64],
                                vg_pair[:, b, 128 * (2 * h + j):128 * (2 * h + j) + 128],
                                pgT[:, h, 64 * b:64 * (b + 1)],
                                start=True, stop=True),
                            waits=([("ve", v_pgT), ("gw", v_vgp_gp)]
                                   if (b == 0 and h == 0 and j == 0) else []),
                            inc=("pe" if (h == 3 and j == 1) else None))
            em.emit("vector",
                    lambda v, b=b, bank=bank: v.tensor_copy(
                        ag_fm[:, :, 64 * b:64 * (b + 1)],
                        bank[:, 0:512].rearrange("p (a k) -> p a k", a=8)),
                    waits=[("pe", em.v("pe"))], inc="ve")
        v_ag = em.v("ve")

        for m in range(8):
            bank = ps_A[:, 0:512] if m % 2 == 0 else ps_B[:, 0:512]
            for k in range(8):
                em.emit("tensor",
                        lambda tn, m=m, k=k, bank=bank: tn.matmul(
                            bank[:, 0:128], wOutTg[:, k, 128 * m:128 * (m + 1)],
                            ag_fm[:, k, :], start=(k == 0), stop=(k == 7)),
                        waits=(([("ve", v_ag)] if m == 0 else [("ac", max(0, em.v("ac") - 1))])
                               if k == 0 else []),
                        inc=("pe" if k == 7 else None))
            em.emit("scalar",
                    lambda s, m=m, bank=bank: s.activation(
                        globB[:, m, :], bank[:, 0:128], AF.Identity,
                        bias=bOutSum[:, m:m + 1], scale=1.0),
                    waits=[("pe", em.v("pe")), ("ve", v_bsum)], inc="ac")
        v_glob = em.v("ac")
        cap("glob", globB[:].rearrange("p a k -> p (a k)"))

        for m in range(8):
            em.emit("tensor",
                    lambda tn, m=m: tn.transpose(
                        ps_C[:, 128 * m:128 * (m + 1)], globB[:, m, :], ident_f32_s[:]),
                    waits=([("ac", v_glob), ("ve", v_ag)] if m == 0 else []),
                    inc=("pe" if m == 7 else None))
        em.emit("vector",
                lambda v: v.tensor_copy(globT[:], ps_C[:, 0:1024]),
                waits=[("pe", em.v("pe"))], inc="ve")
        v_globT = em.v("ve")

        em.emit("gpsimd",
                lambda g: g.dma_start(
                    wInTl[:], wInT_l_dr.ap().rearrange("(c p) f -> p c f", p=128)),
                waits=[("pe", em.v("pe")), ("ac", v_glob)], inc="gw", k=16)
        em.emit("gpsimd",
                lambda g: g.dma_start(
                    wOutTl[:], wOutT_l_dr.ap().rearrange("(c p) f -> p c f", p=128)),
                inc="gw", k=16)
        v_wl = em.v("gw")

        store_x1_prev = 0
        store_xln2_prev = 0
        store_sm_prev = 0
        st = {}

        def emit_head(t, v_xcons, v_out_pe, v_out_ac, v_math1):
            r = NTILE + t
            xb = x_s[:, r % 2]
            s = {"xb": xb}
            em.emit("gpsimd",
                    lambda g, t=t: g.dma_start(globT8[:], globT[8 * t:8 * t + 8, :]),
                    waits=[("ve", v_globT)] + ([("pe", v_out_pe)] if v_out_pe else []),
                    inc="gw", k=16)
            s["v_gt8"] = em.v("gw")
            v_apply = emit_ln_apply(xb, xln_s, ln1_bufs, v_math1,
                                    apply_waits=[("pe", v_xcons)])
            if t == 0:
                cap("xln1", xln_s[:].rearrange("p c f -> p (c f)"))
            evT = {0: 0, 1: 0}
            for c in range(4):
                bankT = ps_t if c % 2 == 0 else ps_t2
                for fc in range(8):
                    w0 = []
                    if fc == 0:
                        w0 = ([("ve", v_apply)] if c < 2 else [("ve", evT[c % 2])])
                    em.emit("tensor",
                            lambda tn, c=c, fc=fc, bankT=bankT: tn.transpose(
                                bankT[:, 128 * fc:128 * (fc + 1)],
                                xln_s[:, c, 128 * fc:128 * (fc + 1)], identb[:]),
                            waits=w0, inc=("pe" if fc == 7 else None))
                em.emit("vector",
                        lambda v, c=c, bankT=bankT: v.tensor_copy(
                            xln_fm[:, 0:8, 128 * c:128 * (c + 1)],
                            bankT[:, 0:1024].rearrange("p (a k) -> p a k", a=8)),
                        waits=[("pe", em.v("pe"))], inc="ve")
                evT[c % 2] = em.v("ve")
            v_xfm = em.v("ve")
            evq = {0: None, 1: None}
            for blk in range(8):
                bank = ps_A if blk % 2 == 0 else ps_B
                for mm in range(2):
                    m = 2 * blk + mm
                    half = bank[:, 512 * mm:512 * (mm + 1)]
                    for k in range(8):
                        w0 = []
                        if k == 0 and mm == 0:
                            if blk == 0:
                                w0 = [("ve", v_xfm), ("gw", v_wl)]
                                if v_out_ac:
                                    w0.append(("ac", v_out_ac))
                            elif blk >= 2:
                                w0 = [evq[blk % 2]]
                        em.emit("tensor",
                                lambda tn, m=m, k=k, half=half: tn.matmul(
                                    half, wInTl[:, k, 128 * m:128 * (m + 1)],
                                    xln_fm[:, k, :], start=(k == 0), stop=False),
                                waits=w0)
                    em.emit("tensor",
                            lambda tn, m=m, half=half: tn.matmul(
                                half, bInL_r[0:1, 128 * m:128 * (m + 1)],
                                ones512_s[:], start=False, stop=True),
                            inc=("pe" if mm == 1 else None))
                if blk % 2 == 0:
                    em.emit("scalar",
                            lambda s_, blk=blk, bank=bank: s_.activation(
                                qk_sb[:, 2 * blk:2 * blk + 2, :].rearrange("p a k -> p (a k)"),
                                bank[:, 0:1024], AF.Copy, bias=0.0, scale=1.0),
                            waits=[("pe", em.v("pe"))], inc="ac")
                    evq[0] = ("ac", em.v("ac"))
                else:
                    em.emit("vector",
                            lambda v, blk=blk, bank=bank: v.tensor_copy(
                                qk_sb[:, 2 * blk:2 * blk + 2, :].rearrange("p a k -> p (a k)"),
                                bank[:, 0:1024]),
                            waits=[("pe", em.v("pe"))], inc="ve")
                    evq[1] = ("ve", em.v("ve"))
            s["evq"] = evq
            return s

        v_math1 = emit_ln_stats(x_s[:, 0], ln1_bufs,
                                [x_load_done[NTILE], ("ve", em.v("ve"))])
        st = emit_head(0, em.v("pe"), 0, 0, v_math1)
        for t in range(NTILE):
            r = NTILE + t
            xb = st["xb"]
            b = t // 8
            ti = t % 8
            evq = st["evq"]
            v_gt8 = st["v_gt8"]

            if t + 1 < NTILE:
                v_math1 = emit_ln_stats(x_s[:, (r + 1) % 2], ln1_bufs,
                                        [x_load_done[r + 1], ("ve", v_math1)])

            evv = {0: None, 1: None}
            for c4 in range(4):
                bank = ps_A if c4 % 2 == 0 else ps_B
                for n in range(2):
                    half = bank[:, 512 * n:512 * (n + 1)]
                    for k in range(8):
                        w0 = []
                        if k == 0 and n == 0:
                            if c4 < 2:
                                w0 = [evq[c4 % 2]]
                            else:
                                w0 = [evv[c4 % 2]]
                        em.emit("tensor",
                                lambda tn, c4=c4, n=n, k=k, half=half: tn.matmul(
                                    half, xln_fm[:, k, 128 * c4:128 * (c4 + 1)],
                                    wInTl[:, k, 2048 + 512 * n:2048 + 512 * (n + 1)],
                                    start=(k == 0), stop=False),
                                waits=w0)
                    em.emit("tensor",
                            lambda tn, n=n, half=half: tn.matmul(
                                half, onesb_s[0:1, 0:128],
                                bInL_r[0:1, 2048 + 512 * n:2048 + 512 * (n + 1)],
                                start=False, stop=True),
                            inc=("pe" if n == 1 else None))
                if c4 % 2 == 0:
                    em.emit("scalar",
                            lambda s_, c4=c4, bank=bank: s_.activation(
                                v_tm[:, c4, :], bank[:, 0:1024],
                                AF.Copy, bias=0.0, scale=1.0),
                            waits=[("pe", em.v("pe"))], inc="ac")
                    evv[0] = ("ac", em.v("ac"))
                else:
                    em.emit("vector",
                            lambda v, c4=c4, bank=bank: v.tensor_copy(
                                v_tm[:, c4, :], bank[:, 0:1024]),
                            waits=[("pe", em.v("pe"))], inc="ve")
                    evv[1] = ("ve", em.v("ve"))
            v_xcons = em.v("pe")

            v_exp = [0] * 4
            v_pT = [0] * 4
            v_pTev = [0] * 4
            v_av = [0] * 4
            v_attnev = [0] * 4
            v_pb = [0] * 4
            v_sc = [0] * 4

            def emit_scores(c, evv=evv, v_exp=v_exp, v_sc=v_sc):
                par = c % 2
                bank = ps_A if par == 0 else ps_B
                for h in range(8):
                    w0 = []
                    if h == 0:
                        w0 = [evv[par]] if c < 2 else [("ac", v_exp[c - 2])]
                    em.emit("tensor",
                            lambda tn, c=c, h=h, bank=bank: tn.matmul(
                                bank[:, 128 * h:128 * (h + 1)],
                                qk_sb[:, h, 128 * c:128 * (c + 1)],
                                qk_sb[:, 8 + h, 128 * c:128 * (c + 1)],
                                start=(h % 4 == 0), stop=False),
                            waits=w0)
                for hf in range(2):
                    half = bank[:, 512 * hf:512 * (hf + 1)]
                    em.emit("tensor",
                            lambda tn, half=half: tn.matmul(
                                half, negrow_s[0:1, :], ones512_s[:],
                                start=False, stop=False))
                    em.emit("tensor",
                            lambda tn, half=half: tn.matmul(
                                half, seg2_s[:], segmov_s[:, 0:512],
                                start=False, stop=True),
                            inc=("pe" if hf == 1 else None))
                v_sc[c] = em.v("pe")

            emit_scores(0)
            emit_scores(1)
            for c in range(4):
                par = c % 2
                bank = ps_A if par == 0 else ps_B
                bankT = ps_t if par == 0 else ps_t2
                em.emit("scalar",
                        lambda s_, par=par, bank=bank: s_.activation(
                            prob2[:, par].rearrange("p a k -> p (a k)"),
                            bank[:, 0:1024], AF.Exp, bias=0.0, scale=SCL),
                        waits=[("pe", v_sc[c])], inc="ac")
                v_exp[c] = em.v("ac")
                em.emit("vector",
                        lambda v, par=par: v.reduce_sum(ssum8[:], prob2[:, par], axis=AX.X),
                        waits=[("ac", v_exp[c])])
                em.emit("vector", lambda v: v.reciprocal(ssum8[:], ssum8[:]))
                em.emit("vector",
                        lambda v, par=par: v.tensor_tensor(
                            prob2[:, par], prob2[:, par],
                            ssum8[:].rearrange("p (g o) -> p g o", o=1)
                            .broadcast_to([128, 8, 128]),
                            ALU.mult),
                        inc="ve")
                v_pb[c] = em.v("ve")
                for h in range(8):
                    w0 = [("ve", v_pb[c])] if h == 0 else []
                    em.emit("tensor",
                            lambda tn, par=par, h=h, bankT=bankT: tn.transpose(
                                bankT[:, 128 * h:128 * (h + 1)],
                                prob2[:, par, h, :], identb[:]),
                            waits=w0, inc=("pe" if h == 7 else None))
                v_pT[c] = em.v("pe")
                if c + 2 < 4:
                    emit_scores(c + 2)
                em.emit("vector",
                        lambda v, par=par, bankT=bankT: v.tensor_copy(
                            probT2[:, par].rearrange("p a k -> p (a k)"),
                            bankT[:, 0:1024]),
                        waits=[("pe", v_pT[c])], inc="ve")
                v_pTev[c] = em.v("ve")
                for h in range(8):
                    w0 = []
                    if h == 0:
                        w0 = [("ve", v_pTev[c])]
                        if c >= 1:
                            w0.append(("ve", v_attnev[c - 1]))
                        elif t > 0:
                            w0.append(("ve", st["v_attnev_prev"]))
                        else:
                            w0.append(("ve", v_ag))
                    em.emit("tensor",
                            lambda tn, c=c, par=par, h=h: tn.matmul(
                                ps_C[:, 128 * h:128 * (h + 1)],
                                v_tm[:, c, 128 * h:128 * (h + 1)],
                                probT2[:, par, h, :], start=True, stop=True),
                            waits=w0, inc=("pe" if h == 7 else None))
                v_av[c] = em.v("pe")
                em.emit("vector",
                        lambda v, c=c: v.tensor_copy(
                            attn_fm[:, 0:8, 128 * c:128 * (c + 1)],
                            ps_C[:, 0:1024].rearrange("p (a k) -> p a k", a=8)),
                        waits=[("pe", v_av[c])], inc="ve")
                v_attnev[c] = em.v("ve")
            st["v_attnev_prev"] = v_attnev[3]

            evo = {0: None, 1: None}
            for pr in range(4):
                bank = ps_A if pr % 2 == 0 else ps_B
                for mm in range(2):
                    m = 2 * pr + mm
                    half = bank[:, 512 * mm:512 * (mm + 1)]
                    for k in range(8):
                        w0 = []
                        if k == 0 and mm == 0:
                            if pr == 0:
                                w0 = [("ve", v_attnev[3]), ("ac", v_exp[2]),
                                      ("gw", v_gt8)]
                            elif pr == 1:
                                w0 = [("ac", v_exp[3])]
                            else:
                                w0 = [evo[pr % 2]]
                        em.emit("tensor",
                                lambda tn, m=m, k=k, half=half: tn.matmul(
                                    half, wOutTl[:, k, 128 * m:128 * (m + 1)],
                                    attn_fm[:, k, :], start=(k == 0), stop=False),
                                waits=w0)
                    em.emit("tensor",
                            lambda tn, m=m, half=half: tn.matmul(
                                half, globT8[0:8, 128 * m:128 * (m + 1)],
                                ind8_s[:], start=False, stop=True),
                            inc=("pe" if mm == 1 else None))
                if pr % 2 == 0:
                    em.emit("scalar",
                            lambda s_, pr=pr, bank=bank: s_.activation(
                                sum_fm[:, 2 * pr:2 * pr + 2, :].rearrange("p a k -> p (a k)"),
                                bank[:, 0:1024], AF.Copy, bias=0.0, scale=1.0),
                            waits=[("pe", em.v("pe"))], inc="ac")
                    evo[0] = ("ac", em.v("ac"))
                else:
                    em.emit("vector",
                            lambda v, pr=pr, bank=bank: v.tensor_copy(
                                sum_fm[:, 2 * pr:2 * pr + 2, :].rearrange("p a k -> p (a k)"),
                                bank[:, 0:1024]),
                            waits=[("pe", em.v("pe"))], inc="ve")
                    evo[1] = ("ve", em.v("ve"))
            v_out_pe = em.v("pe")
            v_sum_ve = em.v("ve")
            v_sum_ac = em.v("ac")

            evT2 = {0: 0, 1: 0}
            resid_ve = [0] * 4
            for c in range(4):
                bankT = ps_t if c % 2 == 0 else ps_t2
                for fc in range(8):
                    w0 = []
                    if fc == 0:
                        if c == 0:
                            w0 = [("ve", v_sum_ve), ("ac", v_sum_ac), ("ve", v_pTev[3])]
                        elif c == 1:
                            w0 = [("ve", v_pTev[3])]
                        else:
                            w0 = [("ve", evT2[c % 2])]
                    em.emit("tensor",
                            lambda tn, c=c, fc=fc, bankT=bankT: tn.transpose(
                                bankT[:, 128 * fc:128 * (fc + 1)],
                                sum_fm[:, fc, 128 * c:128 * (c + 1)], identb[:]),
                            waits=w0, inc=("pe" if fc == 7 else None))
                em.emit("vector",
                        lambda v, c=c, bankT=bankT, xb=xb: v.tensor_tensor(
                            xb[:, c, :], xb[:, c, :], bankT[:, 0:1024], ALU.add),
                        waits=[("pe", em.v("pe"))], inc="ve")
                evT2[c % 2] = em.v("ve")
                resid_ve[c] = em.v("ve")
            v_resid = em.v("ve")

            v_math2 = emit_ln_stats(xb, ln2_bufs, [],
                                    per_c_waits=[[("ve", resid_ve[c])] for c in range(4)])

            em.emit("sync",
                    lambda s_, t=t, xb=xb: s_.dma_start(
                        x1_dr.ap()[t * 512:(t + 1) * 512, :].rearrange("(c p) f -> p c f", p=128),
                        xb[:]),
                    waits=[("ve", v_resid)], inc="st", k=16)
            store_x1_prev = em.v("st")

            if t + 1 < NTILE:
                st = emit_head(t + 1, v_xcons, v_out_pe, v_sum_ac, v_math1)
                st["v_attnev_prev"] = v_attnev[3]

            v_ln2 = emit_ln_apply(xb, xln2_s, ln2_bufs, v_math2,
                                  apply_waits=[("st", store_xln2_prev)])
            if t == 0:
                cap("x1t0", xb[:].rearrange("p c f -> p (c f)"))
            x_cons_done[r] = [("ve", em.v("ve")), ("ac", em.v("ac")), ("st", store_x1_prev)]
            if r + 2 < 2 * NTILE:
                emit_x_load(r + 2)
            em.emit("sync",
                    lambda s_, t=t: s_.dma_start(
                        xln2_dr.ap()[t * 512:(t + 1) * 512, :].rearrange("(c p) f -> p c f", p=128),
                        xln2_s[:]),
                    waits=[("ve", v_ln2)], inc="st", k=16)
            store_xln2_prev = em.v("st")

            evT3 = {0: 0, 1: 0}
            for c in range(4):
                bankT = ps_t if c % 2 == 0 else ps_t2
                for fc in range(8):
                    w0 = []
                    if fc == 0:
                        w0 = ([("ve", v_ln2)] if c < 2 else [("ve", evT3[c % 2])])
                    em.emit("tensor",
                            lambda tn, c=c, fc=fc, bankT=bankT: tn.transpose(
                                bankT[:, 128 * fc:128 * (fc + 1)],
                                xln2_s[:, c, 128 * fc:128 * (fc + 1)], identb[:]),
                            waits=w0, inc=("pe" if fc == 7 else None))
                em.emit("vector",
                        lambda v, c=c, bankT=bankT: v.tensor_copy(
                            xln2_fm[:, 0:8, 128 * c:128 * (c + 1)],
                            bankT[:, 0:1024].rearrange("p (a k) -> p a k", a=8)),
                        waits=[("pe", em.v("pe"))], inc="ve")
                evT3[c % 2] = em.v("ve")
            v_x2fm = em.v("ve")

            for k in range(8):
                em.emit("tensor",
                        lambda tn, k=k: tn.matmul(
                            ps_C[0:8, 0:512], wr_s[:, k, :], xln2_fm[:, k, :],
                            start=(k == 0), stop=(k == 7)),
                        waits=([("ve", v_x2fm)] if k == 0 else []),
                        inc=("pe" if k == 7 else None))
            em.emit("vector",
                    lambda v: v.tensor_scalar(logit_em[:], ps_C[0:8, 0:512],
                                              br_c[:, 0:1], None, ALU.add),
                    waits=[("pe", em.v("pe"))], inc="ve")
            v_lg = em.v("ve")
            for c in range(4):
                em.emit("tensor",
                        lambda tn, c=c: tn.transpose(
                            ps_C[:, 512 + 8 * c:512 + 8 * (c + 1)],
                            logit_em[:, 128 * c:128 * (c + 1)], ident_f32_s[0:8, 0:8]),
                        waits=([("ve", v_lg)] if c == 0 else []),
                        inc=("pe" if c == 3 else None))
            v_lgt = em.v("pe")
            em.emit("vector", lambda v: v.tensor_copy(
                sm_tm[:].rearrange("p a e -> p (a e)"), ps_C[:, 512:544]),
                waits=[("pe", v_lgt)])
            em.emit("vector", lambda v: v.reduce_max(sm_mx[:], sm_tm[:], axis=AX.X))
            em.emit("vector", lambda v: v.tensor_tensor(
                sm_tm[:], sm_tm[:],
                sm_mx[:].rearrange("p (g o) -> p g o", o=1).broadcast_to([128, 4, 8]),
                ALU.subtract), inc="ve")
            em.emit("scalar", lambda s_: s_.activation(
                sm_tm[:].rearrange("p a e -> p (a e)"), sm_tm[:].rearrange("p a e -> p (a e)"),
                AF.Exp, bias=0.0, scale=1.0), waits=[("ve", em.v("ve"))], inc="ac")
            em.emit("vector", lambda v: v.reduce_sum(sm_sum[:], sm_tm[:], axis=AX.X),
                    waits=[("ac", em.v("ac"))])
            em.emit("vector", lambda v: v.reciprocal(sm_sum[:], sm_sum[:]))
            em.emit("vector", lambda v: v.tensor_tensor(
                sm_tm[:], sm_tm[:],
                sm_sum[:].rearrange("p (g o) -> p g o", o=1).broadcast_to([128, 4, 8]),
                ALU.mult), inc="ve")
            v_smtm = em.v("ve")
            for c in range(4):
                em.emit("tensor",
                        lambda tn, c=c: tn.transpose(
                            ps_C[0:8, 128 * c:128 * (c + 1)], sm_tm[:, c, :],
                            ident_f32_s[:]),
                        waits=([("ve", v_smtm)] if c == 0 else []),
                        inc=("pe" if c == 3 else None))
            em.emit("vector", lambda v: v.tensor_copy(sm_em[:], ps_C[0:8, 0:512]),
                    waits=[("pe", em.v("pe")), ("sm", store_sm_prev)], inc="ve")
            em.emit("gpsimd",
                    lambda g, t=t: g.dma_start(
                        sm_dr.ap()[:, t * 512:(t + 1) * 512], sm_em[:]),
                    waits=[("ve", em.v("ve"))], inc="sm", k=16)
            store_sm_prev = em.v("sm")


        v_all_st = em.v("st")
        v_all_sm = em.v("sm")

        @block.sync
        def _(sync):
            em.run_engine("sync", sync)

        @block.gpsimd
        def _(gpsimd):
            em.run_engine("gpsimd", gpsimd)
            gpsimd.wait_ge(h_st, v_all_st)
            gpsimd.wait_ge(h_sm, v_all_sm)

        @block.vector
        def _(vector):
            em.run_engine("vector", vector)

        @block.scalar
        def _(scalar):
            em.run_engine("scalar", scalar)

        @block.tensor
        def _(tensor):
            em.run_engine("tensor", tensor)

    return nc



def _consts_b():
    ident_bf = np.eye(128, dtype=ml_dtypes.bfloat16)
    tril = np.zeros((128, 128), dtype=ml_dtypes.bfloat16)
    for p in range(128):
        tril[p, p + 1:] = 1.0
    diagbig = (np.eye(128) * BIG).astype(ml_dtypes.bfloat16)
    onesb = np.ones((128, 128), dtype=ml_dtypes.bfloat16)
    onesf = np.ones((128, 128), dtype=np.float32)
    t = np.arange(NTOK)
    ids = np.stack([t >> 6, t & 63], axis=1).astype(ml_dtypes.bfloat16)
    ids = ids.reshape(64, 128, 2).transpose(1, 0, 2).copy()
    s = np.arange(CAP)
    im = np.zeros((CAP, 8), dtype=ml_dtypes.bfloat16)
    im[:, 0] = ((NTOK + (s % 128)) >> 6).astype(ml_dtypes.bfloat16)
    im[:, 1] = ((NTOK + (s % 128)) & 63).astype(ml_dtypes.bfloat16)
    return ident_bf, tril, diagbig, onesb, onesf, ids, im


def build_launch_b(dbg_stage=None):
    nc = bass.Bass("TRN2", target_bir_lowering=False, detect_race_conditions=False)

    x1_dr = nc.dram_tensor("x1", [NTOK, D], F32, kind="ExternalInput")
    xln2_dr = nc.dram_tensor("xln2", [NTOK, D], BF16, kind="ExternalInput")
    smg_dr = nc.dram_tensor("smg", [E, 65536], F32, kind="ExternalInput")
    smtm_dr = nc.dram_tensor("smtm", [NTOK, E], F32, kind="ExternalInput")
    wu_dr = nc.dram_tensor("wu", [4, D, FP], F32, kind="ExternalInput")
    buc_dr = nc.dram_tensor("bu_cols", [128, 4 * NFC], F32, kind="ExternalInput")
    wd_dr = nc.dram_tensor("wd", [E, FP, D], F32, kind="ExternalInput")
    bd_dr = nc.dram_tensor("bd", [E, D], F32, kind="ExternalInput")

    y_dr = nc.dram_tensor("y", [YROWS, D], F32, kind="ExternalOutput")
    dbg_dr = nc.dram_tensor("dbg", [128, 8192], F32, kind="ExternalOutput") if dbg_stage else None

    meta_dr = nc.dram_tensor("meta_scr", [E, CAP, 8], BF16, kind="Internal")
    wub16_dr = nc.dram_tensor("wub16", [4, D, FP], BF16, kind="Internal")
    wdb16_dr = nc.dram_tensor("wdb16", [E, FP, D], BF16, kind="Internal")

    identc, trilc, diagc, onesbc, onesfc, idsc, imc = _consts_b()
    ident_dr = nc.inline_tensor(identc, "ident_bf")
    tril_dr = nc.inline_tensor(trilc, "trilx")
    diag_dr = nc.inline_tensor(diagc, "diagbig")
    onesb_dr = nc.inline_tensor(onesbc, "onesb")
    onesf_dr = nc.inline_tensor(onesfc, "onesf")
    ids_dr = nc.inline_tensor(idsc.reshape(128, 128), "ids")
    im_dr = nc.inline_tensor(imc, "initmeta")

    em = EM()

    with ExitStack() as ctx:
        identb = ctx.enter_context(nc.sbuf_tensor("identb", [128, 128], BF16))
        tril_s = ctx.enter_context(nc.sbuf_tensor("tril_s", [128, 128], BF16))
        diag_s = ctx.enter_context(nc.sbuf_tensor("diag_s", [128, 128], BF16))
        onesb_s = ctx.enter_context(nc.sbuf_tensor("onesb_s", [128, 128], BF16))
        onesf_s = ctx.enter_context(nc.sbuf_tensor("onesf_s", [128, 128], F32))
        ids_s = ctx.enter_context(nc.sbuf_tensor("ids_s", [128, 64, 2], BF16))
        im_s = ctx.enter_context(nc.sbuf_tensor("im_s", [128, 10, 8], BF16))
        smtm_s = ctx.enter_context(nc.sbuf_tensor("smtm_s", [128, 64, 8], F32))
        lo_s = ctx.enter_context(nc.sbuf_tensor("lo_s", [128, 8], F32))
        hi_s = ctx.enter_context(nc.sbuf_tensor("hi_s", [128, 8], F32))
        mid_s = ctx.enter_context(nc.sbuf_tensor("mid_s", [128, 8], F32))
        cnt_s = ctx.enter_context(nc.sbuf_tensor("cnt_s", [128, 8], F32))
        flag_s = ctx.enter_context(nc.sbuf_tensor("flag_s", [128, 8], F32))
        tmp1_s = ctx.enter_context(nc.sbuf_tensor("tmp1_s", [128, 8], F32))
        tmp2_s = ctx.enter_context(nc.sbuf_tensor("tmp2_s", [128, 8], F32))
        trash_b = ctx.enter_context(nc.sbuf_tensor("trash_b", [128, 512], BF16))
        m_all = ctx.enter_context(nc.sbuf_tensor("m_all", [128, 64, 8], BF16))
        minv_all = ctx.enter_context(nc.sbuf_tensor("minv_all", [128, 64, 8], BF16))
        pos_all = ctx.enter_context(nc.sbuf_tensor("pos_all", [128, 64, 8], U32))
        meta_all = ctx.enter_context(nc.sbuf_tensor("meta_all", [128, 64, 8, 8], BF16))
        trow = ctx.enter_context(nc.sbuf_tensor("trow", [1, 512], F32))
        trow2 = ctx.enter_context(nc.sbuf_tensor("trow2", [1, 512], F32))
        offs = ctx.enter_context(nc.sbuf_tensor("offs", [1, 512], F32))
        meta_ld = ctx.enter_context(nc.sbuf_tensor("meta_ld", [128, 10, 8], BF16))
        idx_f = ctx.enter_context(nc.sbuf_tensor("idx_f", [128, 10], F32))
        idx_u2 = [ctx.enter_context(nc.sbuf_tensor(f"idx_u{i}", [128, 10], U32))
                  for i in range(2)]
        gates2 = [ctx.enter_context(nc.sbuf_tensor(f"gates{i}", [128, 10], F32))
                  for i in range(2)]
        bu_c = ctx.enter_context(nc.sbuf_tensor("bu_c", [128, 4 * NFC], F32))
        bd_r2 = [ctx.enter_context(nc.sbuf_tensor(f"bd_r{i}", [1, 1024], BF16))
                 for i in range(2)]
        bdb2 = [ctx.enter_context(nc.sbuf_tensor(f"bdb{i}", [128, 1024], BF16))
                for i in range(2)]
        wu_s = ctx.enter_context(nc.sbuf_tensor("wu_s", [128, 8, FP], BF16))
        wd_s = ctx.enter_context(nc.sbuf_tensor("wd_s", [128, NFC, 1024], BF16))
        xg_tm = ctx.enter_context(nc.sbuf_tensor("xg_tm", [128, 2, 1024], BF16))
        xg_fm = ctx.enter_context(nc.sbuf_tensor("xg_fm", [128, 8, CAP], BF16))
        mid_arena = ctx.enter_context(nc.sbuf_tensor("mid_arena", [128, NFC, CAP], BF16))
        contrib = ctx.enter_context(nc.sbuf_tensor("contrib", [128, 2, 1024], F32))
        ps_a = ctx.enter_context(nc.psum_tensor("ps_a", [128, 512], F32))
        ps_b = ctx.enter_context(nc.psum_tensor("ps_b", [128, 512], F32))
        ps_c = ctx.enter_context(nc.psum_tensor("ps_c", [128, 512], F32))
        ps_d = ctx.enter_context(nc.psum_tensor("ps_d", [128, 512], F32))
        ps_t = ctx.enter_context(nc.psum_tensor("ps_t", [128, 512], BF16))
        ps_t2 = ctx.enter_context(nc.psum_tensor("ps_t2", [128, 512], BF16))
        ps_x = ctx.enter_context(nc.psum_tensor("ps_x", [128, 8], F32))
        h_ld = ctx.enter_context(nc.semaphore("s_ld"))
        h_mi = ctx.enter_context(nc.semaphore("s_mi"))
        h_ml = ctx.enter_context(nc.semaphore("s_ml"))
        h_w = ctx.enter_context(nc.semaphore("s_w"))
        h_wc = ctx.enter_context(nc.semaphore("s_wc"))
        h_wu = ctx.enter_context(nc.semaphore("s_wu"))
        h_gx0 = ctx.enter_context(nc.semaphore("s_gx0"))
        h_gx1 = ctx.enter_context(nc.semaphore("s_gx1"))
        h_ys0 = ctx.enter_context(nc.semaphore("s_ys0"))
        h_ys1 = ctx.enter_context(nc.semaphore("s_ys1"))
        h_sc = [ctx.enter_context(nc.semaphore(f"s_sc{i}")) for i in range(E)]
        h_ve = ctx.enter_context(nc.semaphore("s_ve"))
        h_ac = ctx.enter_context(nc.semaphore("s_ac"))
        h_pe = ctx.enter_context(nc.semaphore("s_pe"))
        block = ctx.enter_context(nc.Block())
        for nm, hh in (("ld", h_ld), ("mi", h_mi), ("ml", h_ml), ("w", h_w),
                       ("wc", h_wc), ("wu", h_wu),
                       ("gx0", h_gx0), ("gx1", h_gx1), ("ys0", h_ys0), ("ys1", h_ys1),
                       ("ve", h_ve), ("ac", h_ac), ("pe", h_pe)):
            em.add_sem(nm, hh)
        for i in range(E):
            em.add_sem(f"sc{i}", h_sc[i])

        smg_v = mid_arena[:, 0:8, 0:1024].bitcast(F32)

        regs = {}

        def _mkregs(g):
            regs["cap"] = g.to_reg(CAP - 1)
            regs["ntok"] = g.to_reg(NTOK - 1)

        em.emit("gpsimd", _mkregs)

        def gp_dma(fn, sem, waits=()):
            em.emit("gpsimd", fn, waits=list(waits), inc=sem, k=16)
            return em.v(sem)

        def cap(stage, ap, extra_waits=()):
            if dbg_stage != stage:
                return
            def _cap_dma(g, ap=ap):
                dst = dbg_dr.ap()[0:ap.shape[0], 0:int(np.prod(ap.shape[1:]))]
                if len(ap.shape) == 3:
                    dst = dst.rearrange("p (a k) -> p a k", a=ap.shape[1])
                return g.dma_start(dst, ap)
            em.emit("gpsimd", _cap_dma,
                    waits=[("ve", em.v("ve")), ("ac", em.v("ac")), ("pe", em.v("pe"))]
                          + list(extra_waits),
                    inc="mi", k=16)

        for dst, src in (
            (identb[:], ident_dr.ap()),
            (tril_s[:], tril_dr.ap()),
            (diag_s[:], diag_dr.ap()),
            (onesb_s[:], onesb_dr.ap()),
            (onesf_s[:], onesf_dr.ap()),
            (ids_s[:].rearrange("p a b -> p (a b)"), ids_dr.ap()),
            (bu_c[:], buc_dr.ap()),
            (smtm_s[:], smtm_dr.ap().rearrange("(j p) e -> p j e", p=128)),
        ):
            em.emit("sync", lambda s, d=dst, a=src: s.dma_start(d, a), inc="ld", k=16)
        for e in range(E):
            em.emit("sync",
                    lambda s, e=e: s.dma_start(
                        smg_v[:, e, :], smg_dr.ap()[e, :].rearrange("(p f) -> p f", p=128)),
                    inc="ld", k=16)
        v_smg = em.v("ld")

        for q in range(4):
            em.emit("sync",
                    lambda s, q=q: s.dma_start(
                        y_dr.ap()[q * 2048:(q + 1) * 2048, :],
                        x1_dr.ap()[q * 2048:(q + 1) * 2048, :]),
                    inc="ld", k=16)
        v_yinit = em.v("ld")

        v_ims = gp_dma(
            lambda g: g.dma_start(im_s[:], im_dr.ap().rearrange("(m p) c -> p m c", p=128)),
            "mi")
        v_meta_init = 0
        for e in range(E):
            v_meta_init = gp_dma(
                lambda g, e=e: g.dma_start(
                    meta_dr.ap()[e].rearrange("(m p) c -> p m c", p=128), im_s[:]),
                "mi", waits=[("mi", v_ims)])

        v_wuc = [0] * 4
        v_wdc = [0] * E
        cast_order = [("u", 0), ("d", 0), ("d", 1), ("u", 1), ("d", 2), ("d", 3),
                      ("u", 2), ("d", 4), ("d", 5), ("u", 3), ("d", 6), ("d", 7)]
        for kind, i in cast_order:
            gate = [("wc", em.v("wc"))]
            if kind == "u":
                v_wuc[i] = gp_dma(
                    lambda g, i=i: g.dma_start(wub16_dr.ap()[i], wu_dr.ap()[i]), "wc",
                    waits=gate)
            else:
                v_wdc[i] = gp_dma(
                    lambda g, i=i: g.dma_start(wdb16_dr.ap()[i], wd_dr.ap()[i]), "wc",
                    waits=gate)
        v_wu = [0] * 4
        v_wd = [0] * E
        em.emit("sync",
                lambda s: s.dma_start(wu_s[:],
                                      wub16_dr.ap()[0].rearrange("(c p) f -> p c f", p=128)),
                waits=[("wc", v_wuc[0])], inc="wu", k=16)
        v_wu[0] = em.v("wu")
        em.emit("sync",
                lambda s: s.dma_start(wd_s[:],
                                      wdb16_dr.ap()[0].rearrange("(c p) f -> p c f", p=128)),
                waits=[("wc", v_wdc[0])], inc="w", k=16)
        v_wd[0] = em.v("w")

        em.emit("vector", lambda v: v.memset(xg_tm[:].rearrange("p a b -> p (a b)"), 0.0))
        em.emit("vector", lambda v: v.memset(xg_fm[:].rearrange("p a b -> p (a b)"), 0.0))
        em.emit("vector", lambda v: v.memset(lo_s[:], 0.0))
        em.emit("vector", lambda v: v.memset(tmp1_s[:], 1.0), inc="ve")
        v_binit = em.v("ve")

        for it in range(BISECT_ITERS):
            w_cur = tmp1_s if it % 2 == 0 else tmp2_s
            w_half = tmp2_s if it % 2 == 0 else tmp1_s
            em.emit("vector",
                    lambda v, w_cur=w_cur, w_half=w_half: v.tensor_scalar(
                        w_half[:], w_cur[:], 0.5, None, ALU.mult),
                    waits=([("ld", v_smg), ("ve", v_binit)] if it == 0 else []))
            em.emit("vector",
                    lambda v, w_half=w_half: v.tensor_tensor(
                        mid_s[:], lo_s[:], w_half[:], ALU.add),
                    inc="ve")
            for e in range(E):
                em.emit("vector",
                        lambda v, e=e: v.tensor_scalar(
                            trash_b[:], smg_v[:, e, :], mid_s[:, e:e + 1], 0.0,
                            ALU.is_ge, ALU.add, accum_out=cnt_s[:, e:e + 1]),
                        inc=("ve" if e == E - 1 else None))
            v_cnt = em.v("ve")
            em.emit("tensor",
                    lambda tn: tn.matmul(ps_x[:], onesf_s[:], cnt_s[:], start=True, stop=True),
                    waits=[("ve", v_cnt)], inc="pe")
            v_cntmm = em.v("pe")
            em.emit("vector",
                    lambda v: v.tensor_scalar(flag_s[:], ps_x[:], float(65536 // E), 0.0,
                                              ALU.is_ge, ALU.add),
                    waits=[("pe", v_cntmm)])
            em.emit("vector",
                    lambda v, w_half=w_half: v.tensor_tensor(
                        flag_s[:], flag_s[:], w_half[:], ALU.mult))
            em.emit("vector", lambda v: v.tensor_tensor(lo_s[:], lo_s[:], flag_s[:], ALU.add),
                    inc="ve")

        v_theta = em.v("ve")
        cap("theta", lo_s[:])
        cap("smgchk", smg_v[:, 0:8, :], [("ld", v_smg)])

        em.emit("vector",
                lambda v: v.tensor_tensor(
                    m_all[:], smtm_s[:],
                    lo_s[:].rearrange("p (o e) -> p o e", o=1).broadcast_to([128, 64, E]),
                    ALU.is_ge),
                waits=[("ve", v_theta)])
        em.emit("vector",
                lambda v: v.tensor_scalar(minv_all[:], m_all[:], -1.0, 1.0,
                                          ALU.mult, ALU.add),
                inc="ve")
        em.emit("tensor",
                lambda tn: tn.matmul(ps_a[0:64, 0:512], onesb_s[:, 0:64],
                                     m_all[:].rearrange("p j e -> p (j e)"),
                                     start=True, stop=True),
                waits=[("ve", em.v("ve"))], inc="pe")
        v_tot = em.v("pe")
        if dbg_stage == "tots":
            em.emit("vector", lambda v: v.tensor_copy(trash_b[0:64, :], ps_a[0:64, 0:512]),
                    waits=[("pe", v_tot)], inc="ve")
            cap("tots", trash_b[0:64, :])
        em.emit("vector", lambda v: v.tensor_copy(trow[:], ps_a[0:1, 0:512]),
                waits=[("pe", v_tot)], inc="ve")
        cap("trow", trow[:])

        cur, nxt = trow, trow2
        for st in (8, 16, 32, 64, 128, 256):
            em.emit("vector", lambda v, cur=cur, nxt=nxt, st=st: v.tensor_copy(
                nxt[:, 0:st], cur[:, 0:st]))
            em.emit("vector", lambda v, cur=cur, nxt=nxt, st=st: v.tensor_tensor(
                nxt[:, st:512], cur[:, st:512], cur[:, 0:512 - st], ALU.add), inc="ve")
            cur, nxt = nxt, cur
        em.emit("vector", lambda v: v.memset(offs[:, 0:8], 0.0))
        em.emit("vector", lambda v, cur=cur: v.tensor_copy(offs[:, 8:512], cur[:, 0:504]),
                inc="ve")
        v_offs = em.v("ve")
        cap("m0", m_all[:])
        cap("offs", offs[:])

        em.emit("tensor",
                lambda tn: tn.matmul(ps_b[:, 0:512], tril_s[:],
                                     m_all[:].rearrange("p j e -> p (j e)"),
                                     start=True, stop=False),
                waits=[("ve", v_offs)])
        em.emit("tensor",
                lambda tn: tn.matmul(ps_b[:, 0:512], onesf_s[0:1, :], offs[:],
                                     start=False, stop=False))
        em.emit("tensor",
                lambda tn: tn.matmul(ps_b[:, 0:512], diag_s[:],
                                     minv_all[:].rearrange("p j e -> p (j e)"),
                                     start=False, stop=True), inc="pe")
        v_pos = em.v("pe")
        em.emit("vector",
                lambda v: v.tensor_copy(pos_all[:].rearrange("p j e -> p (j e)"),
                                        ps_b[:, 0:512]),
                waits=[("pe", v_pos)])
        em.emit("vector",
                lambda v: v.tensor_copy(
                    meta_all[:, :, :, 0],
                    ids_s[:, :, 0:1].broadcast_to([128, 64, 8])))
        em.emit("vector",
                lambda v: v.tensor_copy(
                    meta_all[:, :, :, 1],
                    ids_s[:, :, 1:2].broadcast_to([128, 64, 8])))
        em.emit("vector",
                lambda v: v.tensor_copy(meta_all[:, :, :, 2], smtm_s[:]))
        em.emit("vector",
                lambda v: v.tensor_tensor(meta_all[:, :, :, 3], smtm_s[:],
                                          meta_all[:, :, :, 2], ALU.subtract),
                inc="ve")
        v_sel = em.v("ve")
        em.emit("vector", lambda v: v.memset(meta_all[:, :, :, 4:8], 0.0), inc="ve")
        v_meta = em.v("ve")
        cap("pos", pos_all[:].bitcast(F32))
        cap("meta8", meta_all[:, :, :, 0:8].rearrange("p j e c -> p (j e c)"))

        v_scat = [0] * E

        def emit_meta_scatter_chunk(e, jc):
            for j in range(8 * jc, 8 * jc + 8):
                v_scat[e] = gp_dma(
                    lambda g, e=e, j=j: g.indirect_dma_start(
                        meta_dr.ap().rearrange("e m c -> (e m) c"),
                        IndirectOffsetOnAxis(pos_all[:, j, e:e + 1], 0),
                        meta_all[:, j, e, :], None,
                        element_offset=e * CAP * 8,
                        bounds_check=regs["cap"], oob_is_err=False),
                    f"sc{e}",
                    waits=([("ve", v_meta), ("mi", v_meta_init)] if j == 0 else []))

        for jc in range(8):
            emit_meta_scatter_chunk(0, jc)

        cap("meta0", meta_dr.ap()[0].rearrange("(m p) c -> p m c", p=128),
            [(f"sc{e}", v_scat[e]) for e in range(E)] + [("mi", v_meta_init)])

        v_meta_ld = [0] * E
        v_y_final = 0
        v_prev_dec = 0
        scat_end_by_e = {}
        head_st = {}

        def emit_head_gp(e):
            par = e % 2
            s = {"par": par}
            em.emit("sync",
                    lambda s_, e=e: s_.dma_start(
                        meta_ld[:], meta_dr.ap()[e].rearrange("(m p) c -> p m c", p=128)),
                    waits=[(f"sc{e}", v_scat[e]), ("ve", v_prev_dec)], inc="ml", k=16)
            v_ml = em.v("ml")
            dec_w = [("ml", v_ml)] + list(scat_end_by_e.get(e - 2, []))
            em.emit("vector",
                    lambda v: v.tensor_scalar(idx_f[:], meta_ld[:, :, 0], 64.0, None, ALU.mult),
                    waits=dec_w)
            em.emit("vector",
                    lambda v: v.tensor_tensor(idx_f[:], idx_f[:], meta_ld[:, :, 1], ALU.add))
            em.emit("vector", lambda v, par=par: v.tensor_copy(idx_u2[par][:], idx_f[:]))
            em.emit("vector",
                    lambda v, par=par: v.tensor_tensor(
                        gates2[par][:], meta_ld[:, :, 2], meta_ld[:, :, 3], ALU.add),
                    inc="ve")
            s["v_dec"] = em.v("ve")
            s["v_bdr"] = gp_dma(lambda g, e=e, par=par: g.dma_start(
                bd_r2[par][:], bd_dr.ap()[e, :].rearrange("(o f) -> o f", o=1)), "mi",
                waits=[("mi", em.v("mi"))])
            xg_done = head_st.get("xg_done", [0, 0])
            s["v_g"] = [0, 0]
            for m in range(2):
                buf = m % 2
                s["v_g"][buf] = gp_dma(
                    lambda g, m=m, buf=buf, par=par: g.indirect_dma_start(
                        xg_tm[:, buf, :], None, xln2_dr.ap(),
                        IndirectOffsetOnAxis(idx_u2[par][:, m:m + 1], 0),
                        bounds_check=regs["ntok"], oob_is_err=False),
                    f"gx{buf}",
                    waits=[("ve", s["v_dec"]), ("pe", xg_done[buf])])
            return s

        def emit_head_pe(e, s):
            par = s["par"]
            for hhalf in range(2):
                em.emit("tensor",
                        lambda tn, hhalf=hhalf, par=par: tn.matmul(
                            ps_a[:, 0:512],
                            onesb_s[0:1, :], bd_r2[par][0:1, 512 * hhalf:512 * (hhalf + 1)],
                            start=True, stop=True),
                        waits=[("mi", s["v_bdr"]), ("ve", em.v("ve"))], inc="pe")
                em.emit("vector",
                        lambda v, hhalf=hhalf, par=par: v.tensor_copy(
                            bdb2[par][:, 512 * hhalf:512 * (hhalf + 1)], ps_a[:, 0:512]),
                        waits=[("pe", em.v("pe"))], inc="ve")
            s["v_bdb"] = em.v("ve")
            ev_par = {0: 0, 1: 0}
            xg_done = [0, 0]
            v_g_of = {0: s["v_g"][0], 1: s["v_g"][1]}
            for m in range(10):
                buf = m % 2
                if m >= 2:
                    v_g_of[buf] = gp_dma(
                        lambda g, m=m, buf=buf, par=s["par"]: g.indirect_dma_start(
                            xg_tm[:, buf, :], None, xln2_dr.ap(),
                            IndirectOffsetOnAxis(idx_u2[par][:, m:m + 1], 0),
                            bounds_check=regs["ntok"], oob_is_err=False),
                        f"gx{buf}",
                        waits=[("pe", xg_done[buf])])
                v_g_m = v_g_of[buf]
                for hb in range(2):
                    bank = ps_t if hb == 0 else ps_t2
                    for q in range(4):
                        k = 4 * hb + q
                        em.emit("tensor",
                                lambda tn, m=m, buf=buf, k=k, q=q, bank=bank: tn.transpose(
                                    bank[:, 128 * q:128 * (q + 1)],
                                    xg_tm[:, buf, 128 * k:128 * (k + 1)], identb[:]),
                                waits=(([(f"gx{buf}", v_g_m), ("ve", ev_par[hb])]
                                        if q == 0 else [])),
                                inc=("pe" if q == 3 else None))
                    em.emit("vector",
                            lambda v, m=m, hb=hb, bank=bank: v.tensor_copy(
                                xg_fm[:, 4 * hb:4 * hb + 4, 128 * m:128 * (m + 1)],
                                bank[:, 0:512].rearrange("p (a k) -> p a k", a=4)),
                            waits=[("pe", em.v("pe"))], inc="ve")
                    ev_par[hb] = em.v("ve")
                xg_done[buf] = em.v("pe")
            head_st["xg_done"] = xg_done
            s["v_xg"] = em.v("ve")

        hd = emit_head_gp(0)
        emit_head_pe(0, hd)
        for e in range(E):
            g_grp = e // 2
            par = hd["par"]
            v_dec = hd["v_dec"]
            v_bdb = hd["v_bdb"]
            v_xg = hd["v_xg"]
            v_prev_dec = v_dec

            ev_up = {0: 0, 1: 0}
            nn_off = (0, 512, 1024)
            nn_sz = (512, 512, 256)
            for fc in range(NFC):
                if e + 1 < E and fc % 3 == 0 and fc // 3 < 8:
                    emit_meta_scatter_chunk(e + 1, fc // 3)
                for nn in range(3):
                    gi = fc * 3 + nn
                    bank = ps_a if gi % 2 == 0 else ps_b
                    for k in range(8):
                        em.emit("tensor",
                                lambda tn, fc=fc, nn=nn, k=k, bank=bank: tn.matmul(
                                    bank[:, 0:nn_sz[nn]],
                                    wu_s[:, k, 128 * fc:128 * (fc + 1)],
                                    xg_fm[:, k, nn_off[nn]:nn_off[nn] + nn_sz[nn]],
                                    start=(k == 0), stop=(k == 7)),
                                waits=(([("ve", v_xg), ("wu", v_wu[g_grp]),
                                         ("ac", ev_up[gi % 2])] if k == 0 else [])),
                                inc=("pe" if k == 7 else None))
                    em.emit("scalar",
                            lambda s_, fc=fc, nn=nn, g_grp=g_grp, bank=bank: s_.activation(
                                mid_arena[:, fc, nn_off[nn]:nn_off[nn] + nn_sz[nn]],
                                bank[:, 0:nn_sz[nn]], AF.Silu,
                                bias=bu_c[:, NFC * g_grp + fc:NFC * g_grp + fc + 1],
                                scale=1.0),
                            waits=[("pe", em.v("pe")), ("ve", v_theta), ("ld", v_smg)],
                            inc="ac")
                    ev_up[gi % 2] = em.v("ac")
            v_up = em.v("ac")
            v_up_pe = em.v("pe")

            if e + 1 < E and (e + 1) % 2 == 0:
                em.emit("sync",
                        lambda s, gg=g_grp + 1: s.dma_start(
                            wu_s[:], wub16_dr.ap()[gg].rearrange("(c p) f -> p c f", p=128)),
                        waits=[("pe", v_up_pe), ("wc", v_wuc[g_grp + 1]),
                               ("wu", v_wu[g_grp])], inc="wu", k=16)
                v_wu[g_grp + 1] = em.v("wu")

            ev_dn = {0: 0, 1: 0}
            scat_done = [0, 0]
            for m in range(10):
                buf = m % 2
                if e + 1 < E and m == 2:
                    hd = emit_head_gp(e + 1)
                for hhalf in range(2):
                    bank = ps_c if hhalf == 0 else ps_d
                    for k in range(NFC):
                        em.emit("tensor",
                                lambda tn, m=m, hhalf=hhalf, k=k, bank=bank: tn.matmul(
                                    bank[:, 0:512],
                                    mid_arena[:, k, 128 * m:128 * (m + 1)],
                                    wd_s[:, k, 512 * hhalf:512 * (hhalf + 1)],
                                    start=(k == 0), stop=(k == NFC - 1)),
                                waits=(([("ac", v_up), ("w", v_wd[e]),
                                         ("ve", ev_dn[hhalf])] if k == 0 else [])),
                                inc=("pe" if k == NFC - 1 else None))
                    em.emit("vector",
                            lambda v, m=m, buf=buf, hhalf=hhalf, bank=bank, par=par: v.tensor_tensor(
                                contrib[:, buf, 512 * hhalf:512 * (hhalf + 1)],
                                bank[:, 0:512], bdb2[par][:, 512 * hhalf:512 * (hhalf + 1)],
                                ALU.add),
                            waits=[("pe", em.v("pe")), ("ve", v_bdb),
                                   (f"ys{buf}", scat_done[buf])])
                    em.emit("vector",
                            lambda v, m=m, buf=buf, hhalf=hhalf, par=par: v.tensor_scalar(
                                contrib[:, buf, 512 * hhalf:512 * (hhalf + 1)],
                                contrib[:, buf, 512 * hhalf:512 * (hhalf + 1)],
                                gates2[par][:, m:m + 1], None, ALU.mult),
                            inc="ve")
                    ev_dn[hhalf] = em.v("ve")
                v_ct = em.v("ve")
                v_y_final = gp_dma(
                    lambda g, m=m, buf=buf, par=par: g.indirect_dma_start(
                        y_dr.ap(), IndirectOffsetOnAxis(idx_u2[par][:, m:m + 1], 0),
                        contrib[:, buf, :], None,
                        compute_op=ALU.add, oob_is_err=True),
                    f"ys{buf}",
                    waits=[("ve", v_ct)] + ([("ld", v_yinit)] if (e == 0 and m == 0) else []))
                scat_done[buf] = v_y_final
            scat_end_by_e[e] = [("ys0", em.v("ys0")), ("ys1", em.v("ys1"))]

            if e + 1 < E:
                em.emit("sync",
                        lambda s, ee=e + 1: s.dma_start(
                            wd_s[:], wdb16_dr.ap()[ee].rearrange("(c p) f -> p c f", p=128)),
                        waits=[("pe", em.v("pe")), ("wc", v_wdc[e + 1]),
                               ("w", v_wd[e])], inc="w", k=16)
                v_wd[e + 1] = em.v("w")
                emit_head_pe(e + 1, hd)


        v_fin_ys0 = em.v("ys0")
        v_fin_ys1 = em.v("ys1")
        v_fin_mi = em.v("mi")

        @block.sync
        def _(sync):
            em.run_engine("sync", sync)

        @block.gpsimd
        def _(gpsimd):
            em.run_engine("gpsimd", gpsimd)
            gpsimd.wait_ge(h_ys0, v_fin_ys0)
            gpsimd.wait_ge(h_ys1, v_fin_ys1)
            gpsimd.wait_ge(h_mi, v_fin_mi)

        @block.vector
        def _(vector):
            em.run_engine("vector", vector)

        @block.scalar
        def _(scalar):
            em.run_engine("scalar", scalar)

        @block.tensor
        def _(tensor):
            em.run_engine("tensor", tensor)

    return nc



_CACHE = {}


def _get_programs():
    if "a" not in _CACHE:
        _CACHE["a"] = build_launch_a()
    if "b" not in _CACHE:
        _CACHE["b"] = build_launch_b()
    return _CACHE["a"], _CACHE["b"]


def kernel(**inputs):
    x = np.ascontiguousarray(np.asarray(inputs["L0_attn_x"], dtype=np.float32))
    w_in_l = np.asarray(inputs["L0_attn_w_in_local"], dtype=np.float32)
    b_in_l = np.asarray(inputs["L0_attn_b_in_local"], dtype=np.float32)
    w_out_l = np.asarray(inputs["L0_attn_w_out_local"], dtype=np.float32)
    b_out_l = np.asarray(inputs["L0_attn_b_out_local"], dtype=np.float32)
    w_in_g = np.asarray(inputs["L0_attn_w_in_global"], dtype=np.float32)
    b_in_g = np.asarray(inputs["L0_attn_b_in_global"], dtype=np.float32)
    w_out_g = np.asarray(inputs["L0_attn_w_out_global"], dtype=np.float32)
    b_out_g = np.asarray(inputs["L0_attn_b_out_global"], dtype=np.float32)
    wr = np.asarray(inputs["L0_ffn_Wr"], dtype=np.float32)
    br = np.asarray(inputs["L0_ffn_br"], dtype=np.float32)
    wu = np.asarray(inputs["L0_ffn_W_up"], dtype=np.float32)
    bu = np.asarray(inputs["L0_ffn_b_up"], dtype=np.float32)
    wd = np.asarray(inputs["L0_ffn_W_down"], dtype=np.float32)
    bd = np.asarray(inputs["L0_ffn_b_down"], dtype=np.float32)

    wInT_l = np.ascontiguousarray(w_in_l.T)
    wOutT_l = np.ascontiguousarray(w_out_l.T)
    wInT_g = np.ascontiguousarray(w_in_g.T)
    wOutT_g = np.ascontiguousarray(w_out_g.T)
    bInG_c = np.ascontiguousarray(b_in_g.reshape(24, 128).T)
    bOutL_c = np.ascontiguousarray(b_out_l.reshape(8, 128).T)
    bOutG_c = np.ascontiguousarray(b_out_g.reshape(8, 128).T)
    bInL_r = np.ascontiguousarray(b_in_l.reshape(1, 3 * D))
    bInG_r = np.ascontiguousarray(b_in_g.reshape(1, 3 * D))
    wr_l = np.ascontiguousarray(wr.reshape(8, 128, E).transpose(1, 0, 2))
    br_c = br.reshape(E, 1)

    wu_p = np.zeros((4, D, FP), dtype=np.float32)
    wu_p[:, :, :FTRUE] = wu
    bu_p = np.zeros((4, FP), dtype=np.float32)
    bu_p[:, :FTRUE] = bu
    bu_cols = np.ascontiguousarray(
        bu_p.reshape(4, NFC, 128).transpose(2, 0, 1).reshape(128, 4 * NFC))
    wd_p = np.zeros((E, FP, D), dtype=np.float32)
    wd_p[:, :FTRUE, :] = wd

    nca, ncb = _get_programs()

    in_maps_a = []
    for c in range(8):
        in_maps_a.append({
            "x": np.ascontiguousarray(x[2 * c:2 * c + 2].reshape(NTOK, D)),
            "wInT_l": wInT_l, "wOutT_l": wOutT_l,
            "wInT_g": wInT_g, "wOutT_g": wOutT_g,
            "bInG_c": bInG_c,
            "bOutL_c": bOutL_c, "bOutG_c": bOutG_c,
            "bInL_r": bInL_r, "bInG_r": bInG_r,
            "wr_l": wr_l, "br_c": br_c,
        })
    res_a = run_bass_kernel_spmd(nca, in_maps_a, core_ids=list(range(8)))

    sm_all = np.ascontiguousarray(
        np.concatenate([res_a.results[c]["sm"] for c in range(8)], axis=1))

    in_maps_b = []
    for c in range(8):
        in_maps_b.append({
            "x1": res_a.results[c]["x1"],
            "xln2": res_a.results[c]["xln2"],
            "smg": sm_all,
            "smtm": np.ascontiguousarray(res_a.results[c]["sm"].T),
            "wu": wu_p, "bu_cols": bu_cols, "wd": wd_p, "bd": bd,
        })
    res_b = run_bass_kernel_spmd(ncb, in_maps_b, core_ids=list(range(8)))

    y = np.concatenate(
        [res_b.results[c]["y"][0:NTOK].reshape(2, 4096, D) for c in range(8)], axis=0)
    return y
